# version 03
# baseline (speedup 1.0000x reference)
import contextlib
import numpy as np
import ml_dtypes
import concourse.bass as bass
import concourse.mybir as mybir
from concourse.bass_utils import run_bass_kernel_spmd

F32 = mybir.dt.float32
BF16 = mybir.dt.bfloat16
I32 = mybir.dt.int32
AF = mybir.ActivationFunctionType
ALU = mybir.AluOpType
AX = mybir.AxisListType
BF = ml_dtypes.bfloat16
RG = [[0, 1, 2, 3], [4, 5, 6, 7]]
ENGS = ["sync", "scalar", "vector", "tensor", "gpsimd"]
RSQRT_SCALE = float(1.0 / np.sqrt(128.0))


class P:

    def __init__(self, nc, stack):
        self.nc = nc
        self.sems = {e: stack.enter_context(nc.semaphore(f"S_{e}")) for e in ENGS}
        self.cnt = {e: 0 for e in ENGS}
        self.lastw = {}
        self.eng = {e: getattr(nc, e) for e in ENGS}

    def emit(self, ename, instr, dma=False, cc=False):
        n = 16 if dma else 1
        instr.then_inc(self.sems[ename], n)
        self.cnt[ename] += n
        return (ename, self.cnt[ename], dma or cc)

    def wait(self, consumer, *tokens):
        for t in tokens:
            if t is None:
                continue
            pe, v, _ = t
            key = (consumer, pe)
            if self.lastw.get(key, -1) >= v:
                continue
            self.eng[consumer].wait_ge(self.sems[pe], v)
            self.lastw[key] = v

    def tok(self, ename):
        return (ename, self.cnt[ename], False)


def build():
    nc = bass.Bass("TRN2", target_bir_lowering=False)

    def din(name, shape, dt=F32):
        return nc.dram_tensor(name, shape, dt, kind="ExternalInput").ap()

    x_own = din("x_own", [1024, 2048])
    wq_o = din("wq_o", [128, 8192])
    wk_o = din("wk_o", [128, 2048])
    wv_o = din("wv_o", [128, 2048])
    bq_o = din("bq_o", [128, 4])
    bk_o = din("bk_o", [128, 1])
    bv_b = din("bv_b", [128, 128])
    oidx = din("oidx", [128, 32], I32)
    g0 = din("g0", [44, 128, 2048])
    u0 = din("u0", [44, 128, 2048])
    d0 = din("d0", [44, 128, 2048])
    gs0 = din("gs0", [128, 44])
    us0 = din("us0", [128, 44])
    ds_b = din("ds_b", [128, 2048])
    w1q = din("w1q", [16, 128, 2048])
    b1q = din("b1q", [128, 16])
    w1kv = din("w1kv", [2, 4, 128, 8192])
    b1kv_b = din("b1kv_b", [2, 4, 128, 512])
    w1o = din("w1o", [4, 128, 8192])
    b1o_b = din("b1o_b", [4, 128, 512])
    g1a_i = din("g1a", [128, 16])
    be1_i = din("be1", [128, 16])
    clf = din("clf", [128, 192])
    eb_b = din("eb_b", [128, 12])
    g1 = din("g1", [8, 128, 8, 2048])
    u1 = din("u1", [8, 128, 8, 2048])
    d1 = din("d1", [8, 128, 8, 2048])
    rcos = din("rcos", [2, 8, 128, 512])
    rsin = din("rsin", [2, 8, 128, 512])

    out_x = nc.dram_tensor("out_x", [1024, 2048], F32, kind="ExternalOutput").ap()

    ag_in = nc.dram_tensor("ag_in", [2048, 1024], BF16)
    ag_out = nc.dram_tensor("ag_out", [8192, 1024], BF16)
    rs_in = nc.dram_tensor("rs_in", [16384, 512], BF16)
    rs_out = nc.dram_tensor("rs_out", [4096, 512], BF16)
    arA_in = nc.dram_tensor("arA_in", [128, 16], F32)
    arA_out = nc.dram_tensor("arA_out", [128, 16], F32)
    ppd = nc.dram_tensor("ppd", [8, 128, 2048], BF16)

    stack = contextlib.ExitStack()
    p = P(nc, stack)
    E, W = p.emit, p.wait
    SY, SC, VE, TE, GP = nc.sync, nc.scalar, nc.vector, nc.tensor, nc.gpsimd

    def sb(name, shape, dt=F32, st=stack):
        return st.enter_context(nc.sbuf_tensor(name, shape, dt))

    def ps(name, shape, dt=F32, st=stack):
        return st.enter_context(nc.psum_tensor(name, shape, dt))

    def fence(q, engs=("tensor", "vector", "scalar")):
        W(q, *[p.tok(e) for e in engs])

    x_sb = sb("x_sb", [128, 8 * 2048])
    fmA = sb("fmA", [128, 16384], BF16)
    iota = sb("iota", [128, 512], I32)
    masks = sb("masks", [128, 4 * 512])
    ident = sb("ident", [128, 128], BF16)
    bqs = sb("bqs", [128, 4]); bks = sb("bks", [128, 1])
    bvbc = sb("bvbc", [128, 128]); oix = sb("oix", [128, 32], I32)
    gss = sb("gss", [128, 44]); uss = sb("uss", [128, 44])
    b1qs = sb("b1qs", [128, 16])
    g1as = sb("g1as", [128, 16]); be1s = sb("be1s", [128, 16])
    clfs = sb("clfs", [128, 192], BF16); ebbc = sb("ebbc", [128, 12])
    stats_m = sb("stats_m", [128, 8]); stats_r = sb("stats_r", [128, 8])
    stats_v = sb("stats_v", [128, 8])
    s2c = sb("s2c", [128, 8]); o2c = sb("o2c", [128, 8])
    w_all = sb("w_all", [128, 96])
    A1 = sb("A1", [128, 16]); A_raw = sb("A_raw", [128, 16])
    ssK = sb("ssK", [128, 8]); ssV = sb("ssV", [128, 8])
    scol = sb("scol", [128, 2], BF16)
    tmp1 = sb("tmp1", [128, 1]); tmp2 = sb("tmp2", [128, 1]); tmp3 = sb("tmp3", [128, 1])

    t_x = [E("sync", SY.dma_start(out=x_sb[:, ti * 2048:(ti + 1) * 2048],
                                  in_=x_own[ti * 128:(ti + 1) * 128, :]), dma=True)
           for ti in range(8)]
    smalls = [(bqs, bq_o), (bks, bk_o), (bvbc, bv_b), (gss, gs0), (uss, us0),
              (b1qs, b1q), (g1as, g1a_i), (be1s, be1_i), (ebbc, eb_b)]
    t_small = None
    for buf, src in smalls:
        t_small = E("sync", SY.dma_start(out=buf[:], in_=src), dma=True)
    t_oix = E("sync", SY.dma_start(out=oix[:], in_=oidx), dma=True)
    t_clf = E("gpsimd", GP.dma_start(out=clfs[:], in_=clf), dma=True)
    t_io = E("gpsimd", GP.iota(out=iota[:], pattern=[[-1, 512]], base=0,
                               channel_multiplier=1))
    W("vector", t_io)
    t_id = E("vector", VE.tensor_scalar(out=ident[:], in0=iota[:, 0:128], scalar1=0.0,
                                        scalar2=None, op0=ALU.is_equal))
    t_mk = []
    for jm in range(4):
        t_mk.append(E("vector", VE.tensor_scalar(
            out=masks[:, jm * 512:(jm + 1) * 512], in0=iota[:], scalar1=float(-128 * jm),
            scalar2=-1e30, op0=ALU.is_gt, op1=ALU.mult)))

    def ln_stats(ti):
        xt = x_sb[:, ti * 2048:(ti + 1) * 2048]
        t_sum = E("vector", VE.reduce_sum(out=tmp1[:], in_=xt, axis=AX.X))
        t_ssq = E("scalar", SC.activation(out=sq_scratch[0][:], in_=xt, func=AF.Square,
                                          accum_out=tmp2[:]))
        W("vector", t_sum)
        t_m = E("vector", VE.tensor_scalar(out=stats_m[:, ti:ti + 1], in0=tmp1[:],
                                           scalar1=1.0 / 2048.0, scalar2=None, op0=ALU.mult))
        W("vector", t_m, t_ssq)
        t_a = E("vector", VE.tensor_tensor(out=tmp1[:], in0=stats_m[:, ti:ti + 1],
                                           in1=stats_m[:, ti:ti + 1], op=ALU.mult))
        t_b = E("vector", VE.tensor_scalar(out=tmp3[:], in0=tmp2[:],
                                           scalar1=1.0 / 2048.0, scalar2=None, op0=ALU.mult))
        W("vector", t_a, t_b)
        t_v = E("vector", VE.tensor_tensor(out=stats_v[:, ti:ti + 1], in0=tmp3[:],
                                           in1=tmp1[:], op=ALU.subtract))
        W("vector", t_v)
        t_e = E("vector", VE.tensor_scalar(out=tmp1[:], in0=stats_v[:, ti:ti + 1],
                                           scalar1=1e-5, scalar2=None, op0=ALU.add))
        W("scalar", t_e)
        t_s = E("scalar", SC.activation(out=tmp2[:], in_=tmp1[:], func=AF.Sqrt))
        W("vector", t_s)
        t_r = E("vector", VE.reciprocal(stats_r[:, ti:ti + 1], tmp2[:]))
        return t_r, t_ssq

    def build_fmT(dest_fn, stage_buf, t_stage, tr_pss):
        toks = []
        t_tr = None
        for ec in range(16):
            b = ec % len(tr_pss)
            tp, t_ev_prev = tr_pss[b]
            W("tensor", t_stage, t_ev_prev, t_id)
            t_tr = E("tensor", TE.transpose(tp[:], stage_buf[:, ec * 128:(ec + 1) * 128],
                                            ident[:]))
            t_ev = dest_fn(tp, ec, t_tr)
            tr_pss[b] = (tp, t_ev)
            toks.append(t_ev)
        return toks, t_tr

    stA = contextlib.ExitStack()
    xn_stage = [sb(f"xn_st{i}", [128, 2048], BF16, st=stA) for i in range(2)]
    sq_scratch = [sb("sq_scr", [128, 2048], BF16, st=stA)]
    zbuf = sb("zbuf", [128, 2048], BF16, st=stA)
    tr_ps_l = [(ps(f"trp{i}", [128, 128], BF16, st=stA), None) for i in range(4)]
    t_zb = E("gpsimd", GP.memset(zbuf[:], 0.0))
    W("sync", t_zb)
    t_zf = None
    for k in range(32):
        t_zf = E("sync", SY.dma_start(
            out=rs_in.ap()[k * 512:(k + 1) * 512, :].rearrange("(a p) m -> p a m", p=128),
            in_=zbuf[:].rearrange("p (a m) -> p a m", a=4)), dma=True)
    t_fmA_done = []
    stage_rd = [None, None]
    for ti in range(8):
        W("vector", t_x[ti])
        W("scalar", t_x[ti])
        t_r, _ = ln_stats(ti)
        sbuf_st = xn_stage[ti % 2]
        W("vector", t_r, stage_rd[ti % 2])
        t_st = E("vector", VE.tensor_scalar(out=sbuf_st[:],
                                            in0=x_sb[:, ti * 2048:(ti + 1) * 2048],
                                            scalar1=stats_m[:, ti:ti + 1],
                                            scalar2=stats_r[:, ti:ti + 1],
                                            op0=ALU.subtract, op1=ALU.mult))

        def evacA(tp, ec, t_tr, ti=ti):
            W("scalar", t_tr)
            return E("scalar", SC.copy(
                out=fmA[:, ec * 1024 + ti * 128: ec * 1024 + ti * 128 + 128], in_=tp[:]))
        toks, t_trl = build_fmT(evacA, sbuf_st, t_st, tr_ps_l)
        t_fmA_done += toks
        stage_rd[ti % 2] = t_trl
    W("sync", *t_fmA_done)
    t_agst = []
    for ec in range(16):
        t_agst.append(E("sync", SY.dma_start(
            out=ag_in.ap()[ec * 128:(ec + 1) * 128, :],
            in_=fmA[:, ec * 1024:(ec + 1) * 1024]), dma=True))
    W("gpsimd", *t_agst)
    W("gpsimd", t_clf)
    t_ag = E("gpsimd", GP.collective_compute(
        "AllGather", ALU.bypass, replica_groups=RG,
        ins=[ag_in.ap()], outs=[ag_out.ap()]), cc=True)
    stA.close()

    stF = contextlib.ExitStack()
    H = sb("H", [128, 44 * 512], BF16, st=stF)
    gbuf = [sb(f"gb{i}", [128, 2048], BF16, st=stF) for i in range(2)]
    ubuf = [sb(f"ub{i}", [128, 2048], BF16, st=stF) for i in range(2)]
    sgb = [sb(f"sg{i}", [128, 512], F32, st=stF) for i in range(2)]
    dbuf = [sb(f"db{i}", [128, 1024], BF16, st=stF) for i in range(2)]
    dtmp = [sb(f"dt{i}", [128, 1024], F32, st=stF) for i in range(2)]
    ds_bc = sb("ds_bc", [128, 2048], st=stF)
    fence("gpsimd")
    W("gpsimd", t_zf)
    fence("sync")
    t_ds = E("sync", SY.dma_start(out=ds_bc[:], in_=ds_b), dma=True)
    W("tensor", p.tok("scalar"))

    gtok = [None, None]
    utok = [None, None]

    def ffn_half(th):
        stG = contextlib.ExitStack()
        psG = [ps(f"psG{th}{i}", [128, 512], st=stG) for i in range(2)]
        psU = [ps(f"psU{th}{i}", [128, 512], st=stG) for i in range(2)]
        hvtok = [None, None]
        sgtok = [None, None]
        gmt = [None, None]
        umt = [None, None]
        for ic in range(44):
            b = ic % 2
            W("gpsimd", gtok[b])
            t_g = E("gpsimd", GP.dma_start(out=gbuf[b][:], in_=g0[ic]), dma=True)
            W("gpsimd", utok[b])
            t_u = E("gpsimd", GP.dma_start(out=ubuf[b][:], in_=u0[ic]), dma=True)
            W("tensor", t_g, sgtok[b])
            tG = None
            for ec in range(16):
                tG = E("tensor", TE.matmul(psG[b][:], gbuf[b][:, ec * 128:(ec + 1) * 128],
                                           fmA[:, ec * 1024 + th * 512: ec * 1024 + th * 512 + 512],
                                           start=(ec == 0), stop=(ec == 15)))
            gtok[b] = tG
            gmt[b] = tG
            W("scalar", tG, t_small, hvtok[b])
            t_sg = E("scalar", SC.activation(out=sgb[b][:], in_=psG[b][:], func=AF.Silu,
                                             scale=gss[:, ic:ic + 1]))
            sgtok[b] = t_sg
            W("tensor", t_u, hvtok[b])
            tU = None
            for ec in range(16):
                tU = E("tensor", TE.matmul(psU[b][:], ubuf[b][:, ec * 128:(ec + 1) * 128],
                                           fmA[:, ec * 1024 + th * 512: ec * 1024 + th * 512 + 512],
                                           start=(ec == 0), stop=(ec == 15)))
            utok[b] = tU
            W("vector", tU, t_sg)
            hvtok[b] = E("vector", VE.scalar_tensor_tensor(
                out=H[:, ic * 512:(ic + 1) * 512], in0=psU[b][:], scalar=uss[:, ic:ic + 1],
                in1=sgb[b][:], op0=ALU.mult, op1=ALU.mult))
        t_hv = p.tok("vector")
        W("tensor", p.tok("scalar"), t_hv)
        stG.close()
        stD = contextlib.ExitStack()
        psD = [ps(f"psD{th}{i}", [128, 1024], st=stD) for i in range(4)]
        dtok = [None, None]
        evtok = [None] * 4
        for half in range(2):
            for ic in range(44):
                b = ic % 2
                W("gpsimd", dtok[b])
                t_d = E("gpsimd", GP.dma_start(
                    out=dbuf[b][:], in_=d0[ic][:, half * 1024:(half + 1) * 1024]), dma=True)
                W("tensor", t_d)
                if ic == 0:
                    W("tensor", *[t for t in evtok if t])
                tmm = None
                for t4 in range(4):
                    for h2 in range(2):
                        tmm = E("tensor", TE.matmul(
                            psD[t4][:, h2 * 512:(h2 + 1) * 512],
                            H[:, ic * 512 + t4 * 128: ic * 512 + t4 * 128 + 128],
                            dbuf[b][:, h2 * 512:(h2 + 1) * 512],
                            start=(ic == 0), stop=(ic == 43)))
                dtok[b] = tmm
            t_last_mm = p.tok("tensor")
            for t4 in range(4):
                db = t4 % 2
                xsl = x_sb[:, (th * 4 + t4) * 2048 + half * 1024:
                           (th * 4 + t4) * 2048 + half * 1024 + 1024]
                W("vector", t_last_mm, t_ds, evtok[t4])
                t_e1 = E("vector", VE.tensor_tensor(out=dtmp[db][:], in0=psD[t4][:],
                                                    in1=ds_bc[:, half * 1024:half * 1024 + 1024],
                                                    op=ALU.mult))
                W("vector", t_e1)
                evtok[t4] = E("vector", VE.tensor_tensor(out=xsl, in0=xsl, in1=dtmp[db][:],
                                                         op=ALU.add))
        W("tensor", p.tok("vector"))
        stD.close()

    W("gpsimd", t_ag)
    ffn_half(0)
    ffn_half(1)
    stF.close()

    stB = contextlib.ExitStack()
    fmB = sb("fmB", [128, 16384], BF16, st=stB)
    stAT0 = contextlib.ExitStack()
    KT = sb("KT", [128, 4096], BF16, st=stAT0)
    V_aug = sb("V_aug", [128, 32 * 129], BF16, st=stAT0)
    fence("gpsimd")
    t_vone = E("gpsimd", GP.memset(V_aug[:], 1.0))

    stAT1 = contextlib.ExitStack()
    wqs = sb("wqs", [128, 8192], BF16, st=stAT1)
    wks = sb("wks", [128, 2048], BF16, st=stAT1)
    wvs = sb("wvs", [128, 2048], BF16, st=stAT1)
    xt_buf = [sb(f"xtb{i}", [128, 4096], BF16, st=stAT1) for i in range(2)]
    t_wq = E("gpsimd", GP.dma_start(out=wqs[:], in_=wq_o), dma=True)
    t_wk = E("gpsimd", GP.dma_start(out=wks[:], in_=wk_o), dma=True)
    t_wv = E("gpsimd", GP.dma_start(out=wvs[:], in_=wv_o), dma=True)
    fence("sync")
    stQ = contextlib.ExitStack()
    psK = ps("psK", [128, 256], st=stQ)
    psQ = [ps(f"psQ{i}", [128, 256], st=stQ) for i in range(2)]
    psV = [ps(f"psV{i}", [128, 128], st=stQ) for i in range(2)]
    ag3 = ag_out.ap().rearrange("(a p) m -> a p m", p=128)
    xttok = [None, None]
    ktok = None
    qtok = [None, None]
    vtok = [None, None]
    for tb2 in range(16):
        r, s = tb2 // 4, tb2 % 4
        b = tb2 % 2
        W("sync", t_ag, xttok[b])
        t_xt = E("sync", SY.dma_start(
            out=xt_buf[b][:].rearrange("p (a m) -> p a m", a=16),
            in_=ag3[r * 16:(r + 1) * 16, :, s * 256:(s + 1) * 256].rearrange("a p m -> p a m")),
            dma=True)
        W("tensor", t_xt, t_wk, ktok)
        tK = None
        for ec in range(16):
            tK = E("tensor", TE.matmul(psK[:], wks[:, ec * 128:(ec + 1) * 128],
                                       xt_buf[b][:, ec * 256:(ec + 1) * 256],
                                       start=(ec == 0), stop=(ec == 15)))
        W("scalar", tK, t_small)
        ktok = E("scalar", SC.activation(out=KT[:, tb2 * 256:(tb2 + 1) * 256], in_=psK[:],
                                         func=AF.Identity, bias=bks[:],
                                         scale=np.float32(1.0)))
        for hl in range(4):
            qb = hl % 2
            W("tensor", t_wq, qtok[qb])
            tQ = None
            for ec in range(16):
                tQ = E("tensor", TE.matmul(psQ[qb][:],
                                           wqs[:, hl * 2048 + ec * 128: hl * 2048 + ec * 128 + 128],
                                           xt_buf[b][:, ec * 256:(ec + 1) * 256],
                                           start=(ec == 0), stop=(ec == 15)))
            W("scalar", tQ)
            qtok[qb] = E("scalar", SC.activation(
                out=fmB[:, hl * 4096 + tb2 * 256: hl * 4096 + tb2 * 256 + 256], in_=psQ[qb][:],
                func=AF.Identity, bias=bqs[:, hl:hl + 1], scale=np.float32(1.0)))
        for t4 in range(2):
            vb = t4 % 2
            W("tensor", t_wv, vtok[vb])
            tV = None
            for ec in range(16):
                tV = E("tensor", TE.matmul(psV[vb][:],
                                           xt_buf[b][:, ec * 256 + t4 * 128: ec * 256 + t4 * 128 + 128],
                                           wvs[:, ec * 128:(ec + 1) * 128],
                                           start=(ec == 0), stop=(ec == 15)))
            kt = tb2 * 2 + t4
            W("vector", tV, t_small, t_vone)
            vtok[vb] = E("vector", VE.tensor_tensor(
                out=V_aug[:, kt * 129: kt * 129 + 128], in0=psV[vb][:], in1=bvbc[:],
                op=ALU.add))
        xttok[b] = p.tok("tensor")
    W("tensor", p.tok("scalar"), p.tok("vector"))
    stQ.close()
    stAT1.close()

    stS = contextlib.ExitStack()
    eTb = [sb(f"eT{i}", [128, 512], BF16, st=stS) for i in range(2)]
    O_q = [sb(f"Oq{i}", [128, 2048], BF16, st=stS) for i in range(2)]
    rzc = [sb(f"rz{i}", [128, 1], F32, st=stS) for i in range(4)]
    psS = [ps(f"psS{i}", [128, 512], st=stS) for i in range(2)]
    psAV = [ps(f"psAV{i}", [128, 129], st=stS) for i in range(4)]
    setok = [None, None]
    avtok = [None] * 4
    ettok = [None, None]
    sc_tok = [None, None]
    t_scat = []
    for qc in range(8):
        ob = qc % 2
        ev_toks = []
        for hl in range(4):
            nkc = 4 * qc + 4
            for kc in range(nkc):
                b = kc % 2
                W("tensor", setok[b])
                if kc == 0:
                    W("tensor", *[t for t in avtok if t])
                t_s = E("tensor", TE.matmul(psS[b][:], KT[:, kc * 128:(kc + 1) * 128],
                                            fmB[:, hl * 4096 + qc * 512: hl * 4096 + qc * 512 + 512],
                                            start=True, stop=True))
                if kc >= 4 * qc:
                    jm = kc - 4 * qc
                    W("vector", t_s, t_mk[jm])
                    t_s = E("vector", VE.tensor_tensor(
                        out=psS[b][:], in0=psS[b][:],
                        in1=masks[:, jm * 512:(jm + 1) * 512], op=ALU.add))
                W("scalar", t_s, ettok[b])
                t_e = E("scalar", SC.activation(out=eTb[b][:], in_=psS[b][:], func=AF.Exp,
                                                scale=np.float32(RSQRT_SCALE)))
                setok[b] = t_e
                tAV = None
                W("tensor", t_e)
                for qt in range(4):
                    tAV = E("tensor", TE.matmul(psAV[qt][:], eTb[b][:, qt * 128:(qt + 1) * 128],
                                                V_aug[:, kc * 129: kc * 129 + 129],
                                                start=(kc == 0), stop=(kc == nkc - 1)))
                ettok[b] = tAV
            t_fin = p.tok("tensor")
            for qt in range(4):
                W("vector", t_fin, sc_tok[ob] if hl == 0 else None)
                t_rz = E("vector", VE.reciprocal(rzc[qt][:], psAV[qt][:, 128:129]))
                W("vector", t_rz)
                avtok[qt] = E("vector", VE.tensor_scalar(
                    out=O_q[ob][:, qt * 512 + hl * 128: qt * 512 + hl * 128 + 128],
                    in0=psAV[qt][:, 0:128], scalar1=rzc[qt][:], scalar2=None, op0=ALU.mult))
                ev_toks.append(avtok[qt])
        W("gpsimd", t_oix, t_zf, *ev_toks)
        last_sc = None
        for qt in range(4):
            tt_idx = qc * 4 + qt
            last_sc = E("gpsimd", GP.indirect_dma_start(
                out=rs_in.ap(),
                out_offset=bass.IndirectOffsetOnAxis(ap=oix[:, tt_idx:tt_idx + 1], axis=0),
                in_=O_q[ob][:, qt * 512:(qt + 1) * 512], in_offset=None), dma=True)
            t_scat.append(last_sc)
        sc_tok[ob] = last_sc
    W("gpsimd", *t_scat)
    t_rs = E("gpsimd", GP.collective_compute(
        "ReduceScatter", ALU.add, replica_groups=RG,
        ins=[rs_in.ap()], outs=[rs_out.ap()]), cc=True)
    W("tensor", p.tok("scalar"), p.tok("vector"))
    stS.close()
    stAT0.close()

    stX = contextlib.ExitStack()
    ao_buf = [sb(f"ao{i}", [128, 512], F32, st=stX) for i in range(2)]
    fence("gpsimd")
    rs3 = rs_out.ap().rearrange("(t g) c -> t g c", g=4)
    aotok = [None, None]
    for ti in range(8):
        for gb in range(4):
            b = (ti * 4 + gb) % 2
            W("gpsimd", t_rs, aotok[b])
            t_l = E("gpsimd", GP.dma_start(
                out=ao_buf[b][:], in_=rs3[ti * 128:(ti + 1) * 128, gb, :]), dma=True)
            xsl = x_sb[:, ti * 2048 + gb * 512: ti * 2048 + gb * 512 + 512]
            W("vector", t_l)
            aotok[b] = E("vector", VE.tensor_tensor(out=xsl, in0=xsl, in1=ao_buf[b][:],
                                                    op=ALU.add))
    stX.close()

    st5 = contextlib.ExitStack()
    xn_st2 = [sb(f"xn2st{i}", [128, 2048], BF16, st=st5) for i in range(2)]
    sq_scratch = [sb("sq_scr2", [128, 2048], BF16, st=st5)]
    tr_ps2 = [(ps(f"trq{i}", [128, 128], BF16, st=st5), None) for i in range(4)]
    W("scalar", p.tok("vector"))
    W("tensor", p.tok("scalar"), p.tok("vector"))
    stage_rd2 = [None, None]
    for ti in range(8):
        t_r, t_ssq = ln_stats(ti)
        W("vector", t_r)
        t_a = E("vector", VE.tensor_tensor(out=tmp1[:], in0=stats_r[:, ti:ti + 1],
                                           in1=stats_r[:, ti:ti + 1], op=ALU.mult))
        W("vector", t_a)
        t_b = E("vector", VE.tensor_tensor(out=tmp2[:], in0=stats_v[:, ti:ti + 1],
                                           in1=tmp1[:], op=ALU.mult))
        W("vector", t_b)
        t_c = E("vector", VE.tensor_scalar(out=tmp1[:], in0=tmp2[:], scalar1=1e-5,
                                           scalar2=None, op0=ALU.add))
        W("scalar", t_c)
        t_s = E("scalar", SC.activation(out=tmp2[:], in_=tmp1[:], func=AF.Sqrt))
        W("vector", t_s)
        t_r2 = E("vector", VE.reciprocal(tmp3[:], tmp2[:]))
        W("vector", t_r2)
        t_s2 = E("vector", VE.tensor_tensor(out=s2c[:, ti:ti + 1], in0=stats_r[:, ti:ti + 1],
                                            in1=tmp3[:], op=ALU.mult))
        W("vector", t_s2)
        t_o2a = E("vector", VE.tensor_tensor(out=tmp1[:], in0=stats_m[:, ti:ti + 1],
                                             in1=s2c[:, ti:ti + 1], op=ALU.mult))
        W("vector", t_o2a)
        t_o2 = E("vector", VE.tensor_scalar(out=o2c[:, ti:ti + 1], in0=tmp1[:],
                                            scalar1=-1.0, scalar2=None, op0=ALU.mult))
        W("vector", stage_rd2[0])
        t_st = E("vector", VE.tensor_scalar(out=xn_st2[0][:],
                                            in0=x_sb[:, ti * 2048:(ti + 1) * 2048],
                                            scalar1=stats_m[:, ti:ti + 1],
                                            scalar2=stats_r[:, ti:ti + 1],
                                            op0=ALU.subtract, op1=ALU.mult))

        def evacA1(tp, ec, t_tr, ti=ti):
            W("scalar", t_tr)
            return E("scalar", SC.copy(
                out=fmA[:, ec * 1024 + ti * 128: ec * 1024 + ti * 128 + 128], in_=tp[:]))
        _, t_trl = build_fmT(evacA1, xn_st2[0], t_st, tr_ps2)
        stage_rd2[0] = t_trl
        W("vector", t_o2, stage_rd2[1])
        t_st2 = E("vector", VE.tensor_scalar(out=xn_st2[1][:],
                                             in0=x_sb[:, ti * 2048:(ti + 1) * 2048],
                                             scalar1=s2c[:, ti:ti + 1],
                                             scalar2=o2c[:, ti:ti + 1],
                                             op0=ALU.mult, op1=ALU.add))

        def evacB(tp, ec, t_tr, ti=ti):
            W("vector", t_tr, t_small)
            return E("vector", VE.tensor_scalar(
                out=fmB[:, ec * 1024 + ti * 128: ec * 1024 + ti * 128 + 128], in0=tp[:],
                scalar1=g1as[:, ec:ec + 1], scalar2=be1s[:, ec:ec + 1],
                op0=ALU.mult, op1=ALU.add))
        _, t_trl2 = build_fmT(evacB, xn_st2[1], t_st2, tr_ps2)
        stage_rd2[1] = t_trl2
    W("tensor", p.tok("scalar"), p.tok("vector"))
    st5.close()

    st6 = contextlib.ExitStack()
    e_r = sb("e_r", [128, 12], st=st6)
    probs = sb("probs", [128, 12], st=st6)
    biased = sb("biased", [128, 12], st=st6)
    mk1 = sb("mk1", [128, 12], st=st6)
    b2v = sb("b2v", [128, 12], st=st6)
    psL = ps("psL", [128, 12], st=st6)
    t_ltok = None
    for ti in range(8):
        W("tensor", t_clf, t_ltok)
        tL = None
        for ec in range(16):
            tL = E("tensor", TE.matmul(psL[:], fmA[:, ec * 1024 + ti * 128: ec * 1024 + ti * 128 + 128],
                                       clfs[:, ec * 12:(ec + 1) * 12],
                                       start=(ec == 0), stop=(ec == 15)))
        W("vector", tL)
        t1 = E("vector", VE.reduce_max(out=tmp1[:], in_=psL[:], axis=AX.X, negate=True))
        W("scalar", t1)
        t2 = E("scalar", SC.activation(out=e_r[:], in_=psL[:], func=AF.Exp, bias=tmp1[:],
                                       scale=np.float32(1.0), accum_out=tmp2[:]))
        t_ltok = t2
        W("vector", t2)
        t3 = E("vector", VE.reciprocal(tmp3[:], tmp2[:]))
        W("vector", t3)
        t4 = E("vector", VE.tensor_scalar(out=probs[:], in0=e_r[:], scalar1=tmp3[:],
                                          scalar2=None, op0=ALU.mult))
        W("vector", t4, t_small)
        t5 = E("vector", VE.tensor_tensor(out=biased[:], in0=probs[:], in1=ebbc[:], op=ALU.add))
        W("vector", t5)
        t6 = E("vector", VE.reduce_max(out=tmp1[:], in_=biased[:], axis=AX.X))
        W("vector", t6)
        t7 = E("vector", VE.tensor_scalar(out=mk1[:], in0=biased[:], scalar1=tmp1[:],
                                          scalar2=None, op0=ALU.is_ge))
        W("vector", t7)
        t8 = E("vector", VE.scalar_tensor_tensor(out=b2v[:], in0=mk1[:], scalar=-1e30,
                                                 in1=biased[:], op0=ALU.mult, op1=ALU.add))
        W("vector", t8)
        t9 = E("vector", VE.reduce_max(out=tmp1[:], in_=b2v[:], axis=AX.X))
        W("vector", t9)
        t10 = E("vector", VE.tensor_scalar(out=b2v[:], in0=b2v[:], scalar1=tmp1[:],
                                           scalar2=None, op0=ALU.is_ge))
        W("vector", t10)
        t11 = E("vector", VE.tensor_tensor(out=mk1[:], in0=mk1[:], in1=b2v[:], op=ALU.add))
        W("vector", t11)
        t12 = E("vector", VE.scalar_tensor_tensor(out=w_all[:, ti * 12:(ti + 1) * 12],
                                                  in0=probs[:], scalar=1.5, in1=mk1[:],
                                                  op0=ALU.mult, op1=ALU.mult))
        W("vector", t12)
        t13 = E("vector", VE.reduce_sum(out=tmp1[:], in_=w_all[:, ti * 12 + 8: ti * 12 + 12],
                                        axis=AX.X))
        W("vector", t13)
        t14 = E("vector", VE.tensor_scalar(out=tmp2[:], in0=tmp1[:], scalar1=1.0,
                                           scalar2=None, op0=ALU.add))
        W("vector", t14)
        t15 = E("vector", VE.tensor_tensor(out=tmp1[:], in0=tmp2[:], in1=stats_r[:, ti:ti + 1],
                                           op=ALU.mult))
        W("vector", t15)
        t16 = E("vector", VE.tensor_scalar(out=tmp2[:], in0=tmp1[:], scalar1=1.0,
                                           scalar2=None, op0=ALU.add))
        t17 = E("vector", VE.tensor_tensor(out=tmp3[:], in0=stats_m[:, ti:ti + 1],
                                           in1=tmp1[:], op=ALU.mult))
        W("vector", t16, t17)
        xt = x_sb[:, ti * 2048:(ti + 1) * 2048]
        E("vector", VE.tensor_scalar(out=xt, in0=xt, scalar1=tmp2[:], scalar2=tmp3[:],
                                     op0=ALU.mult, op1=ALU.subtract))
    st6.close()

    st7 = contextlib.ExitStack()
    kvw = [sb(f"kvw{i}", [128, 4096], BF16, st=st7) for i in range(2)]
    cosb = [sb(f"cosb{i}", [128, 512], F32, st=st7) for i in range(2)]
    sinb = [sb(f"sinb{i}", [128, 512], F32, st=st7) for i in range(2)]
    ktmp = [sb(f"ktmp{i}", [128, 512], F32, st=st7) for i in range(2)]
    c1b = sb("c1b", [128, 512], F32, st=st7)
    c2b = sb("c2b", [128, 512], F32, st=st7)
    bias_kv = sb("bias_kv", [128, 512], F32, st=st7)
    Vst = sb("Vst", [128, 4096], BF16, st=st7)
    Kb = sb("Kb", [128, 512], BF16, st=st7)
    ppst = [sb(f"ppst{i}", [128, 512], BF16, st=st7) for i in range(2)]
    ssp = [sb(f"ssp{i}", [128, 1], F32, st=st7) for i in range(2)]
    psKV = [ps(f"psKV{i}", [128, 512], st=st7) for i in range(2)]
    kvwtok = [None, None]
    kvwrd = None
    ktmptok = [None, None]
    ktmprd = [[], []]
    ssptok = [None, None]
    tabtok = [None, None]
    tabrd = [None, None]
    c1tok = None
    c2tok = None
    kbtok = None
    vsttok = [None] * 8
    ppdma = [None, None]
    t_biaskv = None
    fence("gpsimd")
    fence("sync")
    W("tensor", p.tok("scalar"), p.tok("vector"))
    for ds in range(4):
        for kv in (1, 0):
            W("sync", t_biaskv)
            t_biaskv_l = E("sync", SY.dma_start(out=bias_kv[:], in_=b1kv_b[kv, ds]), dma=True)
            for half in range(2):
                W("gpsimd", kvwtok[half], kvwrd)
                kvwtok[half] = E("gpsimd", GP.dma_start(
                    out=kvw[half][:], in_=w1kv[kv, ds][:, half * 4096: half * 4096 + 4096]),
                    dma=True)
            for ti in range(8):
                b = ti % 2
                if ds < 2:
                    W("sync", tabtok[b], tabrd[b])
                    tabtok[b] = E("sync", SY.dma_start(out=cosb[b][:], in_=rcos[ds, ti]), dma=True)
                    tabtok[b] = E("sync", SY.dma_start(out=sinb[b][:], in_=rsin[ds, ti]), dma=True)
                W("tensor", kvwtok[0], kvwtok[1], ktmptok[b])
                tkv = None
                for ec in range(16):
                    buf = kvw[ec // 8]
                    tkv = E("tensor", TE.matmul(
                        psKV[b][:], fmB[:, ec * 1024 + ti * 128: ec * 1024 + ti * 128 + 128],
                        buf[:, (ec % 8) * 512: (ec % 8) * 512 + 512],
                        start=(ec == 0), stop=(ec == 15)))
                kvwrd = tkv
                W("vector", tkv, t_biaskv_l, *ktmprd[b])
                ktmprd[b] = []
                t_kt = E("vector", VE.tensor_tensor(out=ktmp[b][:], in0=psKV[b][:],
                                                    in1=bias_kv[:], op=ALU.add))
                ktmptok[b] = t_kt
                t_biaskv = t_kt
                sb_ = b
                W("scalar", t_kt, ssptok[sb_], c1tok)
                t_sq = E("scalar", SC.activation(out=c1b[:], in_=ktmp[b][:], func=AF.Square,
                                                 accum_out=ssp[sb_][:]))
                ktmprd[b].append(t_sq)
                ss_dst = ssK if kv == 0 else ssV
                W("vector", t_sq)
                if ds == 0:
                    tss = E("vector", VE.tensor_copy(ss_dst[:, ti:ti + 1], ssp[sb_][:]))
                else:
                    tss = E("vector", VE.tensor_tensor(out=ss_dst[:, ti:ti + 1],
                                                       in0=ss_dst[:, ti:ti + 1],
                                                       in1=ssp[sb_][:], op=ALU.add))
                ssptok[sb_] = tss
                if ds < 2:
                    W("vector", t_kt, tabtok[b], c2tok, t_sq)
                    ts1 = E("vector", VE.tensor_copy(c2b[:, 0:512:2], ktmp[b][:, 1:512:2]))
                    ts2 = E("vector", VE.tensor_copy(c2b[:, 1:512:2], ktmp[b][:, 0:512:2]))
                    tc1 = E("vector", VE.tensor_tensor(out=c1b[:], in0=ktmp[b][:],
                                                       in1=cosb[b][:], op=ALU.mult))
                    c1tok = tc1
                    ktmprd[b].append(tc1)
                    tabrd[b] = tc1
                    W("vector", ts1, ts2)
                    tc2 = E("vector", VE.tensor_tensor(out=c2b[:], in0=c2b[:],
                                                       in1=sinb[b][:], op=ALU.mult))
                    c2tok = tc2
                    tabrd[b] = tc2
                    W("vector", tc1, tc2)
                    if kv == 1:
                        W("vector", vsttok[ti])
                        t_rot = E("vector", VE.tensor_tensor(
                            out=Vst[:, ti * 512:(ti + 1) * 512], in0=c1b[:], in1=c2b[:],
                            op=ALU.add))
                        vsttok[ti] = t_rot
                        c1tok = t_rot
                        c2tok = t_rot
                    else:
                        W("vector", kbtok)
                        t_kb = E("vector", VE.tensor_tensor(out=Kb[:], in0=c1b[:],
                                                            in1=c2b[:], op=ALU.add))
                        c1tok = t_kb
                        c2tok = t_kb
                        W("vector", t_kb, vsttok[ti], ppdma[b])
                        t_pp = E("vector", VE.tensor_tensor(
                            out=ppst[b][:], in0=Vst[:, ti * 512:(ti + 1) * 512],
                            in1=Kb[:], op=ALU.mult))
                        kbtok = t_pp
                        vsttok[ti] = t_pp
                        W("sync", t_pp)
                        ppdma[b] = E("sync", SY.dma_start(
                            out=ppd.ap()[ti, :, ds * 512:(ds + 1) * 512], in_=ppst[b][:]),
                            dma=True)
                else:
                    if kv == 1:
                        W("scalar", t_kt, vsttok[ti], t_sq)
                        t_cp = E("scalar", SC.copy(out=Vst[:, ti * 512:(ti + 1) * 512],
                                                   in_=ktmp[b][:]))
                        vsttok[ti] = t_cp
                        ktmprd[b].append(t_cp)
                    else:
                        W("scalar", t_kt, kbtok, t_sq)
                        t_kb = E("scalar", SC.copy(out=Kb[:], in_=ktmp[b][:]))
                        ktmprd[b].append(t_kb)
                        W("vector", t_kb, vsttok[ti], ppdma[b])
                        t_pp = E("vector", VE.tensor_tensor(
                            out=ppst[b][:], in0=Vst[:, ti * 512:(ti + 1) * 512],
                            in1=Kb[:], op=ALU.mult))
                        kbtok = t_pp
                        vsttok[ti] = t_pp
                        W("sync", t_pp)
                        ppdma[b] = E("sync", SY.dma_start(
                            out=ppd.ap()[ti, :, ds * 512:(ds + 1) * 512], in_=ppst[b][:]),
                            dma=True)
    t_pp_all = p.tok("sync")
    st7.close()

    stA2 = contextlib.ExitStack()
    ppb = [sb(f"ppb{i}", [128, 2048], BF16, st=stA2) for i in range(2)]
    psA = ps("psA", [128, 16], st=stA2)
    fence("sync")
    ppbtok = [None, None]
    scoltok = [None, None]
    tA = None
    for ti in range(8):
        b = ti % 2
        W("sync", t_pp_all, ppbtok[b])
        t_ld = E("sync", SY.dma_start(out=ppb[b][:], in_=ppd.ap()[ti]), dma=True)
        W("vector", p.tok("vector") if ti == 0 else None)
        tm = E("vector", VE.tensor_tensor(out=tmp1[:], in0=ssK[:, ti:ti + 1],
                                          in1=ssV[:, ti:ti + 1], op=ALU.mult))
        W("scalar", tm)
        tsq = E("scalar", SC.activation(out=tmp2[:], in_=tmp1[:], func=AF.Sqrt))
        W("vector", tsq)
        tc = E("vector", VE.reciprocal(tmp3[:], tmp2[:]))
        W("vector", tc, scoltok[b])
        tsc = E("vector", VE.tensor_copy(scol[:, b:b + 1], tmp3[:]))
        W("tensor", tsc, t_ld)
        for ch in range(16):
            tA = E("tensor", TE.matmul(psA[:, ch:ch + 1], ppb[b][:, ch * 128:(ch + 1) * 128],
                                       scol[:, b:b + 1], start=(ti == 0), stop=(ti == 7)))
        ppbtok[b] = tA
        scoltok[b] = tA
    W("vector", tA)
    t_Ar = E("vector", VE.tensor_copy(A_raw[:], psA[:]))
    W("sync", t_Ar)
    t_As = E("sync", SY.dma_start(out=arA_in.ap(), in_=A_raw[:]), dma=True)
    W("gpsimd", t_As)
    W("gpsimd", kvwtok[0], kvwtok[1])
    t_arc = E("gpsimd", GP.collective_compute(
        "AllReduce", ALU.add, replica_groups=RG,
        ins=[arA_in.ap()], outs=[arA_out.ap()]), cc=True)
    W("tensor", p.tok("vector"))
    stA2.close()

    stB2 = contextlib.ExitStack()
    Q2T = sb("Q2T", [128, 16384], BF16, st=stB2)
    st8a = contextlib.ExitStack()
    qwb = [sb(f"qwb{i}", [128, 2048], BF16, st=st8a) for i in range(2)]
    psQ2 = [ps(f"psQ2{i}", [128, 512], st=st8a) for i in range(2)]
    fence("gpsimd")
    W("gpsimd", t_arc)
    qwtok = [None, None]
    q2tok = [None, None]
    q2ev = []
    for qch in range(16):
        b = qch % 2
        W("gpsimd", qwtok[b])
        t_w = E("gpsimd", GP.dma_start(out=qwb[b][:], in_=w1q[qch]), dma=True)
        for h2 in range(2):
            W("tensor", t_w, q2tok[h2])
            tq = None
            for ec in range(16):
                tq = E("tensor", TE.matmul(psQ2[h2][:], qwb[b][:, ec * 128:(ec + 1) * 128],
                                           fmB[:, ec * 1024 + h2 * 512: ec * 1024 + h2 * 512 + 512],
                                           start=(ec == 0), stop=(ec == 15)))
            qwtok[b] = tq
            W("vector", tq, t_small)
            q2tok[h2] = E("vector", VE.tensor_scalar(
                out=Q2T[:, qch * 1024 + h2 * 512: qch * 1024 + h2 * 512 + 512],
                in0=psQ2[h2][:], scalar1=b1qs[:, qch:qch + 1], scalar2=None,
                op0=ALU.add))
            q2ev.append(q2tok[h2])
    W("sync", t_arc)
    t_Al = E("sync", SY.dma_start(out=A_raw[:], in_=arA_out.ap()), dma=True)
    W("vector", t_Al)
    t_A1 = E("vector", VE.tensor_scalar(out=A1[:], in0=A_raw[:], scalar1=0.5, scalar2=0.5,
                                        op0=ALU.mult, op1=ALU.add))
    sc_toks = []
    for qch in range(16):
        W("vector", t_A1, q2ev[2 * qch], q2ev[2 * qch + 1])
        sc_toks.append(E("vector", VE.tensor_scalar(
            out=Q2T[:, qch * 1024:(qch + 1) * 1024], in0=Q2T[:, qch * 1024:(qch + 1) * 1024],
            scalar1=A1[:, qch:qch + 1], scalar2=None, op0=ALU.mult)))
    W("tensor", p.tok("vector"))
    st8a.close()
    st8b = contextlib.ExitStack()
    owb = [sb(f"owb{i}", [128, 2048], BF16, st=st8b) for i in range(2)]
    ytmp = [sb(f"ytmp{i}", [128, 512], F32, st=st8b) for i in range(2)]
    b1ob = [sb(f"b1ob{i}", [128, 512], F32, st=st8b) for i in range(2)]
    psY = [ps(f"psY{i}", [128, 512], st=st8b) for i in range(8)]
    fence("gpsimd")
    fence("sync")
    owtok = [None, None]
    btok = [None, None]
    ytok = [None] * 8
    for cs in range(4):
        cb = cs % 2
        W("sync", btok[cb])
        t_b = E("sync", SY.dma_start(out=b1ob[cb][:], in_=b1o_b[cs]), dma=True)
        for qg in range(4):
            b = qg % 2
            W("gpsimd", owtok[b])
            t_w = E("gpsimd", GP.dma_start(
                out=owb[b][:], in_=w1o[cs][:, qg * 2048:(qg + 1) * 2048]), dma=True)
            W("tensor", t_w)
            if qg == 0:
                W("tensor", *[t for t in ytok if t])
            tY = None
            for ti in range(8):
                for k in range(4):
                    qch = qg * 4 + k
                    tY = E("tensor", TE.matmul(
                        psY[ti][:],
                        Q2T[:, qch * 1024 + ti * 128: qch * 1024 + ti * 128 + 128],
                        owb[b][:, k * 512:(k + 1) * 512],
                        start=(qg == 0 and k == 0), stop=(qg == 3 and k == 3)))
            owtok[b] = tY
        t_fin = p.tok("tensor")
        for ti in range(8):
            yb = ti % 2
            W("vector", t_fin, t_b, ytok[ti])
            t_y1 = E("vector", VE.tensor_tensor(out=ytmp[yb][:], in0=psY[ti][:],
                                                in1=b1ob[cb][:], op=ALU.add))
            xsl = x_sb[:, ti * 2048 + cs * 512: ti * 2048 + cs * 512 + 512]
            W("vector", t_y1)
            ytok[ti] = E("vector", VE.tensor_tensor(out=xsl, in0=xsl, in1=ytmp[yb][:],
                                                    op=ALU.add))
        btok[cb] = ytok[7]
    W("tensor", p.tok("vector"))
    st8b.close()
    stB2.close()
    stB.close()

    st9 = contextlib.ExitStack()
    H1 = sb("H1", [128, 8192], BF16, st=st9)
    g1b = [sb(f"g1b{i}", [128, 2048], BF16, st=st9) for i in range(2)]
    u1b = [sb(f"u1b{i}", [128, 2048], BF16, st=st9) for i in range(2)]
    sgb2 = [sb(f"sgb2{i}", [128, 512], F32, st=st9) for i in range(2)]
    d1b = sb("d1b", [128, 16384], BF16, st=st9)
    psG1 = [ps(f"psGm{i}", [128, 512], st=st9) for i in range(2)]
    psU1 = [ps(f"psUm{i}", [128, 512], st=st9) for i in range(2)]
    psM = [ps(f"psM{i}", [128, 512], st=st9) for i in range(2)]
    fence("gpsimd")
    g1tok = [None, None]
    u1tok = [None, None]
    sg1tok = [None, None]
    h1tok = [None, None]
    d1tok = None
    mevtok = [None, None]
    for e in range(8):
        t_h1_all = []
        for ic in range(8):
            b = ic % 2
            W("gpsimd", g1tok[b])
            t_g = E("gpsimd", GP.dma_start(out=g1b[b][:], in_=g1[e, :, ic, :]), dma=True)
            W("gpsimd", u1tok[b])
            t_u = E("gpsimd", GP.dma_start(out=u1b[b][:], in_=u1[e, :, ic, :]), dma=True)
            for th in range(2):
                pb = (ic * 2 + th) % 2
                W("tensor", t_g, sg1tok[pb])
                tG = None
                for hc in range(16):
                    tG = E("tensor", TE.matmul(psG1[pb][:], g1b[b][:, hc * 128:(hc + 1) * 128],
                                               fmA[:, hc * 1024 + th * 512: hc * 1024 + th * 512 + 512],
                                               start=(hc == 0), stop=(hc == 15)))
                g1tok[b] = tG
                W("scalar", tG, h1tok[pb])
                t_sg = E("scalar", SC.activation(out=sgb2[pb][:], in_=psG1[pb][:], func=AF.Silu))
                sg1tok[pb] = t_sg
                W("tensor", t_u, h1tok[pb])
                tU = None
                for hc in range(16):
                    tU = E("tensor", TE.matmul(psU1[pb][:], u1b[b][:, hc * 128:(hc + 1) * 128],
                                               fmA[:, hc * 1024 + th * 512: hc * 1024 + th * 512 + 512],
                                               start=(hc == 0), stop=(hc == 15)))
                u1tok[b] = tU
                W("vector", tU, t_sg)
                th1 = E("vector", VE.tensor_tensor(
                    out=H1[:, ic * 1024 + th * 512: ic * 1024 + th * 512 + 512],
                    in0=sgb2[pb][:], in1=psU1[pb][:], op=ALU.mult))
                h1tok[pb] = th1
                t_h1_all.append(th1)
        W("gpsimd", d1tok)
        t_d = E("gpsimd", GP.dma_start(
            out=d1b[:].rearrange("p (a m) -> p a m", a=8), in_=d1[e]), dma=True)
        for ti in range(8):
            for cs in range(4):
                mb = (ti * 4 + cs) % 2
                W("tensor", t_d, mevtok[mb], *t_h1_all)
                t_h1_all = []
                tM = None
                for icl in range(8):
                    tM = E("tensor", TE.matmul(
                        psM[mb][:],
                        H1[:, icl * 1024 + ti * 128: icl * 1024 + ti * 128 + 128],
                        d1b[:, icl * 2048 + cs * 512: icl * 2048 + cs * 512 + 512],
                        start=(icl == 0), stop=(icl == 7)))
                d1tok = tM
                xsl = x_sb[:, ti * 2048 + cs * 512: ti * 2048 + cs * 512 + 512]
                W("vector", tM)
                mevtok[mb] = E("vector", VE.scalar_tensor_tensor(
                    out=xsl, in0=psM[mb][:], scalar=w_all[:, ti * 12 + e: ti * 12 + e + 1],
                    in1=xsl, op0=ALU.mult, op1=ALU.add))
    W("tensor", p.tok("vector"))
    st9.close()

    W("sync", p.tok("vector"))
    st_toks = []
    for ti in range(8):
        st_toks.append(E("sync", SY.dma_start(out=out_x[ti * 128:(ti + 1) * 128, :],
                                              in_=x_sb[:, ti * 2048:(ti + 1) * 2048]), dma=True))
    W("sync", *st_toks)
    return nc, stack


def _host_layouts(inp):
    L = {}
    L["g0"] = np.ascontiguousarray(
        inp["L0_ffn_gate_w"].reshape(44, 128, 16, 128).transpose(0, 3, 2, 1)).reshape(44, 128, 2048)
    L["u0"] = np.ascontiguousarray(
        inp["L0_ffn_up_w"].reshape(44, 128, 16, 128).transpose(0, 3, 2, 1)).reshape(44, 128, 2048)
    L["d0"] = np.ascontiguousarray(inp["L0_ffn_down_w"].T).reshape(44, 128, 2048)
    L["gs0"] = np.ascontiguousarray(inp["L0_ffn_gate_s"].reshape(44, 128).T)
    L["us0"] = np.ascontiguousarray(inp["L0_ffn_up_s"].reshape(44, 128).T)
    L["ds_b"] = np.ascontiguousarray(
        np.broadcast_to(inp["L0_ffn_down_s"].reshape(1, 2048), (128, 2048)))
    W_in = inp["L1_attn_W_in"]
    L["w1q"] = np.ascontiguousarray(
        W_in[:, :2048].reshape(16, 128, 16, 128).transpose(2, 1, 0, 3)).reshape(16, 128, 2048)
    L["w1kv"] = np.ascontiguousarray(
        W_in[:, 2048:].reshape(16, 128, 2, 4, 512).transpose(2, 3, 1, 0, 4)).reshape(2, 4, 128, 8192)
    b_in = inp["L1_attn_b_in"]
    L["b1q"] = np.ascontiguousarray(b_in[:2048].reshape(16, 128).T)
    L["b1kv_b"] = np.ascontiguousarray(
        np.broadcast_to(b_in[2048:].reshape(2, 4, 1, 512), (2, 4, 128, 512)))
    L["w1o"] = np.ascontiguousarray(
        inp["L1_attn_W_out"].reshape(16, 128, 4, 512).transpose(2, 1, 0, 3)).reshape(4, 128, 8192)
    L["b1o_b"] = np.ascontiguousarray(
        np.broadcast_to(inp["L1_attn_b_out"].reshape(4, 1, 512), (4, 128, 512)))
    L["g1a"] = np.ascontiguousarray(inp["L1_attn_gamma"].reshape(16, 128).T)
    L["be1"] = np.ascontiguousarray(inp["L1_attn_beta"].reshape(16, 128).T)
    L["clf"] = np.ascontiguousarray(
        inp["L1_ffn_classifier_w"].T.reshape(16, 128, 12).transpose(1, 0, 2)).reshape(128, 192)
    L["eb_b"] = np.ascontiguousarray(
        np.broadcast_to(inp["L1_ffn_e_score_correction_bias"].reshape(1, 12), (128, 12)))
    L["g1"] = np.ascontiguousarray(
        inp["L1_ffn_gate_w"].reshape(8, 8, 128, 16, 128).transpose(0, 4, 1, 3, 2)).reshape(8, 128, 8, 2048)
    L["u1"] = np.ascontiguousarray(
        inp["L1_ffn_up_w"].reshape(8, 8, 128, 16, 128).transpose(0, 4, 1, 3, 2)).reshape(8, 128, 8, 2048)
    L["d1"] = np.ascontiguousarray(
        inp["L1_ffn_down_w"].reshape(8, 2048, 8, 128).transpose(0, 3, 2, 1)).reshape(8, 128, 8, 2048)
    inv = (10000.0 ** (-(np.arange(0, 1024, 2, dtype=np.float64) / 1024.0)))
    pos = np.arange(4096, dtype=np.float64)
    fr = pos[:, None] * inv[None, :]
    cosr = np.repeat(np.cos(fr), 2, axis=1)
    sinr = np.repeat(np.sin(fr), 2, axis=1)
    sinr[:, 0::2] *= -1.0
    L["_cos_full"] = cosr.astype(np.float32)
    L["_sin_full"] = sinr.astype(np.float32)
    return L


def kernel(**inp):
    inp = {k: np.asarray(v) for k, v in inp.items()}
    L = _host_layouts(inp)
    x = inp["L0_attn_x"]
    Wq, bq = inp["L0_attn_Wq"], inp["L0_attn_bq"]
    Wk, bk = inp["L0_attn_Wk"], inp["L0_attn_bk"]
    Wv, bv = inp["L0_attn_Wv"], inp["L0_attn_bv"]
    shared_keys = ["g0", "u0", "d0", "gs0", "us0", "ds_b", "w1q", "w1kv", "b1q",
                   "b1kv_b", "w1o", "b1o_b", "g1a", "be1", "clf", "eb_b",
                   "g1", "u1", "d1"]
    in_maps = []
    for c in range(8):
        b, j = c // 4, c % 4
        m = {k: L[k] for k in shared_keys}
        m["x_own"] = np.ascontiguousarray(x[b, j * 1024:(j + 1) * 1024, :])
        m["wq_o"] = np.ascontiguousarray(
            Wq[4 * j:4 * j + 4].reshape(4, 16, 128, 128).transpose(2, 0, 1, 3)).reshape(128, 8192)
        m["bq_o"] = np.ascontiguousarray(bq[4 * j:4 * j + 4].T)
        m["wk_o"] = np.ascontiguousarray(
            Wk[j].reshape(16, 128, 128).transpose(1, 0, 2)).reshape(128, 2048)
        m["wv_o"] = np.ascontiguousarray(
            Wv[j].reshape(16, 128, 128).transpose(1, 0, 2)).reshape(128, 2048)
        m["bk_o"] = bk[j].reshape(128, 1).copy()
        m["bv_b"] = np.ascontiguousarray(np.broadcast_to(bv[j].reshape(1, 128), (128, 128)))
        oid = ((np.arange(32)[None, :] * 128 + np.arange(128)[:, None]) * 4 + j).astype(np.int32)
        m["oidx"] = oid
        tsl = slice(j * 1024, (j + 1) * 1024)
        m["rcos"] = np.ascontiguousarray(
            L["_cos_full"][tsl].reshape(8, 128, 2, 512).transpose(2, 0, 1, 3))
        m["rsin"] = np.ascontiguousarray(
            L["_sin_full"][tsl].reshape(8, 128, 2, 512).transpose(2, 0, 1, 3))
        in_maps.append(m)
    nc, stack = build()
    res = run_bass_kernel_spmd(nc, in_maps, core_ids=list(range(8)))
    stack.close()
    out = np.empty((2, 4096, 2048), dtype=np.float32)
    for c in range(8):
        b, j = c // 4, c % 4
        out[b, j * 1024:(j + 1) * 1024, :] = res.results[c]["out_x"]
    return out


# version 04
# speedup vs baseline: 1.0043x; 1.0043x over previous
import contextlib
import numpy as np
import ml_dtypes
import concourse.bass as bass
import concourse.mybir as mybir
from concourse.bass_utils import run_bass_kernel_spmd

F32 = mybir.dt.float32
BF16 = mybir.dt.bfloat16
I32 = mybir.dt.int32
AF = mybir.ActivationFunctionType
ALU = mybir.AluOpType
AX = mybir.AxisListType
BF = ml_dtypes.bfloat16
RG = [[0, 1, 2, 3], [4, 5, 6, 7]]
ENGS = ["sync", "scalar", "vector", "tensor", "gpsimd"]
RSQRT_SCALE = float(1.0 / np.sqrt(128.0))


class P:

    def __init__(self, nc, stack):
        self.nc = nc
        self.sems = {e: stack.enter_context(nc.semaphore(f"S_{e}")) for e in ENGS}
        self.cnt = {e: 0 for e in ENGS}
        self.lastw = {}
        self.eng = {e: getattr(nc, e) for e in ENGS}

    def emit(self, ename, instr, dma=False, cc=False):
        n = 16 if dma else 1
        instr.then_inc(self.sems[ename], n)
        self.cnt[ename] += n
        return (ename, self.cnt[ename], dma or cc)

    def wait(self, consumer, *tokens):
        for t in tokens:
            if t is None:
                continue
            pe, v, _ = t
            key = (consumer, pe)
            if self.lastw.get(key, -1) >= v:
                continue
            self.eng[consumer].wait_ge(self.sems[pe], v)
            self.lastw[key] = v

    def tok(self, ename):
        return (ename, self.cnt[ename], False)


def build():
    nc = bass.Bass("TRN2", target_bir_lowering=False)

    def din(name, shape, dt=F32):
        return nc.dram_tensor(name, shape, dt, kind="ExternalInput").ap()

    x_own = din("x_own", [1024, 2048])
    wq_o = din("wq_o", [128, 8192])
    wk_o = din("wk_o", [128, 2048])
    wv_o = din("wv_o", [128, 2048])
    bq_o = din("bq_o", [128, 4])
    bk_o = din("bk_o", [128, 1])
    bv_b = din("bv_b", [128, 128])
    oidx = din("oidx", [128, 32], I32)
    g0 = din("g0", [44, 128, 2048])
    u0 = din("u0", [44, 128, 2048])
    d0 = din("d0", [44, 128, 2048])
    gs0 = din("gs0", [128, 44])
    us0 = din("us0", [128, 44])
    ds_b = din("ds_b", [128, 2048])
    w1q = din("w1q", [16, 128, 2048])
    b1q = din("b1q", [128, 16])
    w1kv = din("w1kv", [2, 4, 128, 8192])
    b1kv_b = din("b1kv_b", [2, 4, 128, 512])
    w1o = din("w1o", [4, 128, 8192])
    b1o_b = din("b1o_b", [4, 128, 512])
    g1a_i = din("g1a", [128, 16])
    be1_i = din("be1", [128, 16])
    clf = din("clf", [128, 192])
    eb_b = din("eb_b", [128, 12])
    g1 = din("g1", [8, 128, 8, 2048])
    u1 = din("u1", [8, 128, 8, 2048])
    d1 = din("d1", [8, 128, 8, 2048])
    rcos = din("rcos", [2, 8, 128, 512])
    rsin = din("rsin", [2, 8, 128, 512])

    out_x = nc.dram_tensor("out_x", [1024, 2048], F32, kind="ExternalOutput").ap()

    ag_in = nc.dram_tensor("ag_in", [2048, 1024], BF16)
    ag_out = nc.dram_tensor("ag_out", [8192, 1024], BF16)
    rs_in = nc.dram_tensor("rs_in", [16384, 512], BF16)
    rs_out = nc.dram_tensor("rs_out", [4096, 512], BF16)
    arA_in = nc.dram_tensor("arA_in", [128, 16], F32)
    arA_out = nc.dram_tensor("arA_out", [128, 16], F32)
    ppd = nc.dram_tensor("ppd", [8, 128, 2048], BF16)

    stack = contextlib.ExitStack()
    p = P(nc, stack)
    E, W = p.emit, p.wait
    SY, SC, VE, TE, GP = nc.sync, nc.scalar, nc.vector, nc.tensor, nc.gpsimd

    def sb(name, shape, dt=F32, st=stack):
        return st.enter_context(nc.sbuf_tensor(name, shape, dt))

    def ps(name, shape, dt=F32, st=stack):
        return st.enter_context(nc.psum_tensor(name, shape, dt))

    def fence(q, engs=("tensor", "vector", "scalar")):
        W(q, *[p.tok(e) for e in engs])

    x_sb = sb("x_sb", [128, 8 * 2048])
    fmA = sb("fmA", [128, 16384], BF16)
    masks = sb("masks", [128, 4 * 512])
    ident = sb("ident", [128, 128], BF16)
    bqs = sb("bqs", [128, 4]); bks = sb("bks", [128, 1])
    bvbc = sb("bvbc", [128, 128]); oix = sb("oix", [128, 32], I32)
    gss = sb("gss", [128, 44]); uss = sb("uss", [128, 44])
    b1qs = sb("b1qs", [128, 16])
    g1as = sb("g1as", [128, 16]); be1s = sb("be1s", [128, 16])
    clfs = sb("clfs", [128, 192], BF16); ebbc = sb("ebbc", [128, 12])
    stats_m = sb("stats_m", [128, 8]); stats_r = sb("stats_r", [128, 8])
    stats_v = sb("stats_v", [128, 8])
    s2c = sb("s2c", [128, 8]); o2c = sb("o2c", [128, 8])
    w_all = sb("w_all", [128, 96])
    A1 = sb("A1", [128, 16]); A_raw = sb("A_raw", [128, 16])
    ssK = sb("ssK", [128, 8]); ssV = sb("ssV", [128, 8])
    scol = sb("scol", [128, 2], BF16)
    tmp1 = sb("tmp1", [128, 1]); tmp2 = sb("tmp2", [128, 1]); tmp3 = sb("tmp3", [128, 1])
    st_i = contextlib.ExitStack()
    iota = sb("iota", [128, 512], I32, st=st_i)

    t_x = [E("sync", SY.dma_start(out=x_sb[:, ti * 2048:(ti + 1) * 2048],
                                  in_=x_own[ti * 128:(ti + 1) * 128, :]), dma=True)
           for ti in range(8)]
    smalls = [(bqs, bq_o), (bks, bk_o), (bvbc, bv_b), (gss, gs0), (uss, us0),
              (b1qs, b1q), (g1as, g1a_i), (be1s, be1_i), (ebbc, eb_b)]
    t_small = None
    for buf, src in smalls:
        t_small = E("sync", SY.dma_start(out=buf[:], in_=src), dma=True)
    t_oix = E("sync", SY.dma_start(out=oix[:], in_=oidx), dma=True)
    t_clf = E("gpsimd", GP.dma_start(out=clfs[:], in_=clf), dma=True)
    t_io = E("gpsimd", GP.iota(out=iota[:], pattern=[[-1, 512]], base=0,
                               channel_multiplier=1))
    W("vector", t_io)
    t_id = E("vector", VE.tensor_scalar(out=ident[:], in0=iota[:, 0:128], scalar1=0.0,
                                        scalar2=None, op0=ALU.is_equal))
    t_mk = []
    for jm in range(4):
        t_mk.append(E("vector", VE.tensor_scalar(
            out=masks[:, jm * 512:(jm + 1) * 512], in0=iota[:], scalar1=float(-128 * jm),
            scalar2=-1e30, op0=ALU.is_gt, op1=ALU.mult)))
    st_i.close()

    def ln_stats(ti):
        xt = x_sb[:, ti * 2048:(ti + 1) * 2048]
        t_sum = E("vector", VE.reduce_sum(out=tmp1[:], in_=xt, axis=AX.X))
        t_ssq = E("scalar", SC.activation(out=sq_scratch[0][:], in_=xt, func=AF.Square,
                                          accum_out=tmp2[:]))
        W("vector", t_sum)
        t_m = E("vector", VE.tensor_scalar(out=stats_m[:, ti:ti + 1], in0=tmp1[:],
                                           scalar1=1.0 / 2048.0, scalar2=None, op0=ALU.mult))
        W("vector", t_m, t_ssq)
        t_a = E("vector", VE.tensor_tensor(out=tmp1[:], in0=stats_m[:, ti:ti + 1],
                                           in1=stats_m[:, ti:ti + 1], op=ALU.mult))
        t_b = E("vector", VE.tensor_scalar(out=tmp3[:], in0=tmp2[:],
                                           scalar1=1.0 / 2048.0, scalar2=None, op0=ALU.mult))
        W("vector", t_a, t_b)
        t_v = E("vector", VE.tensor_tensor(out=stats_v[:, ti:ti + 1], in0=tmp3[:],
                                           in1=tmp1[:], op=ALU.subtract))
        W("vector", t_v)
        t_e = E("vector", VE.tensor_scalar(out=tmp1[:], in0=stats_v[:, ti:ti + 1],
                                           scalar1=1e-5, scalar2=None, op0=ALU.add))
        W("scalar", t_e)
        t_s = E("scalar", SC.activation(out=tmp2[:], in_=tmp1[:], func=AF.Sqrt))
        W("vector", t_s)
        t_r = E("vector", VE.reciprocal(stats_r[:, ti:ti + 1], tmp2[:]))
        return t_r, t_ssq

    def build_fmT(dest_fn, stage_buf, t_stage, tr_pss):
        toks = []
        t_tr = None
        for ec in range(16):
            b = ec % len(tr_pss)
            tp, t_ev_prev = tr_pss[b]
            W("tensor", t_stage, t_ev_prev, t_id)
            t_tr = E("tensor", TE.transpose(tp[:], stage_buf[:, ec * 128:(ec + 1) * 128],
                                            ident[:]))
            t_ev = dest_fn(tp, ec, t_tr)
            tr_pss[b] = (tp, t_ev)
            toks.append(t_ev)
        return toks, t_tr

    stA = contextlib.ExitStack()
    xn_stage = [sb(f"xn_st{i}", [128, 2048], BF16, st=stA) for i in range(2)]
    sq_scratch = [sb("sq_scr", [128, 2048], BF16, st=stA)]
    zbuf = sb("zbuf", [128, 2048], BF16, st=stA)
    tr_ps_l = [(ps(f"trp{i}", [128, 128], BF16, st=stA), None) for i in range(4)]
    t_zb = E("gpsimd", GP.memset(zbuf[:], 0.0))
    t_zf = None
    for k in range(32):
        t_zf = E("gpsimd", GP.dma_start(
            out=rs_in.ap()[k * 512:(k + 1) * 512, :].rearrange("(a p) m -> p a m", p=128),
            in_=zbuf[:].rearrange("p (a m) -> p a m", a=4)), dma=True)
    t_fmA_done = []
    stage_rd = [None, None]
    for ti in range(8):
        W("vector", t_x[ti])
        W("scalar", t_x[ti])
        t_r, _ = ln_stats(ti)
        sbuf_st = xn_stage[ti % 2]
        W("vector", t_r, stage_rd[ti % 2])
        t_st = E("vector", VE.tensor_scalar(out=sbuf_st[:],
                                            in0=x_sb[:, ti * 2048:(ti + 1) * 2048],
                                            scalar1=stats_m[:, ti:ti + 1],
                                            scalar2=stats_r[:, ti:ti + 1],
                                            op0=ALU.subtract, op1=ALU.mult))

        def evacA(tp, ec, t_tr, ti=ti):
            W("scalar", t_tr)
            return E("scalar", SC.copy(
                out=fmA[:, ec * 1024 + ti * 128: ec * 1024 + ti * 128 + 128], in_=tp[:]))
        toks, t_trl = build_fmT(evacA, sbuf_st, t_st, tr_ps_l)
        t_fmA_done += toks
        stage_rd[ti % 2] = t_trl
    W("sync", *t_fmA_done)
    t_agst = []
    for ec in range(16):
        t_agst.append(E("sync", SY.dma_start(
            out=ag_in.ap()[ec * 128:(ec + 1) * 128, :],
            in_=fmA[:, ec * 1024:(ec + 1) * 1024]), dma=True))
    W("gpsimd", *t_agst)
    W("gpsimd", t_clf, t_zf)
    t_ag = E("gpsimd", GP.collective_compute(
        "AllGather", ALU.bypass, replica_groups=RG,
        ins=[ag_in.ap()], outs=[ag_out.ap()]), cc=True)
    stA.close()

    stF = contextlib.ExitStack()
    H = sb("H", [128, 44 * 512], BF16, st=stF)
    gbuf = [sb(f"gb{i}", [128, 2048], BF16, st=stF) for i in range(2)]
    ubuf = [sb(f"ub{i}", [128, 2048], BF16, st=stF) for i in range(2)]
    sgb = [sb(f"sg{i}", [128, 512], F32, st=stF) for i in range(2)]
    dbuf = [sb(f"db{i}", [128, 1024], BF16, st=stF) for i in range(2)]
    dtmp = [sb(f"dt{i}", [128, 1024], F32, st=stF) for i in range(2)]
    ds_bc = sb("ds_bc", [128, 2048], st=stF)
    stgG = sb("stgG", [128, 2048], F32, st=stF)
    stgU = sb("stgU", [128, 2048], F32, st=stF)
    fence("gpsimd")
    W("gpsimd", t_zf)
    fence("sync")
    t_ds = E("sync", SY.dma_start(out=ds_bc[:], in_=ds_b), dma=True)
    W("tensor", p.tok("scalar"))

    gtok = [None, None]
    utok = [None, None]
    stg_rd = [None, None]

    def ffn_half(th, hw_n=0):
        stG = contextlib.ExitStack()
        psG = [ps(f"psG{th}{i}", [128, 512], st=stG) for i in range(2)]
        psU = [ps(f"psU{th}{i}", [128, 512], st=stG) for i in range(2)]
        hvtok = [None, None]
        sgtok = [None, None]
        gmt = [None, None]
        umt = [None, None]
        for ic in range(44):
            b = ic % 2
            if ic < hw_n:
                W("sync", stg_rd[0])
                t_gf = E("sync", SY.dma_start(out=stgG[:], in_=g0[ic]), dma=True)
                W("scalar", stg_rd[1])
                t_uf = E("scalar", SC.dma_start(out=stgU[:], in_=u0[ic]), dma=True)
                W("scalar", t_gf, gtok[b])
                t_g = E("scalar", SC.copy(out=gbuf[b][:], in_=stgG[:]))
                stg_rd[0] = t_g
                W("scalar", t_uf, utok[b])
                t_u = E("scalar", SC.copy(out=ubuf[b][:], in_=stgU[:]))
                stg_rd[1] = t_u
            else:
                W("gpsimd", gtok[b])
                t_g = E("gpsimd", GP.dma_start(out=gbuf[b][:], in_=g0[ic]), dma=True)
                W("gpsimd", utok[b])
                t_u = E("gpsimd", GP.dma_start(out=ubuf[b][:], in_=u0[ic]), dma=True)
            W("tensor", t_g, sgtok[b])
            tG = None
            for ec in range(16):
                tG = E("tensor", TE.matmul(psG[b][:], gbuf[b][:, ec * 128:(ec + 1) * 128],
                                           fmA[:, ec * 1024 + th * 512: ec * 1024 + th * 512 + 512],
                                           start=(ec == 0), stop=(ec == 15)))
            gtok[b] = tG
            gmt[b] = tG
            W("scalar", tG, t_small, hvtok[b])
            t_sg = E("scalar", SC.activation(out=sgb[b][:], in_=psG[b][:], func=AF.Silu,
                                             scale=gss[:, ic:ic + 1]))
            sgtok[b] = t_sg
            W("tensor", t_u, hvtok[b])
            tU = None
            for ec in range(16):
                tU = E("tensor", TE.matmul(psU[b][:], ubuf[b][:, ec * 128:(ec + 1) * 128],
                                           fmA[:, ec * 1024 + th * 512: ec * 1024 + th * 512 + 512],
                                           start=(ec == 0), stop=(ec == 15)))
            utok[b] = tU
            W("vector", tU, t_sg)
            hvtok[b] = E("vector", VE.scalar_tensor_tensor(
                out=H[:, ic * 512:(ic + 1) * 512], in0=psU[b][:], scalar=uss[:, ic:ic + 1],
                in1=sgb[b][:], op0=ALU.mult, op1=ALU.mult))
        t_hv = p.tok("vector")
        W("tensor", p.tok("scalar"), t_hv)
        stG.close()
        stD = contextlib.ExitStack()
        psD = [ps(f"psD{th}{i}", [128, 1024], st=stD) for i in range(4)]
        dtok = [None, None]
        evtok = [None] * 4
        for half in range(2):
            for ic in range(44):
                b = ic % 2
                W("gpsimd", dtok[b])
                t_d = E("gpsimd", GP.dma_start(
                    out=dbuf[b][:], in_=d0[ic][:, half * 1024:(half + 1) * 1024]), dma=True)
                W("tensor", t_d)
                if ic == 0:
                    W("tensor", *[t for t in evtok if t])
                tmm = None
                for t4 in range(4):
                    for h2 in range(2):
                        tmm = E("tensor", TE.matmul(
                            psD[t4][:, h2 * 512:(h2 + 1) * 512],
                            H[:, ic * 512 + t4 * 128: ic * 512 + t4 * 128 + 128],
                            dbuf[b][:, h2 * 512:(h2 + 1) * 512],
                            start=(ic == 0), stop=(ic == 43)))
                dtok[b] = tmm
            t_last_mm = p.tok("tensor")
            for t4 in range(4):
                db = t4 % 2
                xsl = x_sb[:, (th * 4 + t4) * 2048 + half * 1024:
                           (th * 4 + t4) * 2048 + half * 1024 + 1024]
                W("vector", t_last_mm, t_ds, evtok[t4])
                t_e1 = E("vector", VE.tensor_tensor(out=dtmp[db][:], in0=psD[t4][:],
                                                    in1=ds_bc[:, half * 1024:half * 1024 + 1024],
                                                    op=ALU.mult))
                W("vector", t_e1)
                evtok[t4] = E("vector", VE.tensor_tensor(out=xsl, in0=xsl, in1=dtmp[db][:],
                                                         op=ALU.add))
        W("tensor", p.tok("vector"))
        stD.close()

    W("gpsimd", t_ag)
    W("sync", t_zf)
    W("scalar", p.tok("tensor"), p.tok("vector"), t_zf)
    ffn_half(0, hw_n=44)
    ffn_half(1)
    stF.close()

    stB = contextlib.ExitStack()
    fmB = sb("fmB", [128, 16384], BF16, st=stB)
    stAT0 = contextlib.ExitStack()
    KT = sb("KT", [128, 4096], BF16, st=stAT0)
    V_aug = sb("V_aug", [128, 32 * 129], BF16, st=stAT0)
    fence("gpsimd")
    t_vone = E("gpsimd", GP.memset(V_aug[:], 1.0))

    stAT1 = contextlib.ExitStack()
    wqs = sb("wqs", [128, 8192], BF16, st=stAT1)
    wks = sb("wks", [128, 2048], BF16, st=stAT1)
    wvs = sb("wvs", [128, 2048], BF16, st=stAT1)
    xt_buf = [sb(f"xtb{i}", [128, 4096], BF16, st=stAT1) for i in range(2)]
    t_wq = E("gpsimd", GP.dma_start(out=wqs[:], in_=wq_o), dma=True)
    t_wk = E("gpsimd", GP.dma_start(out=wks[:], in_=wk_o), dma=True)
    t_wv = E("gpsimd", GP.dma_start(out=wvs[:], in_=wv_o), dma=True)
    fence("sync")
    stQ = contextlib.ExitStack()
    psK = ps("psK", [128, 256], st=stQ)
    psQ = [ps(f"psQ{i}", [128, 256], st=stQ) for i in range(2)]
    psV = [ps(f"psV{i}", [128, 128], st=stQ) for i in range(2)]
    ag3 = ag_out.ap().rearrange("(a p) m -> a p m", p=128)
    xttok = [None, None]
    ktok = None
    qtok = [None, None]
    vtok = [None, None]
    for tb2 in range(16):
        r, s = tb2 // 4, tb2 % 4
        b = tb2 % 2
        W("sync", t_ag, xttok[b])
        t_xt = E("sync", SY.dma_start(
            out=xt_buf[b][:].rearrange("p (a m) -> p a m", a=16),
            in_=ag3[r * 16:(r + 1) * 16, :, s * 256:(s + 1) * 256].rearrange("a p m -> p a m")),
            dma=True)
        W("tensor", t_xt, t_wk, ktok)
        tK = None
        for ec in range(16):
            tK = E("tensor", TE.matmul(psK[:], wks[:, ec * 128:(ec + 1) * 128],
                                       xt_buf[b][:, ec * 256:(ec + 1) * 256],
                                       start=(ec == 0), stop=(ec == 15)))
        W("scalar", tK, t_small)
        ktok = E("scalar", SC.activation(out=KT[:, tb2 * 256:(tb2 + 1) * 256], in_=psK[:],
                                         func=AF.Identity, bias=bks[:],
                                         scale=np.float32(1.0)))
        for hl in range(4):
            qb = hl % 2
            W("tensor", t_wq, qtok[qb])
            tQ = None
            for ec in range(16):
                tQ = E("tensor", TE.matmul(psQ[qb][:],
                                           wqs[:, hl * 2048 + ec * 128: hl * 2048 + ec * 128 + 128],
                                           xt_buf[b][:, ec * 256:(ec + 1) * 256],
                                           start=(ec == 0), stop=(ec == 15)))
            W("scalar", tQ)
            qtok[qb] = E("scalar", SC.activation(
                out=fmB[:, hl * 4096 + tb2 * 256: hl * 4096 + tb2 * 256 + 256], in_=psQ[qb][:],
                func=AF.Identity, bias=bqs[:, hl:hl + 1], scale=np.float32(1.0)))
        for t4 in range(2):
            vb = t4 % 2
            W("tensor", t_wv, vtok[vb])
            tV = None
            for ec in range(16):
                tV = E("tensor", TE.matmul(psV[vb][:],
                                           xt_buf[b][:, ec * 256 + t4 * 128: ec * 256 + t4 * 128 + 128],
                                           wvs[:, ec * 128:(ec + 1) * 128],
                                           start=(ec == 0), stop=(ec == 15)))
            kt = tb2 * 2 + t4
            W("vector", tV, t_small, t_vone)
            vtok[vb] = E("vector", VE.tensor_tensor(
                out=V_aug[:, kt * 129: kt * 129 + 128], in0=psV[vb][:], in1=bvbc[:],
                op=ALU.add))
        xttok[b] = p.tok("tensor")
    W("tensor", p.tok("scalar"), p.tok("vector"))
    stQ.close()
    stAT1.close()

    stS = contextlib.ExitStack()
    eTb = [sb(f"eT{i}", [128, 512], BF16, st=stS) for i in range(2)]
    O_q = [sb(f"Oq{i}", [128, 2048], BF16, st=stS) for i in range(2)]
    rzc = [sb(f"rz{i}", [128, 1], F32, st=stS) for i in range(4)]
    psS = [ps(f"psS{i}", [128, 512], st=stS) for i in range(2)]
    psAV = [ps(f"psAV{i}", [128, 129], st=stS) for i in range(4)]
    setok = [None, None]
    avtok = [None] * 4
    ettok = [None, None]
    sc_tok = [None, None]
    t_scat = []
    for qc in range(8):
        ob = qc % 2
        ev_toks = []
        for hl in range(4):
            nkc = 4 * qc + 4
            for kc in range(nkc):
                b = kc % 2
                W("tensor", setok[b])
                if kc == 0:
                    W("tensor", *[t for t in avtok if t])
                t_s = E("tensor", TE.matmul(psS[b][:], KT[:, kc * 128:(kc + 1) * 128],
                                            fmB[:, hl * 4096 + qc * 512: hl * 4096 + qc * 512 + 512],
                                            start=True, stop=True))
                if kc >= 4 * qc:
                    jm = kc - 4 * qc
                    W("vector", t_s, t_mk[jm])
                    t_s = E("vector", VE.tensor_tensor(
                        out=psS[b][:], in0=psS[b][:],
                        in1=masks[:, jm * 512:(jm + 1) * 512], op=ALU.add))
                W("scalar", t_s, ettok[b])
                t_e = E("scalar", SC.activation(out=eTb[b][:], in_=psS[b][:], func=AF.Exp,
                                                scale=np.float32(RSQRT_SCALE)))
                setok[b] = t_e
                tAV = None
                W("tensor", t_e)
                for qt in range(4):
                    tAV = E("tensor", TE.matmul(psAV[qt][:], eTb[b][:, qt * 128:(qt + 1) * 128],
                                                V_aug[:, kc * 129: kc * 129 + 129],
                                                start=(kc == 0), stop=(kc == nkc - 1)))
                ettok[b] = tAV
            t_fin = p.tok("tensor")
            for qt in range(4):
                W("vector", t_fin, sc_tok[ob] if hl == 0 else None)
                t_rz = E("vector", VE.reciprocal(rzc[qt][:], psAV[qt][:, 128:129]))
                W("vector", t_rz)
                avtok[qt] = E("vector", VE.tensor_scalar(
                    out=O_q[ob][:, qt * 512 + hl * 128: qt * 512 + hl * 128 + 128],
                    in0=psAV[qt][:, 0:128], scalar1=rzc[qt][:], scalar2=None, op0=ALU.mult))
                ev_toks.append(avtok[qt])
        W("gpsimd", t_oix, t_zf, *ev_toks)
        last_sc = None
        for qt in range(4):
            tt_idx = qc * 4 + qt
            last_sc = E("gpsimd", GP.indirect_dma_start(
                out=rs_in.ap(),
                out_offset=bass.IndirectOffsetOnAxis(ap=oix[:, tt_idx:tt_idx + 1], axis=0),
                in_=O_q[ob][:, qt * 512:(qt + 1) * 512], in_offset=None), dma=True)
            t_scat.append(last_sc)
        sc_tok[ob] = last_sc
    W("gpsimd", *t_scat)
    t_rs = E("gpsimd", GP.collective_compute(
        "ReduceScatter", ALU.add, replica_groups=RG,
        ins=[rs_in.ap()], outs=[rs_out.ap()]), cc=True)
    W("tensor", p.tok("scalar"), p.tok("vector"))
    stS.close()
    stAT0.close()

    stX = contextlib.ExitStack()
    ao_buf = [sb(f"ao{i}", [128, 512], F32, st=stX) for i in range(2)]
    fence("gpsimd")
    rs3 = rs_out.ap().rearrange("(t g) c -> t g c", g=4)
    aotok = [None, None]
    for ti in range(8):
        for gb in range(4):
            b = (ti * 4 + gb) % 2
            W("gpsimd", t_rs, aotok[b])
            t_l = E("gpsimd", GP.dma_start(
                out=ao_buf[b][:], in_=rs3[ti * 128:(ti + 1) * 128, gb, :]), dma=True)
            xsl = x_sb[:, ti * 2048 + gb * 512: ti * 2048 + gb * 512 + 512]
            W("vector", t_l)
            aotok[b] = E("vector", VE.tensor_tensor(out=xsl, in0=xsl, in1=ao_buf[b][:],
                                                    op=ALU.add))
    stX.close()

    st5 = contextlib.ExitStack()
    xn_st2 = [sb(f"xn2st{i}", [128, 2048], BF16, st=st5) for i in range(2)]
    sq_scratch = [sb("sq_scr2", [128, 2048], BF16, st=st5)]
    tr_ps2 = [(ps(f"trq{i}", [128, 128], BF16, st=st5), None) for i in range(4)]
    W("scalar", p.tok("vector"))
    W("tensor", p.tok("scalar"), p.tok("vector"))
    stage_rd2 = [None, None]
    for ti in range(8):
        t_r, t_ssq = ln_stats(ti)
        W("vector", t_r)
        t_a = E("vector", VE.tensor_tensor(out=tmp1[:], in0=stats_r[:, ti:ti + 1],
                                           in1=stats_r[:, ti:ti + 1], op=ALU.mult))
        W("vector", t_a)
        t_b = E("vector", VE.tensor_tensor(out=tmp2[:], in0=stats_v[:, ti:ti + 1],
                                           in1=tmp1[:], op=ALU.mult))
        W("vector", t_b)
        t_c = E("vector", VE.tensor_scalar(out=tmp1[:], in0=tmp2[:], scalar1=1e-5,
                                           scalar2=None, op0=ALU.add))
        W("scalar", t_c)
        t_s = E("scalar", SC.activation(out=tmp2[:], in_=tmp1[:], func=AF.Sqrt))
        W("vector", t_s)
        t_r2 = E("vector", VE.reciprocal(tmp3[:], tmp2[:]))
        W("vector", t_r2)
        t_s2 = E("vector", VE.tensor_tensor(out=s2c[:, ti:ti + 1], in0=stats_r[:, ti:ti + 1],
                                            in1=tmp3[:], op=ALU.mult))
        W("vector", t_s2)
        t_o2a = E("vector", VE.tensor_tensor(out=tmp1[:], in0=stats_m[:, ti:ti + 1],
                                             in1=s2c[:, ti:ti + 1], op=ALU.mult))
        W("vector", t_o2a)
        t_o2 = E("vector", VE.tensor_scalar(out=o2c[:, ti:ti + 1], in0=tmp1[:],
                                            scalar1=-1.0, scalar2=None, op0=ALU.mult))
        W("vector", stage_rd2[0])
        t_st = E("vector", VE.tensor_scalar(out=xn_st2[0][:],
                                            in0=x_sb[:, ti * 2048:(ti + 1) * 2048],
                                            scalar1=stats_m[:, ti:ti + 1],
                                            scalar2=stats_r[:, ti:ti + 1],
                                            op0=ALU.subtract, op1=ALU.mult))

        def evacA1(tp, ec, t_tr, ti=ti):
            W("scalar", t_tr)
            return E("scalar", SC.copy(
                out=fmA[:, ec * 1024 + ti * 128: ec * 1024 + ti * 128 + 128], in_=tp[:]))
        _, t_trl = build_fmT(evacA1, xn_st2[0], t_st, tr_ps2)
        stage_rd2[0] = t_trl
        W("vector", t_o2, stage_rd2[1])
        t_st2 = E("vector", VE.tensor_scalar(out=xn_st2[1][:],
                                             in0=x_sb[:, ti * 2048:(ti + 1) * 2048],
                                             scalar1=s2c[:, ti:ti + 1],
                                             scalar2=o2c[:, ti:ti + 1],
                                             op0=ALU.mult, op1=ALU.add))

        def evacB(tp, ec, t_tr, ti=ti):
            W("vector", t_tr, t_small)
            return E("vector", VE.tensor_scalar(
                out=fmB[:, ec * 1024 + ti * 128: ec * 1024 + ti * 128 + 128], in0=tp[:],
                scalar1=g1as[:, ec:ec + 1], scalar2=be1s[:, ec:ec + 1],
                op0=ALU.mult, op1=ALU.add))
        _, t_trl2 = build_fmT(evacB, xn_st2[1], t_st2, tr_ps2)
        stage_rd2[1] = t_trl2
    W("tensor", p.tok("scalar"), p.tok("vector"))
    st5.close()

    st6 = contextlib.ExitStack()
    e_r = sb("e_r", [128, 12], st=st6)
    probs = sb("probs", [128, 12], st=st6)
    biased = sb("biased", [128, 12], st=st6)
    mk1 = sb("mk1", [128, 12], st=st6)
    b2v = sb("b2v", [128, 12], st=st6)
    psL = ps("psL", [128, 12], st=st6)
    t_ltok = None
    for ti in range(8):
        W("tensor", t_clf, t_ltok)
        tL = None
        for ec in range(16):
            tL = E("tensor", TE.matmul(psL[:], fmA[:, ec * 1024 + ti * 128: ec * 1024 + ti * 128 + 128],
                                       clfs[:, ec * 12:(ec + 1) * 12],
                                       start=(ec == 0), stop=(ec == 15)))
        W("vector", tL)
        t1 = E("vector", VE.reduce_max(out=tmp1[:], in_=psL[:], axis=AX.X, negate=True))
        W("scalar", t1)
        t2 = E("scalar", SC.activation(out=e_r[:], in_=psL[:], func=AF.Exp, bias=tmp1[:],
                                       scale=np.float32(1.0), accum_out=tmp2[:]))
        t_ltok = t2
        W("vector", t2)
        t3 = E("vector", VE.reciprocal(tmp3[:], tmp2[:]))
        W("vector", t3)
        t4 = E("vector", VE.tensor_scalar(out=probs[:], in0=e_r[:], scalar1=tmp3[:],
                                          scalar2=None, op0=ALU.mult))
        W("vector", t4, t_small)
        t5 = E("vector", VE.tensor_tensor(out=biased[:], in0=probs[:], in1=ebbc[:], op=ALU.add))
        W("vector", t5)
        t6 = E("vector", VE.reduce_max(out=tmp1[:], in_=biased[:], axis=AX.X))
        W("vector", t6)
        t7 = E("vector", VE.tensor_scalar(out=mk1[:], in0=biased[:], scalar1=tmp1[:],
                                          scalar2=None, op0=ALU.is_ge))
        W("vector", t7)
        t8 = E("vector", VE.scalar_tensor_tensor(out=b2v[:], in0=mk1[:], scalar=-1e30,
                                                 in1=biased[:], op0=ALU.mult, op1=ALU.add))
        W("vector", t8)
        t9 = E("vector", VE.reduce_max(out=tmp1[:], in_=b2v[:], axis=AX.X))
        W("vector", t9)
        t10 = E("vector", VE.tensor_scalar(out=b2v[:], in0=b2v[:], scalar1=tmp1[:],
                                           scalar2=None, op0=ALU.is_ge))
        W("vector", t10)
        t11 = E("vector", VE.tensor_tensor(out=mk1[:], in0=mk1[:], in1=b2v[:], op=ALU.add))
        W("vector", t11)
        t12 = E("vector", VE.scalar_tensor_tensor(out=w_all[:, ti * 12:(ti + 1) * 12],
                                                  in0=probs[:], scalar=1.5, in1=mk1[:],
                                                  op0=ALU.mult, op1=ALU.mult))
        W("vector", t12)
        t13 = E("vector", VE.reduce_sum(out=tmp1[:], in_=w_all[:, ti * 12 + 8: ti * 12 + 12],
                                        axis=AX.X))
        W("vector", t13)
        t14 = E("vector", VE.tensor_scalar(out=tmp2[:], in0=tmp1[:], scalar1=1.0,
                                           scalar2=None, op0=ALU.add))
        W("vector", t14)
        t15 = E("vector", VE.tensor_tensor(out=tmp1[:], in0=tmp2[:], in1=stats_r[:, ti:ti + 1],
                                           op=ALU.mult))
        W("vector", t15)
        t16 = E("vector", VE.tensor_scalar(out=tmp2[:], in0=tmp1[:], scalar1=1.0,
                                           scalar2=None, op0=ALU.add))
        t17 = E("vector", VE.tensor_tensor(out=tmp3[:], in0=stats_m[:, ti:ti + 1],
                                           in1=tmp1[:], op=ALU.mult))
        W("vector", t16, t17)
        xt = x_sb[:, ti * 2048:(ti + 1) * 2048]
        E("vector", VE.tensor_scalar(out=xt, in0=xt, scalar1=tmp2[:], scalar2=tmp3[:],
                                     op0=ALU.mult, op1=ALU.subtract))
    st6.close()

    st7 = contextlib.ExitStack()
    kvw = [[sb(f"kvw{s}{i}", [128, 4096], BF16, st=st7) for i in range(2)]
           for s in range(2)]
    cosb = [sb(f"cosb{i}", [128, 512], F32, st=st7) for i in range(2)]
    sinb = [sb(f"sinb{i}", [128, 512], F32, st=st7) for i in range(2)]
    ktmp = [sb(f"ktmp{i}", [128, 512], F32, st=st7) for i in range(2)]
    c1b = sb("c1b", [128, 512], F32, st=st7)
    c2b = sb("c2b", [128, 512], F32, st=st7)
    bias_kv = sb("bias_kv", [128, 512], F32, st=st7)
    Vst = sb("Vst", [128, 4096], BF16, st=st7)
    Kb = sb("Kb", [128, 512], BF16, st=st7)
    ppst = [sb(f"ppst{i}", [128, 512], BF16, st=st7) for i in range(2)]
    ssp = [sb(f"ssp{i}", [128, 1], F32, st=st7) for i in range(2)]
    psKV = [ps(f"psKV{i}", [128, 512], st=st7) for i in range(2)]
    kvwtok = [[None, None], [None, None]]
    kvwrd = [None, None]
    kvpass = 0
    ktmptok = [None, None]
    ktmprd = [[], []]
    ssptok = [None, None]
    tabtok = [None, None]
    tabrd = [None, None]
    c1tok = None
    c2tok = None
    kbtok = None
    vsttok = [None] * 8
    ppdma = [None, None]
    t_biaskv = None
    fence("gpsimd")
    fence("sync")
    W("tensor", p.tok("scalar"), p.tok("vector"))
    for ds in range(4):
        for kv in (1, 0):
            W("sync", t_biaskv)
            t_biaskv_l = E("sync", SY.dma_start(out=bias_kv[:], in_=b1kv_b[kv, ds]), dma=True)
            sset = kvpass % 2
            kvpass += 1
            for half in range(2):
                W("gpsimd", kvwtok[sset][half], kvwrd[sset])
                kvwtok[sset][half] = E("gpsimd", GP.dma_start(
                    out=kvw[sset][half][:],
                    in_=w1kv[kv, ds][:, half * 4096: half * 4096 + 4096]),
                    dma=True)
            for ti in range(8):
                b = ti % 2
                if ds < 2:
                    W("sync", tabtok[b], tabrd[b])
                    tabtok[b] = E("sync", SY.dma_start(out=cosb[b][:], in_=rcos[ds, ti]), dma=True)
                    tabtok[b] = E("sync", SY.dma_start(out=sinb[b][:], in_=rsin[ds, ti]), dma=True)
                W("tensor", kvwtok[sset][0], kvwtok[sset][1], ktmptok[b])
                tkv = None
                for ec in range(16):
                    buf = kvw[sset][ec // 8]
                    tkv = E("tensor", TE.matmul(
                        psKV[b][:], fmB[:, ec * 1024 + ti * 128: ec * 1024 + ti * 128 + 128],
                        buf[:, (ec % 8) * 512: (ec % 8) * 512 + 512],
                        start=(ec == 0), stop=(ec == 15)))
                kvwrd[sset] = tkv
                W("vector", tkv, t_biaskv_l, *ktmprd[b])
                ktmprd[b] = []
                t_kt = E("vector", VE.tensor_tensor(out=ktmp[b][:], in0=psKV[b][:],
                                                    in1=bias_kv[:], op=ALU.add))
                ktmptok[b] = t_kt
                t_biaskv = t_kt
                sb_ = b
                W("scalar", t_kt, ssptok[sb_], c1tok)
                t_sq = E("scalar", SC.activation(out=c1b[:], in_=ktmp[b][:], func=AF.Square,
                                                 accum_out=ssp[sb_][:]))
                ktmprd[b].append(t_sq)
                ss_dst = ssK if kv == 0 else ssV
                W("vector", t_sq)
                if ds == 0:
                    tss = E("vector", VE.tensor_copy(ss_dst[:, ti:ti + 1], ssp[sb_][:]))
                else:
                    tss = E("vector", VE.tensor_tensor(out=ss_dst[:, ti:ti + 1],
                                                       in0=ss_dst[:, ti:ti + 1],
                                                       in1=ssp[sb_][:], op=ALU.add))
                ssptok[sb_] = tss
                if ds < 2:
                    W("vector", t_kt, tabtok[b], c2tok, t_sq)
                    ts1 = E("vector", VE.tensor_copy(c2b[:, 0:512:2], ktmp[b][:, 1:512:2]))
                    ts2 = E("vector", VE.tensor_copy(c2b[:, 1:512:2], ktmp[b][:, 0:512:2]))
                    tc1 = E("vector", VE.tensor_tensor(out=c1b[:], in0=ktmp[b][:],
                                                       in1=cosb[b][:], op=ALU.mult))
                    c1tok = tc1
                    ktmprd[b].append(tc1)
                    tabrd[b] = tc1
                    W("vector", ts1, ts2)
                    tc2 = E("vector", VE.tensor_tensor(out=c2b[:], in0=c2b[:],
                                                       in1=sinb[b][:], op=ALU.mult))
                    c2tok = tc2
                    tabrd[b] = tc2
                    W("vector", tc1, tc2)
                    if kv == 1:
                        W("vector", vsttok[ti])
                        t_rot = E("vector", VE.tensor_tensor(
                            out=Vst[:, ti * 512:(ti + 1) * 512], in0=c1b[:], in1=c2b[:],
                            op=ALU.add))
                        vsttok[ti] = t_rot
                        c1tok = t_rot
                        c2tok = t_rot
                    else:
                        W("vector", kbtok)
                        t_kb = E("vector", VE.tensor_tensor(out=Kb[:], in0=c1b[:],
                                                            in1=c2b[:], op=ALU.add))
                        c1tok = t_kb
                        c2tok = t_kb
                        W("vector", t_kb, vsttok[ti], ppdma[b])
                        t_pp = E("vector", VE.tensor_tensor(
                            out=ppst[b][:], in0=Vst[:, ti * 512:(ti + 1) * 512],
                            in1=Kb[:], op=ALU.mult))
                        kbtok = t_pp
                        vsttok[ti] = t_pp
                        W("sync", t_pp)
                        ppdma[b] = E("sync", SY.dma_start(
                            out=ppd.ap()[ti, :, ds * 512:(ds + 1) * 512], in_=ppst[b][:]),
                            dma=True)
                else:
                    if kv == 1:
                        W("scalar", t_kt, vsttok[ti], t_sq)
                        t_cp = E("scalar", SC.copy(out=Vst[:, ti * 512:(ti + 1) * 512],
                                                   in_=ktmp[b][:]))
                        vsttok[ti] = t_cp
                        ktmprd[b].append(t_cp)
                    else:
                        W("scalar", t_kt, kbtok, t_sq)
                        t_kb = E("scalar", SC.copy(out=Kb[:], in_=ktmp[b][:]))
                        ktmprd[b].append(t_kb)
                        W("vector", t_kb, vsttok[ti], ppdma[b])
                        t_pp = E("vector", VE.tensor_tensor(
                            out=ppst[b][:], in0=Vst[:, ti * 512:(ti + 1) * 512],
                            in1=Kb[:], op=ALU.mult))
                        kbtok = t_pp
                        vsttok[ti] = t_pp
                        W("sync", t_pp)
                        ppdma[b] = E("sync", SY.dma_start(
                            out=ppd.ap()[ti, :, ds * 512:(ds + 1) * 512], in_=ppst[b][:]),
                            dma=True)
    t_pp_all = p.tok("sync")
    st7.close()

    stA2 = contextlib.ExitStack()
    ppb = [sb(f"ppb{i}", [128, 2048], BF16, st=stA2) for i in range(2)]
    psA = ps("psA", [128, 16], st=stA2)
    fence("sync")
    ppbtok = [None, None]
    scoltok = [None, None]
    tA = None
    for ti in range(8):
        b = ti % 2
        W("sync", t_pp_all, ppbtok[b])
        t_ld = E("sync", SY.dma_start(out=ppb[b][:], in_=ppd.ap()[ti]), dma=True)
        W("vector", p.tok("vector") if ti == 0 else None)
        tm = E("vector", VE.tensor_tensor(out=tmp1[:], in0=ssK[:, ti:ti + 1],
                                          in1=ssV[:, ti:ti + 1], op=ALU.mult))
        W("scalar", tm)
        tsq = E("scalar", SC.activation(out=tmp2[:], in_=tmp1[:], func=AF.Sqrt))
        W("vector", tsq)
        tc = E("vector", VE.reciprocal(tmp3[:], tmp2[:]))
        W("vector", tc, scoltok[b])
        tsc = E("vector", VE.tensor_copy(scol[:, b:b + 1], tmp3[:]))
        W("tensor", tsc, t_ld)
        for ch in range(16):
            tA = E("tensor", TE.matmul(psA[:, ch:ch + 1], ppb[b][:, ch * 128:(ch + 1) * 128],
                                       scol[:, b:b + 1], start=(ti == 0), stop=(ti == 7)))
        ppbtok[b] = tA
        scoltok[b] = tA
    W("vector", tA)
    t_Ar = E("vector", VE.tensor_copy(A_raw[:], psA[:]))
    W("sync", t_Ar)
    t_As = E("sync", SY.dma_start(out=arA_in.ap(), in_=A_raw[:]), dma=True)
    W("gpsimd", t_As)
    W("gpsimd", *kvwtok[0], *kvwtok[1])
    t_arc = E("gpsimd", GP.collective_compute(
        "AllReduce", ALU.add, replica_groups=RG,
        ins=[arA_in.ap()], outs=[arA_out.ap()]), cc=True)
    W("tensor", p.tok("vector"))
    stA2.close()

    stB2 = contextlib.ExitStack()
    Q2T = sb("Q2T", [128, 16384], BF16, st=stB2)
    st8a = contextlib.ExitStack()
    qwb = [sb(f"qwb{i}", [128, 2048], BF16, st=st8a) for i in range(2)]
    psQ2 = [ps(f"psQ2{i}", [128, 512], st=st8a) for i in range(2)]
    fence("gpsimd")
    W("gpsimd", t_arc)
    qwtok = [None, None]
    q2tok = [None, None]
    q2ev = []
    for qch in range(16):
        b = qch % 2
        W("gpsimd", qwtok[b])
        t_w = E("gpsimd", GP.dma_start(out=qwb[b][:], in_=w1q[qch]), dma=True)
        for h2 in range(2):
            W("tensor", t_w, q2tok[h2])
            tq = None
            for ec in range(16):
                tq = E("tensor", TE.matmul(psQ2[h2][:], qwb[b][:, ec * 128:(ec + 1) * 128],
                                           fmB[:, ec * 1024 + h2 * 512: ec * 1024 + h2 * 512 + 512],
                                           start=(ec == 0), stop=(ec == 15)))
            qwtok[b] = tq
            W("vector", tq, t_small)
            q2tok[h2] = E("vector", VE.tensor_scalar(
                out=Q2T[:, qch * 1024 + h2 * 512: qch * 1024 + h2 * 512 + 512],
                in0=psQ2[h2][:], scalar1=b1qs[:, qch:qch + 1], scalar2=None,
                op0=ALU.add))
            q2ev.append(q2tok[h2])
    W("sync", t_arc)
    t_Al = E("sync", SY.dma_start(out=A_raw[:], in_=arA_out.ap()), dma=True)
    W("vector", t_Al)
    t_A1 = E("vector", VE.tensor_scalar(out=A1[:], in0=A_raw[:], scalar1=0.5, scalar2=0.5,
                                        op0=ALU.mult, op1=ALU.add))
    sc_toks = []
    for qch in range(16):
        W("vector", t_A1, q2ev[2 * qch], q2ev[2 * qch + 1])
        sc_toks.append(E("vector", VE.tensor_scalar(
            out=Q2T[:, qch * 1024:(qch + 1) * 1024], in0=Q2T[:, qch * 1024:(qch + 1) * 1024],
            scalar1=A1[:, qch:qch + 1], scalar2=None, op0=ALU.mult)))
    W("tensor", p.tok("vector"))
    st8a.close()
    st8b = contextlib.ExitStack()
    owb = [sb(f"owb{i}", [128, 2048], BF16, st=st8b) for i in range(2)]
    ytmp = [sb(f"ytmp{i}", [128, 512], F32, st=st8b) for i in range(2)]
    b1ob = [sb(f"b1ob{i}", [128, 512], F32, st=st8b) for i in range(2)]
    psY = [ps(f"psY{i}", [128, 512], st=st8b) for i in range(8)]
    fence("gpsimd")
    fence("sync")
    owtok = [None, None]
    btok = [None, None]
    ytok = [None] * 8
    for cs in range(4):
        cb = cs % 2
        W("sync", btok[cb])
        t_b = E("sync", SY.dma_start(out=b1ob[cb][:], in_=b1o_b[cs]), dma=True)
        for qg in range(4):
            b = qg % 2
            W("gpsimd", owtok[b])
            t_w = E("gpsimd", GP.dma_start(
                out=owb[b][:], in_=w1o[cs][:, qg * 2048:(qg + 1) * 2048]), dma=True)
            W("tensor", t_w)
            if qg == 0:
                W("tensor", *[t for t in ytok if t])
            tY = None
            for ti in range(8):
                for k in range(4):
                    qch = qg * 4 + k
                    tY = E("tensor", TE.matmul(
                        psY[ti][:],
                        Q2T[:, qch * 1024 + ti * 128: qch * 1024 + ti * 128 + 128],
                        owb[b][:, k * 512:(k + 1) * 512],
                        start=(qg == 0 and k == 0), stop=(qg == 3 and k == 3)))
            owtok[b] = tY
        t_fin = p.tok("tensor")
        for ti in range(8):
            yb = ti % 2
            W("vector", t_fin, t_b, ytok[ti])
            t_y1 = E("vector", VE.tensor_tensor(out=ytmp[yb][:], in0=psY[ti][:],
                                                in1=b1ob[cb][:], op=ALU.add))
            xsl = x_sb[:, ti * 2048 + cs * 512: ti * 2048 + cs * 512 + 512]
            W("vector", t_y1)
            ytok[ti] = E("vector", VE.tensor_tensor(out=xsl, in0=xsl, in1=ytmp[yb][:],
                                                    op=ALU.add))
        btok[cb] = ytok[7]
    W("tensor", p.tok("vector"))
    st8b.close()
    stB2.close()
    stB.close()

    st9 = contextlib.ExitStack()
    H1 = sb("H1", [128, 8192], BF16, st=st9)
    g1b = [sb(f"g1b{i}", [128, 2048], BF16, st=st9) for i in range(2)]
    u1b = [sb(f"u1b{i}", [128, 2048], BF16, st=st9) for i in range(2)]
    sgb2 = [sb(f"sgb2{i}", [128, 512], F32, st=st9) for i in range(2)]
    d1b = sb("d1b", [128, 16384], BF16, st=st9)
    psG1 = [ps(f"psGm{i}", [128, 512], st=st9) for i in range(2)]
    psU1 = [ps(f"psUm{i}", [128, 512], st=st9) for i in range(2)]
    psM = [ps(f"psM{i}", [128, 512], st=st9) for i in range(2)]
    fence("gpsimd")
    g1tok = [None, None]
    u1tok = [None, None]
    sg1tok = [None, None]
    h1tok = [None, None]
    d1tok = None
    mevtok = [None, None]
    for e in range(8):
        t_h1_all = []
        for ic in range(8):
            b = ic % 2
            W("gpsimd", g1tok[b])
            t_g = E("gpsimd", GP.dma_start(out=g1b[b][:], in_=g1[e, :, ic, :]), dma=True)
            W("gpsimd", u1tok[b])
            t_u = E("gpsimd", GP.dma_start(out=u1b[b][:], in_=u1[e, :, ic, :]), dma=True)
            for th in range(2):
                pb = (ic * 2 + th) % 2
                W("tensor", t_g, sg1tok[pb])
                tG = None
                for hc in range(16):
                    tG = E("tensor", TE.matmul(psG1[pb][:], g1b[b][:, hc * 128:(hc + 1) * 128],
                                               fmA[:, hc * 1024 + th * 512: hc * 1024 + th * 512 + 512],
                                               start=(hc == 0), stop=(hc == 15)))
                g1tok[b] = tG
                W("scalar", tG, h1tok[pb])
                t_sg = E("scalar", SC.activation(out=sgb2[pb][:], in_=psG1[pb][:], func=AF.Silu))
                sg1tok[pb] = t_sg
                W("tensor", t_u, h1tok[pb])
                tU = None
                for hc in range(16):
                    tU = E("tensor", TE.matmul(psU1[pb][:], u1b[b][:, hc * 128:(hc + 1) * 128],
                                               fmA[:, hc * 1024 + th * 512: hc * 1024 + th * 512 + 512],
                                               start=(hc == 0), stop=(hc == 15)))
                u1tok[b] = tU
                W("vector", tU, t_sg)
                th1 = E("vector", VE.tensor_tensor(
                    out=H1[:, ic * 1024 + th * 512: ic * 1024 + th * 512 + 512],
                    in0=sgb2[pb][:], in1=psU1[pb][:], op=ALU.mult))
                h1tok[pb] = th1
                t_h1_all.append(th1)
        W("gpsimd", d1tok)
        t_d = E("gpsimd", GP.dma_start(
            out=d1b[:].rearrange("p (a m) -> p a m", a=8), in_=d1[e]), dma=True)
        for ti in range(8):
            for cs in range(4):
                mb = (ti * 4 + cs) % 2
                W("tensor", t_d, mevtok[mb], *t_h1_all)
                t_h1_all = []
                tM = None
                for icl in range(8):
                    tM = E("tensor", TE.matmul(
                        psM[mb][:],
                        H1[:, icl * 1024 + ti * 128: icl * 1024 + ti * 128 + 128],
                        d1b[:, icl * 2048 + cs * 512: icl * 2048 + cs * 512 + 512],
                        start=(icl == 0), stop=(icl == 7)))
                d1tok = tM
                xsl = x_sb[:, ti * 2048 + cs * 512: ti * 2048 + cs * 512 + 512]
                W("vector", tM)
                mevtok[mb] = E("vector", VE.scalar_tensor_tensor(
                    out=xsl, in0=psM[mb][:], scalar=w_all[:, ti * 12 + e: ti * 12 + e + 1],
                    in1=xsl, op0=ALU.mult, op1=ALU.add))
    W("tensor", p.tok("vector"))
    st9.close()

    W("sync", p.tok("vector"))
    st_toks = []
    for ti in range(8):
        st_toks.append(E("sync", SY.dma_start(out=out_x[ti * 128:(ti + 1) * 128, :],
                                              in_=x_sb[:, ti * 2048:(ti + 1) * 2048]), dma=True))
    W("sync", *st_toks)
    return nc, stack


def _host_layouts(inp):
    L = {}
    L["g0"] = np.ascontiguousarray(
        inp["L0_ffn_gate_w"].reshape(44, 128, 16, 128).transpose(0, 3, 2, 1)).reshape(44, 128, 2048)
    L["u0"] = np.ascontiguousarray(
        inp["L0_ffn_up_w"].reshape(44, 128, 16, 128).transpose(0, 3, 2, 1)).reshape(44, 128, 2048)
    L["d0"] = np.ascontiguousarray(inp["L0_ffn_down_w"].T).reshape(44, 128, 2048)
    L["gs0"] = np.ascontiguousarray(inp["L0_ffn_gate_s"].reshape(44, 128).T)
    L["us0"] = np.ascontiguousarray(inp["L0_ffn_up_s"].reshape(44, 128).T)
    L["ds_b"] = np.ascontiguousarray(
        np.broadcast_to(inp["L0_ffn_down_s"].reshape(1, 2048), (128, 2048)))
    W_in = inp["L1_attn_W_in"]
    L["w1q"] = np.ascontiguousarray(
        W_in[:, :2048].reshape(16, 128, 16, 128).transpose(2, 1, 0, 3)).reshape(16, 128, 2048)
    L["w1kv"] = np.ascontiguousarray(
        W_in[:, 2048:].reshape(16, 128, 2, 4, 512).transpose(2, 3, 1, 0, 4)).reshape(2, 4, 128, 8192)
    b_in = inp["L1_attn_b_in"]
    L["b1q"] = np.ascontiguousarray(b_in[:2048].reshape(16, 128).T)
    L["b1kv_b"] = np.ascontiguousarray(
        np.broadcast_to(b_in[2048:].reshape(2, 4, 1, 512), (2, 4, 128, 512)))
    L["w1o"] = np.ascontiguousarray(
        inp["L1_attn_W_out"].reshape(16, 128, 4, 512).transpose(2, 1, 0, 3)).reshape(4, 128, 8192)
    L["b1o_b"] = np.ascontiguousarray(
        np.broadcast_to(inp["L1_attn_b_out"].reshape(4, 1, 512), (4, 128, 512)))
    L["g1a"] = np.ascontiguousarray(inp["L1_attn_gamma"].reshape(16, 128).T)
    L["be1"] = np.ascontiguousarray(inp["L1_attn_beta"].reshape(16, 128).T)
    L["clf"] = np.ascontiguousarray(
        inp["L1_ffn_classifier_w"].T.reshape(16, 128, 12).transpose(1, 0, 2)).reshape(128, 192)
    L["eb_b"] = np.ascontiguousarray(
        np.broadcast_to(inp["L1_ffn_e_score_correction_bias"].reshape(1, 12), (128, 12)))
    L["g1"] = np.ascontiguousarray(
        inp["L1_ffn_gate_w"].reshape(8, 8, 128, 16, 128).transpose(0, 4, 1, 3, 2)).reshape(8, 128, 8, 2048)
    L["u1"] = np.ascontiguousarray(
        inp["L1_ffn_up_w"].reshape(8, 8, 128, 16, 128).transpose(0, 4, 1, 3, 2)).reshape(8, 128, 8, 2048)
    L["d1"] = np.ascontiguousarray(
        inp["L1_ffn_down_w"].reshape(8, 2048, 8, 128).transpose(0, 3, 2, 1)).reshape(8, 128, 8, 2048)
    inv = (10000.0 ** (-(np.arange(0, 1024, 2, dtype=np.float64) / 1024.0)))
    pos = np.arange(4096, dtype=np.float64)
    fr = pos[:, None] * inv[None, :]
    cosr = np.repeat(np.cos(fr), 2, axis=1)
    sinr = np.repeat(np.sin(fr), 2, axis=1)
    sinr[:, 0::2] *= -1.0
    L["_cos_full"] = cosr.astype(np.float32)
    L["_sin_full"] = sinr.astype(np.float32)
    return L


def kernel(**inp):
    inp = {k: np.asarray(v) for k, v in inp.items()}
    L = _host_layouts(inp)
    x = inp["L0_attn_x"]
    Wq, bq = inp["L0_attn_Wq"], inp["L0_attn_bq"]
    Wk, bk = inp["L0_attn_Wk"], inp["L0_attn_bk"]
    Wv, bv = inp["L0_attn_Wv"], inp["L0_attn_bv"]
    shared_keys = ["g0", "u0", "d0", "gs0", "us0", "ds_b", "w1q", "w1kv", "b1q",
                   "b1kv_b", "w1o", "b1o_b", "g1a", "be1", "clf", "eb_b",
                   "g1", "u1", "d1"]
    in_maps = []
    for c in range(8):
        b, j = c // 4, c % 4
        m = {k: L[k] for k in shared_keys}
        m["x_own"] = np.ascontiguousarray(x[b, j * 1024:(j + 1) * 1024, :])
        m["wq_o"] = np.ascontiguousarray(
            Wq[4 * j:4 * j + 4].reshape(4, 16, 128, 128).transpose(2, 0, 1, 3)).reshape(128, 8192)
        m["bq_o"] = np.ascontiguousarray(bq[4 * j:4 * j + 4].T)
        m["wk_o"] = np.ascontiguousarray(
            Wk[j].reshape(16, 128, 128).transpose(1, 0, 2)).reshape(128, 2048)
        m["wv_o"] = np.ascontiguousarray(
            Wv[j].reshape(16, 128, 128).transpose(1, 0, 2)).reshape(128, 2048)
        m["bk_o"] = bk[j].reshape(128, 1).copy()
        m["bv_b"] = np.ascontiguousarray(np.broadcast_to(bv[j].reshape(1, 128), (128, 128)))
        oid = ((np.arange(32)[None, :] * 128 + np.arange(128)[:, None]) * 4 + j).astype(np.int32)
        m["oidx"] = oid
        tsl = slice(j * 1024, (j + 1) * 1024)
        m["rcos"] = np.ascontiguousarray(
            L["_cos_full"][tsl].reshape(8, 128, 2, 512).transpose(2, 0, 1, 3))
        m["rsin"] = np.ascontiguousarray(
            L["_sin_full"][tsl].reshape(8, 128, 2, 512).transpose(2, 0, 1, 3))
        in_maps.append(m)
    nc, stack = build()
    res = run_bass_kernel_spmd(nc, in_maps, core_ids=list(range(8)))
    stack.close()
    out = np.empty((2, 4096, 2048), dtype=np.float32)
    for c in range(8):
        b, j = c // 4, c % 4
        out[b, j * 1024:(j + 1) * 1024, :] = res.results[c]["out_x"]
    return out


# version 08
# speedup vs baseline: 1.0462x; 1.0417x over previous
import contextlib
import numpy as np
import ml_dtypes
import concourse.bass as bass
import concourse.mybir as mybir
from concourse.bass_utils import run_bass_kernel_spmd

F32 = mybir.dt.float32
BF16 = mybir.dt.bfloat16
I32 = mybir.dt.int32
AF = mybir.ActivationFunctionType
ALU = mybir.AluOpType
AX = mybir.AxisListType
BF = ml_dtypes.bfloat16
RG = [[0, 1, 2, 3], [4, 5, 6, 7]]
ENGS = ["sync", "scalar", "vector", "tensor", "gpsimd"]
RSQRT_SCALE = float(1.0 / np.sqrt(128.0))


class P:

    def __init__(self, nc, stack):
        self.nc = nc
        self.sems = {e: stack.enter_context(nc.semaphore(f"S_{e}")) for e in ENGS}
        self.cnt = {e: 0 for e in ENGS}
        self.lastw = {}
        self.eng = {e: getattr(nc, e) for e in ENGS}

    def emit(self, ename, instr, dma=False, cc=False):
        n = 16 if dma else 1
        instr.then_inc(self.sems[ename], n)
        self.cnt[ename] += n
        return (ename, self.cnt[ename], dma or cc)

    def wait(self, consumer, *tokens):
        for t in tokens:
            if t is None:
                continue
            pe, v, _ = t
            key = (consumer, pe)
            if self.lastw.get(key, -1) >= v:
                continue
            self.eng[consumer].wait_ge(self.sems[pe], v)
            self.lastw[key] = v

    def tok(self, ename):
        return (ename, self.cnt[ename], False)


def build():
    nc = bass.Bass("TRN2", target_bir_lowering=False)

    def din(name, shape, dt=F32):
        return nc.dram_tensor(name, shape, dt, kind="ExternalInput").ap()

    x_own = din("x_own", [1024, 2048])
    wq_o = din("wq_o", [128, 8192])
    wk_o = din("wk_o", [128, 2048])
    wv_o = din("wv_o", [128, 2048])
    bq_o = din("bq_o", [128, 4])
    bk_o = din("bk_o", [128, 1])
    bv_b = din("bv_b", [128, 128])
    oidx = din("oidx", [128, 32], I32)
    g0 = din("g0", [44, 128, 2048])
    u0 = din("u0", [44, 128, 2048])
    d0 = din("d0", [44, 128, 2048])
    gs0 = din("gs0", [128, 44])
    us0 = din("us0", [128, 44])
    ds_b = din("ds_b", [128, 2048])
    w1q = din("w1q", [16, 128, 2048])
    b1q = din("b1q", [128, 16])
    w1kv = din("w1kv", [2, 4, 128, 8192])
    b1kv_b = din("b1kv_b", [2, 4, 128, 512])
    w1o = din("w1o", [4, 128, 8192])
    b1o_b = din("b1o_b", [4, 128, 512])
    g1a_i = din("g1a", [128, 16])
    be1_i = din("be1", [128, 16])
    clf = din("clf", [128, 192])
    eb_b = din("eb_b", [128, 12])
    g1 = din("g1", [8, 128, 8, 2048])
    u1 = din("u1", [8, 128, 8, 2048])
    d1 = din("d1", [8, 128, 8, 2048])
    rcos = din("rcos", [2, 8, 128, 512])
    rsin = din("rsin", [2, 8, 128, 512])

    out_x = nc.dram_tensor("out_x", [1024, 2048], F32, kind="ExternalOutput").ap()

    ag_in = nc.dram_tensor("ag_in", [2048, 1024], BF16)
    ag_out = nc.dram_tensor("ag_out", [8192, 1024], BF16)
    rs_in = [nc.dram_tensor(f"rs_in{m}", [8192, 512], BF16) for m in range(2)]
    rs_out = [nc.dram_tensor(f"rs_out{m}", [2048, 512], BF16) for m in range(2)]
    arA_in = nc.dram_tensor("arA_in", [128, 16], F32)
    arA_out = nc.dram_tensor("arA_out", [128, 16], F32)
    ppd = nc.dram_tensor("ppd", [8, 128, 2048], BF16)

    stack = contextlib.ExitStack()
    p = P(nc, stack)
    E, W = p.emit, p.wait
    SY, SC, VE, TE, GP = nc.sync, nc.scalar, nc.vector, nc.tensor, nc.gpsimd

    def sb(name, shape, dt=F32, st=stack):
        return st.enter_context(nc.sbuf_tensor(name, shape, dt))

    def ps(name, shape, dt=F32, st=stack):
        return st.enter_context(nc.psum_tensor(name, shape, dt))

    def fence(q, engs=("tensor", "vector", "scalar")):
        W(q, *[p.tok(e) for e in engs])

    x_sb = sb("x_sb", [128, 8 * 2048])
    fmA = sb("fmA", [128, 16384], BF16)
    masks = sb("masks", [128, 4 * 512])
    ident = sb("ident", [128, 128], BF16)
    bqs = sb("bqs", [128, 4]); bks = sb("bks", [128, 1])
    bvbc = sb("bvbc", [128, 128]); oix = sb("oix", [128, 32], I32)
    gss = sb("gss", [128, 44]); uss = sb("uss", [128, 44])
    b1qs = sb("b1qs", [128, 16])
    g1as = sb("g1as", [128, 16]); be1s = sb("be1s", [128, 16])
    clfs = sb("clfs", [128, 192], BF16); ebbc = sb("ebbc", [128, 12])
    stats_m = sb("stats_m", [128, 8]); stats_r = sb("stats_r", [128, 8])
    stats_v = sb("stats_v", [128, 8])
    s2c = sb("s2c", [128, 8]); o2c = sb("o2c", [128, 8])
    w_all = sb("w_all", [128, 96])
    A1 = sb("A1", [128, 16]); A_raw = sb("A_raw", [128, 16])
    ssK = sb("ssK", [128, 8]); ssV = sb("ssV", [128, 8])
    scol = sb("scol", [128, 2], BF16)
    tmp1 = sb("tmp1", [128, 1]); tmp2 = sb("tmp2", [128, 1]); tmp3 = sb("tmp3", [128, 1])
    st_i = contextlib.ExitStack()
    iota = sb("iota", [128, 512], I32, st=st_i)

    t_x = [E("sync", SY.dma_start(out=x_sb[:, ti * 2048:(ti + 1) * 2048],
                                  in_=x_own[ti * 128:(ti + 1) * 128, :]), dma=True)
           for ti in range(8)]
    smalls = [(bqs, bq_o), (bks, bk_o), (bvbc, bv_b), (gss, gs0), (uss, us0),
              (b1qs, b1q), (g1as, g1a_i), (be1s, be1_i), (ebbc, eb_b)]
    t_small = None
    for buf, src in smalls:
        t_small = E("sync", SY.dma_start(out=buf[:], in_=src), dma=True)
    t_oix = E("sync", SY.dma_start(out=oix[:], in_=oidx), dma=True)
    t_clf = E("gpsimd", GP.dma_start(out=clfs[:], in_=clf), dma=True)
    t_io = E("gpsimd", GP.iota(out=iota[:], pattern=[[-1, 512]], base=0,
                               channel_multiplier=1))
    W("vector", t_io)
    t_id = E("vector", VE.tensor_scalar(out=ident[:], in0=iota[:, 0:128], scalar1=0.0,
                                        scalar2=None, op0=ALU.is_equal))
    t_mk = []
    for jm in range(4):
        t_mk.append(E("vector", VE.tensor_scalar(
            out=masks[:, jm * 512:(jm + 1) * 512], in0=iota[:], scalar1=float(-128 * jm),
            scalar2=-1e30, op0=ALU.is_gt, op1=ALU.mult)))
    st_i.close()

    def ln_stats(ti):
        xt = x_sb[:, ti * 2048:(ti + 1) * 2048]
        t_sum = E("vector", VE.reduce_sum(out=tmp1[:], in_=xt, axis=AX.X))
        t_ssq = E("scalar", SC.activation(out=sq_scratch[0][:], in_=xt, func=AF.Square,
                                          accum_out=tmp2[:]))
        W("vector", t_sum)
        t_m = E("vector", VE.tensor_scalar(out=stats_m[:, ti:ti + 1], in0=tmp1[:],
                                           scalar1=1.0 / 2048.0, scalar2=None, op0=ALU.mult))
        W("vector", t_m, t_ssq)
        t_a = E("vector", VE.tensor_tensor(out=tmp1[:], in0=stats_m[:, ti:ti + 1],
                                           in1=stats_m[:, ti:ti + 1], op=ALU.mult))
        t_b = E("vector", VE.tensor_scalar(out=tmp3[:], in0=tmp2[:],
                                           scalar1=1.0 / 2048.0, scalar2=None, op0=ALU.mult))
        W("vector", t_a, t_b)
        t_v = E("vector", VE.tensor_tensor(out=stats_v[:, ti:ti + 1], in0=tmp3[:],
                                           in1=tmp1[:], op=ALU.subtract))
        W("vector", t_v)
        t_e = E("vector", VE.tensor_scalar(out=tmp1[:], in0=stats_v[:, ti:ti + 1],
                                           scalar1=1e-5, scalar2=None, op0=ALU.add))
        W("scalar", t_e)
        t_s = E("scalar", SC.activation(out=tmp2[:], in_=tmp1[:], func=AF.Sqrt))
        W("vector", t_s)
        t_r = E("vector", VE.reciprocal(stats_r[:, ti:ti + 1], tmp2[:]))
        return t_r, t_ssq

    def build_fmT(dest_fn, stage_buf, t_stage, tr_pss):
        toks = []
        t_tr = None
        for ec in range(16):
            b = ec % len(tr_pss)
            tp, t_ev_prev = tr_pss[b]
            W("tensor", t_stage, t_ev_prev, t_id)
            t_tr = E("tensor", TE.transpose(tp[:], stage_buf[:, ec * 128:(ec + 1) * 128],
                                            ident[:]))
            t_ev = dest_fn(tp, ec, t_tr)
            tr_pss[b] = (tp, t_ev)
            toks.append(t_ev)
        return toks, t_tr

    stA = contextlib.ExitStack()
    xn_stage = [sb(f"xn_st{i}", [128, 2048], BF16, st=stA) for i in range(2)]
    sq_scratch = [sb("sq_scr", [128, 2048], BF16, st=stA)]
    zbuf = sb("zbuf", [128, 2048], BF16, st=stA)
    tr_ps_l = [(ps(f"trp{i}", [128, 128], BF16, st=stA), None) for i in range(4)]
    t_zb = E("gpsimd", GP.memset(zbuf[:], 0.0))
    t_zf = None
    for m in range(2):
        for k in range(16):
            t_zf = E("gpsimd", GP.dma_start(
                out=rs_in[m].ap()[k * 512:(k + 1) * 512, :].rearrange(
                    "(a p) m -> p a m", p=128),
                in_=zbuf[:].rearrange("p (a m) -> p a m", a=4)), dma=True)
    t_fmA_done = []
    stage_rd = [None, None]
    for ti in range(8):
        W("vector", t_x[ti])
        W("scalar", t_x[ti])
        t_r, _ = ln_stats(ti)
        sbuf_st = xn_stage[ti % 2]
        W("vector", t_r, stage_rd[ti % 2])
        t_st = E("vector", VE.tensor_scalar(out=sbuf_st[:],
                                            in0=x_sb[:, ti * 2048:(ti + 1) * 2048],
                                            scalar1=stats_m[:, ti:ti + 1],
                                            scalar2=stats_r[:, ti:ti + 1],
                                            op0=ALU.subtract, op1=ALU.mult))

        def evacA(tp, ec, t_tr, ti=ti):
            W("scalar", t_tr)
            return E("scalar", SC.copy(
                out=fmA[:, ec * 1024 + ti * 128: ec * 1024 + ti * 128 + 128], in_=tp[:]))
        toks, t_trl = build_fmT(evacA, sbuf_st, t_st, tr_ps_l)
        t_fmA_done += toks
        stage_rd[ti % 2] = t_trl
    W("sync", *t_fmA_done)
    t_agst = []
    for ec in range(16):
        t_agst.append(E("sync", SY.dma_start(
            out=ag_in.ap()[ec * 128:(ec + 1) * 128, :],
            in_=fmA[:, ec * 1024:(ec + 1) * 1024]), dma=True))
    stA.close()
    agc = [None]

    stF = contextlib.ExitStack()
    H = sb("H", [128, 44 * 512], BF16, st=stF)
    gbuf = [sb(f"gb{i}", [128, 2048], BF16, st=stF) for i in range(2)]
    ubuf = [sb(f"ub{i}", [128, 2048], BF16, st=stF) for i in range(2)]
    sgb = [sb(f"sg{i}", [128, 512], F32, st=stF) for i in range(2)]
    dbuf = [sb(f"db{i}", [128, 1024], BF16, st=stF) for i in range(2)]
    dtmp = [sb(f"dt{i}", [128, 1024], F32, st=stF) for i in range(2)]
    ds_bc = sb("ds_bc", [128, 2048], st=stF)
    stgG = sb("stgG", [128, 2048], F32, st=stF)
    stgU = sb("stgU", [128, 2048], F32, st=stF)
    fence("gpsimd")
    W("gpsimd", t_zf)
    fence("sync")
    t_ds = E("sync", SY.dma_start(out=ds_bc[:], in_=ds_b), dma=True)
    W("tensor", p.tok("scalar"))

    gtok = [None, None]
    utok = [None, None]
    stg_rd = [None, None]
    last_sw = [None]

    def ffn_half(th, hw_n=0, kick_ag=False, hw_down=False):
        stG = contextlib.ExitStack()
        psG = [ps(f"psG{th}{i}", [128, 512], st=stG) for i in range(2)]
        psU = [ps(f"psU{th}{i}", [128, 512], st=stG) for i in range(2)]
        hvtok = [None, None]
        sgtok = [None, None]
        gmt = [None, None]
        umt = [None, None]
        for ic in range(44):
            b = ic % 2
            if ic < hw_n:
                W("sync", stg_rd[0])
                t_gf = E("sync", SY.dma_start(out=stgG[:], in_=g0[ic]), dma=True)
                W("scalar", stg_rd[1])
                t_uf = E("scalar", SC.dma_start(out=stgU[:], in_=u0[ic]), dma=True)
                W("scalar", t_gf, gtok[b])
                t_g = E("scalar", SC.copy(out=gbuf[b][:], in_=stgG[:]))
                stg_rd[0] = t_g
                W("scalar", t_uf, utok[b])
                t_u = E("scalar", SC.copy(out=ubuf[b][:], in_=stgU[:]))
                stg_rd[1] = t_u
            else:
                W("gpsimd", agc[0], gtok[b])
                t_g = E("gpsimd", GP.dma_start(out=gbuf[b][:], in_=g0[ic]), dma=True)
                W("gpsimd", utok[b])
                t_u = E("gpsimd", GP.dma_start(out=ubuf[b][:], in_=u0[ic]), dma=True)
                last_sw[0] = t_u
            W("tensor", t_g, sgtok[b])
            tG = None
            for ec in range(16):
                tG = E("tensor", TE.matmul(psG[b][:], gbuf[b][:, ec * 128:(ec + 1) * 128],
                                           fmA[:, ec * 1024 + th * 512: ec * 1024 + th * 512 + 512],
                                           start=(ec == 0), stop=(ec == 15)))
            gtok[b] = tG
            gmt[b] = tG
            W("scalar", tG, t_small, hvtok[b])
            t_sg = E("scalar", SC.activation(out=sgb[b][:], in_=psG[b][:], func=AF.Silu,
                                             scale=gss[:, ic:ic + 1]))
            sgtok[b] = t_sg
            W("tensor", t_u, hvtok[b])
            tU = None
            for ec in range(16):
                tU = E("tensor", TE.matmul(psU[b][:], ubuf[b][:, ec * 128:(ec + 1) * 128],
                                           fmA[:, ec * 1024 + th * 512: ec * 1024 + th * 512 + 512],
                                           start=(ec == 0), stop=(ec == 15)))
            utok[b] = tU
            W("vector", tU, t_sg)
            hvtok[b] = E("vector", VE.scalar_tensor_tensor(
                out=H[:, ic * 512:(ic + 1) * 512], in0=psU[b][:], scalar=uss[:, ic:ic + 1],
                in1=sgb[b][:], op0=ALU.mult, op1=ALU.mult))
        t_hv = p.tok("vector")
        if kick_ag:
            W("gpsimd", *t_agst)
            W("gpsimd", t_clf, t_zf, last_sw[0])
            agc[0] = E("gpsimd", GP.collective_compute(
                "AllGather", ALU.bypass, replica_groups=RG,
                ins=[ag_in.ap()], outs=[ag_out.ap()]), cc=True)
        W("tensor", p.tok("scalar"), t_hv)
        stG.close()
        stD = contextlib.ExitStack()
        psD = [ps(f"psD{th}{i}", [128, 1024], st=stD) for i in range(4)]
        dtok = [None, None]
        evtok = [None] * 4
        for half in range(2):
            for ic in range(44):
                b = ic % 2
                if hw_down:
                    sref = stgG if b == 0 else stgU
                    W("sync", stg_rd[b])
                    t_df = E("sync", SY.dma_start(
                        out=sref[:, 0:1024],
                        in_=d0[ic][:, half * 1024:(half + 1) * 1024]), dma=True)
                    W("scalar", t_df, dtok[b])
                    t_d = E("scalar", SC.copy(out=dbuf[b][:], in_=sref[:, 0:1024]))
                    stg_rd[b] = t_d
                else:
                    W("gpsimd", agc[0], dtok[b])
                    t_d = E("gpsimd", GP.dma_start(
                        out=dbuf[b][:], in_=d0[ic][:, half * 1024:(half + 1) * 1024]),
                        dma=True)
                W("tensor", t_d)
                if ic == 0:
                    W("tensor", *[t for t in evtok if t])
                tmm = None
                for t4 in range(4):
                    for h2 in range(2):
                        tmm = E("tensor", TE.matmul(
                            psD[t4][:, h2 * 512:(h2 + 1) * 512],
                            H[:, ic * 512 + t4 * 128: ic * 512 + t4 * 128 + 128],
                            dbuf[b][:, h2 * 512:(h2 + 1) * 512],
                            start=(ic == 0), stop=(ic == 43)))
                dtok[b] = tmm
            t_last_mm = p.tok("tensor")
            for t4 in range(4):
                db = t4 % 2
                xsl = x_sb[:, (th * 4 + t4) * 2048 + half * 1024:
                           (th * 4 + t4) * 2048 + half * 1024 + 1024]
                W("vector", t_last_mm, t_ds, evtok[t4])
                t_e1 = E("vector", VE.tensor_tensor(out=dtmp[db][:], in0=psD[t4][:],
                                                    in1=ds_bc[:, half * 1024:half * 1024 + 1024],
                                                    op=ALU.mult))
                W("vector", t_e1)
                evtok[t4] = E("vector", VE.tensor_tensor(out=xsl, in0=xsl, in1=dtmp[db][:],
                                                         op=ALU.add))
        W("tensor", p.tok("vector"))
        stD.close()

    W("sync", t_zf)
    W("scalar", p.tok("tensor"), p.tok("vector"), t_zf)
    ffn_half(0, hw_n=0, kick_ag=False, hw_down=False)
    W("gpsimd", *t_agst)
    W("gpsimd", t_clf, t_zf, last_sw[0])
    agc[0] = E("gpsimd", GP.collective_compute(
        "AllGather", ALU.bypass, replica_groups=RG,
        ins=[ag_in.ap()], outs=[ag_out.ap()]), cc=True)
    ffn_half(1, hw_n=20)
    stF.close()

    stB = contextlib.ExitStack()
    fmB = sb("fmB", [128, 16384], BF16, st=stB)
    stAT0 = contextlib.ExitStack()
    KT = sb("KT", [128, 4096], BF16, st=stAT0)
    V_aug = sb("V_aug", [128, 32 * 129], BF16, st=stAT0)
    fence("gpsimd")
    t_vone = E("gpsimd", GP.memset(V_aug[:], 1.0))

    stAT1 = contextlib.ExitStack()
    wqs = sb("wqs", [128, 8192], BF16, st=stAT1)
    wks = sb("wks", [128, 2048], BF16, st=stAT1)
    wvs = sb("wvs", [128, 2048], BF16, st=stAT1)
    xt_buf = [sb(f"xtb{i}", [128, 4096], BF16, st=stAT1) for i in range(2)]
    t_wq = E("gpsimd", GP.dma_start(out=wqs[:], in_=wq_o), dma=True)
    t_wk = E("gpsimd", GP.dma_start(out=wks[:], in_=wk_o), dma=True)
    t_wv = E("gpsimd", GP.dma_start(out=wvs[:], in_=wv_o), dma=True)
    fence("sync")
    stQ = contextlib.ExitStack()
    psK = ps("psK", [128, 256], st=stQ)
    psQ = [ps(f"psQ{i}", [128, 256], st=stQ) for i in range(2)]
    psV = [ps(f"psV{i}", [128, 128], st=stQ) for i in range(2)]
    ag3 = ag_out.ap().rearrange("(a p) m -> a p m", p=128)
    xttok = [None, None]
    ktok = None
    qtok = [None, None]
    vtok = [None, None]
    for tb2 in range(16):
        r, s = tb2 // 4, tb2 % 4
        b = tb2 % 2
        W("sync", agc[0], xttok[b])
        t_xt = E("sync", SY.dma_start(
            out=xt_buf[b][:].rearrange("p (a m) -> p a m", a=16),
            in_=ag3[r * 16:(r + 1) * 16, :, s * 256:(s + 1) * 256].rearrange("a p m -> p a m")),
            dma=True)
        W("tensor", t_xt, t_wk, ktok)
        tK = None
        for ec in range(16):
            tK = E("tensor", TE.matmul(psK[:], wks[:, ec * 128:(ec + 1) * 128],
                                       xt_buf[b][:, ec * 256:(ec + 1) * 256],
                                       start=(ec == 0), stop=(ec == 15)))
        W("scalar", tK, t_small)
        ktok = E("scalar", SC.activation(out=KT[:, tb2 * 256:(tb2 + 1) * 256], in_=psK[:],
                                         func=AF.Identity, bias=bks[:],
                                         scale=np.float32(1.0)))
        for hl in range(4):
            qb = hl % 2
            W("tensor", t_wq, qtok[qb])
            tQ = None
            for ec in range(16):
                tQ = E("tensor", TE.matmul(psQ[qb][:],
                                           wqs[:, hl * 2048 + ec * 128: hl * 2048 + ec * 128 + 128],
                                           xt_buf[b][:, ec * 256:(ec + 1) * 256],
                                           start=(ec == 0), stop=(ec == 15)))
            W("scalar", tQ)
            qtok[qb] = E("scalar", SC.activation(
                out=fmB[:, hl * 4096 + tb2 * 256: hl * 4096 + tb2 * 256 + 256], in_=psQ[qb][:],
                func=AF.Identity, bias=bqs[:, hl:hl + 1], scale=np.float32(1.0)))
        for t4 in range(2):
            vb = t4 % 2
            W("tensor", t_wv, vtok[vb])
            tV = None
            for ec in range(16):
                tV = E("tensor", TE.matmul(psV[vb][:],
                                           xt_buf[b][:, ec * 256 + t4 * 128: ec * 256 + t4 * 128 + 128],
                                           wvs[:, ec * 128:(ec + 1) * 128],
                                           start=(ec == 0), stop=(ec == 15)))
            kt = tb2 * 2 + t4
            W("vector", tV, t_small, t_vone)
            vtok[vb] = E("vector", VE.tensor_tensor(
                out=V_aug[:, kt * 129: kt * 129 + 128], in0=psV[vb][:], in1=bvbc[:],
                op=ALU.add))
        xttok[b] = p.tok("tensor")
    W("tensor", p.tok("scalar"), p.tok("vector"))
    stQ.close()
    stAT1.close()

    stS = contextlib.ExitStack()
    eTb = [sb(f"eT{i}", [128, 512], BF16, st=stS) for i in range(2)]
    O_q = [sb(f"Oq{i}", [128, 2048], BF16, st=stS) for i in range(2)]
    rzc = [sb(f"rz{i}", [128, 1], F32, st=stS) for i in range(4)]
    ao_wide = sb("ao_wide", [128, 8192], BF16, st=stS)
    psS = [ps(f"psS{i}", [128, 512], st=stS) for i in range(2)]
    psAV = [ps(f"psAV{i}", [128, 129], st=stS) for i in range(4)]
    setok = [None, None]
    avtok = [None] * 4
    ettok = [None, None]
    sc_tok = [None, None]
    t_scat = []
    rs6 = [rs_out[m].ap().rearrange("(t f) c -> f t c", f=4) for m in range(2)]
    t_aol = [None] * 16
    t_rs0 = None
    for qc in range(8):
        ob = qc % 2
        ev_toks = []
        for hl in range(4):
            nkc = 4 * qc + 4
            for kc in range(nkc):
                b = kc % 2
                W("tensor", setok[b])
                if kc == 0:
                    W("tensor", *[t for t in avtok if t])
                t_s = E("tensor", TE.matmul(psS[b][:], KT[:, kc * 128:(kc + 1) * 128],
                                            fmB[:, hl * 4096 + qc * 512: hl * 4096 + qc * 512 + 512],
                                            start=True, stop=True))
                if kc >= 4 * qc:
                    jm = kc - 4 * qc
                    W("vector", t_s, t_mk[jm])
                    t_s = E("vector", VE.tensor_tensor(
                        out=psS[b][:], in0=psS[b][:],
                        in1=masks[:, jm * 512:(jm + 1) * 512], op=ALU.add))
                W("scalar", t_s, ettok[b])
                t_e = E("scalar", SC.activation(out=eTb[b][:], in_=psS[b][:], func=AF.Exp,
                                                scale=np.float32(RSQRT_SCALE)))
                setok[b] = t_e
                tAV = None
                W("tensor", t_e)
                for qt in range(4):
                    tAV = E("tensor", TE.matmul(psAV[qt][:], eTb[b][:, qt * 128:(qt + 1) * 128],
                                                V_aug[:, kc * 129: kc * 129 + 129],
                                                start=(kc == 0), stop=(kc == nkc - 1)))
                ettok[b] = tAV
            t_fin = p.tok("tensor")
            for qt in range(4):
                W("vector", t_fin, sc_tok[ob] if hl == 0 else None)
                t_rz = E("vector", VE.reciprocal(rzc[qt][:], psAV[qt][:, 128:129]))
                W("vector", t_rz)
                avtok[qt] = E("vector", VE.tensor_scalar(
                    out=O_q[ob][:, qt * 512 + hl * 128: qt * 512 + hl * 128 + 128],
                    in0=psAV[qt][:, 0:128], scalar1=rzc[qt][:], scalar2=None, op0=ALU.mult))
                ev_toks.append(avtok[qt])
        m2 = qc % 2
        W("gpsimd", t_oix, t_zf, *ev_toks)
        if qc == 7:
            W("gpsimd", t_rs0)
        last_sc = None
        for qt in range(4):
            tt_idx = qc * 4 + qt
            last_sc = E("gpsimd", GP.indirect_dma_start(
                out=rs_in[m2].ap(),
                out_offset=bass.IndirectOffsetOnAxis(ap=oix[:, tt_idx:tt_idx + 1], axis=0),
                in_=O_q[ob][:, qt * 512:(qt + 1) * 512], in_offset=None), dma=True)
            t_scat.append(last_sc)
        sc_tok[ob] = last_sc
        if qc == 6:
            W("gpsimd", *t_scat)
            t_rs0 = E("gpsimd", GP.collective_compute(
                "ReduceScatter", ALU.add, replica_groups=RG,
                ins=[rs_in[0].ap()], outs=[rs_out[0].ap()]), cc=True)
            W("sync", t_rs0)
            for sl in range(16):
                ti_, j_ = sl // 4, sl % 4
                t_aol[sl] = E("sync", SY.dma_start(
                    out=ao_wide[:, sl * 512:(sl + 1) * 512],
                    in_=rs6[0][j_, (ti_ % 4) * 128:(ti_ % 4) * 128 + 128, :]), dma=True)
    W("gpsimd", *t_scat)
    t_rs1 = E("gpsimd", GP.collective_compute(
        "ReduceScatter", ALU.add, replica_groups=RG,
        ins=[rs_in[1].ap()], outs=[rs_out[1].ap()]), cc=True)
    ao_add = [None] * 16
    for sl in range(16):
        ti_, j_ = sl // 4, sl % 4
        xsl = x_sb[:, ti_ * 2048 + j_ * 512: ti_ * 2048 + j_ * 512 + 512]
        W("vector", t_aol[sl])
        ao_add[sl] = E("vector", VE.tensor_tensor(
            out=xsl, in0=xsl, in1=ao_wide[:, sl * 512:(sl + 1) * 512], op=ALU.add))
    W("sync", t_rs1)
    for sl in range(16):
        ti_, j_ = 4 + sl // 4, sl % 4
        W("sync", ao_add[sl])
        t_aol[sl] = E("sync", SY.dma_start(
            out=ao_wide[:, sl * 512:(sl + 1) * 512],
            in_=rs6[1][j_, (ti_ % 4) * 128:(ti_ % 4) * 128 + 128, :]), dma=True)
    for sl in range(16):
        ti_, j_ = 4 + sl // 4, sl % 4
        xsl = x_sb[:, ti_ * 2048 + j_ * 512: ti_ * 2048 + j_ * 512 + 512]
        W("vector", t_aol[sl])
        E("vector", VE.tensor_tensor(
            out=xsl, in0=xsl, in1=ao_wide[:, sl * 512:(sl + 1) * 512], op=ALU.add))
    W("tensor", p.tok("scalar"), p.tok("vector"))
    stS.close()
    stAT0.close()

    st5 = contextlib.ExitStack()
    xn_st2 = [sb(f"xn2st{i}", [128, 2048], BF16, st=st5) for i in range(2)]
    sq_scratch = [sb("sq_scr2", [128, 2048], BF16, st=st5)]
    tr_ps2 = [(ps(f"trq{i}", [128, 128], BF16, st=st5), None) for i in range(4)]
    W("scalar", p.tok("vector"))
    W("tensor", p.tok("scalar"), p.tok("vector"))
    stage_rd2 = [None, None]
    for ti in range(8):
        t_r, t_ssq = ln_stats(ti)
        W("vector", t_r)
        t_a = E("vector", VE.tensor_tensor(out=tmp1[:], in0=stats_r[:, ti:ti + 1],
                                           in1=stats_r[:, ti:ti + 1], op=ALU.mult))
        W("vector", t_a)
        t_b = E("vector", VE.tensor_tensor(out=tmp2[:], in0=stats_v[:, ti:ti + 1],
                                           in1=tmp1[:], op=ALU.mult))
        W("vector", t_b)
        t_c = E("vector", VE.tensor_scalar(out=tmp1[:], in0=tmp2[:], scalar1=1e-5,
                                           scalar2=None, op0=ALU.add))
        W("scalar", t_c)
        t_s = E("scalar", SC.activation(out=tmp2[:], in_=tmp1[:], func=AF.Sqrt))
        W("vector", t_s)
        t_r2 = E("vector", VE.reciprocal(tmp3[:], tmp2[:]))
        W("vector", t_r2)
        t_s2 = E("vector", VE.tensor_tensor(out=s2c[:, ti:ti + 1], in0=stats_r[:, ti:ti + 1],
                                            in1=tmp3[:], op=ALU.mult))
        W("vector", t_s2)
        t_o2a = E("vector", VE.tensor_tensor(out=tmp1[:], in0=stats_m[:, ti:ti + 1],
                                             in1=s2c[:, ti:ti + 1], op=ALU.mult))
        W("vector", t_o2a)
        t_o2 = E("vector", VE.tensor_scalar(out=o2c[:, ti:ti + 1], in0=tmp1[:],
                                            scalar1=-1.0, scalar2=None, op0=ALU.mult))
        W("vector", stage_rd2[0])
        t_st = E("vector", VE.tensor_scalar(out=xn_st2[0][:],
                                            in0=x_sb[:, ti * 2048:(ti + 1) * 2048],
                                            scalar1=stats_m[:, ti:ti + 1],
                                            scalar2=stats_r[:, ti:ti + 1],
                                            op0=ALU.subtract, op1=ALU.mult))

        def evacA1(tp, ec, t_tr, ti=ti):
            W("scalar", t_tr)
            return E("scalar", SC.copy(
                out=fmA[:, ec * 1024 + ti * 128: ec * 1024 + ti * 128 + 128], in_=tp[:]))
        _, t_trl = build_fmT(evacA1, xn_st2[0], t_st, tr_ps2)
        stage_rd2[0] = t_trl
        W("vector", t_o2, stage_rd2[1])
        t_st2 = E("vector", VE.tensor_scalar(out=xn_st2[1][:],
                                             in0=x_sb[:, ti * 2048:(ti + 1) * 2048],
                                             scalar1=s2c[:, ti:ti + 1],
                                             scalar2=o2c[:, ti:ti + 1],
                                             op0=ALU.mult, op1=ALU.add))

        def evacB(tp, ec, t_tr, ti=ti):
            W("vector", t_tr, t_small)
            return E("vector", VE.tensor_scalar(
                out=fmB[:, ec * 1024 + ti * 128: ec * 1024 + ti * 128 + 128], in0=tp[:],
                scalar1=g1as[:, ec:ec + 1], scalar2=be1s[:, ec:ec + 1],
                op0=ALU.mult, op1=ALU.add))
        _, t_trl2 = build_fmT(evacB, xn_st2[1], t_st2, tr_ps2)
        stage_rd2[1] = t_trl2
    W("tensor", p.tok("scalar"), p.tok("vector"))
    st5.close()

    st6 = contextlib.ExitStack()
    e_r = sb("e_r", [128, 12], st=st6)
    probs = sb("probs", [128, 12], st=st6)
    biased = sb("biased", [128, 12], st=st6)
    mk1 = sb("mk1", [128, 12], st=st6)
    b2v = sb("b2v", [128, 12], st=st6)
    psL = ps("psL", [128, 12], st=st6)
    t_ltok = None
    for ti in range(8):
        W("tensor", t_clf, t_ltok)
        tL = None
        for ec in range(16):
            tL = E("tensor", TE.matmul(psL[:], fmA[:, ec * 1024 + ti * 128: ec * 1024 + ti * 128 + 128],
                                       clfs[:, ec * 12:(ec + 1) * 12],
                                       start=(ec == 0), stop=(ec == 15)))
        W("vector", tL)
        t1 = E("vector", VE.reduce_max(out=tmp1[:], in_=psL[:], axis=AX.X, negate=True))
        W("scalar", t1)
        t2 = E("scalar", SC.activation(out=e_r[:], in_=psL[:], func=AF.Exp, bias=tmp1[:],
                                       scale=np.float32(1.0), accum_out=tmp2[:]))
        t_ltok = t2
        W("vector", t2)
        t3 = E("vector", VE.reciprocal(tmp3[:], tmp2[:]))
        W("vector", t3)
        t4 = E("vector", VE.tensor_scalar(out=probs[:], in0=e_r[:], scalar1=tmp3[:],
                                          scalar2=None, op0=ALU.mult))
        W("vector", t4, t_small)
        t5 = E("vector", VE.tensor_tensor(out=biased[:], in0=probs[:], in1=ebbc[:], op=ALU.add))
        W("vector", t5)
        t6 = E("vector", VE.reduce_max(out=tmp1[:], in_=biased[:], axis=AX.X))
        W("vector", t6)
        t7 = E("vector", VE.tensor_scalar(out=mk1[:], in0=biased[:], scalar1=tmp1[:],
                                          scalar2=None, op0=ALU.is_ge))
        W("vector", t7)
        t8 = E("vector", VE.scalar_tensor_tensor(out=b2v[:], in0=mk1[:], scalar=-1e30,
                                                 in1=biased[:], op0=ALU.mult, op1=ALU.add))
        W("vector", t8)
        t9 = E("vector", VE.reduce_max(out=tmp1[:], in_=b2v[:], axis=AX.X))
        W("vector", t9)
        t10 = E("vector", VE.tensor_scalar(out=b2v[:], in0=b2v[:], scalar1=tmp1[:],
                                           scalar2=None, op0=ALU.is_ge))
        W("vector", t10)
        t11 = E("vector", VE.tensor_tensor(out=mk1[:], in0=mk1[:], in1=b2v[:], op=ALU.add))
        W("vector", t11)
        t12 = E("vector", VE.scalar_tensor_tensor(out=w_all[:, ti * 12:(ti + 1) * 12],
                                                  in0=probs[:], scalar=1.5, in1=mk1[:],
                                                  op0=ALU.mult, op1=ALU.mult))
        W("vector", t12)
        t13 = E("vector", VE.reduce_sum(out=tmp1[:], in_=w_all[:, ti * 12 + 8: ti * 12 + 12],
                                        axis=AX.X))
        W("vector", t13)
        t14 = E("vector", VE.tensor_scalar(out=tmp2[:], in0=tmp1[:], scalar1=1.0,
                                           scalar2=None, op0=ALU.add))
        W("vector", t14)
        t15 = E("vector", VE.tensor_tensor(out=tmp1[:], in0=tmp2[:], in1=stats_r[:, ti:ti + 1],
                                           op=ALU.mult))
        W("vector", t15)
        t16 = E("vector", VE.tensor_scalar(out=tmp2[:], in0=tmp1[:], scalar1=1.0,
                                           scalar2=None, op0=ALU.add))
        t17 = E("vector", VE.tensor_tensor(out=tmp3[:], in0=stats_m[:, ti:ti + 1],
                                           in1=tmp1[:], op=ALU.mult))
        W("vector", t16, t17)
        xt = x_sb[:, ti * 2048:(ti + 1) * 2048]
        E("vector", VE.tensor_scalar(out=xt, in0=xt, scalar1=tmp2[:], scalar2=tmp3[:],
                                     op0=ALU.mult, op1=ALU.subtract))
    st6.close()

    st7 = contextlib.ExitStack()
    kvw = [[sb(f"kvw{s}{i}", [128, 4096], BF16, st=st7) for i in range(2)]
           for s in range(2)]
    cosb = [sb(f"cosb{i}", [128, 512], F32, st=st7) for i in range(2)]
    sinb = [sb(f"sinb{i}", [128, 512], F32, st=st7) for i in range(2)]
    ktmp = [sb(f"ktmp{i}", [128, 512], F32, st=st7) for i in range(2)]
    c1b = sb("c1b", [128, 512], F32, st=st7)
    c2b = sb("c2b", [128, 512], F32, st=st7)
    bias_kv = sb("bias_kv", [128, 512], F32, st=st7)
    Vst = sb("Vst", [128, 4096], BF16, st=st7)
    Kb = sb("Kb", [128, 512], BF16, st=st7)
    ppst = [sb(f"ppst{i}", [128, 512], BF16, st=st7) for i in range(2)]
    ssp = [sb(f"ssp{i}", [128, 1], F32, st=st7) for i in range(2)]
    psKV = [ps(f"psKV{i}", [128, 512], st=st7) for i in range(2)]
    kvwtok = [[None, None], [None, None]]
    kvwrd = [None, None]
    kvpass = 0
    ktmptok = [None, None]
    ktmprd = [[], []]
    ssptok = [None, None]
    tabtok = [None, None]
    tabrd = [None, None]
    c1tok = None
    c2tok = None
    kbtok = None
    vsttok = [None] * 8
    ppdma = [None, None]
    t_biaskv = None
    fence("gpsimd")
    W("gpsimd", t_rs1)
    fence("sync")
    W("tensor", p.tok("scalar"), p.tok("vector"))
    for ds in range(4):
        for kv in (1, 0):
            W("sync", t_biaskv)
            t_biaskv_l = E("sync", SY.dma_start(out=bias_kv[:], in_=b1kv_b[kv, ds]), dma=True)
            sset = kvpass % 2
            kvpass += 1
            for half in range(2):
                W("gpsimd", kvwtok[sset][half], kvwrd[sset])
                kvwtok[sset][half] = E("gpsimd", GP.dma_start(
                    out=kvw[sset][half][:],
                    in_=w1kv[kv, ds][:, half * 4096: half * 4096 + 4096]),
                    dma=True)
            for ti in range(8):
                b = ti % 2
                if ds < 2:
                    W("sync", tabtok[b], tabrd[b])
                    tabtok[b] = E("sync", SY.dma_start(out=cosb[b][:], in_=rcos[ds, ti]), dma=True)
                    tabtok[b] = E("sync", SY.dma_start(out=sinb[b][:], in_=rsin[ds, ti]), dma=True)
                W("tensor", kvwtok[sset][0], kvwtok[sset][1], ktmptok[b])
                tkv = None
                for ec in range(16):
                    buf = kvw[sset][ec // 8]
                    tkv = E("tensor", TE.matmul(
                        psKV[b][:], fmB[:, ec * 1024 + ti * 128: ec * 1024 + ti * 128 + 128],
                        buf[:, (ec % 8) * 512: (ec % 8) * 512 + 512],
                        start=(ec == 0), stop=(ec == 15)))
                kvwrd[sset] = tkv
                W("vector", tkv, t_biaskv_l, *ktmprd[b])
                ktmprd[b] = []
                t_kt = E("vector", VE.tensor_tensor(out=ktmp[b][:], in0=psKV[b][:],
                                                    in1=bias_kv[:], op=ALU.add))
                ktmptok[b] = t_kt
                t_biaskv = t_kt
                sb_ = b
                W("scalar", t_kt, ssptok[sb_], c1tok)
                t_sq = E("scalar", SC.activation(out=c1b[:], in_=ktmp[b][:], func=AF.Square,
                                                 accum_out=ssp[sb_][:]))
                ktmprd[b].append(t_sq)
                ss_dst = ssK if kv == 0 else ssV
                W("vector", t_sq)
                if ds == 0:
                    tss = E("vector", VE.tensor_copy(ss_dst[:, ti:ti + 1], ssp[sb_][:]))
                else:
                    tss = E("vector", VE.tensor_tensor(out=ss_dst[:, ti:ti + 1],
                                                       in0=ss_dst[:, ti:ti + 1],
                                                       in1=ssp[sb_][:], op=ALU.add))
                ssptok[sb_] = tss
                if ds < 2:
                    W("vector", t_kt, tabtok[b], c2tok, t_sq)
                    ts1 = E("vector", VE.tensor_copy(c2b[:, 0:512:2], ktmp[b][:, 1:512:2]))
                    ts2 = E("vector", VE.tensor_copy(c2b[:, 1:512:2], ktmp[b][:, 0:512:2]))
                    tc1 = E("vector", VE.tensor_tensor(out=c1b[:], in0=ktmp[b][:],
                                                       in1=cosb[b][:], op=ALU.mult))
                    c1tok = tc1
                    ktmprd[b].append(tc1)
                    tabrd[b] = tc1
                    W("vector", ts1, ts2)
                    tc2 = E("vector", VE.tensor_tensor(out=c2b[:], in0=c2b[:],
                                                       in1=sinb[b][:], op=ALU.mult))
                    c2tok = tc2
                    tabrd[b] = tc2
                    W("vector", tc1, tc2)
                    if kv == 1:
                        W("vector", vsttok[ti])
                        t_rot = E("vector", VE.tensor_tensor(
                            out=Vst[:, ti * 512:(ti + 1) * 512], in0=c1b[:], in1=c2b[:],
                            op=ALU.add))
                        vsttok[ti] = t_rot
                        c1tok = t_rot
                        c2tok = t_rot
                    else:
                        W("vector", kbtok)
                        t_kb = E("vector", VE.tensor_tensor(out=Kb[:], in0=c1b[:],
                                                            in1=c2b[:], op=ALU.add))
                        c1tok = t_kb
                        c2tok = t_kb
                        W("vector", t_kb, vsttok[ti], ppdma[b])
                        t_pp = E("vector", VE.tensor_tensor(
                            out=ppst[b][:], in0=Vst[:, ti * 512:(ti + 1) * 512],
                            in1=Kb[:], op=ALU.mult))
                        kbtok = t_pp
                        vsttok[ti] = t_pp
                        W("sync", t_pp)
                        ppdma[b] = E("sync", SY.dma_start(
                            out=ppd.ap()[ti, :, ds * 512:(ds + 1) * 512], in_=ppst[b][:]),
                            dma=True)
                else:
                    if kv == 1:
                        W("scalar", t_kt, vsttok[ti], t_sq)
                        t_cp = E("scalar", SC.copy(out=Vst[:, ti * 512:(ti + 1) * 512],
                                                   in_=ktmp[b][:]))
                        vsttok[ti] = t_cp
                        ktmprd[b].append(t_cp)
                    else:
                        W("scalar", t_kt, kbtok, t_sq)
                        t_kb = E("scalar", SC.copy(out=Kb[:], in_=ktmp[b][:]))
                        ktmprd[b].append(t_kb)
                        W("vector", t_kb, vsttok[ti], ppdma[b])
                        t_pp = E("vector", VE.tensor_tensor(
                            out=ppst[b][:], in0=Vst[:, ti * 512:(ti + 1) * 512],
                            in1=Kb[:], op=ALU.mult))
                        kbtok = t_pp
                        vsttok[ti] = t_pp
                        W("sync", t_pp)
                        ppdma[b] = E("sync", SY.dma_start(
                            out=ppd.ap()[ti, :, ds * 512:(ds + 1) * 512], in_=ppst[b][:]),
                            dma=True)
    t_pp_all = p.tok("sync")
    st7.close()

    stA2 = contextlib.ExitStack()
    ppb = [sb(f"ppb{i}", [128, 2048], BF16, st=stA2) for i in range(2)]
    psA = ps("psA", [128, 16], st=stA2)
    fence("sync")
    ppbtok = [None, None]
    scoltok = [None, None]
    tA = None
    for ti in range(8):
        b = ti % 2
        W("sync", t_pp_all, ppbtok[b])
        t_ld = E("sync", SY.dma_start(out=ppb[b][:], in_=ppd.ap()[ti]), dma=True)
        W("vector", p.tok("vector") if ti == 0 else None)
        tm = E("vector", VE.tensor_tensor(out=tmp1[:], in0=ssK[:, ti:ti + 1],
                                          in1=ssV[:, ti:ti + 1], op=ALU.mult))
        W("scalar", tm)
        tsq = E("scalar", SC.activation(out=tmp2[:], in_=tmp1[:], func=AF.Sqrt))
        W("vector", tsq)
        tc = E("vector", VE.reciprocal(tmp3[:], tmp2[:]))
        W("vector", tc, scoltok[b])
        tsc = E("vector", VE.tensor_copy(scol[:, b:b + 1], tmp3[:]))
        W("tensor", tsc, t_ld)
        for ch in range(16):
            tA = E("tensor", TE.matmul(psA[:, ch:ch + 1], ppb[b][:, ch * 128:(ch + 1) * 128],
                                       scol[:, b:b + 1], start=(ti == 0), stop=(ti == 7)))
        ppbtok[b] = tA
        scoltok[b] = tA
    W("vector", tA)
    t_Ar = E("vector", VE.tensor_copy(A_raw[:], psA[:]))
    W("sync", t_Ar)
    t_As = E("sync", SY.dma_start(out=arA_in.ap(), in_=A_raw[:]), dma=True)
    W("gpsimd", t_As)
    W("gpsimd", *kvwtok[0], *kvwtok[1])
    t_arc = E("gpsimd", GP.collective_compute(
        "AllReduce", ALU.add, replica_groups=RG,
        ins=[arA_in.ap()], outs=[arA_out.ap()]), cc=True)
    W("tensor", p.tok("vector"))
    stA2.close()

    stB2 = contextlib.ExitStack()
    Q2T = sb("Q2T", [128, 16384], BF16, st=stB2)
    st8a = contextlib.ExitStack()
    qwb = [sb(f"qwb{i}", [128, 2048], BF16, st=st8a) for i in range(2)]
    psQ2 = [ps(f"psQ2{i}", [128, 512], st=st8a) for i in range(2)]
    fence("gpsimd")
    W("gpsimd", t_arc)
    qwtok = [None, None]
    q2tok = [None, None]
    q2ev = []
    for qch in range(16):
        b = qch % 2
        W("gpsimd", qwtok[b])
        t_w = E("gpsimd", GP.dma_start(out=qwb[b][:], in_=w1q[qch]), dma=True)
        for h2 in range(2):
            W("tensor", t_w, q2tok[h2])
            tq = None
            for ec in range(16):
                tq = E("tensor", TE.matmul(psQ2[h2][:], qwb[b][:, ec * 128:(ec + 1) * 128],
                                           fmB[:, ec * 1024 + h2 * 512: ec * 1024 + h2 * 512 + 512],
                                           start=(ec == 0), stop=(ec == 15)))
            qwtok[b] = tq
            W("vector", tq, t_small)
            q2tok[h2] = E("vector", VE.tensor_scalar(
                out=Q2T[:, qch * 1024 + h2 * 512: qch * 1024 + h2 * 512 + 512],
                in0=psQ2[h2][:], scalar1=b1qs[:, qch:qch + 1], scalar2=None,
                op0=ALU.add))
            q2ev.append(q2tok[h2])
    W("sync", t_arc)
    t_Al = E("sync", SY.dma_start(out=A_raw[:], in_=arA_out.ap()), dma=True)
    W("vector", t_Al)
    t_A1 = E("vector", VE.tensor_scalar(out=A1[:], in0=A_raw[:], scalar1=0.5, scalar2=0.5,
                                        op0=ALU.mult, op1=ALU.add))
    sc_toks = []
    for qch in range(16):
        W("vector", t_A1, q2ev[2 * qch], q2ev[2 * qch + 1])
        sc_toks.append(E("vector", VE.tensor_scalar(
            out=Q2T[:, qch * 1024:(qch + 1) * 1024], in0=Q2T[:, qch * 1024:(qch + 1) * 1024],
            scalar1=A1[:, qch:qch + 1], scalar2=None, op0=ALU.mult)))
    W("tensor", p.tok("vector"))
    st8a.close()
    st8b = contextlib.ExitStack()
    owb = [sb(f"owb{i}", [128, 2048], BF16, st=st8b) for i in range(2)]
    ytmp = [sb(f"ytmp{i}", [128, 512], F32, st=st8b) for i in range(2)]
    b1ob = [sb(f"b1ob{i}", [128, 512], F32, st=st8b) for i in range(2)]
    psY = [ps(f"psY{i}", [128, 512], st=st8b) for i in range(8)]
    fence("gpsimd")
    fence("sync")
    owtok = [None, None]
    btok = [None, None]
    ytok = [None] * 8
    for cs in range(4):
        cb = cs % 2
        W("sync", btok[cb])
        t_b = E("sync", SY.dma_start(out=b1ob[cb][:], in_=b1o_b[cs]), dma=True)
        for qg in range(4):
            b = qg % 2
            W("gpsimd", owtok[b])
            t_w = E("gpsimd", GP.dma_start(
                out=owb[b][:], in_=w1o[cs][:, qg * 2048:(qg + 1) * 2048]), dma=True)
            W("tensor", t_w)
            if qg == 0:
                W("tensor", *[t for t in ytok if t])
            tY = None
            for ti in range(8):
                for k in range(4):
                    qch = qg * 4 + k
                    tY = E("tensor", TE.matmul(
                        psY[ti][:],
                        Q2T[:, qch * 1024 + ti * 128: qch * 1024 + ti * 128 + 128],
                        owb[b][:, k * 512:(k + 1) * 512],
                        start=(qg == 0 and k == 0), stop=(qg == 3 and k == 3)))
            owtok[b] = tY
        t_fin = p.tok("tensor")
        for ti in range(8):
            yb = ti % 2
            W("vector", t_fin, t_b, ytok[ti])
            t_y1 = E("vector", VE.tensor_tensor(out=ytmp[yb][:], in0=psY[ti][:],
                                                in1=b1ob[cb][:], op=ALU.add))
            xsl = x_sb[:, ti * 2048 + cs * 512: ti * 2048 + cs * 512 + 512]
            W("vector", t_y1)
            ytok[ti] = E("vector", VE.tensor_tensor(out=xsl, in0=xsl, in1=ytmp[yb][:],
                                                    op=ALU.add))
        btok[cb] = ytok[7]
    W("tensor", p.tok("vector"))
    st8b.close()
    stB2.close()
    stB.close()

    st9 = contextlib.ExitStack()
    H1 = sb("H1", [128, 8192], BF16, st=st9)
    g1b = [sb(f"g1b{i}", [128, 2048], BF16, st=st9) for i in range(2)]
    u1b = [sb(f"u1b{i}", [128, 2048], BF16, st=st9) for i in range(2)]
    sgb2 = [sb(f"sgb2{i}", [128, 512], F32, st=st9) for i in range(2)]
    d1b = sb("d1b", [128, 16384], BF16, st=st9)
    psG1 = [ps(f"psGm{i}", [128, 512], st=st9) for i in range(2)]
    psU1 = [ps(f"psUm{i}", [128, 512], st=st9) for i in range(2)]
    psM = [ps(f"psM{i}", [128, 512], st=st9) for i in range(2)]
    fence("gpsimd")
    g1tok = [None, None]
    u1tok = [None, None]
    sg1tok = [None, None]
    h1tok = [None, None]
    d1tok = None
    mevtok = [None, None]
    for e in range(8):
        t_h1_all = []
        for ic in range(8):
            b = ic % 2
            W("gpsimd", g1tok[b])
            t_g = E("gpsimd", GP.dma_start(out=g1b[b][:], in_=g1[e, :, ic, :]), dma=True)
            W("gpsimd", u1tok[b])
            t_u = E("gpsimd", GP.dma_start(out=u1b[b][:], in_=u1[e, :, ic, :]), dma=True)
            for th in range(2):
                pb = (ic * 2 + th) % 2
                W("tensor", t_g, sg1tok[pb])
                tG = None
                for hc in range(16):
                    tG = E("tensor", TE.matmul(psG1[pb][:], g1b[b][:, hc * 128:(hc + 1) * 128],
                                               fmA[:, hc * 1024 + th * 512: hc * 1024 + th * 512 + 512],
                                               start=(hc == 0), stop=(hc == 15)))
                g1tok[b] = tG
                W("scalar", tG, h1tok[pb])
                t_sg = E("scalar", SC.activation(out=sgb2[pb][:], in_=psG1[pb][:], func=AF.Silu))
                sg1tok[pb] = t_sg
                W("tensor", t_u, h1tok[pb])
                tU = None
                for hc in range(16):
                    tU = E("tensor", TE.matmul(psU1[pb][:], u1b[b][:, hc * 128:(hc + 1) * 128],
                                               fmA[:, hc * 1024 + th * 512: hc * 1024 + th * 512 + 512],
                                               start=(hc == 0), stop=(hc == 15)))
                u1tok[b] = tU
                W("vector", tU, t_sg)
                th1 = E("vector", VE.tensor_tensor(
                    out=H1[:, ic * 1024 + th * 512: ic * 1024 + th * 512 + 512],
                    in0=sgb2[pb][:], in1=psU1[pb][:], op=ALU.mult))
                h1tok[pb] = th1
                t_h1_all.append(th1)
        W("gpsimd", d1tok)
        t_d = E("gpsimd", GP.dma_start(
            out=d1b[:].rearrange("p (a m) -> p a m", a=8), in_=d1[e]), dma=True)
        for ti in range(8):
            for cs in range(4):
                mb = (ti * 4 + cs) % 2
                W("tensor", t_d, mevtok[mb], *t_h1_all)
                t_h1_all = []
                tM = None
                for icl in range(8):
                    tM = E("tensor", TE.matmul(
                        psM[mb][:],
                        H1[:, icl * 1024 + ti * 128: icl * 1024 + ti * 128 + 128],
                        d1b[:, icl * 2048 + cs * 512: icl * 2048 + cs * 512 + 512],
                        start=(icl == 0), stop=(icl == 7)))
                d1tok = tM
                xsl = x_sb[:, ti * 2048 + cs * 512: ti * 2048 + cs * 512 + 512]
                W("vector", tM)
                mevtok[mb] = E("vector", VE.scalar_tensor_tensor(
                    out=xsl, in0=psM[mb][:], scalar=w_all[:, ti * 12 + e: ti * 12 + e + 1],
                    in1=xsl, op0=ALU.mult, op1=ALU.add))
    W("tensor", p.tok("vector"))
    st9.close()

    W("sync", p.tok("vector"))
    st_toks = []
    for ti in range(8):
        st_toks.append(E("sync", SY.dma_start(out=out_x[ti * 128:(ti + 1) * 128, :],
                                              in_=x_sb[:, ti * 2048:(ti + 1) * 2048]), dma=True))
    W("sync", *st_toks)
    return nc, stack


def _host_layouts(inp):
    L = {}
    L["g0"] = np.ascontiguousarray(
        inp["L0_ffn_gate_w"].reshape(44, 128, 16, 128).transpose(0, 3, 2, 1)).reshape(44, 128, 2048)
    L["u0"] = np.ascontiguousarray(
        inp["L0_ffn_up_w"].reshape(44, 128, 16, 128).transpose(0, 3, 2, 1)).reshape(44, 128, 2048)
    L["d0"] = np.ascontiguousarray(inp["L0_ffn_down_w"].T).reshape(44, 128, 2048)
    L["gs0"] = np.ascontiguousarray(inp["L0_ffn_gate_s"].reshape(44, 128).T)
    L["us0"] = np.ascontiguousarray(inp["L0_ffn_up_s"].reshape(44, 128).T)
    L["ds_b"] = np.ascontiguousarray(
        np.broadcast_to(inp["L0_ffn_down_s"].reshape(1, 2048), (128, 2048)))
    W_in = inp["L1_attn_W_in"]
    L["w1q"] = np.ascontiguousarray(
        W_in[:, :2048].reshape(16, 128, 16, 128).transpose(2, 1, 0, 3)).reshape(16, 128, 2048)
    L["w1kv"] = np.ascontiguousarray(
        W_in[:, 2048:].reshape(16, 128, 2, 4, 512).transpose(2, 3, 1, 0, 4)).reshape(2, 4, 128, 8192)
    b_in = inp["L1_attn_b_in"]
    L["b1q"] = np.ascontiguousarray(b_in[:2048].reshape(16, 128).T)
    L["b1kv_b"] = np.ascontiguousarray(
        np.broadcast_to(b_in[2048:].reshape(2, 4, 1, 512), (2, 4, 128, 512)))
    L["w1o"] = np.ascontiguousarray(
        inp["L1_attn_W_out"].reshape(16, 128, 4, 512).transpose(2, 1, 0, 3)).reshape(4, 128, 8192)
    L["b1o_b"] = np.ascontiguousarray(
        np.broadcast_to(inp["L1_attn_b_out"].reshape(4, 1, 512), (4, 128, 512)))
    L["g1a"] = np.ascontiguousarray(inp["L1_attn_gamma"].reshape(16, 128).T)
    L["be1"] = np.ascontiguousarray(inp["L1_attn_beta"].reshape(16, 128).T)
    L["clf"] = np.ascontiguousarray(
        inp["L1_ffn_classifier_w"].T.reshape(16, 128, 12).transpose(1, 0, 2)).reshape(128, 192)
    L["eb_b"] = np.ascontiguousarray(
        np.broadcast_to(inp["L1_ffn_e_score_correction_bias"].reshape(1, 12), (128, 12)))
    L["g1"] = np.ascontiguousarray(
        inp["L1_ffn_gate_w"].reshape(8, 8, 128, 16, 128).transpose(0, 4, 1, 3, 2)).reshape(8, 128, 8, 2048)
    L["u1"] = np.ascontiguousarray(
        inp["L1_ffn_up_w"].reshape(8, 8, 128, 16, 128).transpose(0, 4, 1, 3, 2)).reshape(8, 128, 8, 2048)
    L["d1"] = np.ascontiguousarray(
        inp["L1_ffn_down_w"].reshape(8, 2048, 8, 128).transpose(0, 3, 2, 1)).reshape(8, 128, 8, 2048)
    inv = (10000.0 ** (-(np.arange(0, 1024, 2, dtype=np.float64) / 1024.0)))
    pos = np.arange(4096, dtype=np.float64)
    fr = pos[:, None] * inv[None, :]
    cosr = np.repeat(np.cos(fr), 2, axis=1)
    sinr = np.repeat(np.sin(fr), 2, axis=1)
    sinr[:, 0::2] *= -1.0
    L["_cos_full"] = cosr.astype(np.float32)
    L["_sin_full"] = sinr.astype(np.float32)
    return L


def kernel(**inp):
    inp = {k: np.asarray(v) for k, v in inp.items()}
    L = _host_layouts(inp)
    x = inp["L0_attn_x"]
    Wq, bq = inp["L0_attn_Wq"], inp["L0_attn_bq"]
    Wk, bk = inp["L0_attn_Wk"], inp["L0_attn_bk"]
    Wv, bv = inp["L0_attn_Wv"], inp["L0_attn_bv"]
    shared_keys = ["g0", "u0", "d0", "gs0", "us0", "ds_b", "w1q", "w1kv", "b1q",
                   "b1kv_b", "w1o", "b1o_b", "g1a", "be1", "clf", "eb_b",
                   "g1", "u1", "d1"]
    in_maps = []
    for c in range(8):
        b, j = c // 4, c % 4
        m = {k: L[k] for k in shared_keys}
        m["x_own"] = np.ascontiguousarray(x[b, j * 1024:(j + 1) * 1024, :])
        m["wq_o"] = np.ascontiguousarray(
            Wq[4 * j:4 * j + 4].reshape(4, 16, 128, 128).transpose(2, 0, 1, 3)).reshape(128, 8192)
        m["bq_o"] = np.ascontiguousarray(bq[4 * j:4 * j + 4].T)
        m["wk_o"] = np.ascontiguousarray(
            Wk[j].reshape(16, 128, 128).transpose(1, 0, 2)).reshape(128, 2048)
        m["wv_o"] = np.ascontiguousarray(
            Wv[j].reshape(16, 128, 128).transpose(1, 0, 2)).reshape(128, 2048)
        m["bk_o"] = bk[j].reshape(128, 1).copy()
        m["bv_b"] = np.ascontiguousarray(np.broadcast_to(bv[j].reshape(1, 128), (128, 128)))
        tt = np.arange(32)[None, :]
        pp = np.arange(128)[:, None]
        m["oidx"] = ((tt // 8) * 2048 + ((tt % 4) * 128 + pp) * 4 + j).astype(np.int32)
        tsl = slice(j * 1024, (j + 1) * 1024)
        m["rcos"] = np.ascontiguousarray(
            L["_cos_full"][tsl].reshape(8, 128, 2, 512).transpose(2, 0, 1, 3))
        m["rsin"] = np.ascontiguousarray(
            L["_sin_full"][tsl].reshape(8, 128, 2, 512).transpose(2, 0, 1, 3))
        in_maps.append(m)
    nc, stack = build()
    res = run_bass_kernel_spmd(nc, in_maps, core_ids=list(range(8)))
    stack.close()
    out = np.empty((2, 4096, 2048), dtype=np.float32)
    for c in range(8):
        b, j = c // 4, c % 4
        out[b, j * 1024:(j + 1) * 1024, :] = res.results[c]["out_x"]
    return out


# version 12
# speedup vs baseline: 1.0880x; 1.0400x over previous
import contextlib
import numpy as np
import ml_dtypes
import concourse.bass as bass
import concourse.mybir as mybir
from concourse.bass_utils import run_bass_kernel_spmd

F32 = mybir.dt.float32
BF16 = mybir.dt.bfloat16
I32 = mybir.dt.int32
AF = mybir.ActivationFunctionType
ALU = mybir.AluOpType
AX = mybir.AxisListType
BF = ml_dtypes.bfloat16
RG = [[0, 1, 2, 3], [4, 5, 6, 7]]
ENGS = ["sync", "scalar", "vector", "tensor", "gpsimd"]
RSQRT_SCALE = float(1.0 / np.sqrt(128.0))


class P:

    def __init__(self, nc, stack):
        self.nc = nc
        self.sems = {e: stack.enter_context(nc.semaphore(f"S_{e}")) for e in ENGS}
        self.cnt = {e: 0 for e in ENGS}
        self.lastw = {}
        self.eng = {e: getattr(nc, e) for e in ENGS}

    def emit(self, ename, instr, dma=False, cc=False):
        n = 16 if dma else 1
        instr.then_inc(self.sems[ename], n)
        self.cnt[ename] += n
        return (ename, self.cnt[ename], dma or cc)

    def wait(self, consumer, *tokens):
        for t in tokens:
            if t is None:
                continue
            pe, v, _ = t
            key = (consumer, pe)
            if self.lastw.get(key, -1) >= v:
                continue
            self.eng[consumer].wait_ge(self.sems[pe], v)
            self.lastw[key] = v

    def tok(self, ename):
        return (ename, self.cnt[ename], False)


def build():
    nc = bass.Bass("TRN2", target_bir_lowering=False)

    def din(name, shape, dt=F32):
        return nc.dram_tensor(name, shape, dt, kind="ExternalInput").ap()

    x_own = din("x_own", [1024, 2048])
    wq_o = din("wq_o", [128, 8192])
    wk_o = din("wk_o", [128, 2048])
    wv_o = din("wv_o", [128, 2048])
    bq_o = din("bq_o", [128, 4])
    bk_o = din("bk_o", [128, 1])
    bv_b = din("bv_b", [128, 128])
    oidx = din("oidx", [128, 32], I32)
    g0 = din("g0", [44, 128, 2048])
    u0 = din("u0", [44, 128, 2048])
    d0 = din("d0", [2, 44, 128, 1024])
    gs0 = din("gs0", [128, 44])
    us0 = din("us0", [128, 44])
    ds_b = din("ds_b", [128, 2048])
    w1q = din("w1q", [16, 128, 2048])
    b1q = din("b1q", [128, 16])
    w1kv = din("w1kv", [2, 4, 128, 8192])
    b1kv_b = din("b1kv_b", [2, 4, 128, 512])
    w1o = din("w1o", [4, 128, 8192])
    b1o_b = din("b1o_b", [4, 128, 512])
    g1a_i = din("g1a", [128, 16])
    be1_i = din("be1", [128, 16])
    clf = din("clf", [128, 192])
    eb_b = din("eb_b", [128, 12])
    g1 = din("g1", [8, 128, 8, 2048])
    u1 = din("u1", [8, 128, 8, 2048])
    d1 = din("d1", [8, 128, 8, 2048])
    rcos = din("rcos", [2, 8, 128, 512])
    rsin = din("rsin", [2, 8, 128, 512])

    out_x = nc.dram_tensor("out_x", [1024, 2048], F32, kind="ExternalOutput").ap()

    ag_in = nc.dram_tensor("ag_in", [2048, 1024], BF16)
    ag_out = nc.dram_tensor("ag_out", [8192, 1024], BF16)
    rs_in = [nc.dram_tensor(f"rs_in{m}", [8192, 512], BF16) for m in range(2)]
    rs_out = [nc.dram_tensor(f"rs_out{m}", [2048, 512], BF16) for m in range(2)]
    arA_in = nc.dram_tensor("arA_in", [128, 16], F32)
    arA_out = nc.dram_tensor("arA_out", [128, 16], F32)
    ppd = nc.dram_tensor("ppd", [8, 128, 2048], BF16)

    stack = contextlib.ExitStack()
    p = P(nc, stack)
    E, W = p.emit, p.wait
    SY, SC, VE, TE, GP = nc.sync, nc.scalar, nc.vector, nc.tensor, nc.gpsimd

    def sb(name, shape, dt=F32, st=stack):
        return st.enter_context(nc.sbuf_tensor(name, shape, dt))

    def ps(name, shape, dt=F32, st=stack):
        return st.enter_context(nc.psum_tensor(name, shape, dt))

    def fence(q, engs=("tensor", "vector", "scalar")):
        W(q, *[p.tok(e) for e in engs])

    x_sb = sb("x_sb", [128, 8 * 2048])
    fmA = sb("fmA", [128, 16384], BF16)
    masks = sb("masks", [128, 4 * 512])
    ident = sb("ident", [128, 128], BF16)
    bqs = sb("bqs", [128, 4]); bks = sb("bks", [128, 1])
    bvbc = sb("bvbc", [128, 128]); oix = sb("oix", [128, 32], I32)
    gss = sb("gss", [128, 44]); uss = sb("uss", [128, 44])
    b1qs = sb("b1qs", [128, 16])
    g1as = sb("g1as", [128, 16]); be1s = sb("be1s", [128, 16])
    clfs = sb("clfs", [128, 192], BF16); ebbc = sb("ebbc", [128, 12])
    stats_m = sb("stats_m", [128, 8]); stats_r = sb("stats_r", [128, 8])
    stats_v = sb("stats_v", [128, 8])
    s2c = sb("s2c", [128, 8]); o2c = sb("o2c", [128, 8])
    w_all = sb("w_all", [128, 96])
    A1 = sb("A1", [128, 16]); A_raw = sb("A_raw", [128, 16])
    ssK = sb("ssK", [128, 8]); ssV = sb("ssV", [128, 8])
    scol = sb("scol", [128, 2], BF16)
    tmp1 = sb("tmp1", [128, 1]); tmp2 = sb("tmp2", [128, 1]); tmp3 = sb("tmp3", [128, 1])
    wks = sb("wks", [128, 2048], BF16)
    wvs = sb("wvs", [128, 2048], BF16)
    st_i = contextlib.ExitStack()
    iota = sb("iota", [128, 512], I32, st=st_i)

    t_x = [E("sync", SY.dma_start(out=x_sb[:, ti * 2048:(ti + 1) * 2048],
                                  in_=x_own[ti * 128:(ti + 1) * 128, :]), dma=True)
           for ti in range(8)]
    smalls = [(bqs, bq_o), (bks, bk_o), (bvbc, bv_b), (gss, gs0), (uss, us0),
              (b1qs, b1q), (g1as, g1a_i), (be1s, be1_i), (ebbc, eb_b)]
    t_small = None
    for buf, src in smalls:
        t_small = E("sync", SY.dma_start(out=buf[:], in_=src), dma=True)
    t_oix = E("sync", SY.dma_start(out=oix[:], in_=oidx), dma=True)
    t_clf = E("gpsimd", GP.dma_start(out=clfs[:], in_=clf), dma=True)
    t_io = E("gpsimd", GP.iota(out=iota[:], pattern=[[-1, 512]], base=0,
                               channel_multiplier=1))
    W("vector", t_io)
    t_id = E("vector", VE.tensor_scalar(out=ident[:], in0=iota[:, 0:128], scalar1=0.0,
                                        scalar2=None, op0=ALU.is_equal))
    t_mk = []
    for jm in range(4):
        t_mk.append(E("vector", VE.tensor_scalar(
            out=masks[:, jm * 512:(jm + 1) * 512], in0=iota[:], scalar1=float(-128 * jm),
            scalar2=-1e30, op0=ALU.is_gt, op1=ALU.mult)))
    st_i.close()

    def ln_stats(ti):
        xt = x_sb[:, ti * 2048:(ti + 1) * 2048]
        t_sum = E("vector", VE.reduce_sum(out=tmp1[:], in_=xt, axis=AX.X))
        t_ssq = E("scalar", SC.activation(out=sq_scratch[0][:], in_=xt, func=AF.Square,
                                          accum_out=tmp2[:]))
        W("vector", t_sum)
        t_m = E("vector", VE.tensor_scalar(out=stats_m[:, ti:ti + 1], in0=tmp1[:],
                                           scalar1=1.0 / 2048.0, scalar2=None, op0=ALU.mult))
        W("vector", t_m, t_ssq)
        t_a = E("vector", VE.tensor_tensor(out=tmp1[:], in0=stats_m[:, ti:ti + 1],
                                           in1=stats_m[:, ti:ti + 1], op=ALU.mult))
        t_b = E("vector", VE.tensor_scalar(out=tmp3[:], in0=tmp2[:],
                                           scalar1=1.0 / 2048.0, scalar2=None, op0=ALU.mult))
        W("vector", t_a, t_b)
        t_v = E("vector", VE.tensor_tensor(out=stats_v[:, ti:ti + 1], in0=tmp3[:],
                                           in1=tmp1[:], op=ALU.subtract))
        W("vector", t_v)
        t_e = E("vector", VE.tensor_scalar(out=tmp1[:], in0=stats_v[:, ti:ti + 1],
                                           scalar1=1e-5, scalar2=None, op0=ALU.add))
        W("scalar", t_e)
        t_s = E("scalar", SC.activation(out=tmp2[:], in_=tmp1[:], func=AF.Sqrt))
        W("vector", t_s)
        t_r = E("vector", VE.reciprocal(stats_r[:, ti:ti + 1], tmp2[:]))
        return t_r, t_ssq

    def build_fmT(dest_fn, stage_buf, t_stage, tr_pss):
        toks = []
        t_tr = None
        for ec in range(16):
            b = ec % len(tr_pss)
            tp, t_ev_prev = tr_pss[b]
            W("tensor", t_stage, t_ev_prev, t_id)
            t_tr = E("tensor", TE.transpose(tp[:], stage_buf[:, ec * 128:(ec + 1) * 128],
                                            ident[:]))
            t_ev = dest_fn(tp, ec, t_tr)
            tr_pss[b] = (tp, t_ev)
            toks.append(t_ev)
        return toks, t_tr

    stA = contextlib.ExitStack()
    xn_stage = [sb(f"xn_st{i}", [128, 2048], BF16, st=stA) for i in range(2)]
    sq_scratch = [sb("sq_scr", [128, 2048], BF16, st=stA)]
    zbuf = sb("zbuf", [128, 2048], BF16, st=stA)
    tr_ps_l = [(ps(f"trp{i}", [128, 128], BF16, st=stA), None) for i in range(4)]
    t_zb = E("gpsimd", GP.memset(zbuf[:], 0.0))
    t_zf = None
    for m in range(2):
        for k in range(16):
            t_zf = E("gpsimd", GP.dma_start(
                out=rs_in[m].ap()[k * 512:(k + 1) * 512, :].rearrange(
                    "(a p) m -> p a m", p=128),
                in_=zbuf[:].rearrange("p (a m) -> p a m", a=4)), dma=True)
    t_fmA_done = []
    stage_rd = [None, None]
    for ti in range(8):
        W("vector", t_x[ti])
        W("scalar", t_x[ti])
        t_r, _ = ln_stats(ti)
        sbuf_st = xn_stage[ti % 2]
        W("vector", t_r, stage_rd[ti % 2])
        t_st = E("vector", VE.tensor_scalar(out=sbuf_st[:],
                                            in0=x_sb[:, ti * 2048:(ti + 1) * 2048],
                                            scalar1=stats_m[:, ti:ti + 1],
                                            scalar2=stats_r[:, ti:ti + 1],
                                            op0=ALU.subtract, op1=ALU.mult))

        def evacA(tp, ec, t_tr, ti=ti):
            W("scalar", t_tr)
            return E("scalar", SC.copy(
                out=fmA[:, ec * 1024 + ti * 128: ec * 1024 + ti * 128 + 128], in_=tp[:]))
        toks, t_trl = build_fmT(evacA, sbuf_st, t_st, tr_ps_l)
        t_fmA_done += toks
        stage_rd[ti % 2] = t_trl
    W("sync", *t_fmA_done)
    t_agst = []
    for ec in range(16):
        t_agst.append(E("sync", SY.dma_start(
            out=ag_in.ap()[ec * 128:(ec + 1) * 128, :],
            in_=fmA[:, ec * 1024:(ec + 1) * 1024]), dma=True))
    stA.close()
    agc = [None]

    stF = contextlib.ExitStack()
    H = sb("H", [128, 44 * 512], BF16, st=stF)
    gbuf = [sb(f"gb{i}", [128, 2048], BF16, st=stF) for i in range(2)]
    ubuf = [sb(f"ub{i}", [128, 2048], BF16, st=stF) for i in range(2)]
    sgb = [sb(f"sg{i}", [128, 512], F32, st=stF) for i in range(2)]
    dbuf = [sb(f"db{i}", [128, 4096], BF16, st=stF) for i in range(2)]
    dtmp = [sb("dt0", [128, 1024], F32, st=stF)] * 2
    ds_bc = sb("ds_bc", [128, 2048], st=stF)
    fence("gpsimd")
    W("gpsimd", t_zf)
    fence("sync")
    t_ds = E("sync", SY.dma_start(out=ds_bc[:], in_=ds_b), dma=True)
    W("tensor", p.tok("scalar"))

    gtok = [None, None]
    utok = [None, None]
    last_sw = [None]

    def ffn_half(th, kick_ag=False):
        stG = contextlib.ExitStack()
        psG = [ps(f"psG{th}{i}", [128, 512], st=stG) for i in range(2)]
        psU = [ps(f"psU{th}{i}", [128, 512], st=stG) for i in range(2)]
        hvtok = [None, None]
        sgtok = [None, None]
        gmt = [None, None]
        umt = [None, None]
        for ic in range(44):
            b = ic % 2
            W("gpsimd", agc[0], gtok[b])
            t_g = E("gpsimd", GP.dma_start(out=gbuf[b][:], in_=g0[ic]), dma=True)
            W("gpsimd", utok[b])
            t_u = E("gpsimd", GP.dma_start(out=ubuf[b][:], in_=u0[ic]), dma=True)
            last_sw[0] = t_u
            W("tensor", t_g, sgtok[b])
            tG = None
            for ec in range(16):
                tG = E("tensor", TE.matmul(psG[b][:], gbuf[b][:, ec * 128:(ec + 1) * 128],
                                           fmA[:, ec * 1024 + th * 512: ec * 1024 + th * 512 + 512],
                                           start=(ec == 0), stop=(ec == 15)))
            gtok[b] = tG
            gmt[b] = tG
            W("scalar", tG, t_small, hvtok[b])
            t_sg = E("scalar", SC.activation(out=sgb[b][:], in_=psG[b][:], func=AF.Silu,
                                             scale=gss[:, ic:ic + 1]))
            sgtok[b] = t_sg
            W("tensor", t_u, hvtok[b])
            tU = None
            for ec in range(16):
                tU = E("tensor", TE.matmul(psU[b][:], ubuf[b][:, ec * 128:(ec + 1) * 128],
                                           fmA[:, ec * 1024 + th * 512: ec * 1024 + th * 512 + 512],
                                           start=(ec == 0), stop=(ec == 15)))
            utok[b] = tU
            W("vector", tU, t_sg)
            hvtok[b] = E("vector", VE.scalar_tensor_tensor(
                out=H[:, ic * 512:(ic + 1) * 512], in0=psU[b][:], scalar=uss[:, ic:ic + 1],
                in1=sgb[b][:], op0=ALU.mult, op1=ALU.mult))
        t_hv = p.tok("vector")
        if kick_ag:
            W("gpsimd", *t_agst)
            W("gpsimd", t_clf, t_zf, last_sw[0])
            agc[0] = E("gpsimd", GP.collective_compute(
                "AllGather", ALU.bypass, replica_groups=RG,
                ins=[ag_in.ap()], outs=[ag_out.ap()]), cc=True)
        W("tensor", p.tok("scalar"), t_hv)
        stG.close()
        stD = contextlib.ExitStack()
        psD = [ps(f"psD{th}{i}", [128, 1024], st=stD) for i in range(4)]
        dtok = [None, None]
        evtok = [None] * 4
        for half in range(2):
            for icb in range(11):
                b = icb % 2
                W("gpsimd", agc[0], dtok[b])
                t_d = E("gpsimd", GP.dma_start(
                    out=dbuf[b][:].rearrange("p (a m) -> p a m", a=4),
                    in_=d0[half, icb * 4: icb * 4 + 4].rearrange("a p m -> p a m")),
                    dma=True)
                W("tensor", t_d)
                if icb == 0:
                    W("tensor", *[t for t in evtok if t])
                tmm = None
                for ici in range(4):
                    ic = icb * 4 + ici
                    for t4 in range(4):
                        for h2 in range(2):
                            tmm = E("tensor", TE.matmul(
                                psD[t4][:, h2 * 512:(h2 + 1) * 512],
                                H[:, ic * 512 + t4 * 128: ic * 512 + t4 * 128 + 128],
                                dbuf[b][:, ici * 1024 + h2 * 512: ici * 1024 + h2 * 512 + 512],
                                start=(ic == 0), stop=(ic == 43)))
                dtok[b] = tmm
            t_last_mm = p.tok("tensor")
            for t4 in range(4):
                db = t4 % 2
                xsl = x_sb[:, (th * 4 + t4) * 2048 + half * 1024:
                           (th * 4 + t4) * 2048 + half * 1024 + 1024]
                W("vector", t_last_mm, t_ds, evtok[t4])
                t_e1 = E("vector", VE.tensor_tensor(out=dtmp[db][:], in0=psD[t4][:],
                                                    in1=ds_bc[:, half * 1024:half * 1024 + 1024],
                                                    op=ALU.mult))
                W("vector", t_e1)
                evtok[t4] = E("vector", VE.tensor_tensor(out=xsl, in0=xsl, in1=dtmp[db][:],
                                                         op=ALU.add))
        W("tensor", p.tok("vector"))
        stD.close()

    W("sync", t_zf)
    W("scalar", p.tok("tensor"), p.tok("vector"), t_zf)
    ffn_half(0, kick_ag=True)
    ffn_half(1)
    t_wk = E("gpsimd", GP.dma_start(out=wks[:], in_=wk_o), dma=True)
    t_wv = E("gpsimd", GP.dma_start(out=wvs[:], in_=wv_o), dma=True)
    stF.close()

    stB = contextlib.ExitStack()
    fmB = sb("fmB", [128, 16384], BF16, st=stB)
    stAT0 = contextlib.ExitStack()
    KT = sb("KT", [128, 4096], BF16, st=stAT0)
    V_aug = sb("V_aug", [128, 32 * 129], BF16, st=stAT0)
    fence("gpsimd")
    t_vone = E("gpsimd", GP.memset(V_aug[:], 1.0))

    stAT1 = contextlib.ExitStack()
    wqs = sb("wqs", [128, 8192], BF16, st=stAT1)
    xt_buf = [sb(f"xtb{i}", [128, 4096], BF16, st=stAT1) for i in range(2)]
    t_wq = E("gpsimd", GP.dma_start(out=wqs[:], in_=wq_o), dma=True)
    fence("sync")
    stQ = contextlib.ExitStack()
    psK = ps("psK", [128, 256], st=stQ)
    psQ = [ps(f"psQ{i}", [128, 256], st=stQ) for i in range(2)]
    psV = [ps(f"psV{i}", [128, 128], st=stQ) for i in range(2)]
    ag3 = ag_out.ap().rearrange("(a p) m -> a p m", p=128)
    xttok = [None, None]
    ktok = None
    qtok = [None, None]
    vtok = [None, None]
    for tb2 in range(16):
        r, s = tb2 // 4, tb2 % 4
        b = tb2 % 2
        W("sync", agc[0], xttok[b])
        t_xt = E("sync", SY.dma_start(
            out=xt_buf[b][:].rearrange("p (a m) -> p a m", a=16),
            in_=ag3[r * 16:(r + 1) * 16, :, s * 256:(s + 1) * 256].rearrange("a p m -> p a m")),
            dma=True)
        W("tensor", t_xt, t_wk, ktok)
        tK = None
        for ec in range(16):
            tK = E("tensor", TE.matmul(psK[:], wks[:, ec * 128:(ec + 1) * 128],
                                       xt_buf[b][:, ec * 256:(ec + 1) * 256],
                                       start=(ec == 0), stop=(ec == 15)))
        W("scalar", tK, t_small)
        ktok = E("scalar", SC.activation(out=KT[:, tb2 * 256:(tb2 + 1) * 256], in_=psK[:],
                                         func=AF.Identity, bias=bks[:],
                                         scale=np.float32(1.0)))
        for hl in range(4):
            qb = hl % 2
            W("tensor", t_wq, qtok[qb])
            tQ = None
            for ec in range(16):
                tQ = E("tensor", TE.matmul(psQ[qb][:],
                                           wqs[:, hl * 2048 + ec * 128: hl * 2048 + ec * 128 + 128],
                                           xt_buf[b][:, ec * 256:(ec + 1) * 256],
                                           start=(ec == 0), stop=(ec == 15)))
            W("scalar", tQ)
            qtok[qb] = E("scalar", SC.activation(
                out=fmB[:, hl * 4096 + tb2 * 256: hl * 4096 + tb2 * 256 + 256], in_=psQ[qb][:],
                func=AF.Identity, bias=bqs[:, hl:hl + 1], scale=np.float32(1.0)))
        for t4 in range(2):
            vb = t4 % 2
            W("tensor", t_wv, vtok[vb])
            tV = None
            for ec in range(16):
                tV = E("tensor", TE.matmul(psV[vb][:],
                                           xt_buf[b][:, ec * 256 + t4 * 128: ec * 256 + t4 * 128 + 128],
                                           wvs[:, ec * 128:(ec + 1) * 128],
                                           start=(ec == 0), stop=(ec == 15)))
            kt = tb2 * 2 + t4
            W("vector", tV, t_small, t_vone)
            vtok[vb] = E("vector", VE.tensor_tensor(
                out=V_aug[:, kt * 129: kt * 129 + 128], in0=psV[vb][:], in1=bvbc[:],
                op=ALU.add))
        xttok[b] = p.tok("tensor")
    W("tensor", p.tok("scalar"), p.tok("vector"))
    stQ.close()
    stAT1.close()

    stS = contextlib.ExitStack()
    eTb = [sb(f"eT{i}", [128, 512], BF16, st=stS) for i in range(2)]
    O_q = [sb(f"Oq{i}", [128, 2048], BF16, st=stS) for i in range(2)]
    rzc = [sb(f"rz{i}", [128, 1], F32, st=stS) for i in range(4)]
    ao_wide = sb("ao_wide", [128, 8192], BF16, st=stS)
    psS = [ps(f"psS{i}", [128, 512], st=stS) for i in range(2)]
    psAV = [ps(f"psAV{i}", [128, 129], st=stS) for i in range(4)]
    setok = [None, None]
    avtok = [None] * 4
    ettok = [None, None]
    sc_tok = [None, None]
    t_scat = []
    rs6 = [rs_out[m].ap().rearrange("(t f) c -> f t c", f=4) for m in range(2)]
    t_aol = [None] * 16
    t_rs0 = None
    for qc in range(8):
        ob = qc % 2
        ev_toks = []
        for hl in range(4):
            nkc = 4 * qc + 4
            for kc in range(nkc):
                b = kc % 2
                W("tensor", setok[b])
                if kc == 0:
                    W("tensor", *[t for t in avtok if t])
                t_s = E("tensor", TE.matmul(psS[b][:], KT[:, kc * 128:(kc + 1) * 128],
                                            fmB[:, hl * 4096 + qc * 512: hl * 4096 + qc * 512 + 512],
                                            start=True, stop=True))
                if kc >= 4 * qc:
                    jm = kc - 4 * qc
                    W("vector", t_s, t_mk[jm])
                    t_s = E("vector", VE.tensor_tensor(
                        out=psS[b][:], in0=psS[b][:],
                        in1=masks[:, jm * 512:(jm + 1) * 512], op=ALU.add))
                W("scalar", t_s, ettok[b])
                t_e = E("scalar", SC.activation(out=eTb[b][:], in_=psS[b][:], func=AF.Exp,
                                                scale=np.float32(RSQRT_SCALE)))
                setok[b] = t_e
                tAV = None
                W("tensor", t_e)
                for qt in range(4):
                    tAV = E("tensor", TE.matmul(psAV[qt][:], eTb[b][:, qt * 128:(qt + 1) * 128],
                                                V_aug[:, kc * 129: kc * 129 + 129],
                                                start=(kc == 0), stop=(kc == nkc - 1)))
                ettok[b] = tAV
            t_fin = p.tok("tensor")
            for qt in range(4):
                W("vector", t_fin, sc_tok[ob] if hl == 0 else None)
                t_rz = E("vector", VE.reciprocal(rzc[qt][:], psAV[qt][:, 128:129]))
                W("vector", t_rz)
                avtok[qt] = E("vector", VE.tensor_scalar(
                    out=O_q[ob][:, qt * 512 + hl * 128: qt * 512 + hl * 128 + 128],
                    in0=psAV[qt][:, 0:128], scalar1=rzc[qt][:], scalar2=None, op0=ALU.mult))
                ev_toks.append(avtok[qt])
        m2 = qc % 2
        W("gpsimd", t_oix, t_zf, *ev_toks)
        if qc == 7:
            W("gpsimd", t_rs0)
        last_sc = None
        for qt in range(4):
            tt_idx = qc * 4 + qt
            last_sc = E("gpsimd", GP.indirect_dma_start(
                out=rs_in[m2].ap(),
                out_offset=bass.IndirectOffsetOnAxis(ap=oix[:, tt_idx:tt_idx + 1], axis=0),
                in_=O_q[ob][:, qt * 512:(qt + 1) * 512], in_offset=None), dma=True)
            t_scat.append(last_sc)
        sc_tok[ob] = last_sc
        if qc == 6:
            W("gpsimd", *t_scat)
            t_rs0 = E("gpsimd", GP.collective_compute(
                "ReduceScatter", ALU.add, replica_groups=RG,
                ins=[rs_in[0].ap()], outs=[rs_out[0].ap()]), cc=True)
            W("sync", t_rs0)
            for sl in range(16):
                ti_, j_ = sl // 4, sl % 4
                t_aol[sl] = E("sync", SY.dma_start(
                    out=ao_wide[:, sl * 512:(sl + 1) * 512],
                    in_=rs6[0][j_, (ti_ % 4) * 128:(ti_ % 4) * 128 + 128, :]), dma=True)
    W("gpsimd", *t_scat)
    W("gpsimd", t_aol[15])
    t_rs1 = E("gpsimd", GP.collective_compute(
        "ReduceScatter", ALU.add, replica_groups=RG,
        ins=[rs_in[1].ap()], outs=[rs_out[1].ap()]), cc=True)
    ao_add = [None] * 16
    for sl in range(16):
        ti_, j_ = sl // 4, sl % 4
        xsl = x_sb[:, ti_ * 2048 + j_ * 512: ti_ * 2048 + j_ * 512 + 512]
        W("vector", t_aol[sl])
        ao_add[sl] = E("vector", VE.tensor_tensor(
            out=xsl, in0=xsl, in1=ao_wide[:, sl * 512:(sl + 1) * 512], op=ALU.add))
    W("sync", t_rs1)
    for sl in range(16):
        ti_, j_ = 4 + sl // 4, sl % 4
        W("sync", ao_add[sl])
        t_aol[sl] = E("sync", SY.dma_start(
            out=ao_wide[:, sl * 512:(sl + 1) * 512],
            in_=rs6[1][j_, (ti_ % 4) * 128:(ti_ % 4) * 128 + 128, :]), dma=True)
    for sl in range(16):
        ti_, j_ = 4 + sl // 4, sl % 4
        xsl = x_sb[:, ti_ * 2048 + j_ * 512: ti_ * 2048 + j_ * 512 + 512]
        W("vector", t_aol[sl])
        E("vector", VE.tensor_tensor(
            out=xsl, in0=xsl, in1=ao_wide[:, sl * 512:(sl + 1) * 512], op=ALU.add))
    W("tensor", p.tok("scalar"), p.tok("vector"))
    stS.close()
    stAT0.close()

    st5 = contextlib.ExitStack()
    xn_st2 = [sb(f"xn2st{i}", [128, 2048], BF16, st=st5) for i in range(2)]
    sq_scratch = [sb("sq_scr2", [128, 2048], BF16, st=st5)]
    tr_ps2 = [(ps(f"trq{i}", [128, 128], BF16, st=st5), None) for i in range(4)]
    W("scalar", p.tok("vector"))
    W("tensor", p.tok("scalar"), p.tok("vector"))
    stage_rd2 = [None, None]
    for ti in range(8):
        t_r, t_ssq = ln_stats(ti)
        W("vector", t_r)
        t_a = E("vector", VE.tensor_tensor(out=tmp1[:], in0=stats_r[:, ti:ti + 1],
                                           in1=stats_r[:, ti:ti + 1], op=ALU.mult))
        W("vector", t_a)
        t_b = E("vector", VE.tensor_tensor(out=tmp2[:], in0=stats_v[:, ti:ti + 1],
                                           in1=tmp1[:], op=ALU.mult))
        W("vector", t_b)
        t_c = E("vector", VE.tensor_scalar(out=tmp1[:], in0=tmp2[:], scalar1=1e-5,
                                           scalar2=None, op0=ALU.add))
        W("scalar", t_c)
        t_s = E("scalar", SC.activation(out=tmp2[:], in_=tmp1[:], func=AF.Sqrt))
        W("vector", t_s)
        t_r2 = E("vector", VE.reciprocal(tmp3[:], tmp2[:]))
        W("vector", t_r2)
        t_s2 = E("vector", VE.tensor_tensor(out=s2c[:, ti:ti + 1], in0=stats_r[:, ti:ti + 1],
                                            in1=tmp3[:], op=ALU.mult))
        W("vector", t_s2)
        t_o2a = E("vector", VE.tensor_tensor(out=tmp1[:], in0=stats_m[:, ti:ti + 1],
                                             in1=s2c[:, ti:ti + 1], op=ALU.mult))
        W("vector", t_o2a)
        t_o2 = E("vector", VE.tensor_scalar(out=o2c[:, ti:ti + 1], in0=tmp1[:],
                                            scalar1=-1.0, scalar2=None, op0=ALU.mult))
        W("vector", stage_rd2[0])
        t_st = E("vector", VE.tensor_scalar(out=xn_st2[0][:],
                                            in0=x_sb[:, ti * 2048:(ti + 1) * 2048],
                                            scalar1=stats_m[:, ti:ti + 1],
                                            scalar2=stats_r[:, ti:ti + 1],
                                            op0=ALU.subtract, op1=ALU.mult))

        def evacA1(tp, ec, t_tr, ti=ti):
            W("scalar", t_tr)
            return E("scalar", SC.copy(
                out=fmA[:, ec * 1024 + ti * 128: ec * 1024 + ti * 128 + 128], in_=tp[:]))
        _, t_trl = build_fmT(evacA1, xn_st2[0], t_st, tr_ps2)
        stage_rd2[0] = t_trl
        W("vector", t_o2, stage_rd2[1])
        t_st2 = E("vector", VE.tensor_scalar(out=xn_st2[1][:],
                                             in0=x_sb[:, ti * 2048:(ti + 1) * 2048],
                                             scalar1=s2c[:, ti:ti + 1],
                                             scalar2=o2c[:, ti:ti + 1],
                                             op0=ALU.mult, op1=ALU.add))

        def evacB(tp, ec, t_tr, ti=ti):
            W("vector", t_tr, t_small)
            return E("vector", VE.tensor_scalar(
                out=fmB[:, ec * 1024 + ti * 128: ec * 1024 + ti * 128 + 128], in0=tp[:],
                scalar1=g1as[:, ec:ec + 1], scalar2=be1s[:, ec:ec + 1],
                op0=ALU.mult, op1=ALU.add))
        _, t_trl2 = build_fmT(evacB, xn_st2[1], t_st2, tr_ps2)
        stage_rd2[1] = t_trl2
    W("tensor", p.tok("scalar"), p.tok("vector"))
    st5.close()

    st6 = contextlib.ExitStack()
    e_r = sb("e_r", [128, 12], st=st6)
    probs = sb("probs", [128, 12], st=st6)
    biased = sb("biased", [128, 12], st=st6)
    mk1 = sb("mk1", [128, 12], st=st6)
    b2v = sb("b2v", [128, 12], st=st6)
    psL = ps("psL", [128, 12], st=st6)
    t_ltok = None
    for ti in range(8):
        W("tensor", t_clf, t_ltok)
        tL = None
        for ec in range(16):
            tL = E("tensor", TE.matmul(psL[:], fmA[:, ec * 1024 + ti * 128: ec * 1024 + ti * 128 + 128],
                                       clfs[:, ec * 12:(ec + 1) * 12],
                                       start=(ec == 0), stop=(ec == 15)))
        W("vector", tL)
        t1 = E("vector", VE.reduce_max(out=tmp1[:], in_=psL[:], axis=AX.X, negate=True))
        W("scalar", t1)
        t2 = E("scalar", SC.activation(out=e_r[:], in_=psL[:], func=AF.Exp, bias=tmp1[:],
                                       scale=np.float32(1.0), accum_out=tmp2[:]))
        t_ltok = t2
        W("vector", t2)
        t3 = E("vector", VE.reciprocal(tmp3[:], tmp2[:]))
        W("vector", t3)
        t4 = E("vector", VE.tensor_scalar(out=probs[:], in0=e_r[:], scalar1=tmp3[:],
                                          scalar2=None, op0=ALU.mult))
        W("vector", t4, t_small)
        t5 = E("vector", VE.tensor_tensor(out=biased[:], in0=probs[:], in1=ebbc[:], op=ALU.add))
        W("vector", t5)
        t6 = E("vector", VE.reduce_max(out=tmp1[:], in_=biased[:], axis=AX.X))
        W("vector", t6)
        t7 = E("vector", VE.tensor_scalar(out=mk1[:], in0=biased[:], scalar1=tmp1[:],
                                          scalar2=None, op0=ALU.is_ge))
        W("vector", t7)
        t8 = E("vector", VE.scalar_tensor_tensor(out=b2v[:], in0=mk1[:], scalar=-1e30,
                                                 in1=biased[:], op0=ALU.mult, op1=ALU.add))
        W("vector", t8)
        t9 = E("vector", VE.reduce_max(out=tmp1[:], in_=b2v[:], axis=AX.X))
        W("vector", t9)
        t10 = E("vector", VE.tensor_scalar(out=b2v[:], in0=b2v[:], scalar1=tmp1[:],
                                           scalar2=None, op0=ALU.is_ge))
        W("vector", t10)
        t11 = E("vector", VE.tensor_tensor(out=mk1[:], in0=mk1[:], in1=b2v[:], op=ALU.add))
        W("vector", t11)
        t12 = E("vector", VE.scalar_tensor_tensor(out=w_all[:, ti * 12:(ti + 1) * 12],
                                                  in0=probs[:], scalar=1.5, in1=mk1[:],
                                                  op0=ALU.mult, op1=ALU.mult))
        W("vector", t12)
        t13 = E("vector", VE.reduce_sum(out=tmp1[:], in_=w_all[:, ti * 12 + 8: ti * 12 + 12],
                                        axis=AX.X))
        W("vector", t13)
        t14 = E("vector", VE.tensor_scalar(out=tmp2[:], in0=tmp1[:], scalar1=1.0,
                                           scalar2=None, op0=ALU.add))
        W("vector", t14)
        t15 = E("vector", VE.tensor_tensor(out=tmp1[:], in0=tmp2[:], in1=stats_r[:, ti:ti + 1],
                                           op=ALU.mult))
        W("vector", t15)
        t16 = E("vector", VE.tensor_scalar(out=tmp2[:], in0=tmp1[:], scalar1=1.0,
                                           scalar2=None, op0=ALU.add))
        t17 = E("vector", VE.tensor_tensor(out=tmp3[:], in0=stats_m[:, ti:ti + 1],
                                           in1=tmp1[:], op=ALU.mult))
        W("vector", t16, t17)
        xt = x_sb[:, ti * 2048:(ti + 1) * 2048]
        E("vector", VE.tensor_scalar(out=xt, in0=xt, scalar1=tmp2[:], scalar2=tmp3[:],
                                     op0=ALU.mult, op1=ALU.subtract))
    st6.close()

    st7 = contextlib.ExitStack()
    kvw = [[sb(f"kvw{s}{i}", [128, 4096], BF16, st=st7) for i in range(2)]
           for s in range(2)]
    cosb = [sb(f"cosb{i}", [128, 512], F32, st=st7) for i in range(2)]
    sinb = [sb(f"sinb{i}", [128, 512], F32, st=st7) for i in range(2)]
    ktmp = [sb(f"ktmp{i}", [128, 512], F32, st=st7) for i in range(2)]
    c1b = sb("c1b", [128, 512], F32, st=st7)
    c2b = sb("c2b", [128, 512], F32, st=st7)
    bias_kv = sb("bias_kv", [128, 512], F32, st=st7)
    Vst = sb("Vst", [128, 4096], BF16, st=st7)
    Kb = sb("Kb", [128, 512], BF16, st=st7)
    ppst = [sb(f"ppst{i}", [128, 512], BF16, st=st7) for i in range(2)]
    ssp = [sb(f"ssp{i}", [128, 1], F32, st=st7) for i in range(2)]
    psKV = [ps(f"psKV{i}", [128, 512], st=st7) for i in range(2)]
    kvwtok = [[None, None], [None, None]]
    kvwrd = [None, None]
    kvpass = 0
    ktmptok = [None, None]
    ktmprd = [[], []]
    ssptok = [None, None]
    tabtok = [None, None]
    tabrd = [None, None]
    c1tok = None
    c2tok = None
    kbtok = None
    vsttok = [None] * 8
    ppdma = [None, None]
    t_biaskv = None
    fence("gpsimd")
    W("gpsimd", t_rs1)
    fence("sync")
    W("tensor", p.tok("scalar"), p.tok("vector"))
    for ds in range(4):
        for kv in (1, 0):
            W("sync", t_biaskv)
            t_biaskv_l = E("sync", SY.dma_start(out=bias_kv[:], in_=b1kv_b[kv, ds]), dma=True)
            sset = kvpass % 2
            kvpass += 1
            for half in range(2):
                W("gpsimd", kvwtok[sset][half], kvwrd[sset])
                kvwtok[sset][half] = E("gpsimd", GP.dma_start(
                    out=kvw[sset][half][:],
                    in_=w1kv[kv, ds][:, half * 4096: half * 4096 + 4096]),
                    dma=True)
            for ti in range(8):
                b = ti % 2
                if ds < 2:
                    W("sync", tabtok[b], tabrd[b])
                    tabtok[b] = E("sync", SY.dma_start(out=cosb[b][:], in_=rcos[ds, ti]), dma=True)
                    tabtok[b] = E("sync", SY.dma_start(out=sinb[b][:], in_=rsin[ds, ti]), dma=True)
                W("tensor", kvwtok[sset][0], kvwtok[sset][1], ktmptok[b])
                tkv = None
                for ec in range(16):
                    buf = kvw[sset][ec // 8]
                    tkv = E("tensor", TE.matmul(
                        psKV[b][:], fmB[:, ec * 1024 + ti * 128: ec * 1024 + ti * 128 + 128],
                        buf[:, (ec % 8) * 512: (ec % 8) * 512 + 512],
                        start=(ec == 0), stop=(ec == 15)))
                kvwrd[sset] = tkv
                W("vector", tkv, t_biaskv_l, *ktmprd[b])
                ktmprd[b] = []
                t_kt = E("vector", VE.tensor_tensor(out=ktmp[b][:], in0=psKV[b][:],
                                                    in1=bias_kv[:], op=ALU.add))
                ktmptok[b] = t_kt
                t_biaskv = t_kt
                sb_ = b
                W("scalar", t_kt, ssptok[sb_], c1tok)
                t_sq = E("scalar", SC.activation(out=c1b[:], in_=ktmp[b][:], func=AF.Square,
                                                 accum_out=ssp[sb_][:]))
                ktmprd[b].append(t_sq)
                ss_dst = ssK if kv == 0 else ssV
                W("vector", t_sq)
                if ds == 0:
                    tss = E("vector", VE.tensor_copy(ss_dst[:, ti:ti + 1], ssp[sb_][:]))
                else:
                    tss = E("vector", VE.tensor_tensor(out=ss_dst[:, ti:ti + 1],
                                                       in0=ss_dst[:, ti:ti + 1],
                                                       in1=ssp[sb_][:], op=ALU.add))
                ssptok[sb_] = tss
                if ds < 2:
                    W("vector", t_kt, tabtok[b], c2tok, t_sq)
                    ts1 = E("vector", VE.tensor_copy(c2b[:, 0:512:2], ktmp[b][:, 1:512:2]))
                    ts2 = E("vector", VE.tensor_copy(c2b[:, 1:512:2], ktmp[b][:, 0:512:2]))
                    tc1 = E("vector", VE.tensor_tensor(out=c1b[:], in0=ktmp[b][:],
                                                       in1=cosb[b][:], op=ALU.mult))
                    c1tok = tc1
                    ktmprd[b].append(tc1)
                    tabrd[b] = tc1
                    W("vector", ts1, ts2)
                    tc2 = E("vector", VE.tensor_tensor(out=c2b[:], in0=c2b[:],
                                                       in1=sinb[b][:], op=ALU.mult))
                    c2tok = tc2
                    tabrd[b] = tc2
                    W("vector", tc1, tc2)
                    if kv == 1:
                        W("vector", vsttok[ti])
                        t_rot = E("vector", VE.tensor_tensor(
                            out=Vst[:, ti * 512:(ti + 1) * 512], in0=c1b[:], in1=c2b[:],
                            op=ALU.add))
                        vsttok[ti] = t_rot
                        c1tok = t_rot
                        c2tok = t_rot
                    else:
                        W("vector", kbtok)
                        t_kb = E("vector", VE.tensor_tensor(out=Kb[:], in0=c1b[:],
                                                            in1=c2b[:], op=ALU.add))
                        c1tok = t_kb
                        c2tok = t_kb
                        W("vector", t_kb, vsttok[ti], ppdma[b])
                        t_pp = E("vector", VE.tensor_tensor(
                            out=ppst[b][:], in0=Vst[:, ti * 512:(ti + 1) * 512],
                            in1=Kb[:], op=ALU.mult))
                        kbtok = t_pp
                        vsttok[ti] = t_pp
                        W("sync", t_pp)
                        ppdma[b] = E("sync", SY.dma_start(
                            out=ppd.ap()[ti, :, ds * 512:(ds + 1) * 512], in_=ppst[b][:]),
                            dma=True)
                else:
                    if kv == 1:
                        W("scalar", t_kt, vsttok[ti], t_sq)
                        t_cp = E("scalar", SC.copy(out=Vst[:, ti * 512:(ti + 1) * 512],
                                                   in_=ktmp[b][:]))
                        vsttok[ti] = t_cp
                        ktmprd[b].append(t_cp)
                    else:
                        W("scalar", t_kt, kbtok, t_sq)
                        t_kb = E("scalar", SC.copy(out=Kb[:], in_=ktmp[b][:]))
                        ktmprd[b].append(t_kb)
                        W("vector", t_kb, vsttok[ti], ppdma[b])
                        t_pp = E("vector", VE.tensor_tensor(
                            out=ppst[b][:], in0=Vst[:, ti * 512:(ti + 1) * 512],
                            in1=Kb[:], op=ALU.mult))
                        kbtok = t_pp
                        vsttok[ti] = t_pp
                        W("sync", t_pp)
                        ppdma[b] = E("sync", SY.dma_start(
                            out=ppd.ap()[ti, :, ds * 512:(ds + 1) * 512], in_=ppst[b][:]),
                            dma=True)
    t_pp_all = p.tok("sync")
    st7.close()

    stA2 = contextlib.ExitStack()
    ppb = [sb(f"ppb{i}", [128, 2048], BF16, st=stA2) for i in range(2)]
    psA = ps("psA", [128, 16], st=stA2)
    fence("sync")
    ppbtok = [None, None]
    scoltok = [None, None]
    tA = None
    for ti in range(8):
        b = ti % 2
        W("sync", t_pp_all, ppbtok[b])
        t_ld = E("sync", SY.dma_start(out=ppb[b][:], in_=ppd.ap()[ti]), dma=True)
        W("vector", p.tok("vector") if ti == 0 else None)
        tm = E("vector", VE.tensor_tensor(out=tmp1[:], in0=ssK[:, ti:ti + 1],
                                          in1=ssV[:, ti:ti + 1], op=ALU.mult))
        W("scalar", tm)
        tsq = E("scalar", SC.activation(out=tmp2[:], in_=tmp1[:], func=AF.Sqrt))
        W("vector", tsq)
        tc = E("vector", VE.reciprocal(tmp3[:], tmp2[:]))
        W("vector", tc, scoltok[b])
        tsc = E("vector", VE.tensor_copy(scol[:, b:b + 1], tmp3[:]))
        W("tensor", tsc, t_ld)
        for ch in range(16):
            tA = E("tensor", TE.matmul(psA[:, ch:ch + 1], ppb[b][:, ch * 128:(ch + 1) * 128],
                                       scol[:, b:b + 1], start=(ti == 0), stop=(ti == 7)))
        ppbtok[b] = tA
        scoltok[b] = tA
    W("vector", tA)
    t_Ar = E("vector", VE.tensor_copy(A_raw[:], psA[:]))
    W("sync", t_Ar)
    t_As = E("sync", SY.dma_start(out=arA_in.ap(), in_=A_raw[:]), dma=True)
    W("tensor", p.tok("vector"))
    stA2.close()

    stB2 = contextlib.ExitStack()
    Q2T = sb("Q2T", [128, 16384], BF16, st=stB2)
    st8a = contextlib.ExitStack()
    qwb = [sb(f"qwb{i}", [128, 2048], BF16, st=st8a) for i in range(2)]
    psQ2 = [ps(f"psQ2{i}", [128, 512], st=st8a) for i in range(2)]
    fence("gpsimd")
    pre_q = [E("gpsimd", GP.dma_start(out=qwb[i][:], in_=w1q[i]), dma=True)
             for i in range(2)]
    W("gpsimd", pre_q[1], t_As)
    t_arc = E("gpsimd", GP.collective_compute(
        "AllReduce", ALU.add, replica_groups=RG,
        ins=[arA_in.ap()], outs=[arA_out.ap()]), cc=True)
    qwtok = [None, None]
    q2tok = [None, None]
    q2ev = []
    for qch in range(16):
        b = qch % 2
        if qch < 2:
            t_w = pre_q[qch]
        else:
            W("gpsimd", qwtok[b], t_arc)
            t_w = E("gpsimd", GP.dma_start(out=qwb[b][:], in_=w1q[qch]), dma=True)
        for h2 in range(2):
            W("tensor", t_w, q2tok[h2])
            tq = None
            for ec in range(16):
                tq = E("tensor", TE.matmul(psQ2[h2][:], qwb[b][:, ec * 128:(ec + 1) * 128],
                                           fmB[:, ec * 1024 + h2 * 512: ec * 1024 + h2 * 512 + 512],
                                           start=(ec == 0), stop=(ec == 15)))
            qwtok[b] = tq
            W("vector", tq, t_small)
            q2tok[h2] = E("vector", VE.tensor_scalar(
                out=Q2T[:, qch * 1024 + h2 * 512: qch * 1024 + h2 * 512 + 512],
                in0=psQ2[h2][:], scalar1=b1qs[:, qch:qch + 1], scalar2=None,
                op0=ALU.add))
            q2ev.append(q2tok[h2])
    W("sync", t_arc)
    t_Al = E("sync", SY.dma_start(out=A_raw[:], in_=arA_out.ap()), dma=True)
    W("vector", t_Al)
    t_A1 = E("vector", VE.tensor_scalar(out=A1[:], in0=A_raw[:], scalar1=0.5, scalar2=0.5,
                                        op0=ALU.mult, op1=ALU.add))
    sc_toks = []
    for qch in range(16):
        W("vector", t_A1, q2ev[2 * qch], q2ev[2 * qch + 1])
        sc_toks.append(E("vector", VE.tensor_scalar(
            out=Q2T[:, qch * 1024:(qch + 1) * 1024], in0=Q2T[:, qch * 1024:(qch + 1) * 1024],
            scalar1=A1[:, qch:qch + 1], scalar2=None, op0=ALU.mult)))
    W("tensor", p.tok("vector"))
    st8a.close()
    st8b = contextlib.ExitStack()
    owb = [sb(f"owb{i}", [128, 2048], BF16, st=st8b) for i in range(2)]
    ytmp = [sb(f"ytmp{i}", [128, 512], F32, st=st8b) for i in range(2)]
    b1ob = [sb(f"b1ob{i}", [128, 512], F32, st=st8b) for i in range(2)]
    psY = [ps(f"psY{i}", [128, 512], st=st8b) for i in range(8)]
    fence("gpsimd")
    fence("sync")
    owtok = [None, None]
    btok = [None, None]
    ytok = [None] * 8
    for cs in range(4):
        cb = cs % 2
        W("sync", btok[cb])
        t_b = E("sync", SY.dma_start(out=b1ob[cb][:], in_=b1o_b[cs]), dma=True)
        for qg in range(4):
            b = qg % 2
            W("gpsimd", owtok[b])
            t_w = E("gpsimd", GP.dma_start(
                out=owb[b][:], in_=w1o[cs][:, qg * 2048:(qg + 1) * 2048]), dma=True)
            W("tensor", t_w)
            if qg == 0:
                W("tensor", *[t for t in ytok if t])
            tY = None
            for ti in range(8):
                for k in range(4):
                    qch = qg * 4 + k
                    tY = E("tensor", TE.matmul(
                        psY[ti][:],
                        Q2T[:, qch * 1024 + ti * 128: qch * 1024 + ti * 128 + 128],
                        owb[b][:, k * 512:(k + 1) * 512],
                        start=(qg == 0 and k == 0), stop=(qg == 3 and k == 3)))
            owtok[b] = tY
        t_fin = p.tok("tensor")
        for ti in range(8):
            yb = ti % 2
            W("vector", t_fin, t_b, ytok[ti])
            t_y1 = E("vector", VE.tensor_tensor(out=ytmp[yb][:], in0=psY[ti][:],
                                                in1=b1ob[cb][:], op=ALU.add))
            xsl = x_sb[:, ti * 2048 + cs * 512: ti * 2048 + cs * 512 + 512]
            W("vector", t_y1)
            ytok[ti] = E("vector", VE.tensor_tensor(out=xsl, in0=xsl, in1=ytmp[yb][:],
                                                    op=ALU.add))
        btok[cb] = ytok[7]
    W("tensor", p.tok("vector"))
    st8b.close()
    stB2.close()
    stB.close()

    st9 = contextlib.ExitStack()
    H1 = sb("H1", [128, 8192], BF16, st=st9)
    g1b = [sb(f"g1b{i}", [128, 2048], BF16, st=st9) for i in range(2)]
    u1b = [sb(f"u1b{i}", [128, 2048], BF16, st=st9) for i in range(2)]
    sgb2 = [sb(f"sgb2{i}", [128, 512], F32, st=st9) for i in range(2)]
    d1b = sb("d1b", [128, 16384], BF16, st=st9)
    psG1 = [ps(f"psGm{i}", [128, 512], st=st9) for i in range(2)]
    psU1 = [ps(f"psUm{i}", [128, 512], st=st9) for i in range(2)]
    psM = [ps(f"psM{i}", [128, 512], st=st9) for i in range(2)]
    fence("gpsimd")
    g1tok = [None, None]
    u1tok = [None, None]
    sg1tok = [None, None]
    h1tok = [None, None]
    d1tok = None
    mevtok = [None, None]
    for e in range(8):
        t_h1_all = []
        for ic in range(8):
            b = ic % 2
            W("gpsimd", g1tok[b])
            t_g = E("gpsimd", GP.dma_start(out=g1b[b][:], in_=g1[e, :, ic, :]), dma=True)
            W("gpsimd", u1tok[b])
            t_u = E("gpsimd", GP.dma_start(out=u1b[b][:], in_=u1[e, :, ic, :]), dma=True)
            for th in range(2):
                pb = (ic * 2 + th) % 2
                W("tensor", t_g, sg1tok[pb])
                tG = None
                for hc in range(16):
                    tG = E("tensor", TE.matmul(psG1[pb][:], g1b[b][:, hc * 128:(hc + 1) * 128],
                                               fmA[:, hc * 1024 + th * 512: hc * 1024 + th * 512 + 512],
                                               start=(hc == 0), stop=(hc == 15)))
                g1tok[b] = tG
                W("scalar", tG, h1tok[pb])
                t_sg = E("scalar", SC.activation(out=sgb2[pb][:], in_=psG1[pb][:], func=AF.Silu))
                sg1tok[pb] = t_sg
                W("tensor", t_u, h1tok[pb])
                tU = None
                for hc in range(16):
                    tU = E("tensor", TE.matmul(psU1[pb][:], u1b[b][:, hc * 128:(hc + 1) * 128],
                                               fmA[:, hc * 1024 + th * 512: hc * 1024 + th * 512 + 512],
                                               start=(hc == 0), stop=(hc == 15)))
                u1tok[b] = tU
                W("vector", tU, t_sg)
                th1 = E("vector", VE.tensor_tensor(
                    out=H1[:, ic * 1024 + th * 512: ic * 1024 + th * 512 + 512],
                    in0=sgb2[pb][:], in1=psU1[pb][:], op=ALU.mult))
                h1tok[pb] = th1
                t_h1_all.append(th1)
        W("gpsimd", d1tok)
        t_d = E("gpsimd", GP.dma_start(
            out=d1b[:].rearrange("p (a m) -> p a m", a=8), in_=d1[e]), dma=True)
        for ti in range(8):
            for cs in range(4):
                mb = (ti * 4 + cs) % 2
                W("tensor", t_d, mevtok[mb], *t_h1_all)
                t_h1_all = []
                tM = None
                for icl in range(8):
                    tM = E("tensor", TE.matmul(
                        psM[mb][:],
                        H1[:, icl * 1024 + ti * 128: icl * 1024 + ti * 128 + 128],
                        d1b[:, icl * 2048 + cs * 512: icl * 2048 + cs * 512 + 512],
                        start=(icl == 0), stop=(icl == 7)))
                d1tok = tM
                xsl = x_sb[:, ti * 2048 + cs * 512: ti * 2048 + cs * 512 + 512]
                W("vector", tM)
                mevtok[mb] = E("vector", VE.scalar_tensor_tensor(
                    out=xsl, in0=psM[mb][:], scalar=w_all[:, ti * 12 + e: ti * 12 + e + 1],
                    in1=xsl, op0=ALU.mult, op1=ALU.add))
    W("tensor", p.tok("vector"))
    st9.close()

    W("sync", p.tok("vector"))
    st_toks = []
    for ti in range(8):
        st_toks.append(E("sync", SY.dma_start(out=out_x[ti * 128:(ti + 1) * 128, :],
                                              in_=x_sb[:, ti * 2048:(ti + 1) * 2048]), dma=True))
    W("sync", *st_toks)
    return nc, stack


def _host_layouts(inp):
    L = {}
    L["g0"] = np.ascontiguousarray(
        inp["L0_ffn_gate_w"].reshape(44, 128, 16, 128).transpose(0, 3, 2, 1)).reshape(44, 128, 2048)
    L["u0"] = np.ascontiguousarray(
        inp["L0_ffn_up_w"].reshape(44, 128, 16, 128).transpose(0, 3, 2, 1)).reshape(44, 128, 2048)
    L["d0"] = np.ascontiguousarray(
        inp["L0_ffn_down_w"].T.reshape(44, 128, 2, 1024).transpose(2, 0, 1, 3))
    L["gs0"] = np.ascontiguousarray(inp["L0_ffn_gate_s"].reshape(44, 128).T)
    L["us0"] = np.ascontiguousarray(inp["L0_ffn_up_s"].reshape(44, 128).T)
    L["ds_b"] = np.ascontiguousarray(
        np.broadcast_to(inp["L0_ffn_down_s"].reshape(1, 2048), (128, 2048)))
    W_in = inp["L1_attn_W_in"]
    L["w1q"] = np.ascontiguousarray(
        W_in[:, :2048].reshape(16, 128, 16, 128).transpose(2, 1, 0, 3)).reshape(16, 128, 2048)
    L["w1kv"] = np.ascontiguousarray(
        W_in[:, 2048:].reshape(16, 128, 2, 4, 512).transpose(2, 3, 1, 0, 4)).reshape(2, 4, 128, 8192)
    b_in = inp["L1_attn_b_in"]
    L["b1q"] = np.ascontiguousarray(b_in[:2048].reshape(16, 128).T)
    L["b1kv_b"] = np.ascontiguousarray(
        np.broadcast_to(b_in[2048:].reshape(2, 4, 1, 512), (2, 4, 128, 512)))
    L["w1o"] = np.ascontiguousarray(
        inp["L1_attn_W_out"].reshape(16, 128, 4, 512).transpose(2, 1, 0, 3)).reshape(4, 128, 8192)
    L["b1o_b"] = np.ascontiguousarray(
        np.broadcast_to(inp["L1_attn_b_out"].reshape(4, 1, 512), (4, 128, 512)))
    L["g1a"] = np.ascontiguousarray(inp["L1_attn_gamma"].reshape(16, 128).T)
    L["be1"] = np.ascontiguousarray(inp["L1_attn_beta"].reshape(16, 128).T)
    L["clf"] = np.ascontiguousarray(
        inp["L1_ffn_classifier_w"].T.reshape(16, 128, 12).transpose(1, 0, 2)).reshape(128, 192)
    L["eb_b"] = np.ascontiguousarray(
        np.broadcast_to(inp["L1_ffn_e_score_correction_bias"].reshape(1, 12), (128, 12)))
    L["g1"] = np.ascontiguousarray(
        inp["L1_ffn_gate_w"].reshape(8, 8, 128, 16, 128).transpose(0, 4, 1, 3, 2)).reshape(8, 128, 8, 2048)
    L["u1"] = np.ascontiguousarray(
        inp["L1_ffn_up_w"].reshape(8, 8, 128, 16, 128).transpose(0, 4, 1, 3, 2)).reshape(8, 128, 8, 2048)
    L["d1"] = np.ascontiguousarray(
        inp["L1_ffn_down_w"].reshape(8, 2048, 8, 128).transpose(0, 3, 2, 1)).reshape(8, 128, 8, 2048)
    inv = (10000.0 ** (-(np.arange(0, 1024, 2, dtype=np.float64) / 1024.0)))
    pos = np.arange(4096, dtype=np.float64)
    fr = pos[:, None] * inv[None, :]
    cosr = np.repeat(np.cos(fr), 2, axis=1)
    sinr = np.repeat(np.sin(fr), 2, axis=1)
    sinr[:, 0::2] *= -1.0
    L["_cos_full"] = cosr.astype(np.float32)
    L["_sin_full"] = sinr.astype(np.float32)
    return L


def kernel(**inp):
    inp = {k: np.asarray(v) for k, v in inp.items()}
    L = _host_layouts(inp)
    x = inp["L0_attn_x"]
    Wq, bq = inp["L0_attn_Wq"], inp["L0_attn_bq"]
    Wk, bk = inp["L0_attn_Wk"], inp["L0_attn_bk"]
    Wv, bv = inp["L0_attn_Wv"], inp["L0_attn_bv"]
    shared_keys = ["g0", "u0", "d0", "gs0", "us0", "ds_b", "w1q", "w1kv", "b1q",
                   "b1kv_b", "w1o", "b1o_b", "g1a", "be1", "clf", "eb_b",
                   "g1", "u1", "d1"]
    in_maps = []
    for c in range(8):
        b, j = c // 4, c % 4
        m = {k: L[k] for k in shared_keys}
        m["x_own"] = np.ascontiguousarray(x[b, j * 1024:(j + 1) * 1024, :])
        m["wq_o"] = np.ascontiguousarray(
            Wq[4 * j:4 * j + 4].reshape(4, 16, 128, 128).transpose(2, 0, 1, 3)).reshape(128, 8192)
        m["bq_o"] = np.ascontiguousarray(bq[4 * j:4 * j + 4].T)
        m["wk_o"] = np.ascontiguousarray(
            Wk[j].reshape(16, 128, 128).transpose(1, 0, 2)).reshape(128, 2048)
        m["wv_o"] = np.ascontiguousarray(
            Wv[j].reshape(16, 128, 128).transpose(1, 0, 2)).reshape(128, 2048)
        m["bk_o"] = bk[j].reshape(128, 1).copy()
        m["bv_b"] = np.ascontiguousarray(np.broadcast_to(bv[j].reshape(1, 128), (128, 128)))
        tt = np.arange(32)[None, :]
        pp = np.arange(128)[:, None]
        m["oidx"] = ((tt // 8) * 2048 + ((tt % 4) * 128 + pp) * 4 + j).astype(np.int32)
        tsl = slice(j * 1024, (j + 1) * 1024)
        m["rcos"] = np.ascontiguousarray(
            L["_cos_full"][tsl].reshape(8, 128, 2, 512).transpose(2, 0, 1, 3))
        m["rsin"] = np.ascontiguousarray(
            L["_sin_full"][tsl].reshape(8, 128, 2, 512).transpose(2, 0, 1, 3))
        in_maps.append(m)
    nc, stack = build()
    res = run_bass_kernel_spmd(nc, in_maps, core_ids=list(range(8)))
    stack.close()
    out = np.empty((2, 4096, 2048), dtype=np.float32)
    for c in range(8):
        b, j = c // 4, c % 4
        out[b, j * 1024:(j + 1) * 1024, :] = res.results[c]["out_x"]
    return out


# version 13
# speedup vs baseline: 1.0936x; 1.0051x over previous
import contextlib
import numpy as np
import ml_dtypes
import concourse.bass as bass
import concourse.mybir as mybir
from concourse.bass_utils import run_bass_kernel_spmd

F32 = mybir.dt.float32
BF16 = mybir.dt.bfloat16
I32 = mybir.dt.int32
AF = mybir.ActivationFunctionType
ALU = mybir.AluOpType
AX = mybir.AxisListType
BF = ml_dtypes.bfloat16
RG = [[0, 1, 2, 3], [4, 5, 6, 7]]
ENGS = ["sync", "scalar", "vector", "tensor", "gpsimd"]
RSQRT_SCALE = float(1.0 / np.sqrt(128.0))


class P:

    def __init__(self, nc, stack):
        self.nc = nc
        self.sems = {e: stack.enter_context(nc.semaphore(f"S_{e}")) for e in ENGS}
        self.cnt = {e: 0 for e in ENGS}
        self.lastw = {}
        self.eng = {e: getattr(nc, e) for e in ENGS}

    def emit(self, ename, instr, dma=False, cc=False):
        n = 16 if dma else 1
        instr.then_inc(self.sems[ename], n)
        self.cnt[ename] += n
        return (ename, self.cnt[ename], dma or cc)

    def wait(self, consumer, *tokens):
        for t in tokens:
            if t is None:
                continue
            pe, v, _ = t
            key = (consumer, pe)
            if self.lastw.get(key, -1) >= v:
                continue
            self.eng[consumer].wait_ge(self.sems[pe], v)
            self.lastw[key] = v

    def tok(self, ename):
        return (ename, self.cnt[ename], False)


def build():
    nc = bass.Bass("TRN2", target_bir_lowering=False)

    def din(name, shape, dt=F32):
        return nc.dram_tensor(name, shape, dt, kind="ExternalInput").ap()

    x_own = din("x_own", [1024, 2048])
    wq_o = din("wq_o", [128, 8192])
    wk_o = din("wk_o", [128, 2048])
    wv_o = din("wv_o", [128, 2048])
    bq_o = din("bq_o", [128, 4])
    bk_o = din("bk_o", [128, 1])
    bv_b = din("bv_b", [128, 128])
    oidx = din("oidx", [128, 32], I32)
    g0 = din("g0", [44, 128, 2048])
    u0 = din("u0", [44, 128, 2048])
    d0 = din("d0", [2, 44, 128, 1024])
    gs0 = din("gs0", [128, 44])
    us0 = din("us0", [128, 44])
    ds_b = din("ds_b", [128, 2048])
    w1q = din("w1q", [16, 128, 2048])
    b1q = din("b1q", [128, 16])
    w1kv = din("w1kv", [2, 4, 128, 8192])
    b1kv_b = din("b1kv_b", [2, 4, 128, 512])
    w1o = din("w1o", [4, 128, 8192])
    b1o_b = din("b1o_b", [4, 128, 512])
    g1a_i = din("g1a", [128, 16])
    be1_i = din("be1", [128, 16])
    clf = din("clf", [128, 192])
    eb_b = din("eb_b", [128, 12])
    g1 = din("g1", [8, 128, 8, 2048])
    u1 = din("u1", [8, 128, 8, 2048])
    d1 = din("d1", [8, 128, 8, 2048])
    rcos = din("rcos", [2, 8, 128, 512])
    rsin = din("rsin", [2, 8, 128, 512])

    out_x = nc.dram_tensor("out_x", [1024, 2048], F32, kind="ExternalOutput").ap()

    ag_in = nc.dram_tensor("ag_in", [2048, 1024], BF16)
    ag_out = nc.dram_tensor("ag_out", [8192, 1024], BF16)
    rs_in = [nc.dram_tensor(f"rs_in{m}", [8192, 512], BF16) for m in range(2)]
    rs_out = [nc.dram_tensor(f"rs_out{m}", [2048, 512], BF16) for m in range(2)]
    arA_in = nc.dram_tensor("arA_in", [128, 16], F32)
    arA_out = nc.dram_tensor("arA_out", [128, 16], F32)
    ppd = nc.dram_tensor("ppd", [8, 128, 2048], BF16)

    stack = contextlib.ExitStack()
    p = P(nc, stack)
    E, W = p.emit, p.wait
    SY, SC, VE, TE, GP = nc.sync, nc.scalar, nc.vector, nc.tensor, nc.gpsimd

    def sb(name, shape, dt=F32, st=stack):
        return st.enter_context(nc.sbuf_tensor(name, shape, dt))

    def ps(name, shape, dt=F32, st=stack):
        return st.enter_context(nc.psum_tensor(name, shape, dt))

    def fence(q, engs=("tensor", "vector", "scalar")):
        W(q, *[p.tok(e) for e in engs])

    x_sb = sb("x_sb", [128, 8 * 2048])
    fmA = sb("fmA", [128, 16384], BF16)
    masks = sb("masks", [128, 4 * 512])
    ident = sb("ident", [128, 128], BF16)
    bqs = sb("bqs", [128, 4]); bks = sb("bks", [128, 1])
    bvbc = sb("bvbc", [128, 128]); oix = sb("oix", [128, 32], I32)
    gss = sb("gss", [128, 44]); uss = sb("uss", [128, 44])
    b1qs = sb("b1qs", [128, 16])
    g1as = sb("g1as", [128, 16]); be1s = sb("be1s", [128, 16])
    clfs = sb("clfs", [128, 192], BF16); ebbc = sb("ebbc", [128, 12])
    stats_m = sb("stats_m", [128, 8]); stats_r = sb("stats_r", [128, 8])
    stats_v = sb("stats_v", [128, 8])
    s2c = sb("s2c", [128, 8]); o2c = sb("o2c", [128, 8])
    w_all = sb("w_all", [128, 96])
    A1 = sb("A1", [128, 16]); A_raw = sb("A_raw", [128, 16])
    ssK = sb("ssK", [128, 8]); ssV = sb("ssV", [128, 8])
    scol = sb("scol", [128, 2], BF16)
    tmp1 = sb("tmp1", [128, 1]); tmp2 = sb("tmp2", [128, 1]); tmp3 = sb("tmp3", [128, 1])
    wks = sb("wks", [128, 2048], BF16)
    wvs = sb("wvs", [128, 2048], BF16)
    st_i = contextlib.ExitStack()
    iota = sb("iota", [128, 512], I32, st=st_i)

    t_x = [E("sync", SY.dma_start(out=x_sb[:, ti * 2048:(ti + 1) * 2048],
                                  in_=x_own[ti * 128:(ti + 1) * 128, :]), dma=True)
           for ti in range(8)]
    smalls = [(bqs, bq_o), (bks, bk_o), (bvbc, bv_b), (gss, gs0), (uss, us0),
              (b1qs, b1q), (g1as, g1a_i), (be1s, be1_i), (ebbc, eb_b)]
    t_small = None
    for buf, src in smalls:
        t_small = E("sync", SY.dma_start(out=buf[:], in_=src), dma=True)
    t_oix = E("sync", SY.dma_start(out=oix[:], in_=oidx), dma=True)
    t_clf = E("gpsimd", GP.dma_start(out=clfs[:], in_=clf), dma=True)
    t_io = E("gpsimd", GP.iota(out=iota[:], pattern=[[-1, 512]], base=0,
                               channel_multiplier=1))
    W("vector", t_io)
    t_id = E("vector", VE.tensor_scalar(out=ident[:], in0=iota[:, 0:128], scalar1=0.0,
                                        scalar2=None, op0=ALU.is_equal))
    t_mk = []
    for jm in range(4):
        t_mk.append(E("vector", VE.tensor_scalar(
            out=masks[:, jm * 512:(jm + 1) * 512], in0=iota[:], scalar1=float(-128 * jm),
            scalar2=-1e30, op0=ALU.is_gt, op1=ALU.mult)))
    st_i.close()

    def ln_stats(ti):
        xt = x_sb[:, ti * 2048:(ti + 1) * 2048]
        t_sum = E("vector", VE.reduce_sum(out=tmp1[:], in_=xt, axis=AX.X))
        t_ssq = E("scalar", SC.activation(out=sq_scratch[0][:], in_=xt, func=AF.Square,
                                          accum_out=tmp2[:]))
        W("vector", t_sum)
        t_m = E("vector", VE.tensor_scalar(out=stats_m[:, ti:ti + 1], in0=tmp1[:],
                                           scalar1=1.0 / 2048.0, scalar2=None, op0=ALU.mult))
        W("vector", t_m, t_ssq)
        t_a = E("vector", VE.tensor_tensor(out=tmp1[:], in0=stats_m[:, ti:ti + 1],
                                           in1=stats_m[:, ti:ti + 1], op=ALU.mult))
        t_b = E("vector", VE.tensor_scalar(out=tmp3[:], in0=tmp2[:],
                                           scalar1=1.0 / 2048.0, scalar2=None, op0=ALU.mult))
        W("vector", t_a, t_b)
        t_v = E("vector", VE.tensor_tensor(out=stats_v[:, ti:ti + 1], in0=tmp3[:],
                                           in1=tmp1[:], op=ALU.subtract))
        W("vector", t_v)
        t_e = E("vector", VE.tensor_scalar(out=tmp1[:], in0=stats_v[:, ti:ti + 1],
                                           scalar1=1e-5, scalar2=None, op0=ALU.add))
        W("scalar", t_e)
        t_s = E("scalar", SC.activation(out=tmp2[:], in_=tmp1[:], func=AF.Sqrt))
        W("vector", t_s)
        t_r = E("vector", VE.reciprocal(stats_r[:, ti:ti + 1], tmp2[:]))
        return t_r, t_ssq

    def build_fmT(dest_fn, stage_buf, t_stage, tr_pss):
        toks = []
        t_tr = None
        for ec in range(16):
            b = ec % len(tr_pss)
            tp, t_ev_prev = tr_pss[b]
            W("tensor", t_stage, t_ev_prev, t_id)
            t_tr = E("tensor", TE.transpose(tp[:], stage_buf[:, ec * 128:(ec + 1) * 128],
                                            ident[:]))
            t_ev = dest_fn(tp, ec, t_tr)
            tr_pss[b] = (tp, t_ev)
            toks.append(t_ev)
        return toks, t_tr

    stA = contextlib.ExitStack()
    xn_stage = [sb(f"xn_st{i}", [128, 2048], BF16, st=stA) for i in range(2)]
    sq_scratch = [sb("sq_scr", [128, 2048], BF16, st=stA)]
    zbuf = sb("zbuf", [128, 2048], BF16, st=stA)
    tr_ps_l = [(ps(f"trp{i}", [128, 128], BF16, st=stA), None) for i in range(4)]
    t_zb = E("gpsimd", GP.memset(zbuf[:], 0.0))
    t_zf = None
    for m in range(2):
        for k in range(16):
            t_zf = E("gpsimd", GP.dma_start(
                out=rs_in[m].ap()[k * 512:(k + 1) * 512, :].rearrange(
                    "(a p) m -> p a m", p=128),
                in_=zbuf[:].rearrange("p (a m) -> p a m", a=4)), dma=True)
    t_fmA_done = []
    stage_rd = [None, None]
    for ti in range(8):
        W("vector", t_x[ti])
        W("scalar", t_x[ti])
        t_r, _ = ln_stats(ti)
        sbuf_st = xn_stage[ti % 2]
        W("vector", t_r, stage_rd[ti % 2])
        t_st = E("vector", VE.tensor_scalar(out=sbuf_st[:],
                                            in0=x_sb[:, ti * 2048:(ti + 1) * 2048],
                                            scalar1=stats_m[:, ti:ti + 1],
                                            scalar2=stats_r[:, ti:ti + 1],
                                            op0=ALU.subtract, op1=ALU.mult))

        def evacA(tp, ec, t_tr, ti=ti):
            W("scalar", t_tr)
            return E("scalar", SC.copy(
                out=fmA[:, ec * 1024 + ti * 128: ec * 1024 + ti * 128 + 128], in_=tp[:]))
        toks, t_trl = build_fmT(evacA, sbuf_st, t_st, tr_ps_l)
        t_fmA_done += toks
        stage_rd[ti % 2] = t_trl
    W("sync", *t_fmA_done)
    t_agst = []
    for ec in range(16):
        t_agst.append(E("sync", SY.dma_start(
            out=ag_in.ap()[ec * 128:(ec + 1) * 128, :],
            in_=fmA[:, ec * 1024:(ec + 1) * 1024]), dma=True))
    stA.close()
    agc = [None]

    stF = contextlib.ExitStack()
    H = sb("H", [128, 44 * 512], BF16, st=stF)
    gbuf = [sb(f"gb{i}", [128, 2048], BF16, st=stF) for i in range(2)]
    ubuf = [sb(f"ub{i}", [128, 2048], BF16, st=stF) for i in range(2)]
    sgb = [sb(f"sg{i}", [128, 512], F32, st=stF) for i in range(2)]
    dbuf = [sb(f"db{i}", [128, 4096], BF16, st=stF) for i in range(2)]
    dtmp = [sb("dt0", [128, 1024], F32, st=stF)] * 2
    ds_bc = sb("ds_bc", [128, 2048], st=stF)
    fence("gpsimd")
    W("gpsimd", t_zf)
    fence("sync")
    t_ds = E("sync", SY.dma_start(out=ds_bc[:], in_=ds_b), dma=True)
    W("tensor", p.tok("scalar"))

    gtok = [None, None]
    utok = [None, None]
    last_sw = [None]

    def ffn_half(th, kick_ag=False):
        stG = contextlib.ExitStack()
        psG = [ps(f"psG{th}{i}", [128, 512], st=stG) for i in range(2)]
        psU = [ps(f"psU{th}{i}", [128, 512], st=stG) for i in range(2)]
        hvtok = [None, None]
        sgtok = [None, None]
        gmt = [None, None]
        umt = [None, None]
        for ic in range(44):
            b = ic % 2
            W("gpsimd", agc[0], gtok[b])
            t_g = E("gpsimd", GP.dma_start(out=gbuf[b][:], in_=g0[ic]), dma=True)
            W("gpsimd", utok[b])
            t_u = E("gpsimd", GP.dma_start(out=ubuf[b][:], in_=u0[ic]), dma=True)
            last_sw[0] = t_u
            W("tensor", t_g, sgtok[b])
            tG = None
            for ec in range(16):
                tG = E("tensor", TE.matmul(psG[b][:], gbuf[b][:, ec * 128:(ec + 1) * 128],
                                           fmA[:, ec * 1024 + th * 512: ec * 1024 + th * 512 + 512],
                                           start=(ec == 0), stop=(ec == 15)))
            gtok[b] = tG
            gmt[b] = tG
            W("scalar", tG, t_small, hvtok[b])
            t_sg = E("scalar", SC.activation(out=sgb[b][:], in_=psG[b][:], func=AF.Silu,
                                             scale=gss[:, ic:ic + 1]))
            sgtok[b] = t_sg
            W("tensor", t_u, hvtok[b])
            tU = None
            for ec in range(16):
                tU = E("tensor", TE.matmul(psU[b][:], ubuf[b][:, ec * 128:(ec + 1) * 128],
                                           fmA[:, ec * 1024 + th * 512: ec * 1024 + th * 512 + 512],
                                           start=(ec == 0), stop=(ec == 15)))
            utok[b] = tU
            W("vector", tU, t_sg)
            hvtok[b] = E("vector", VE.scalar_tensor_tensor(
                out=H[:, ic * 512:(ic + 1) * 512], in0=psU[b][:], scalar=uss[:, ic:ic + 1],
                in1=sgb[b][:], op0=ALU.mult, op1=ALU.mult))
        t_hv = p.tok("vector")
        if kick_ag:
            W("gpsimd", *t_agst)
            W("gpsimd", t_clf, t_zf, last_sw[0])
            agc[0] = E("gpsimd", GP.collective_compute(
                "AllGather", ALU.bypass, replica_groups=RG,
                ins=[ag_in.ap()], outs=[ag_out.ap()]), cc=True)
        W("tensor", p.tok("scalar"), t_hv)
        stG.close()
        stD = contextlib.ExitStack()
        psD = [ps(f"psD{th}{i}", [128, 1024], st=stD) for i in range(4)]
        dtok = [None, None]
        evtok = [None] * 4
        for half in range(2):
            for icb in range(11):
                b = icb % 2
                W("gpsimd", agc[0], dtok[b])
                t_d = E("gpsimd", GP.dma_start(
                    out=dbuf[b][:].rearrange("p (a m) -> p a m", a=4),
                    in_=d0[half, icb * 4: icb * 4 + 4].rearrange("a p m -> p a m")),
                    dma=True)
                W("tensor", t_d)
                if icb == 0:
                    W("tensor", *[t for t in evtok if t])
                tmm = None
                for ici in range(4):
                    ic = icb * 4 + ici
                    for t4 in range(4):
                        for h2 in range(2):
                            tmm = E("tensor", TE.matmul(
                                psD[t4][:, h2 * 512:(h2 + 1) * 512],
                                H[:, ic * 512 + t4 * 128: ic * 512 + t4 * 128 + 128],
                                dbuf[b][:, ici * 1024 + h2 * 512: ici * 1024 + h2 * 512 + 512],
                                start=(ic == 0), stop=(ic == 43)))
                dtok[b] = tmm
            t_last_mm = p.tok("tensor")
            for t4 in range(4):
                db = t4 % 2
                xsl = x_sb[:, (th * 4 + t4) * 2048 + half * 1024:
                           (th * 4 + t4) * 2048 + half * 1024 + 1024]
                W("vector", t_last_mm, t_ds, evtok[t4])
                t_e1 = E("vector", VE.tensor_tensor(out=dtmp[db][:], in0=psD[t4][:],
                                                    in1=ds_bc[:, half * 1024:half * 1024 + 1024],
                                                    op=ALU.mult))
                W("vector", t_e1)
                evtok[t4] = E("vector", VE.tensor_tensor(out=xsl, in0=xsl, in1=dtmp[db][:],
                                                         op=ALU.add))
        W("tensor", p.tok("vector"))
        stD.close()

    W("sync", t_zf)
    W("scalar", p.tok("tensor"), p.tok("vector"), t_zf)
    ffn_half(0, kick_ag=True)
    ffn_half(1)
    t_wk = E("gpsimd", GP.dma_start(out=wks[:], in_=wk_o), dma=True)
    t_wv = E("gpsimd", GP.dma_start(out=wvs[:], in_=wv_o), dma=True)
    stF.close()

    stB = contextlib.ExitStack()
    fmB = sb("fmB", [128, 16384], BF16, st=stB)
    stAT0 = contextlib.ExitStack()
    KT = sb("KT", [128, 4096], BF16, st=stAT0)
    V_aug = sb("V_aug", [128, 32 * 129], BF16, st=stAT0)
    fence("gpsimd")
    t_vone = E("gpsimd", GP.memset(V_aug[:], 1.0))

    stAT1 = contextlib.ExitStack()
    wqs = sb("wqs", [128, 8192], BF16, st=stAT1)
    xt_buf = [sb(f"xtb{i}", [128, 4096], BF16, st=stAT1) for i in range(2)]
    t_wq = E("gpsimd", GP.dma_start(out=wqs[:], in_=wq_o), dma=True)
    fence("sync")
    stQ = contextlib.ExitStack()
    psK = ps("psK", [128, 256], st=stQ)
    psQ = [ps(f"psQ{i}", [128, 256], st=stQ) for i in range(2)]
    psV = [ps(f"psV{i}", [128, 128], st=stQ) for i in range(2)]
    ag3 = ag_out.ap().rearrange("(a p) m -> a p m", p=128)
    xttok = [None, None]
    ktok = None
    qtok = [None, None]
    vtok = [None, None]
    for tb2 in range(16):
        r, s = tb2 // 4, tb2 % 4
        b = tb2 % 2
        W("sync", agc[0], xttok[b])
        t_xt = E("sync", SY.dma_start(
            out=xt_buf[b][:].rearrange("p (a m) -> p a m", a=16),
            in_=ag3[r * 16:(r + 1) * 16, :, s * 256:(s + 1) * 256].rearrange("a p m -> p a m")),
            dma=True)
        W("tensor", t_xt, t_wk, ktok)
        tK = None
        for ec in range(16):
            tK = E("tensor", TE.matmul(psK[:], wks[:, ec * 128:(ec + 1) * 128],
                                       xt_buf[b][:, ec * 256:(ec + 1) * 256],
                                       start=(ec == 0), stop=(ec == 15)))
        W("scalar", tK, t_small)
        ktok = E("scalar", SC.activation(out=KT[:, tb2 * 256:(tb2 + 1) * 256], in_=psK[:],
                                         func=AF.Identity, bias=bks[:],
                                         scale=np.float32(1.0)))
        for hl in range(4):
            qb = hl % 2
            W("tensor", t_wq, qtok[qb])
            tQ = None
            for ec in range(16):
                tQ = E("tensor", TE.matmul(psQ[qb][:],
                                           wqs[:, hl * 2048 + ec * 128: hl * 2048 + ec * 128 + 128],
                                           xt_buf[b][:, ec * 256:(ec + 1) * 256],
                                           start=(ec == 0), stop=(ec == 15)))
            W("scalar", tQ)
            qtok[qb] = E("scalar", SC.activation(
                out=fmB[:, hl * 4096 + tb2 * 256: hl * 4096 + tb2 * 256 + 256], in_=psQ[qb][:],
                func=AF.Identity, bias=bqs[:, hl:hl + 1], scale=np.float32(1.0)))
        for t4 in range(2):
            vb = t4 % 2
            W("tensor", t_wv, vtok[vb])
            tV = None
            for ec in range(16):
                tV = E("tensor", TE.matmul(psV[vb][:],
                                           xt_buf[b][:, ec * 256 + t4 * 128: ec * 256 + t4 * 128 + 128],
                                           wvs[:, ec * 128:(ec + 1) * 128],
                                           start=(ec == 0), stop=(ec == 15)))
            kt = tb2 * 2 + t4
            W("vector", tV, t_small, t_vone)
            vtok[vb] = E("vector", VE.tensor_tensor(
                out=V_aug[:, kt * 129: kt * 129 + 128], in0=psV[vb][:], in1=bvbc[:],
                op=ALU.add))
        xttok[b] = p.tok("tensor")
    W("tensor", p.tok("scalar"), p.tok("vector"))
    stQ.close()
    stAT1.close()

    stS = contextlib.ExitStack()
    eTb = [sb(f"eT{i}", [128, 512], BF16, st=stS) for i in range(2)]
    O_q = [sb(f"Oq{i}", [128, 2048], BF16, st=stS) for i in range(2)]
    rzc = [sb(f"rz{i}", [128, 1], F32, st=stS) for i in range(4)]
    ao_wide = sb("ao_wide", [128, 8192], BF16, st=stS)
    xn_st2 = [sb(f"xn2st{i}", [128, 2048], BF16, st=stS) for i in range(2)]
    sq_scratch = [sb("sq_scr2", [128, 2048], BF16, st=stS)]
    tr_ps2 = [(ps(f"trq{i}", [128, 128], BF16, st=stS), None) for i in range(2)]
    stage_rd2 = [None, None]

    def phase5_tile(ti, t_xadd):
        W("vector", t_xadd)
        W("scalar", t_xadd)
        t_r, t_ssq = ln_stats(ti)
        W("vector", t_r)
        t_a = E("vector", VE.tensor_tensor(out=tmp1[:], in0=stats_r[:, ti:ti + 1],
                                           in1=stats_r[:, ti:ti + 1], op=ALU.mult))
        W("vector", t_a)
        t_b = E("vector", VE.tensor_tensor(out=tmp2[:], in0=stats_v[:, ti:ti + 1],
                                           in1=tmp1[:], op=ALU.mult))
        W("vector", t_b)
        t_c = E("vector", VE.tensor_scalar(out=tmp1[:], in0=tmp2[:], scalar1=1e-5,
                                           scalar2=None, op0=ALU.add))
        W("scalar", t_c)
        t_s = E("scalar", SC.activation(out=tmp2[:], in_=tmp1[:], func=AF.Sqrt))
        W("vector", t_s)
        t_r2 = E("vector", VE.reciprocal(tmp3[:], tmp2[:]))
        W("vector", t_r2)
        t_s2 = E("vector", VE.tensor_tensor(out=s2c[:, ti:ti + 1], in0=stats_r[:, ti:ti + 1],
                                            in1=tmp3[:], op=ALU.mult))
        W("vector", t_s2)
        t_o2a = E("vector", VE.tensor_tensor(out=tmp1[:], in0=stats_m[:, ti:ti + 1],
                                             in1=s2c[:, ti:ti + 1], op=ALU.mult))
        W("vector", t_o2a)
        t_o2 = E("vector", VE.tensor_scalar(out=o2c[:, ti:ti + 1], in0=tmp1[:],
                                            scalar1=-1.0, scalar2=None, op0=ALU.mult))
        W("vector", stage_rd2[0])
        t_st = E("vector", VE.tensor_scalar(out=xn_st2[0][:],
                                            in0=x_sb[:, ti * 2048:(ti + 1) * 2048],
                                            scalar1=stats_m[:, ti:ti + 1],
                                            scalar2=stats_r[:, ti:ti + 1],
                                            op0=ALU.subtract, op1=ALU.mult))

        def evacA1(tp, ec, t_tr, ti=ti):
            W("scalar", t_tr)
            return E("scalar", SC.copy(
                out=fmA[:, ec * 1024 + ti * 128: ec * 1024 + ti * 128 + 128], in_=tp[:]))
        _, t_trl = build_fmT(evacA1, xn_st2[0], t_st, tr_ps2)
        stage_rd2[0] = t_trl
        W("vector", t_o2, stage_rd2[1])
        t_st2 = E("vector", VE.tensor_scalar(out=xn_st2[1][:],
                                             in0=x_sb[:, ti * 2048:(ti + 1) * 2048],
                                             scalar1=s2c[:, ti:ti + 1],
                                             scalar2=o2c[:, ti:ti + 1],
                                             op0=ALU.mult, op1=ALU.add))

        def evacB(tp, ec, t_tr, ti=ti):
            W("vector", t_tr, t_small)
            return E("vector", VE.tensor_scalar(
                out=fmB[:, ec * 1024 + ti * 128: ec * 1024 + ti * 128 + 128], in0=tp[:],
                scalar1=g1as[:, ec:ec + 1], scalar2=be1s[:, ec:ec + 1],
                op0=ALU.mult, op1=ALU.add))
        _, t_trl2 = build_fmT(evacB, xn_st2[1], t_st2, tr_ps2)
        stage_rd2[1] = t_trl2
    psS = [ps(f"psS{i}", [128, 512], st=stS) for i in range(2)]
    psAV = [ps(f"psAV{i}", [128, 129], st=stS) for i in range(4)]
    setok = [None, None]
    avtok = [None] * 4
    ettok = [None, None]
    sc_tok = [None, None]
    t_scat = []
    rs6 = [rs_out[m].ap().rearrange("(t f) c -> f t c", f=4) for m in range(2)]
    t_aol = [None] * 16
    t_rs0 = None
    for qc in range(8):
        ob = qc % 2
        ev_toks = []
        for hl in range(4):
            nkc = 4 * qc + 4
            for kc in range(nkc):
                b = kc % 2
                W("tensor", setok[b])
                if kc == 0:
                    W("tensor", *[t for t in avtok if t])
                t_s = E("tensor", TE.matmul(psS[b][:], KT[:, kc * 128:(kc + 1) * 128],
                                            fmB[:, hl * 4096 + qc * 512: hl * 4096 + qc * 512 + 512],
                                            start=True, stop=True))
                if kc >= 4 * qc:
                    jm = kc - 4 * qc
                    W("vector", t_s, t_mk[jm])
                    t_s = E("vector", VE.tensor_tensor(
                        out=psS[b][:], in0=psS[b][:],
                        in1=masks[:, jm * 512:(jm + 1) * 512], op=ALU.add))
                W("scalar", t_s, ettok[b])
                t_e = E("scalar", SC.activation(out=eTb[b][:], in_=psS[b][:], func=AF.Exp,
                                                scale=np.float32(RSQRT_SCALE)))
                setok[b] = t_e
                tAV = None
                W("tensor", t_e)
                for qt in range(4):
                    tAV = E("tensor", TE.matmul(psAV[qt][:], eTb[b][:, qt * 128:(qt + 1) * 128],
                                                V_aug[:, kc * 129: kc * 129 + 129],
                                                start=(kc == 0), stop=(kc == nkc - 1)))
                ettok[b] = tAV
            t_fin = p.tok("tensor")
            for qt in range(4):
                W("vector", t_fin, sc_tok[ob] if hl == 0 else None)
                t_rz = E("vector", VE.reciprocal(rzc[qt][:], psAV[qt][:, 128:129]))
                W("vector", t_rz)
                avtok[qt] = E("vector", VE.tensor_scalar(
                    out=O_q[ob][:, qt * 512 + hl * 128: qt * 512 + hl * 128 + 128],
                    in0=psAV[qt][:, 0:128], scalar1=rzc[qt][:], scalar2=None, op0=ALU.mult))
                ev_toks.append(avtok[qt])
        m2 = qc % 2
        W("gpsimd", t_oix, t_zf, *ev_toks)
        if qc == 7:
            W("gpsimd", t_rs0)
        last_sc = None
        for qt in range(4):
            tt_idx = qc * 4 + qt
            last_sc = E("gpsimd", GP.indirect_dma_start(
                out=rs_in[m2].ap(),
                out_offset=bass.IndirectOffsetOnAxis(ap=oix[:, tt_idx:tt_idx + 1], axis=0),
                in_=O_q[ob][:, qt * 512:(qt + 1) * 512], in_offset=None), dma=True)
            t_scat.append(last_sc)
        sc_tok[ob] = last_sc
        if qc == 6:
            W("gpsimd", *t_scat)
            t_rs0 = E("gpsimd", GP.collective_compute(
                "ReduceScatter", ALU.add, replica_groups=RG,
                ins=[rs_in[0].ap()], outs=[rs_out[0].ap()]), cc=True)
            W("sync", t_rs0)
            for sl in range(16):
                ti_, j_ = sl // 4, sl % 4
                t_aol[sl] = E("sync", SY.dma_start(
                    out=ao_wide[:, sl * 512:(sl + 1) * 512],
                    in_=rs6[0][j_, (ti_ % 4) * 128:(ti_ % 4) * 128 + 128, :]), dma=True)
    W("gpsimd", *t_scat)
    W("gpsimd", t_aol[15])
    t_rs1 = E("gpsimd", GP.collective_compute(
        "ReduceScatter", ALU.add, replica_groups=RG,
        ins=[rs_in[1].ap()], outs=[rs_out[1].ap()]), cc=True)
    ao_add = [None] * 16
    for sl in range(16):
        ti_, j_ = sl // 4, sl % 4
        xsl = x_sb[:, ti_ * 2048 + j_ * 512: ti_ * 2048 + j_ * 512 + 512]
        W("vector", t_aol[sl])
        ao_add[sl] = E("vector", VE.tensor_tensor(
            out=xsl, in0=xsl, in1=ao_wide[:, sl * 512:(sl + 1) * 512], op=ALU.add))
    W("sync", t_rs1)
    for sl in range(16):
        ti_, j_ = 4 + sl // 4, sl % 4
        W("sync", ao_add[sl])
        t_aol[sl] = E("sync", SY.dma_start(
            out=ao_wide[:, sl * 512:(sl + 1) * 512],
            in_=rs6[1][j_, (ti_ % 4) * 128:(ti_ % 4) * 128 + 128, :]), dma=True)
    for ti in range(4):
        phase5_tile(ti, ao_add[ti * 4 + 3])
    for ti in range(4, 8):
        for j_ in range(4):
            sl = (ti - 4) * 4 + j_
            xsl = x_sb[:, ti * 2048 + j_ * 512: ti * 2048 + j_ * 512 + 512]
            W("vector", t_aol[sl])
            ao_add[sl] = E("vector", VE.tensor_tensor(
                out=xsl, in0=xsl, in1=ao_wide[:, sl * 512:(sl + 1) * 512], op=ALU.add))
        phase5_tile(ti, ao_add[(ti - 4) * 4 + 3])
    W("tensor", p.tok("scalar"), p.tok("vector"))
    stS.close()
    stAT0.close()


    st6 = contextlib.ExitStack()
    e_r = sb("e_r", [128, 12], st=st6)
    probs = sb("probs", [128, 12], st=st6)
    biased = sb("biased", [128, 12], st=st6)
    mk1 = sb("mk1", [128, 12], st=st6)
    b2v = sb("b2v", [128, 12], st=st6)
    psL = ps("psL", [128, 12], st=st6)
    t_ltok = None
    for ti in range(8):
        W("tensor", t_clf, t_ltok)
        tL = None
        for ec in range(16):
            tL = E("tensor", TE.matmul(psL[:], fmA[:, ec * 1024 + ti * 128: ec * 1024 + ti * 128 + 128],
                                       clfs[:, ec * 12:(ec + 1) * 12],
                                       start=(ec == 0), stop=(ec == 15)))
        W("vector", tL)
        t1 = E("vector", VE.reduce_max(out=tmp1[:], in_=psL[:], axis=AX.X, negate=True))
        W("scalar", t1)
        t2 = E("scalar", SC.activation(out=e_r[:], in_=psL[:], func=AF.Exp, bias=tmp1[:],
                                       scale=np.float32(1.0), accum_out=tmp2[:]))
        t_ltok = t2
        W("vector", t2)
        t3 = E("vector", VE.reciprocal(tmp3[:], tmp2[:]))
        W("vector", t3)
        t4 = E("vector", VE.tensor_scalar(out=probs[:], in0=e_r[:], scalar1=tmp3[:],
                                          scalar2=None, op0=ALU.mult))
        W("vector", t4, t_small)
        t5 = E("vector", VE.tensor_tensor(out=biased[:], in0=probs[:], in1=ebbc[:], op=ALU.add))
        W("vector", t5)
        t6 = E("vector", VE.reduce_max(out=tmp1[:], in_=biased[:], axis=AX.X))
        W("vector", t6)
        t7 = E("vector", VE.tensor_scalar(out=mk1[:], in0=biased[:], scalar1=tmp1[:],
                                          scalar2=None, op0=ALU.is_ge))
        W("vector", t7)
        t8 = E("vector", VE.scalar_tensor_tensor(out=b2v[:], in0=mk1[:], scalar=-1e30,
                                                 in1=biased[:], op0=ALU.mult, op1=ALU.add))
        W("vector", t8)
        t9 = E("vector", VE.reduce_max(out=tmp1[:], in_=b2v[:], axis=AX.X))
        W("vector", t9)
        t10 = E("vector", VE.tensor_scalar(out=b2v[:], in0=b2v[:], scalar1=tmp1[:],
                                           scalar2=None, op0=ALU.is_ge))
        W("vector", t10)
        t11 = E("vector", VE.tensor_tensor(out=mk1[:], in0=mk1[:], in1=b2v[:], op=ALU.add))
        W("vector", t11)
        t12 = E("vector", VE.scalar_tensor_tensor(out=w_all[:, ti * 12:(ti + 1) * 12],
                                                  in0=probs[:], scalar=1.5, in1=mk1[:],
                                                  op0=ALU.mult, op1=ALU.mult))
        W("vector", t12)
        t13 = E("vector", VE.reduce_sum(out=tmp1[:], in_=w_all[:, ti * 12 + 8: ti * 12 + 12],
                                        axis=AX.X))
        W("vector", t13)
        t14 = E("vector", VE.tensor_scalar(out=tmp2[:], in0=tmp1[:], scalar1=1.0,
                                           scalar2=None, op0=ALU.add))
        W("vector", t14)
        t15 = E("vector", VE.tensor_tensor(out=tmp1[:], in0=tmp2[:], in1=stats_r[:, ti:ti + 1],
                                           op=ALU.mult))
        W("vector", t15)
        t16 = E("vector", VE.tensor_scalar(out=tmp2[:], in0=tmp1[:], scalar1=1.0,
                                           scalar2=None, op0=ALU.add))
        t17 = E("vector", VE.tensor_tensor(out=tmp3[:], in0=stats_m[:, ti:ti + 1],
                                           in1=tmp1[:], op=ALU.mult))
        W("vector", t16, t17)
        xt = x_sb[:, ti * 2048:(ti + 1) * 2048]
        E("vector", VE.tensor_scalar(out=xt, in0=xt, scalar1=tmp2[:], scalar2=tmp3[:],
                                     op0=ALU.mult, op1=ALU.subtract))
    st6.close()

    st7 = contextlib.ExitStack()
    kvw = [[sb(f"kvw{s}{i}", [128, 4096], BF16, st=st7) for i in range(2)]
           for s in range(2)]
    cosb = [sb(f"cosb{i}", [128, 512], F32, st=st7) for i in range(2)]
    sinb = [sb(f"sinb{i}", [128, 512], F32, st=st7) for i in range(2)]
    ktmp = [sb(f"ktmp{i}", [128, 512], F32, st=st7) for i in range(2)]
    c1b = sb("c1b", [128, 512], F32, st=st7)
    c2b = sb("c2b", [128, 512], F32, st=st7)
    bias_kv = sb("bias_kv", [128, 512], F32, st=st7)
    Vst = sb("Vst", [128, 4096], BF16, st=st7)
    Kb = sb("Kb", [128, 512], BF16, st=st7)
    ppst = [sb(f"ppst{i}", [128, 512], BF16, st=st7) for i in range(2)]
    ssp = [sb(f"ssp{i}", [128, 1], F32, st=st7) for i in range(2)]
    psKV = [ps(f"psKV{i}", [128, 512], st=st7) for i in range(2)]
    kvwtok = [[None, None], [None, None]]
    kvwrd = [None, None]
    kvpass = 0
    ktmptok = [None, None]
    ktmprd = [[], []]
    ssptok = [None, None]
    tabtok = [None, None]
    tabrd = [None, None]
    c1tok = None
    c2tok = None
    kbtok = None
    vsttok = [None] * 8
    ppdma = [None, None]
    t_biaskv = None
    fence("gpsimd")
    W("gpsimd", t_rs1)
    fence("sync")
    W("tensor", p.tok("scalar"), p.tok("vector"))
    for ds in range(4):
        for kv in (1, 0):
            W("sync", t_biaskv)
            t_biaskv_l = E("sync", SY.dma_start(out=bias_kv[:], in_=b1kv_b[kv, ds]), dma=True)
            sset = kvpass % 2
            kvpass += 1
            for half in range(2):
                W("gpsimd", kvwtok[sset][half], kvwrd[sset])
                kvwtok[sset][half] = E("gpsimd", GP.dma_start(
                    out=kvw[sset][half][:],
                    in_=w1kv[kv, ds][:, half * 4096: half * 4096 + 4096]),
                    dma=True)
            for ti in range(8):
                b = ti % 2
                if ds < 2:
                    W("sync", tabtok[b], tabrd[b])
                    tabtok[b] = E("sync", SY.dma_start(out=cosb[b][:], in_=rcos[ds, ti]), dma=True)
                    tabtok[b] = E("sync", SY.dma_start(out=sinb[b][:], in_=rsin[ds, ti]), dma=True)
                W("tensor", kvwtok[sset][0], kvwtok[sset][1], ktmptok[b])
                tkv = None
                for ec in range(16):
                    buf = kvw[sset][ec // 8]
                    tkv = E("tensor", TE.matmul(
                        psKV[b][:], fmB[:, ec * 1024 + ti * 128: ec * 1024 + ti * 128 + 128],
                        buf[:, (ec % 8) * 512: (ec % 8) * 512 + 512],
                        start=(ec == 0), stop=(ec == 15)))
                kvwrd[sset] = tkv
                W("vector", tkv, t_biaskv_l, *ktmprd[b])
                ktmprd[b] = []
                t_kt = E("vector", VE.tensor_tensor(out=ktmp[b][:], in0=psKV[b][:],
                                                    in1=bias_kv[:], op=ALU.add))
                ktmptok[b] = t_kt
                t_biaskv = t_kt
                sb_ = b
                W("scalar", t_kt, ssptok[sb_], c1tok)
                t_sq = E("scalar", SC.activation(out=c1b[:], in_=ktmp[b][:], func=AF.Square,
                                                 accum_out=ssp[sb_][:]))
                ktmprd[b].append(t_sq)
                ss_dst = ssK if kv == 0 else ssV
                W("vector", t_sq)
                if ds == 0:
                    tss = E("vector", VE.tensor_copy(ss_dst[:, ti:ti + 1], ssp[sb_][:]))
                else:
                    tss = E("vector", VE.tensor_tensor(out=ss_dst[:, ti:ti + 1],
                                                       in0=ss_dst[:, ti:ti + 1],
                                                       in1=ssp[sb_][:], op=ALU.add))
                ssptok[sb_] = tss
                if ds < 2:
                    W("vector", t_kt, tabtok[b], c2tok, t_sq)
                    ts1 = E("vector", VE.tensor_copy(c2b[:, 0:512:2], ktmp[b][:, 1:512:2]))
                    ts2 = E("vector", VE.tensor_copy(c2b[:, 1:512:2], ktmp[b][:, 0:512:2]))
                    tc1 = E("vector", VE.tensor_tensor(out=c1b[:], in0=ktmp[b][:],
                                                       in1=cosb[b][:], op=ALU.mult))
                    c1tok = tc1
                    ktmprd[b].append(tc1)
                    tabrd[b] = tc1
                    W("vector", ts1, ts2)
                    tc2 = E("vector", VE.tensor_tensor(out=c2b[:], in0=c2b[:],
                                                       in1=sinb[b][:], op=ALU.mult))
                    c2tok = tc2
                    tabrd[b] = tc2
                    W("vector", tc1, tc2)
                    if kv == 1:
                        W("vector", vsttok[ti])
                        t_rot = E("vector", VE.tensor_tensor(
                            out=Vst[:, ti * 512:(ti + 1) * 512], in0=c1b[:], in1=c2b[:],
                            op=ALU.add))
                        vsttok[ti] = t_rot
                        c1tok = t_rot
                        c2tok = t_rot
                    else:
                        W("vector", kbtok)
                        t_kb = E("vector", VE.tensor_tensor(out=Kb[:], in0=c1b[:],
                                                            in1=c2b[:], op=ALU.add))
                        c1tok = t_kb
                        c2tok = t_kb
                        W("vector", t_kb, vsttok[ti], ppdma[b])
                        t_pp = E("vector", VE.tensor_tensor(
                            out=ppst[b][:], in0=Vst[:, ti * 512:(ti + 1) * 512],
                            in1=Kb[:], op=ALU.mult))
                        kbtok = t_pp
                        vsttok[ti] = t_pp
                        W("sync", t_pp)
                        ppdma[b] = E("sync", SY.dma_start(
                            out=ppd.ap()[ti, :, ds * 512:(ds + 1) * 512], in_=ppst[b][:]),
                            dma=True)
                else:
                    if kv == 1:
                        W("scalar", t_kt, vsttok[ti], t_sq)
                        t_cp = E("scalar", SC.copy(out=Vst[:, ti * 512:(ti + 1) * 512],
                                                   in_=ktmp[b][:]))
                        vsttok[ti] = t_cp
                        ktmprd[b].append(t_cp)
                    else:
                        W("scalar", t_kt, kbtok, t_sq)
                        t_kb = E("scalar", SC.copy(out=Kb[:], in_=ktmp[b][:]))
                        ktmprd[b].append(t_kb)
                        W("vector", t_kb, vsttok[ti], ppdma[b])
                        t_pp = E("vector", VE.tensor_tensor(
                            out=ppst[b][:], in0=Vst[:, ti * 512:(ti + 1) * 512],
                            in1=Kb[:], op=ALU.mult))
                        kbtok = t_pp
                        vsttok[ti] = t_pp
                        W("sync", t_pp)
                        ppdma[b] = E("sync", SY.dma_start(
                            out=ppd.ap()[ti, :, ds * 512:(ds + 1) * 512], in_=ppst[b][:]),
                            dma=True)
    t_pp_all = p.tok("sync")
    st7.close()

    stA2 = contextlib.ExitStack()
    ppb = [sb(f"ppb{i}", [128, 2048], BF16, st=stA2) for i in range(2)]
    psA = ps("psA", [128, 16], st=stA2)
    fence("sync")
    ppbtok = [None, None]
    scoltok = [None, None]
    tA = None
    for ti in range(8):
        b = ti % 2
        W("sync", t_pp_all, ppbtok[b])
        t_ld = E("sync", SY.dma_start(out=ppb[b][:], in_=ppd.ap()[ti]), dma=True)
        W("vector", p.tok("vector") if ti == 0 else None)
        tm = E("vector", VE.tensor_tensor(out=tmp1[:], in0=ssK[:, ti:ti + 1],
                                          in1=ssV[:, ti:ti + 1], op=ALU.mult))
        W("scalar", tm)
        tsq = E("scalar", SC.activation(out=tmp2[:], in_=tmp1[:], func=AF.Sqrt))
        W("vector", tsq)
        tc = E("vector", VE.reciprocal(tmp3[:], tmp2[:]))
        W("vector", tc, scoltok[b])
        tsc = E("vector", VE.tensor_copy(scol[:, b:b + 1], tmp3[:]))
        W("tensor", tsc, t_ld)
        for ch in range(16):
            tA = E("tensor", TE.matmul(psA[:, ch:ch + 1], ppb[b][:, ch * 128:(ch + 1) * 128],
                                       scol[:, b:b + 1], start=(ti == 0), stop=(ti == 7)))
        ppbtok[b] = tA
        scoltok[b] = tA
    W("vector", tA)
    t_Ar = E("vector", VE.tensor_copy(A_raw[:], psA[:]))
    W("sync", t_Ar)
    t_As = E("sync", SY.dma_start(out=arA_in.ap(), in_=A_raw[:]), dma=True)
    W("tensor", p.tok("vector"))
    stA2.close()

    stB2 = contextlib.ExitStack()
    Q2T = sb("Q2T", [128, 16384], BF16, st=stB2)
    st8a = contextlib.ExitStack()
    qwb = [sb(f"qwb{i}", [128, 2048], BF16, st=st8a) for i in range(2)]
    psQ2 = [ps(f"psQ2{i}", [128, 512], st=st8a) for i in range(2)]
    fence("gpsimd")
    pre_q = [E("gpsimd", GP.dma_start(out=qwb[i][:], in_=w1q[i]), dma=True)
             for i in range(2)]
    W("gpsimd", pre_q[1], t_As)
    t_arc = E("gpsimd", GP.collective_compute(
        "AllReduce", ALU.add, replica_groups=RG,
        ins=[arA_in.ap()], outs=[arA_out.ap()]), cc=True)
    qwtok = [None, None]
    q2tok = [None, None]
    q2ev = []
    for qch in range(16):
        b = qch % 2
        if qch < 2:
            t_w = pre_q[qch]
        else:
            W("gpsimd", qwtok[b], t_arc)
            t_w = E("gpsimd", GP.dma_start(out=qwb[b][:], in_=w1q[qch]), dma=True)
        for h2 in range(2):
            W("tensor", t_w, q2tok[h2])
            tq = None
            for ec in range(16):
                tq = E("tensor", TE.matmul(psQ2[h2][:], qwb[b][:, ec * 128:(ec + 1) * 128],
                                           fmB[:, ec * 1024 + h2 * 512: ec * 1024 + h2 * 512 + 512],
                                           start=(ec == 0), stop=(ec == 15)))
            qwtok[b] = tq
            W("vector", tq, t_small)
            q2tok[h2] = E("vector", VE.tensor_scalar(
                out=Q2T[:, qch * 1024 + h2 * 512: qch * 1024 + h2 * 512 + 512],
                in0=psQ2[h2][:], scalar1=b1qs[:, qch:qch + 1], scalar2=None,
                op0=ALU.add))
            q2ev.append(q2tok[h2])
    W("sync", t_arc)
    t_Al = E("sync", SY.dma_start(out=A_raw[:], in_=arA_out.ap()), dma=True)
    W("vector", t_Al)
    t_A1 = E("vector", VE.tensor_scalar(out=A1[:], in0=A_raw[:], scalar1=0.5, scalar2=0.5,
                                        op0=ALU.mult, op1=ALU.add))
    sc_toks = []
    for qch in range(16):
        W("vector", t_A1, q2ev[2 * qch], q2ev[2 * qch + 1])
        sc_toks.append(E("vector", VE.tensor_scalar(
            out=Q2T[:, qch * 1024:(qch + 1) * 1024], in0=Q2T[:, qch * 1024:(qch + 1) * 1024],
            scalar1=A1[:, qch:qch + 1], scalar2=None, op0=ALU.mult)))
    W("tensor", p.tok("vector"))
    st8a.close()
    st8b = contextlib.ExitStack()
    owb = [sb(f"owb{i}", [128, 2048], BF16, st=st8b) for i in range(2)]
    ytmp = [sb(f"ytmp{i}", [128, 512], F32, st=st8b) for i in range(2)]
    b1ob = [sb(f"b1ob{i}", [128, 512], F32, st=st8b) for i in range(2)]
    psY = [ps(f"psY{i}", [128, 512], st=st8b) for i in range(8)]
    fence("gpsimd")
    fence("sync")
    owtok = [None, None]
    btok = [None, None]
    ytok = [None] * 8
    for cs in range(4):
        cb = cs % 2
        W("sync", btok[cb])
        t_b = E("sync", SY.dma_start(out=b1ob[cb][:], in_=b1o_b[cs]), dma=True)
        for qg in range(4):
            b = qg % 2
            W("gpsimd", owtok[b])
            t_w = E("gpsimd", GP.dma_start(
                out=owb[b][:], in_=w1o[cs][:, qg * 2048:(qg + 1) * 2048]), dma=True)
            W("tensor", t_w)
            if qg == 0:
                W("tensor", *[t for t in ytok if t])
            tY = None
            for ti in range(8):
                for k in range(4):
                    qch = qg * 4 + k
                    tY = E("tensor", TE.matmul(
                        psY[ti][:],
                        Q2T[:, qch * 1024 + ti * 128: qch * 1024 + ti * 128 + 128],
                        owb[b][:, k * 512:(k + 1) * 512],
                        start=(qg == 0 and k == 0), stop=(qg == 3 and k == 3)))
            owtok[b] = tY
        t_fin = p.tok("tensor")
        for ti in range(8):
            yb = ti % 2
            W("vector", t_fin, t_b, ytok[ti])
            t_y1 = E("vector", VE.tensor_tensor(out=ytmp[yb][:], in0=psY[ti][:],
                                                in1=b1ob[cb][:], op=ALU.add))
            xsl = x_sb[:, ti * 2048 + cs * 512: ti * 2048 + cs * 512 + 512]
            W("vector", t_y1)
            ytok[ti] = E("vector", VE.tensor_tensor(out=xsl, in0=xsl, in1=ytmp[yb][:],
                                                    op=ALU.add))
        btok[cb] = ytok[7]
    W("tensor", p.tok("vector"))
    st8b.close()
    stB2.close()
    stB.close()

    st9 = contextlib.ExitStack()
    H1 = sb("H1", [128, 8192], BF16, st=st9)
    g1b = [sb(f"g1b{i}", [128, 2048], BF16, st=st9) for i in range(2)]
    u1b = [sb(f"u1b{i}", [128, 2048], BF16, st=st9) for i in range(2)]
    sgb2 = [sb(f"sgb2{i}", [128, 512], F32, st=st9) for i in range(2)]
    d1b = sb("d1b", [128, 16384], BF16, st=st9)
    psG1 = [ps(f"psGm{i}", [128, 512], st=st9) for i in range(2)]
    psU1 = [ps(f"psUm{i}", [128, 512], st=st9) for i in range(2)]
    psM = [ps(f"psM{i}", [128, 512], st=st9) for i in range(2)]
    fence("gpsimd")
    g1tok = [None, None]
    u1tok = [None, None]
    sg1tok = [None, None]
    h1tok = [None, None]
    d1tok = None
    mevtok = [None, None]
    for e in range(8):
        t_h1_all = []
        for ic in range(8):
            b = ic % 2
            W("gpsimd", g1tok[b])
            t_g = E("gpsimd", GP.dma_start(out=g1b[b][:], in_=g1[e, :, ic, :]), dma=True)
            W("gpsimd", u1tok[b])
            t_u = E("gpsimd", GP.dma_start(out=u1b[b][:], in_=u1[e, :, ic, :]), dma=True)
            for th in range(2):
                pb = (ic * 2 + th) % 2
                W("tensor", t_g, sg1tok[pb])
                tG = None
                for hc in range(16):
                    tG = E("tensor", TE.matmul(psG1[pb][:], g1b[b][:, hc * 128:(hc + 1) * 128],
                                               fmA[:, hc * 1024 + th * 512: hc * 1024 + th * 512 + 512],
                                               start=(hc == 0), stop=(hc == 15)))
                g1tok[b] = tG
                W("scalar", tG, h1tok[pb])
                t_sg = E("scalar", SC.activation(out=sgb2[pb][:], in_=psG1[pb][:], func=AF.Silu))
                sg1tok[pb] = t_sg
                W("tensor", t_u, h1tok[pb])
                tU = None
                for hc in range(16):
                    tU = E("tensor", TE.matmul(psU1[pb][:], u1b[b][:, hc * 128:(hc + 1) * 128],
                                               fmA[:, hc * 1024 + th * 512: hc * 1024 + th * 512 + 512],
                                               start=(hc == 0), stop=(hc == 15)))
                u1tok[b] = tU
                W("vector", tU, t_sg)
                th1 = E("vector", VE.tensor_tensor(
                    out=H1[:, ic * 1024 + th * 512: ic * 1024 + th * 512 + 512],
                    in0=sgb2[pb][:], in1=psU1[pb][:], op=ALU.mult))
                h1tok[pb] = th1
                t_h1_all.append(th1)
        W("gpsimd", d1tok)
        t_d = E("gpsimd", GP.dma_start(
            out=d1b[:].rearrange("p (a m) -> p a m", a=8), in_=d1[e]), dma=True)
        for ti in range(8):
            for cs in range(4):
                mb = (ti * 4 + cs) % 2
                W("tensor", t_d, mevtok[mb], *t_h1_all)
                t_h1_all = []
                tM = None
                for icl in range(8):
                    tM = E("tensor", TE.matmul(
                        psM[mb][:],
                        H1[:, icl * 1024 + ti * 128: icl * 1024 + ti * 128 + 128],
                        d1b[:, icl * 2048 + cs * 512: icl * 2048 + cs * 512 + 512],
                        start=(icl == 0), stop=(icl == 7)))
                d1tok = tM
                xsl = x_sb[:, ti * 2048 + cs * 512: ti * 2048 + cs * 512 + 512]
                W("vector", tM)
                mevtok[mb] = E("vector", VE.scalar_tensor_tensor(
                    out=xsl, in0=psM[mb][:], scalar=w_all[:, ti * 12 + e: ti * 12 + e + 1],
                    in1=xsl, op0=ALU.mult, op1=ALU.add))
    W("tensor", p.tok("vector"))
    st9.close()

    W("sync", p.tok("vector"))
    st_toks = []
    for ti in range(8):
        st_toks.append(E("sync", SY.dma_start(out=out_x[ti * 128:(ti + 1) * 128, :],
                                              in_=x_sb[:, ti * 2048:(ti + 1) * 2048]), dma=True))
    W("sync", *st_toks)
    return nc, stack


def _host_layouts(inp):
    L = {}
    L["g0"] = np.ascontiguousarray(
        inp["L0_ffn_gate_w"].reshape(44, 128, 16, 128).transpose(0, 3, 2, 1)).reshape(44, 128, 2048)
    L["u0"] = np.ascontiguousarray(
        inp["L0_ffn_up_w"].reshape(44, 128, 16, 128).transpose(0, 3, 2, 1)).reshape(44, 128, 2048)
    L["d0"] = np.ascontiguousarray(
        inp["L0_ffn_down_w"].T.reshape(44, 128, 2, 1024).transpose(2, 0, 1, 3))
    L["gs0"] = np.ascontiguousarray(inp["L0_ffn_gate_s"].reshape(44, 128).T)
    L["us0"] = np.ascontiguousarray(inp["L0_ffn_up_s"].reshape(44, 128).T)
    L["ds_b"] = np.ascontiguousarray(
        np.broadcast_to(inp["L0_ffn_down_s"].reshape(1, 2048), (128, 2048)))
    W_in = inp["L1_attn_W_in"]
    L["w1q"] = np.ascontiguousarray(
        W_in[:, :2048].reshape(16, 128, 16, 128).transpose(2, 1, 0, 3)).reshape(16, 128, 2048)
    L["w1kv"] = np.ascontiguousarray(
        W_in[:, 2048:].reshape(16, 128, 2, 4, 512).transpose(2, 3, 1, 0, 4)).reshape(2, 4, 128, 8192)
    b_in = inp["L1_attn_b_in"]
    L["b1q"] = np.ascontiguousarray(b_in[:2048].reshape(16, 128).T)
    L["b1kv_b"] = np.ascontiguousarray(
        np.broadcast_to(b_in[2048:].reshape(2, 4, 1, 512), (2, 4, 128, 512)))
    L["w1o"] = np.ascontiguousarray(
        inp["L1_attn_W_out"].reshape(16, 128, 4, 512).transpose(2, 1, 0, 3)).reshape(4, 128, 8192)
    L["b1o_b"] = np.ascontiguousarray(
        np.broadcast_to(inp["L1_attn_b_out"].reshape(4, 1, 512), (4, 128, 512)))
    L["g1a"] = np.ascontiguousarray(inp["L1_attn_gamma"].reshape(16, 128).T)
    L["be1"] = np.ascontiguousarray(inp["L1_attn_beta"].reshape(16, 128).T)
    L["clf"] = np.ascontiguousarray(
        inp["L1_ffn_classifier_w"].T.reshape(16, 128, 12).transpose(1, 0, 2)).reshape(128, 192)
    L["eb_b"] = np.ascontiguousarray(
        np.broadcast_to(inp["L1_ffn_e_score_correction_bias"].reshape(1, 12), (128, 12)))
    L["g1"] = np.ascontiguousarray(
        inp["L1_ffn_gate_w"].reshape(8, 8, 128, 16, 128).transpose(0, 4, 1, 3, 2)).reshape(8, 128, 8, 2048)
    L["u1"] = np.ascontiguousarray(
        inp["L1_ffn_up_w"].reshape(8, 8, 128, 16, 128).transpose(0, 4, 1, 3, 2)).reshape(8, 128, 8, 2048)
    L["d1"] = np.ascontiguousarray(
        inp["L1_ffn_down_w"].reshape(8, 2048, 8, 128).transpose(0, 3, 2, 1)).reshape(8, 128, 8, 2048)
    inv = (10000.0 ** (-(np.arange(0, 1024, 2, dtype=np.float64) / 1024.0)))
    pos = np.arange(4096, dtype=np.float64)
    fr = pos[:, None] * inv[None, :]
    cosr = np.repeat(np.cos(fr), 2, axis=1)
    sinr = np.repeat(np.sin(fr), 2, axis=1)
    sinr[:, 0::2] *= -1.0
    L["_cos_full"] = cosr.astype(np.float32)
    L["_sin_full"] = sinr.astype(np.float32)
    return L


def kernel(**inp):
    inp = {k: np.asarray(v) for k, v in inp.items()}
    L = _host_layouts(inp)
    x = inp["L0_attn_x"]
    Wq, bq = inp["L0_attn_Wq"], inp["L0_attn_bq"]
    Wk, bk = inp["L0_attn_Wk"], inp["L0_attn_bk"]
    Wv, bv = inp["L0_attn_Wv"], inp["L0_attn_bv"]
    shared_keys = ["g0", "u0", "d0", "gs0", "us0", "ds_b", "w1q", "w1kv", "b1q",
                   "b1kv_b", "w1o", "b1o_b", "g1a", "be1", "clf", "eb_b",
                   "g1", "u1", "d1"]
    in_maps = []
    for c in range(8):
        b, j = c // 4, c % 4
        m = {k: L[k] for k in shared_keys}
        m["x_own"] = np.ascontiguousarray(x[b, j * 1024:(j + 1) * 1024, :])
        m["wq_o"] = np.ascontiguousarray(
            Wq[4 * j:4 * j + 4].reshape(4, 16, 128, 128).transpose(2, 0, 1, 3)).reshape(128, 8192)
        m["bq_o"] = np.ascontiguousarray(bq[4 * j:4 * j + 4].T)
        m["wk_o"] = np.ascontiguousarray(
            Wk[j].reshape(16, 128, 128).transpose(1, 0, 2)).reshape(128, 2048)
        m["wv_o"] = np.ascontiguousarray(
            Wv[j].reshape(16, 128, 128).transpose(1, 0, 2)).reshape(128, 2048)
        m["bk_o"] = bk[j].reshape(128, 1).copy()
        m["bv_b"] = np.ascontiguousarray(np.broadcast_to(bv[j].reshape(1, 128), (128, 128)))
        tt = np.arange(32)[None, :]
        pp = np.arange(128)[:, None]
        m["oidx"] = ((tt // 8) * 2048 + ((tt % 4) * 128 + pp) * 4 + j).astype(np.int32)
        tsl = slice(j * 1024, (j + 1) * 1024)
        m["rcos"] = np.ascontiguousarray(
            L["_cos_full"][tsl].reshape(8, 128, 2, 512).transpose(2, 0, 1, 3))
        m["rsin"] = np.ascontiguousarray(
            L["_sin_full"][tsl].reshape(8, 128, 2, 512).transpose(2, 0, 1, 3))
        in_maps.append(m)
    nc, stack = build()
    res = run_bass_kernel_spmd(nc, in_maps, core_ids=list(range(8)))
    stack.close()
    out = np.empty((2, 4096, 2048), dtype=np.float32)
    for c in range(8):
        b, j = c // 4, c % 4
        out[b, j * 1024:(j + 1) * 1024, :] = res.results[c]["out_x"]
    return out


# version 14
# speedup vs baseline: 1.1089x; 1.0140x over previous
import contextlib
import numpy as np
import ml_dtypes
import concourse.bass as bass
import concourse.mybir as mybir
from concourse.bass_utils import run_bass_kernel_spmd

F32 = mybir.dt.float32
BF16 = mybir.dt.bfloat16
I32 = mybir.dt.int32
AF = mybir.ActivationFunctionType
ALU = mybir.AluOpType
AX = mybir.AxisListType
BF = ml_dtypes.bfloat16
RG = [[0, 1, 2, 3], [4, 5, 6, 7]]
ENGS = ["sync", "scalar", "vector", "tensor", "gpsimd"]
RSQRT_SCALE = float(1.0 / np.sqrt(128.0))


class P:

    def __init__(self, nc, stack):
        self.nc = nc
        self.sems = {e: stack.enter_context(nc.semaphore(f"S_{e}")) for e in ENGS}
        self.cnt = {e: 0 for e in ENGS}
        self.lastw = {}
        self.eng = {e: getattr(nc, e) for e in ENGS}

    def emit(self, ename, instr, dma=False, cc=False):
        n = 16 if dma else 1
        instr.then_inc(self.sems[ename], n)
        self.cnt[ename] += n
        return (ename, self.cnt[ename], dma or cc)

    def wait(self, consumer, *tokens):
        for t in tokens:
            if t is None:
                continue
            pe, v, _ = t
            key = (consumer, pe)
            if self.lastw.get(key, -1) >= v:
                continue
            self.eng[consumer].wait_ge(self.sems[pe], v)
            self.lastw[key] = v

    def tok(self, ename):
        return (ename, self.cnt[ename], False)


def build():
    nc = bass.Bass("TRN2", target_bir_lowering=False)

    def din(name, shape, dt=F32):
        return nc.dram_tensor(name, shape, dt, kind="ExternalInput").ap()

    x_own = din("x_own", [1024, 2048])
    wq_o = din("wq_o", [128, 8192])
    wk_o = din("wk_o", [128, 2048])
    wv_o = din("wv_o", [128, 2048])
    bq_o = din("bq_o", [128, 4])
    bk_o = din("bk_o", [128, 1])
    bv_b = din("bv_b", [128, 128])
    oidx = din("oidx", [128, 32], I32)
    g0 = din("g0", [44, 128, 2048])
    u0 = din("u0", [44, 128, 2048])
    d0 = din("d0", [2, 44, 128, 1024])
    gs0 = din("gs0", [128, 44])
    us0 = din("us0", [128, 44])
    ds_b = din("ds_b", [128, 2048])
    w1q = din("w1q", [16, 128, 2048])
    b1q = din("b1q", [128, 16])
    w1kv = din("w1kv", [2, 4, 128, 8192])
    b1kv_b = din("b1kv_b", [2, 4, 128, 512])
    w1o = din("w1o", [4, 128, 8192])
    b1o_b = din("b1o_b", [4, 128, 512])
    g1a_i = din("g1a", [128, 16])
    be1_i = din("be1", [128, 16])
    clf = din("clf", [128, 192])
    eb_b = din("eb_b", [128, 12])
    g1 = din("g1", [8, 128, 8, 2048])
    u1 = din("u1", [8, 128, 8, 2048])
    d1 = din("d1", [8, 128, 8, 2048])
    rcos = din("rcos", [2, 8, 128, 512])
    rsin = din("rsin", [2, 8, 128, 512])

    out_x = nc.dram_tensor("out_x", [1024, 2048], F32, kind="ExternalOutput").ap()

    ag_in = nc.dram_tensor("ag_in", [2048, 1024], BF16)
    ag_out = nc.dram_tensor("ag_out", [8192, 1024], BF16)
    rs_in = [nc.dram_tensor(f"rs_in{m}", [8192, 512], BF16) for m in range(2)]
    rs_out = [nc.dram_tensor(f"rs_out{m}", [2048, 512], BF16) for m in range(2)]
    arA_in = nc.dram_tensor("arA_in", [128, 16], F32)
    arA_out = nc.dram_tensor("arA_out", [128, 16], F32)
    ppd = nc.dram_tensor("ppd", [8, 128, 2048], BF16)

    stack = contextlib.ExitStack()
    p = P(nc, stack)
    E, W = p.emit, p.wait
    SY, SC, VE, TE, GP = nc.sync, nc.scalar, nc.vector, nc.tensor, nc.gpsimd

    def sb(name, shape, dt=F32, st=stack):
        return st.enter_context(nc.sbuf_tensor(name, shape, dt))

    def ps(name, shape, dt=F32, st=stack):
        return st.enter_context(nc.psum_tensor(name, shape, dt))

    def fence(q, engs=("tensor", "vector", "scalar")):
        W(q, *[p.tok(e) for e in engs])

    x_sb = sb("x_sb", [128, 8 * 2048])
    fmA = sb("fmA", [128, 16384], BF16)
    masks = sb("masks", [128, 4 * 512])
    ident = sb("ident", [128, 128], BF16)
    bqs = sb("bqs", [128, 4]); bks = sb("bks", [128, 1])
    bvbc = sb("bvbc", [128, 128]); oix = sb("oix", [128, 32], I32)
    gss = sb("gss", [128, 44]); uss = sb("uss", [128, 44])
    b1qs = sb("b1qs", [128, 16])
    g1as = sb("g1as", [128, 16]); be1s = sb("be1s", [128, 16])
    clfs = sb("clfs", [128, 192], BF16); ebbc = sb("ebbc", [128, 12])
    stats_m = sb("stats_m", [128, 8]); stats_r = sb("stats_r", [128, 8])
    stats_v = sb("stats_v", [128, 8])
    s2c = sb("s2c", [128, 8]); o2c = sb("o2c", [128, 8])
    w_all = sb("w_all", [128, 96])
    A1 = sb("A1", [128, 16]); A_raw = sb("A_raw", [128, 16])
    ssK = sb("ssK", [128, 8]); ssV = sb("ssV", [128, 8])
    scol = sb("scol", [128, 2], BF16)
    tmp1 = sb("tmp1", [128, 1]); tmp2 = sb("tmp2", [128, 1]); tmp3 = sb("tmp3", [128, 1])
    wks = sb("wks", [128, 2048], BF16)
    wvs = sb("wvs", [128, 2048], BF16)
    st_i = contextlib.ExitStack()
    iota = sb("iota", [128, 512], I32, st=st_i)

    t_x = [E("sync", SY.dma_start(out=x_sb[:, ti * 2048:(ti + 1) * 2048],
                                  in_=x_own[ti * 128:(ti + 1) * 128, :]), dma=True)
           for ti in range(8)]
    smalls = [(bqs, bq_o), (bks, bk_o), (bvbc, bv_b), (gss, gs0), (uss, us0),
              (b1qs, b1q), (g1as, g1a_i), (be1s, be1_i), (ebbc, eb_b)]
    t_small = None
    for buf, src in smalls:
        t_small = E("sync", SY.dma_start(out=buf[:], in_=src), dma=True)
    t_oix = E("sync", SY.dma_start(out=oix[:], in_=oidx), dma=True)
    t_clf = E("gpsimd", GP.dma_start(out=clfs[:], in_=clf), dma=True)
    t_io = E("gpsimd", GP.iota(out=iota[:], pattern=[[-1, 512]], base=0,
                               channel_multiplier=1))
    W("vector", t_io)
    t_id = E("vector", VE.tensor_scalar(out=ident[:], in0=iota[:, 0:128], scalar1=0.0,
                                        scalar2=None, op0=ALU.is_equal))
    t_mk = []
    for jm in range(4):
        t_mk.append(E("vector", VE.tensor_scalar(
            out=masks[:, jm * 512:(jm + 1) * 512], in0=iota[:], scalar1=float(-128 * jm),
            scalar2=-1e30, op0=ALU.is_gt, op1=ALU.mult)))
    st_i.close()

    def ln_stats(ti):
        xt = x_sb[:, ti * 2048:(ti + 1) * 2048]
        t_sum = E("vector", VE.reduce_sum(out=tmp1[:], in_=xt, axis=AX.X))
        t_ssq = E("scalar", SC.activation(out=sq_scratch[0][:], in_=xt, func=AF.Square,
                                          accum_out=tmp2[:]))
        W("vector", t_sum)
        t_m = E("vector", VE.tensor_scalar(out=stats_m[:, ti:ti + 1], in0=tmp1[:],
                                           scalar1=1.0 / 2048.0, scalar2=None, op0=ALU.mult))
        W("vector", t_m, t_ssq)
        t_a = E("vector", VE.tensor_tensor(out=tmp1[:], in0=stats_m[:, ti:ti + 1],
                                           in1=stats_m[:, ti:ti + 1], op=ALU.mult))
        t_b = E("vector", VE.tensor_scalar(out=tmp3[:], in0=tmp2[:],
                                           scalar1=1.0 / 2048.0, scalar2=None, op0=ALU.mult))
        W("vector", t_a, t_b)
        t_v = E("vector", VE.tensor_tensor(out=stats_v[:, ti:ti + 1], in0=tmp3[:],
                                           in1=tmp1[:], op=ALU.subtract))
        W("vector", t_v)
        t_e = E("vector", VE.tensor_scalar(out=tmp1[:], in0=stats_v[:, ti:ti + 1],
                                           scalar1=1e-5, scalar2=None, op0=ALU.add))
        W("scalar", t_e)
        t_s = E("scalar", SC.activation(out=tmp2[:], in_=tmp1[:], func=AF.Sqrt))
        W("vector", t_s)
        t_r = E("vector", VE.reciprocal(stats_r[:, ti:ti + 1], tmp2[:]))
        return t_r, t_ssq

    def build_fmT(dest_fn, stage_buf, t_stage, tr_pss):
        toks = []
        t_tr = None
        for ec in range(16):
            b = ec % len(tr_pss)
            tp, t_ev_prev = tr_pss[b]
            W("tensor", t_stage, t_ev_prev, t_id)
            t_tr = E("tensor", TE.transpose(tp[:], stage_buf[:, ec * 128:(ec + 1) * 128],
                                            ident[:]))
            t_ev = dest_fn(tp, ec, t_tr)
            tr_pss[b] = (tp, t_ev)
            toks.append(t_ev)
        return toks, t_tr

    stA = contextlib.ExitStack()
    xn_stage = [sb(f"xn_st{i}", [128, 2048], BF16, st=stA) for i in range(2)]
    sq_scratch = [sb("sq_scr", [128, 2048], BF16, st=stA)]
    zbuf = sb("zbuf", [128, 2048], BF16, st=stA)
    tr_ps_l = [(ps(f"trp{i}", [128, 128], BF16, st=stA), None) for i in range(4)]
    t_zb = E("gpsimd", GP.memset(zbuf[:], 0.0))
    t_zf = None
    for m in range(2):
        for k in range(16):
            t_zf = E("gpsimd", GP.dma_start(
                out=rs_in[m].ap()[k * 512:(k + 1) * 512, :].rearrange(
                    "(a p) m -> p a m", p=128),
                in_=zbuf[:].rearrange("p (a m) -> p a m", a=4)), dma=True)
    t_fmA_done = []
    stage_rd = [None, None]
    for ti in range(8):
        W("vector", t_x[ti])
        W("scalar", t_x[ti])
        t_r, _ = ln_stats(ti)
        sbuf_st = xn_stage[ti % 2]
        W("vector", t_r, stage_rd[ti % 2])
        t_st = E("vector", VE.tensor_scalar(out=sbuf_st[:],
                                            in0=x_sb[:, ti * 2048:(ti + 1) * 2048],
                                            scalar1=stats_m[:, ti:ti + 1],
                                            scalar2=stats_r[:, ti:ti + 1],
                                            op0=ALU.subtract, op1=ALU.mult))

        def evacA(tp, ec, t_tr, ti=ti):
            W("scalar", t_tr)
            return E("scalar", SC.copy(
                out=fmA[:, ec * 1024 + ti * 128: ec * 1024 + ti * 128 + 128], in_=tp[:]))
        toks, t_trl = build_fmT(evacA, sbuf_st, t_st, tr_ps_l)
        t_fmA_done += toks
        stage_rd[ti % 2] = t_trl
    W("sync", *t_fmA_done)
    t_agst = []
    for ec in range(16):
        t_agst.append(E("sync", SY.dma_start(
            out=ag_in.ap()[ec * 128:(ec + 1) * 128, :],
            in_=fmA[:, ec * 1024:(ec + 1) * 1024]), dma=True))
    stA.close()
    agc = [None]

    stF = contextlib.ExitStack()
    H = sb("H", [128, 44 * 512], BF16, st=stF)
    gbuf = [sb(f"gb{i}", [128, 2048], BF16, st=stF) for i in range(2)]
    ubuf = [sb(f"ub{i}", [128, 2048], BF16, st=stF) for i in range(2)]
    sgb = [sb(f"sg{i}", [128, 512], F32, st=stF) for i in range(2)]
    dbuf = [sb(f"db{i}", [128, 4096], BF16, st=stF) for i in range(2)]
    dtmp = [sb("dt0", [128, 1024], F32, st=stF)] * 2
    ds_bc = sb("ds_bc", [128, 2048], st=stF)
    fence("gpsimd")
    W("gpsimd", t_zf)
    fence("sync")
    t_ds = E("sync", SY.dma_start(out=ds_bc[:], in_=ds_b), dma=True)
    W("tensor", p.tok("scalar"))

    gtok = [None, None]
    utok = [None, None]
    last_sw = [None]

    def ffn_half(th, kick_ag=False):
        stG = contextlib.ExitStack()
        psG = [ps(f"psG{th}{i}", [128, 512], st=stG) for i in range(2)]
        psU = [ps(f"psU{th}{i}", [128, 512], st=stG) for i in range(2)]
        hvtok = [None, None]
        sgtok = [None, None]
        gmt = [None, None]
        umt = [None, None]
        for ic in range(44):
            b = ic % 2
            W("gpsimd", agc[0], gtok[b])
            t_g = E("gpsimd", GP.dma_start(out=gbuf[b][:], in_=g0[ic]), dma=True)
            W("gpsimd", utok[b])
            t_u = E("gpsimd", GP.dma_start(out=ubuf[b][:], in_=u0[ic]), dma=True)
            last_sw[0] = t_u
            W("tensor", t_g, sgtok[b])
            tG = None
            for ec in range(16):
                tG = E("tensor", TE.matmul(psG[b][:], gbuf[b][:, ec * 128:(ec + 1) * 128],
                                           fmA[:, ec * 1024 + th * 512: ec * 1024 + th * 512 + 512],
                                           start=(ec == 0), stop=(ec == 15)))
            gtok[b] = tG
            gmt[b] = tG
            W("scalar", tG, t_small, hvtok[b])
            t_sg = E("scalar", SC.activation(out=sgb[b][:], in_=psG[b][:], func=AF.Silu,
                                             scale=gss[:, ic:ic + 1]))
            sgtok[b] = t_sg
            W("tensor", t_u, hvtok[b])
            tU = None
            for ec in range(16):
                tU = E("tensor", TE.matmul(psU[b][:], ubuf[b][:, ec * 128:(ec + 1) * 128],
                                           fmA[:, ec * 1024 + th * 512: ec * 1024 + th * 512 + 512],
                                           start=(ec == 0), stop=(ec == 15)))
            utok[b] = tU
            W("vector", tU, t_sg)
            hvtok[b] = E("vector", VE.scalar_tensor_tensor(
                out=H[:, ic * 512:(ic + 1) * 512], in0=psU[b][:], scalar=uss[:, ic:ic + 1],
                in1=sgb[b][:], op0=ALU.mult, op1=ALU.mult))
        t_hv = p.tok("vector")
        if kick_ag:
            W("gpsimd", *t_agst)
            W("gpsimd", t_clf, t_zf, last_sw[0])
            agc[0] = E("gpsimd", GP.collective_compute(
                "AllGather", ALU.bypass, replica_groups=RG,
                ins=[ag_in.ap()], outs=[ag_out.ap()]), cc=True)
        W("tensor", p.tok("scalar"), t_hv)
        stG.close()
        stD = contextlib.ExitStack()
        psD = [ps(f"psD{th}{i}", [128, 1024], st=stD) for i in range(4)]
        dtok = [None, None]
        evtok = [None] * 4
        for half in range(2):
            for icb in range(11):
                b = icb % 2
                W("gpsimd", agc[0], dtok[b])
                t_d = E("gpsimd", GP.dma_start(
                    out=dbuf[b][:].rearrange("p (a m) -> p a m", a=4),
                    in_=d0[half, icb * 4: icb * 4 + 4].rearrange("a p m -> p a m")),
                    dma=True)
                W("tensor", t_d)
                tmm = None
                for ici in range(4):
                    ic = icb * 4 + ici
                    for t4 in range(4):
                        if ic == 0:
                            W("tensor", evtok[t4])
                        for h2 in range(2):
                            tmm = E("tensor", TE.matmul(
                                psD[t4][:, h2 * 512:(h2 + 1) * 512],
                                H[:, ic * 512 + t4 * 128: ic * 512 + t4 * 128 + 128],
                                dbuf[b][:, ici * 1024 + h2 * 512: ici * 1024 + h2 * 512 + 512],
                                start=(ic == 0), stop=(ic == 43)))
                dtok[b] = tmm
            t_last_mm = p.tok("tensor")
            for t4 in range(4):
                db = t4 % 2
                xsl = x_sb[:, (th * 4 + t4) * 2048 + half * 1024:
                           (th * 4 + t4) * 2048 + half * 1024 + 1024]
                W("vector", t_last_mm, t_ds, evtok[t4])
                t_e1 = E("vector", VE.tensor_tensor(out=dtmp[db][:], in0=psD[t4][:],
                                                    in1=ds_bc[:, half * 1024:half * 1024 + 1024],
                                                    op=ALU.mult))
                W("vector", t_e1)
                evtok[t4] = E("vector", VE.tensor_tensor(out=xsl, in0=xsl, in1=dtmp[db][:],
                                                         op=ALU.add))
        W("tensor", p.tok("vector"))
        stD.close()

    W("sync", t_zf)
    W("scalar", p.tok("tensor"), p.tok("vector"), t_zf)
    ffn_half(0, kick_ag=True)
    ffn_half(1)
    t_wk = E("gpsimd", GP.dma_start(out=wks[:], in_=wk_o), dma=True)
    t_wv = E("gpsimd", GP.dma_start(out=wvs[:], in_=wv_o), dma=True)
    stF.close()

    stB = contextlib.ExitStack()
    fmB = sb("fmB", [128, 16384], BF16, st=stB)
    stAT0 = contextlib.ExitStack()
    KT = sb("KT", [128, 4096], BF16, st=stAT0)
    V_aug = sb("V_aug", [128, 32 * 129], BF16, st=stAT0)
    fence("gpsimd")
    t_vone = E("gpsimd", GP.memset(V_aug[:], 1.0))

    stAT1 = contextlib.ExitStack()
    wqs = sb("wqs", [128, 8192], BF16, st=stAT1)
    xt_buf = [sb(f"xtb{i}", [128, 4096], BF16, st=stAT1) for i in range(2)]
    t_wq = E("gpsimd", GP.dma_start(out=wqs[:], in_=wq_o), dma=True)
    fence("sync")
    stQ = contextlib.ExitStack()
    psK = ps("psK", [128, 256], st=stQ)
    psQ = [ps(f"psQ{i}", [128, 256], st=stQ) for i in range(2)]
    psV = [ps(f"psV{i}", [128, 128], st=stQ) for i in range(2)]
    ag3 = ag_out.ap().rearrange("(a p) m -> a p m", p=128)
    xttok = [None, None]
    ktok = None
    qtok = [None, None]
    vtok = [None, None]
    for tb2 in range(16):
        r, s = tb2 // 4, tb2 % 4
        b = tb2 % 2
        W("sync", agc[0], xttok[b])
        t_xt = E("sync", SY.dma_start(
            out=xt_buf[b][:].rearrange("p (a m) -> p a m", a=16),
            in_=ag3[r * 16:(r + 1) * 16, :, s * 256:(s + 1) * 256].rearrange("a p m -> p a m")),
            dma=True)
        W("tensor", t_xt, t_wk, ktok)
        tK = None
        for ec in range(16):
            tK = E("tensor", TE.matmul(psK[:], wks[:, ec * 128:(ec + 1) * 128],
                                       xt_buf[b][:, ec * 256:(ec + 1) * 256],
                                       start=(ec == 0), stop=(ec == 15)))
        W("scalar", tK, t_small)
        ktok = E("scalar", SC.activation(out=KT[:, tb2 * 256:(tb2 + 1) * 256], in_=psK[:],
                                         func=AF.Identity, bias=bks[:],
                                         scale=np.float32(1.0)))
        for hl in range(4):
            qb = hl % 2
            W("tensor", t_wq, qtok[qb])
            tQ = None
            for ec in range(16):
                tQ = E("tensor", TE.matmul(psQ[qb][:],
                                           wqs[:, hl * 2048 + ec * 128: hl * 2048 + ec * 128 + 128],
                                           xt_buf[b][:, ec * 256:(ec + 1) * 256],
                                           start=(ec == 0), stop=(ec == 15)))
            W("scalar", tQ)
            qtok[qb] = E("scalar", SC.activation(
                out=fmB[:, hl * 4096 + tb2 * 256: hl * 4096 + tb2 * 256 + 256], in_=psQ[qb][:],
                func=AF.Identity, bias=bqs[:, hl:hl + 1], scale=np.float32(1.0)))
        for t4 in range(2):
            vb = t4 % 2
            W("tensor", t_wv, vtok[vb])
            tV = None
            for ec in range(16):
                tV = E("tensor", TE.matmul(psV[vb][:],
                                           xt_buf[b][:, ec * 256 + t4 * 128: ec * 256 + t4 * 128 + 128],
                                           wvs[:, ec * 128:(ec + 1) * 128],
                                           start=(ec == 0), stop=(ec == 15)))
            kt = tb2 * 2 + t4
            W("vector", tV, t_small, t_vone)
            vtok[vb] = E("vector", VE.tensor_tensor(
                out=V_aug[:, kt * 129: kt * 129 + 128], in0=psV[vb][:], in1=bvbc[:],
                op=ALU.add))
        xttok[b] = p.tok("tensor")
    W("tensor", p.tok("scalar"), p.tok("vector"))
    stQ.close()
    stAT1.close()

    stS = contextlib.ExitStack()
    eTb = [sb(f"eT{i}", [128, 512], BF16, st=stS) for i in range(2)]
    O_q = [sb(f"Oq{i}", [128, 2048], BF16, st=stS) for i in range(2)]
    rzc = [sb(f"rz{i}", [128, 1], F32, st=stS) for i in range(4)]
    ao_wide = sb("ao_wide", [128, 8192], BF16, st=stS)
    xn_st2 = [sb(f"xn2st{i}", [128, 2048], BF16, st=stS) for i in range(2)]
    sq_scratch = [sb("sq_scr2", [128, 2048], BF16, st=stS)]
    tr_ps2 = [(ps(f"trq{i}", [128, 128], BF16, st=stS), None) for i in range(2)]
    stage_rd2 = [None, None]

    def phase5_tile(ti, t_xadd):
        W("vector", t_xadd)
        W("scalar", t_xadd)
        t_r, t_ssq = ln_stats(ti)
        W("vector", t_r)
        t_a = E("vector", VE.tensor_tensor(out=tmp1[:], in0=stats_r[:, ti:ti + 1],
                                           in1=stats_r[:, ti:ti + 1], op=ALU.mult))
        W("vector", t_a)
        t_b = E("vector", VE.tensor_tensor(out=tmp2[:], in0=stats_v[:, ti:ti + 1],
                                           in1=tmp1[:], op=ALU.mult))
        W("vector", t_b)
        t_c = E("vector", VE.tensor_scalar(out=tmp1[:], in0=tmp2[:], scalar1=1e-5,
                                           scalar2=None, op0=ALU.add))
        W("scalar", t_c)
        t_s = E("scalar", SC.activation(out=tmp2[:], in_=tmp1[:], func=AF.Sqrt))
        W("vector", t_s)
        t_r2 = E("vector", VE.reciprocal(tmp3[:], tmp2[:]))
        W("vector", t_r2)
        t_s2 = E("vector", VE.tensor_tensor(out=s2c[:, ti:ti + 1], in0=stats_r[:, ti:ti + 1],
                                            in1=tmp3[:], op=ALU.mult))
        W("vector", t_s2)
        t_o2a = E("vector", VE.tensor_tensor(out=tmp1[:], in0=stats_m[:, ti:ti + 1],
                                             in1=s2c[:, ti:ti + 1], op=ALU.mult))
        W("vector", t_o2a)
        t_o2 = E("vector", VE.tensor_scalar(out=o2c[:, ti:ti + 1], in0=tmp1[:],
                                            scalar1=-1.0, scalar2=None, op0=ALU.mult))
        W("vector", stage_rd2[0])
        t_st = E("vector", VE.tensor_scalar(out=xn_st2[0][:],
                                            in0=x_sb[:, ti * 2048:(ti + 1) * 2048],
                                            scalar1=stats_m[:, ti:ti + 1],
                                            scalar2=stats_r[:, ti:ti + 1],
                                            op0=ALU.subtract, op1=ALU.mult))

        def evacA1(tp, ec, t_tr, ti=ti):
            W("scalar", t_tr)
            return E("scalar", SC.copy(
                out=fmA[:, ec * 1024 + ti * 128: ec * 1024 + ti * 128 + 128], in_=tp[:]))
        _, t_trl = build_fmT(evacA1, xn_st2[0], t_st, tr_ps2)
        stage_rd2[0] = t_trl
        W("vector", t_o2, stage_rd2[1])
        t_st2 = E("vector", VE.tensor_scalar(out=xn_st2[1][:],
                                             in0=x_sb[:, ti * 2048:(ti + 1) * 2048],
                                             scalar1=s2c[:, ti:ti + 1],
                                             scalar2=o2c[:, ti:ti + 1],
                                             op0=ALU.mult, op1=ALU.add))

        def evacB(tp, ec, t_tr, ti=ti):
            W("vector", t_tr, t_small)
            return E("vector", VE.tensor_scalar(
                out=fmB[:, ec * 1024 + ti * 128: ec * 1024 + ti * 128 + 128], in0=tp[:],
                scalar1=g1as[:, ec:ec + 1], scalar2=be1s[:, ec:ec + 1],
                op0=ALU.mult, op1=ALU.add))
        _, t_trl2 = build_fmT(evacB, xn_st2[1], t_st2, tr_ps2)
        stage_rd2[1] = t_trl2
    psS = [ps(f"psS{i}", [128, 512], st=stS) for i in range(2)]
    psAV = [ps(f"psAV{i}", [128, 129], st=stS) for i in range(4)]
    setok = [None, None]
    avtok = [None] * 4
    ettok = [None, None]
    sc_tok = [None, None]
    t_scat = []
    rs6 = [rs_out[m].ap().rearrange("(t f) c -> f t c", f=4) for m in range(2)]
    t_aol = [None] * 16
    t_rs0 = None
    for qc in range(8):
        ob = qc % 2
        ev_toks = []
        for hl in range(4):
            nkc = 4 * qc + 4
            for kc in range(nkc):
                b = kc % 2
                W("tensor", setok[b])
                t_s = E("tensor", TE.matmul(psS[b][:], KT[:, kc * 128:(kc + 1) * 128],
                                            fmB[:, hl * 4096 + qc * 512: hl * 4096 + qc * 512 + 512],
                                            start=True, stop=True))
                if kc >= 4 * qc:
                    jm = kc - 4 * qc
                    W("vector", t_s, t_mk[jm])
                    t_s = E("vector", VE.tensor_tensor(
                        out=psS[b][:], in0=psS[b][:],
                        in1=masks[:, jm * 512:(jm + 1) * 512], op=ALU.add))
                W("scalar", t_s, ettok[b])
                t_e = E("scalar", SC.activation(out=eTb[b][:], in_=psS[b][:], func=AF.Exp,
                                                scale=np.float32(RSQRT_SCALE)))
                setok[b] = t_e
                tAV = None
                W("tensor", t_e)
                for qt in range(4):
                    if kc == 0:
                        W("tensor", avtok[qt])
                    tAV = E("tensor", TE.matmul(psAV[qt][:], eTb[b][:, qt * 128:(qt + 1) * 128],
                                                V_aug[:, kc * 129: kc * 129 + 129],
                                                start=(kc == 0), stop=(kc == nkc - 1)))
                ettok[b] = tAV
            t_fin = p.tok("tensor")
            for qt in range(4):
                W("vector", t_fin, sc_tok[ob] if hl == 0 else None)
                t_rz = E("vector", VE.reciprocal(rzc[qt][:], psAV[qt][:, 128:129]))
                W("vector", t_rz)
                avtok[qt] = E("vector", VE.tensor_scalar(
                    out=O_q[ob][:, qt * 512 + hl * 128: qt * 512 + hl * 128 + 128],
                    in0=psAV[qt][:, 0:128], scalar1=rzc[qt][:], scalar2=None, op0=ALU.mult))
                ev_toks.append(avtok[qt])
        m2 = qc % 2
        W("gpsimd", t_oix, t_zf, *ev_toks)
        if qc == 7:
            W("gpsimd", t_rs0)
        last_sc = None
        for qt in range(4):
            tt_idx = qc * 4 + qt
            last_sc = E("gpsimd", GP.indirect_dma_start(
                out=rs_in[m2].ap(),
                out_offset=bass.IndirectOffsetOnAxis(ap=oix[:, tt_idx:tt_idx + 1], axis=0),
                in_=O_q[ob][:, qt * 512:(qt + 1) * 512], in_offset=None), dma=True)
            t_scat.append(last_sc)
        sc_tok[ob] = last_sc
        if qc == 6:
            W("gpsimd", *t_scat)
            t_rs0 = E("gpsimd", GP.collective_compute(
                "ReduceScatter", ALU.add, replica_groups=RG,
                ins=[rs_in[0].ap()], outs=[rs_out[0].ap()]), cc=True)
            W("sync", t_rs0)
            for sl in range(16):
                ti_, j_ = sl // 4, sl % 4
                t_aol[sl] = E("sync", SY.dma_start(
                    out=ao_wide[:, sl * 512:(sl + 1) * 512],
                    in_=rs6[0][j_, (ti_ % 4) * 128:(ti_ % 4) * 128 + 128, :]), dma=True)
    W("gpsimd", *t_scat)
    W("gpsimd", t_aol[15])
    t_rs1 = E("gpsimd", GP.collective_compute(
        "ReduceScatter", ALU.add, replica_groups=RG,
        ins=[rs_in[1].ap()], outs=[rs_out[1].ap()]), cc=True)
    ao_add = [None] * 16
    for sl in range(16):
        ti_, j_ = sl // 4, sl % 4
        xsl = x_sb[:, ti_ * 2048 + j_ * 512: ti_ * 2048 + j_ * 512 + 512]
        W("vector", t_aol[sl])
        ao_add[sl] = E("vector", VE.tensor_tensor(
            out=xsl, in0=xsl, in1=ao_wide[:, sl * 512:(sl + 1) * 512], op=ALU.add))
    W("sync", t_rs1)
    for sl in range(16):
        ti_, j_ = 4 + sl // 4, sl % 4
        W("sync", ao_add[sl])
        t_aol[sl] = E("sync", SY.dma_start(
            out=ao_wide[:, sl * 512:(sl + 1) * 512],
            in_=rs6[1][j_, (ti_ % 4) * 128:(ti_ % 4) * 128 + 128, :]), dma=True)
    for ti in range(4):
        phase5_tile(ti, ao_add[ti * 4 + 3])
    for ti in range(4, 8):
        for j_ in range(4):
            sl = (ti - 4) * 4 + j_
            xsl = x_sb[:, ti * 2048 + j_ * 512: ti * 2048 + j_ * 512 + 512]
            W("vector", t_aol[sl])
            ao_add[sl] = E("vector", VE.tensor_tensor(
                out=xsl, in0=xsl, in1=ao_wide[:, sl * 512:(sl + 1) * 512], op=ALU.add))
        phase5_tile(ti, ao_add[(ti - 4) * 4 + 3])
    W("tensor", p.tok("scalar"), p.tok("vector"))
    stS.close()
    stAT0.close()


    st6 = contextlib.ExitStack()
    e_r = sb("e_r", [128, 12], st=st6)
    probs = sb("probs", [128, 12], st=st6)
    biased = sb("biased", [128, 12], st=st6)
    mk1 = sb("mk1", [128, 12], st=st6)
    b2v = sb("b2v", [128, 12], st=st6)
    psL = ps("psL", [128, 12], st=st6)
    t_ltok = None
    for ti in range(8):
        W("tensor", t_clf, t_ltok)
        tL = None
        for ec in range(16):
            tL = E("tensor", TE.matmul(psL[:], fmA[:, ec * 1024 + ti * 128: ec * 1024 + ti * 128 + 128],
                                       clfs[:, ec * 12:(ec + 1) * 12],
                                       start=(ec == 0), stop=(ec == 15)))
        W("vector", tL)
        t1 = E("vector", VE.reduce_max(out=tmp1[:], in_=psL[:], axis=AX.X, negate=True))
        W("scalar", t1)
        t2 = E("scalar", SC.activation(out=e_r[:], in_=psL[:], func=AF.Exp, bias=tmp1[:],
                                       scale=np.float32(1.0), accum_out=tmp2[:]))
        t_ltok = t2
        W("vector", t2)
        t3 = E("vector", VE.reciprocal(tmp3[:], tmp2[:]))
        W("vector", t3)
        t4 = E("vector", VE.tensor_scalar(out=probs[:], in0=e_r[:], scalar1=tmp3[:],
                                          scalar2=None, op0=ALU.mult))
        W("vector", t4, t_small)
        t5 = E("vector", VE.tensor_tensor(out=biased[:], in0=probs[:], in1=ebbc[:], op=ALU.add))
        W("vector", t5)
        t6 = E("vector", VE.reduce_max(out=tmp1[:], in_=biased[:], axis=AX.X))
        W("vector", t6)
        t7 = E("vector", VE.tensor_scalar(out=mk1[:], in0=biased[:], scalar1=tmp1[:],
                                          scalar2=None, op0=ALU.is_ge))
        W("vector", t7)
        t8 = E("vector", VE.scalar_tensor_tensor(out=b2v[:], in0=mk1[:], scalar=-1e30,
                                                 in1=biased[:], op0=ALU.mult, op1=ALU.add))
        W("vector", t8)
        t9 = E("vector", VE.reduce_max(out=tmp1[:], in_=b2v[:], axis=AX.X))
        W("vector", t9)
        t10 = E("vector", VE.tensor_scalar(out=b2v[:], in0=b2v[:], scalar1=tmp1[:],
                                           scalar2=None, op0=ALU.is_ge))
        W("vector", t10)
        t11 = E("vector", VE.tensor_tensor(out=mk1[:], in0=mk1[:], in1=b2v[:], op=ALU.add))
        W("vector", t11)
        t12 = E("vector", VE.scalar_tensor_tensor(out=w_all[:, ti * 12:(ti + 1) * 12],
                                                  in0=probs[:], scalar=1.5, in1=mk1[:],
                                                  op0=ALU.mult, op1=ALU.mult))
        W("vector", t12)
        t13 = E("vector", VE.reduce_sum(out=tmp1[:], in_=w_all[:, ti * 12 + 8: ti * 12 + 12],
                                        axis=AX.X))
        W("vector", t13)
        t14 = E("vector", VE.tensor_scalar(out=tmp2[:], in0=tmp1[:], scalar1=1.0,
                                           scalar2=None, op0=ALU.add))
        W("vector", t14)
        t15 = E("vector", VE.tensor_tensor(out=tmp1[:], in0=tmp2[:], in1=stats_r[:, ti:ti + 1],
                                           op=ALU.mult))
        W("vector", t15)
        t16 = E("vector", VE.tensor_scalar(out=tmp2[:], in0=tmp1[:], scalar1=1.0,
                                           scalar2=None, op0=ALU.add))
        t17 = E("vector", VE.tensor_tensor(out=tmp3[:], in0=stats_m[:, ti:ti + 1],
                                           in1=tmp1[:], op=ALU.mult))
        W("vector", t16, t17)
        xt = x_sb[:, ti * 2048:(ti + 1) * 2048]
        E("vector", VE.tensor_scalar(out=xt, in0=xt, scalar1=tmp2[:], scalar2=tmp3[:],
                                     op0=ALU.mult, op1=ALU.subtract))
    st6.close()

    st7 = contextlib.ExitStack()
    kvw = [[sb(f"kvw{s}{i}", [128, 4096], BF16, st=st7) for i in range(2)]
           for s in range(2)]
    cosb = [sb(f"cosb{i}", [128, 512], F32, st=st7) for i in range(2)]
    sinb = [sb(f"sinb{i}", [128, 512], F32, st=st7) for i in range(2)]
    ktmp = [sb(f"ktmp{i}", [128, 512], F32, st=st7) for i in range(2)]
    c1b = sb("c1b", [128, 512], F32, st=st7)
    c2b = sb("c2b", [128, 512], F32, st=st7)
    bias_kv = sb("bias_kv", [128, 512], F32, st=st7)
    Vst = sb("Vst", [128, 4096], BF16, st=st7)
    Kb = sb("Kb", [128, 512], BF16, st=st7)
    ppst = [sb(f"ppst{i}", [128, 512], BF16, st=st7) for i in range(2)]
    ssp = [sb(f"ssp{i}", [128, 1], F32, st=st7) for i in range(2)]
    psKV = [ps(f"psKV{i}", [128, 512], st=st7) for i in range(2)]
    kvwtok = [[None, None], [None, None]]
    kvwrd = [None, None]
    kvpass = 0
    ktmptok = [None, None]
    ktmprd = [[], []]
    ssptok = [None, None]
    tabtok = [None, None]
    tabrd = [None, None]
    c1tok = None
    c2tok = None
    kbtok = None
    vsttok = [None] * 8
    ppdma = [None, None]
    t_biaskv = None
    fence("gpsimd")
    W("gpsimd", t_rs1)
    fence("sync")
    W("tensor", p.tok("scalar"), p.tok("vector"))
    for ds in range(4):
        for kv in (1, 0):
            W("sync", t_biaskv)
            t_biaskv_l = E("sync", SY.dma_start(out=bias_kv[:], in_=b1kv_b[kv, ds]), dma=True)
            sset = kvpass % 2
            kvpass += 1
            for half in range(2):
                W("gpsimd", kvwtok[sset][half], kvwrd[sset])
                kvwtok[sset][half] = E("gpsimd", GP.dma_start(
                    out=kvw[sset][half][:],
                    in_=w1kv[kv, ds][:, half * 4096: half * 4096 + 4096]),
                    dma=True)
            for ti in range(8):
                b = ti % 2
                if ds < 2:
                    W("sync", tabtok[b], tabrd[b])
                    tabtok[b] = E("sync", SY.dma_start(out=cosb[b][:], in_=rcos[ds, ti]), dma=True)
                    tabtok[b] = E("sync", SY.dma_start(out=sinb[b][:], in_=rsin[ds, ti]), dma=True)
                W("tensor", kvwtok[sset][0], kvwtok[sset][1], ktmptok[b])
                tkv = None
                for ec in range(16):
                    buf = kvw[sset][ec // 8]
                    tkv = E("tensor", TE.matmul(
                        psKV[b][:], fmB[:, ec * 1024 + ti * 128: ec * 1024 + ti * 128 + 128],
                        buf[:, (ec % 8) * 512: (ec % 8) * 512 + 512],
                        start=(ec == 0), stop=(ec == 15)))
                kvwrd[sset] = tkv
                W("vector", tkv, t_biaskv_l, *ktmprd[b])
                ktmprd[b] = []
                t_kt = E("vector", VE.tensor_tensor(out=ktmp[b][:], in0=psKV[b][:],
                                                    in1=bias_kv[:], op=ALU.add))
                ktmptok[b] = t_kt
                t_biaskv = t_kt
                sb_ = b
                W("scalar", t_kt, ssptok[sb_], c1tok)
                t_sq = E("scalar", SC.activation(out=c1b[:], in_=ktmp[b][:], func=AF.Square,
                                                 accum_out=ssp[sb_][:]))
                ktmprd[b].append(t_sq)
                ss_dst = ssK if kv == 0 else ssV
                W("vector", t_sq)
                if ds == 0:
                    tss = E("vector", VE.tensor_copy(ss_dst[:, ti:ti + 1], ssp[sb_][:]))
                else:
                    tss = E("vector", VE.tensor_tensor(out=ss_dst[:, ti:ti + 1],
                                                       in0=ss_dst[:, ti:ti + 1],
                                                       in1=ssp[sb_][:], op=ALU.add))
                ssptok[sb_] = tss
                if ds < 2:
                    W("vector", t_kt, tabtok[b], c2tok, t_sq)
                    ts1 = E("vector", VE.tensor_copy(c2b[:, 0:512:2], ktmp[b][:, 1:512:2]))
                    ts2 = E("vector", VE.tensor_copy(c2b[:, 1:512:2], ktmp[b][:, 0:512:2]))
                    tc1 = E("vector", VE.tensor_tensor(out=c1b[:], in0=ktmp[b][:],
                                                       in1=cosb[b][:], op=ALU.mult))
                    c1tok = tc1
                    ktmprd[b].append(tc1)
                    tabrd[b] = tc1
                    W("vector", ts1, ts2)
                    tc2 = E("vector", VE.tensor_tensor(out=c2b[:], in0=c2b[:],
                                                       in1=sinb[b][:], op=ALU.mult))
                    c2tok = tc2
                    tabrd[b] = tc2
                    W("vector", tc1, tc2)
                    if kv == 1:
                        W("vector", vsttok[ti])
                        t_rot = E("vector", VE.tensor_tensor(
                            out=Vst[:, ti * 512:(ti + 1) * 512], in0=c1b[:], in1=c2b[:],
                            op=ALU.add))
                        vsttok[ti] = t_rot
                        c1tok = t_rot
                        c2tok = t_rot
                    else:
                        W("vector", kbtok)
                        t_kb = E("vector", VE.tensor_tensor(out=Kb[:], in0=c1b[:],
                                                            in1=c2b[:], op=ALU.add))
                        c1tok = t_kb
                        c2tok = t_kb
                        W("vector", t_kb, vsttok[ti], ppdma[b])
                        t_pp = E("vector", VE.tensor_tensor(
                            out=ppst[b][:], in0=Vst[:, ti * 512:(ti + 1) * 512],
                            in1=Kb[:], op=ALU.mult))
                        kbtok = t_pp
                        vsttok[ti] = t_pp
                        W("sync", t_pp)
                        ppdma[b] = E("sync", SY.dma_start(
                            out=ppd.ap()[ti, :, ds * 512:(ds + 1) * 512], in_=ppst[b][:]),
                            dma=True)
                else:
                    if kv == 1:
                        W("scalar", t_kt, vsttok[ti], t_sq)
                        t_cp = E("scalar", SC.copy(out=Vst[:, ti * 512:(ti + 1) * 512],
                                                   in_=ktmp[b][:]))
                        vsttok[ti] = t_cp
                        ktmprd[b].append(t_cp)
                    else:
                        W("scalar", t_kt, kbtok, t_sq)
                        t_kb = E("scalar", SC.copy(out=Kb[:], in_=ktmp[b][:]))
                        ktmprd[b].append(t_kb)
                        W("vector", t_kb, vsttok[ti], ppdma[b])
                        t_pp = E("vector", VE.tensor_tensor(
                            out=ppst[b][:], in0=Vst[:, ti * 512:(ti + 1) * 512],
                            in1=Kb[:], op=ALU.mult))
                        kbtok = t_pp
                        vsttok[ti] = t_pp
                        W("sync", t_pp)
                        ppdma[b] = E("sync", SY.dma_start(
                            out=ppd.ap()[ti, :, ds * 512:(ds + 1) * 512], in_=ppst[b][:]),
                            dma=True)
    t_pp_all = p.tok("sync")
    st7.close()

    stA2 = contextlib.ExitStack()
    ppb = [sb(f"ppb{i}", [128, 2048], BF16, st=stA2) for i in range(2)]
    psA = ps("psA", [128, 16], st=stA2)
    fence("sync")
    ppbtok = [None, None]
    scoltok = [None, None]
    tA = None
    for ti in range(8):
        b = ti % 2
        W("sync", t_pp_all, ppbtok[b])
        t_ld = E("sync", SY.dma_start(out=ppb[b][:], in_=ppd.ap()[ti]), dma=True)
        W("vector", p.tok("vector") if ti == 0 else None)
        tm = E("vector", VE.tensor_tensor(out=tmp1[:], in0=ssK[:, ti:ti + 1],
                                          in1=ssV[:, ti:ti + 1], op=ALU.mult))
        W("scalar", tm)
        tsq = E("scalar", SC.activation(out=tmp2[:], in_=tmp1[:], func=AF.Sqrt))
        W("vector", tsq)
        tc = E("vector", VE.reciprocal(tmp3[:], tmp2[:]))
        W("vector", tc, scoltok[b])
        tsc = E("vector", VE.tensor_copy(scol[:, b:b + 1], tmp3[:]))
        W("tensor", tsc, t_ld)
        for ch in range(16):
            tA = E("tensor", TE.matmul(psA[:, ch:ch + 1], ppb[b][:, ch * 128:(ch + 1) * 128],
                                       scol[:, b:b + 1], start=(ti == 0), stop=(ti == 7)))
        ppbtok[b] = tA
        scoltok[b] = tA
    W("vector", tA)
    t_Ar = E("vector", VE.tensor_copy(A_raw[:], psA[:]))
    W("sync", t_Ar)
    t_As = E("sync", SY.dma_start(out=arA_in.ap(), in_=A_raw[:]), dma=True)
    W("tensor", p.tok("vector"))
    stA2.close()

    stB2 = contextlib.ExitStack()
    Q2T = sb("Q2T", [128, 16384], BF16, st=stB2)
    st8a = contextlib.ExitStack()
    qwb = [sb(f"qwb{i}", [128, 2048], BF16, st=st8a) for i in range(2)]
    psQ2 = [ps(f"psQ2{i}", [128, 512], st=st8a) for i in range(2)]
    fence("gpsimd")
    pre_q = [E("gpsimd", GP.dma_start(out=qwb[i][:], in_=w1q[i]), dma=True)
             for i in range(2)]
    W("gpsimd", pre_q[1], t_As)
    t_arc = E("gpsimd", GP.collective_compute(
        "AllReduce", ALU.add, replica_groups=RG,
        ins=[arA_in.ap()], outs=[arA_out.ap()]), cc=True)
    qwtok = [None, None]
    q2tok = [None, None]
    q2ev = []
    for qch in range(16):
        b = qch % 2
        if qch < 2:
            t_w = pre_q[qch]
        else:
            W("gpsimd", qwtok[b], t_arc)
            t_w = E("gpsimd", GP.dma_start(out=qwb[b][:], in_=w1q[qch]), dma=True)
        for h2 in range(2):
            W("tensor", t_w, q2tok[h2])
            tq = None
            for ec in range(16):
                tq = E("tensor", TE.matmul(psQ2[h2][:], qwb[b][:, ec * 128:(ec + 1) * 128],
                                           fmB[:, ec * 1024 + h2 * 512: ec * 1024 + h2 * 512 + 512],
                                           start=(ec == 0), stop=(ec == 15)))
            qwtok[b] = tq
            W("vector", tq, t_small)
            q2tok[h2] = E("vector", VE.tensor_scalar(
                out=Q2T[:, qch * 1024 + h2 * 512: qch * 1024 + h2 * 512 + 512],
                in0=psQ2[h2][:], scalar1=b1qs[:, qch:qch + 1], scalar2=None,
                op0=ALU.add))
            q2ev.append(q2tok[h2])
    W("sync", t_arc)
    t_Al = E("sync", SY.dma_start(out=A_raw[:], in_=arA_out.ap()), dma=True)
    W("vector", t_Al)
    t_A1 = E("vector", VE.tensor_scalar(out=A1[:], in0=A_raw[:], scalar1=0.5, scalar2=0.5,
                                        op0=ALU.mult, op1=ALU.add))
    sc_toks = []
    for qch in range(16):
        W("vector", t_A1, q2ev[2 * qch], q2ev[2 * qch + 1])
        sc_toks.append(E("vector", VE.tensor_scalar(
            out=Q2T[:, qch * 1024:(qch + 1) * 1024], in0=Q2T[:, qch * 1024:(qch + 1) * 1024],
            scalar1=A1[:, qch:qch + 1], scalar2=None, op0=ALU.mult)))
    W("tensor", p.tok("vector"))
    st8a.close()
    st8b = contextlib.ExitStack()
    owb = [sb(f"owb{i}", [128, 2048], BF16, st=st8b) for i in range(2)]
    ytmp = [sb(f"ytmp{i}", [128, 512], F32, st=st8b) for i in range(2)]
    b1ob = [sb(f"b1ob{i}", [128, 512], F32, st=st8b) for i in range(2)]
    psY = [ps(f"psY{i}", [128, 512], st=st8b) for i in range(8)]
    fence("gpsimd")
    fence("sync")
    owtok = [None, None]
    btok = [None, None]
    ytok = [None] * 8
    for cs in range(4):
        cb = cs % 2
        W("sync", btok[cb])
        t_b = E("sync", SY.dma_start(out=b1ob[cb][:], in_=b1o_b[cs]), dma=True)
        for qg in range(4):
            b = qg % 2
            W("gpsimd", owtok[b])
            t_w = E("gpsimd", GP.dma_start(
                out=owb[b][:], in_=w1o[cs][:, qg * 2048:(qg + 1) * 2048]), dma=True)
            W("tensor", t_w)
            tY = None
            for ti in range(8):
                if qg == 0:
                    W("tensor", ytok[ti])
                for k in range(4):
                    qch = qg * 4 + k
                    tY = E("tensor", TE.matmul(
                        psY[ti][:],
                        Q2T[:, qch * 1024 + ti * 128: qch * 1024 + ti * 128 + 128],
                        owb[b][:, k * 512:(k + 1) * 512],
                        start=(qg == 0 and k == 0), stop=(qg == 3 and k == 3)))
            owtok[b] = tY
        t_fin = p.tok("tensor")
        for ti in range(8):
            yb = ti % 2
            W("vector", t_fin, t_b, ytok[ti])
            t_y1 = E("vector", VE.tensor_tensor(out=ytmp[yb][:], in0=psY[ti][:],
                                                in1=b1ob[cb][:], op=ALU.add))
            xsl = x_sb[:, ti * 2048 + cs * 512: ti * 2048 + cs * 512 + 512]
            W("vector", t_y1)
            ytok[ti] = E("vector", VE.tensor_tensor(out=xsl, in0=xsl, in1=ytmp[yb][:],
                                                    op=ALU.add))
        btok[cb] = ytok[7]
    W("tensor", p.tok("vector"))
    st8b.close()
    stB2.close()
    stB.close()

    st9 = contextlib.ExitStack()
    H1 = sb("H1", [128, 8192], BF16, st=st9)
    g1b = [sb(f"g1b{i}", [128, 2048], BF16, st=st9) for i in range(2)]
    u1b = [sb(f"u1b{i}", [128, 2048], BF16, st=st9) for i in range(2)]
    sgb2 = [sb(f"sgb2{i}", [128, 512], F32, st=st9) for i in range(2)]
    d1b = sb("d1b", [128, 16384], BF16, st=st9)
    psG1 = [ps(f"psGm{i}", [128, 512], st=st9) for i in range(2)]
    psU1 = [ps(f"psUm{i}", [128, 512], st=st9) for i in range(2)]
    psM = [ps(f"psM{i}", [128, 512], st=st9) for i in range(2)]
    fence("gpsimd")
    g1tok = [None, None]
    u1tok = [None, None]
    sg1tok = [None, None]
    h1tok = [None, None]
    d1tok = None
    mevtok = [None, None]
    for e in range(8):
        t_h1_all = []
        for ic in range(8):
            b = ic % 2
            W("gpsimd", g1tok[b])
            t_g = E("gpsimd", GP.dma_start(out=g1b[b][:], in_=g1[e, :, ic, :]), dma=True)
            W("gpsimd", u1tok[b])
            t_u = E("gpsimd", GP.dma_start(out=u1b[b][:], in_=u1[e, :, ic, :]), dma=True)
            for th in range(2):
                pb = (ic * 2 + th) % 2
                W("tensor", t_g, sg1tok[pb])
                tG = None
                for hc in range(16):
                    tG = E("tensor", TE.matmul(psG1[pb][:], g1b[b][:, hc * 128:(hc + 1) * 128],
                                               fmA[:, hc * 1024 + th * 512: hc * 1024 + th * 512 + 512],
                                               start=(hc == 0), stop=(hc == 15)))
                g1tok[b] = tG
                W("scalar", tG, h1tok[pb])
                t_sg = E("scalar", SC.activation(out=sgb2[pb][:], in_=psG1[pb][:], func=AF.Silu))
                sg1tok[pb] = t_sg
                W("tensor", t_u, h1tok[pb])
                tU = None
                for hc in range(16):
                    tU = E("tensor", TE.matmul(psU1[pb][:], u1b[b][:, hc * 128:(hc + 1) * 128],
                                               fmA[:, hc * 1024 + th * 512: hc * 1024 + th * 512 + 512],
                                               start=(hc == 0), stop=(hc == 15)))
                u1tok[b] = tU
                W("vector", tU, t_sg)
                th1 = E("vector", VE.tensor_tensor(
                    out=H1[:, ic * 1024 + th * 512: ic * 1024 + th * 512 + 512],
                    in0=sgb2[pb][:], in1=psU1[pb][:], op=ALU.mult))
                h1tok[pb] = th1
                t_h1_all.append(th1)
        W("gpsimd", d1tok)
        t_d = E("gpsimd", GP.dma_start(
            out=d1b[:].rearrange("p (a m) -> p a m", a=8), in_=d1[e]), dma=True)
        for ti in range(8):
            for cs in range(4):
                mb = (ti * 4 + cs) % 2
                W("tensor", t_d, mevtok[mb], *t_h1_all)
                t_h1_all = []
                tM = None
                for icl in range(8):
                    tM = E("tensor", TE.matmul(
                        psM[mb][:],
                        H1[:, icl * 1024 + ti * 128: icl * 1024 + ti * 128 + 128],
                        d1b[:, icl * 2048 + cs * 512: icl * 2048 + cs * 512 + 512],
                        start=(icl == 0), stop=(icl == 7)))
                d1tok = tM
                xsl = x_sb[:, ti * 2048 + cs * 512: ti * 2048 + cs * 512 + 512]
                W("vector", tM)
                mevtok[mb] = E("vector", VE.scalar_tensor_tensor(
                    out=xsl, in0=psM[mb][:], scalar=w_all[:, ti * 12 + e: ti * 12 + e + 1],
                    in1=xsl, op0=ALU.mult, op1=ALU.add))
    W("tensor", p.tok("vector"))
    st9.close()

    W("sync", p.tok("vector"))
    st_toks = []
    for ti in range(8):
        st_toks.append(E("sync", SY.dma_start(out=out_x[ti * 128:(ti + 1) * 128, :],
                                              in_=x_sb[:, ti * 2048:(ti + 1) * 2048]), dma=True))
    W("sync", *st_toks)
    return nc, stack


def _host_layouts(inp):
    L = {}
    L["g0"] = np.ascontiguousarray(
        inp["L0_ffn_gate_w"].reshape(44, 128, 16, 128).transpose(0, 3, 2, 1)).reshape(44, 128, 2048)
    L["u0"] = np.ascontiguousarray(
        inp["L0_ffn_up_w"].reshape(44, 128, 16, 128).transpose(0, 3, 2, 1)).reshape(44, 128, 2048)
    L["d0"] = np.ascontiguousarray(
        inp["L0_ffn_down_w"].T.reshape(44, 128, 2, 1024).transpose(2, 0, 1, 3))
    L["gs0"] = np.ascontiguousarray(inp["L0_ffn_gate_s"].reshape(44, 128).T)
    L["us0"] = np.ascontiguousarray(inp["L0_ffn_up_s"].reshape(44, 128).T)
    L["ds_b"] = np.ascontiguousarray(
        np.broadcast_to(inp["L0_ffn_down_s"].reshape(1, 2048), (128, 2048)))
    W_in = inp["L1_attn_W_in"]
    L["w1q"] = np.ascontiguousarray(
        W_in[:, :2048].reshape(16, 128, 16, 128).transpose(2, 1, 0, 3)).reshape(16, 128, 2048)
    L["w1kv"] = np.ascontiguousarray(
        W_in[:, 2048:].reshape(16, 128, 2, 4, 512).transpose(2, 3, 1, 0, 4)).reshape(2, 4, 128, 8192)
    b_in = inp["L1_attn_b_in"]
    L["b1q"] = np.ascontiguousarray(b_in[:2048].reshape(16, 128).T)
    L["b1kv_b"] = np.ascontiguousarray(
        np.broadcast_to(b_in[2048:].reshape(2, 4, 1, 512), (2, 4, 128, 512)))
    L["w1o"] = np.ascontiguousarray(
        inp["L1_attn_W_out"].reshape(16, 128, 4, 512).transpose(2, 1, 0, 3)).reshape(4, 128, 8192)
    L["b1o_b"] = np.ascontiguousarray(
        np.broadcast_to(inp["L1_attn_b_out"].reshape(4, 1, 512), (4, 128, 512)))
    L["g1a"] = np.ascontiguousarray(inp["L1_attn_gamma"].reshape(16, 128).T)
    L["be1"] = np.ascontiguousarray(inp["L1_attn_beta"].reshape(16, 128).T)
    L["clf"] = np.ascontiguousarray(
        inp["L1_ffn_classifier_w"].T.reshape(16, 128, 12).transpose(1, 0, 2)).reshape(128, 192)
    L["eb_b"] = np.ascontiguousarray(
        np.broadcast_to(inp["L1_ffn_e_score_correction_bias"].reshape(1, 12), (128, 12)))
    L["g1"] = np.ascontiguousarray(
        inp["L1_ffn_gate_w"].reshape(8, 8, 128, 16, 128).transpose(0, 4, 1, 3, 2)).reshape(8, 128, 8, 2048)
    L["u1"] = np.ascontiguousarray(
        inp["L1_ffn_up_w"].reshape(8, 8, 128, 16, 128).transpose(0, 4, 1, 3, 2)).reshape(8, 128, 8, 2048)
    L["d1"] = np.ascontiguousarray(
        inp["L1_ffn_down_w"].reshape(8, 2048, 8, 128).transpose(0, 3, 2, 1)).reshape(8, 128, 8, 2048)
    inv = (10000.0 ** (-(np.arange(0, 1024, 2, dtype=np.float64) / 1024.0)))
    pos = np.arange(4096, dtype=np.float64)
    fr = pos[:, None] * inv[None, :]
    cosr = np.repeat(np.cos(fr), 2, axis=1)
    sinr = np.repeat(np.sin(fr), 2, axis=1)
    sinr[:, 0::2] *= -1.0
    L["_cos_full"] = cosr.astype(np.float32)
    L["_sin_full"] = sinr.astype(np.float32)
    return L


def kernel(**inp):
    inp = {k: np.asarray(v) for k, v in inp.items()}
    L = _host_layouts(inp)
    x = inp["L0_attn_x"]
    Wq, bq = inp["L0_attn_Wq"], inp["L0_attn_bq"]
    Wk, bk = inp["L0_attn_Wk"], inp["L0_attn_bk"]
    Wv, bv = inp["L0_attn_Wv"], inp["L0_attn_bv"]
    shared_keys = ["g0", "u0", "d0", "gs0", "us0", "ds_b", "w1q", "w1kv", "b1q",
                   "b1kv_b", "w1o", "b1o_b", "g1a", "be1", "clf", "eb_b",
                   "g1", "u1", "d1"]
    in_maps = []
    for c in range(8):
        b, j = c // 4, c % 4
        m = {k: L[k] for k in shared_keys}
        m["x_own"] = np.ascontiguousarray(x[b, j * 1024:(j + 1) * 1024, :])
        m["wq_o"] = np.ascontiguousarray(
            Wq[4 * j:4 * j + 4].reshape(4, 16, 128, 128).transpose(2, 0, 1, 3)).reshape(128, 8192)
        m["bq_o"] = np.ascontiguousarray(bq[4 * j:4 * j + 4].T)
        m["wk_o"] = np.ascontiguousarray(
            Wk[j].reshape(16, 128, 128).transpose(1, 0, 2)).reshape(128, 2048)
        m["wv_o"] = np.ascontiguousarray(
            Wv[j].reshape(16, 128, 128).transpose(1, 0, 2)).reshape(128, 2048)
        m["bk_o"] = bk[j].reshape(128, 1).copy()
        m["bv_b"] = np.ascontiguousarray(np.broadcast_to(bv[j].reshape(1, 128), (128, 128)))
        tt = np.arange(32)[None, :]
        pp = np.arange(128)[:, None]
        m["oidx"] = ((tt // 8) * 2048 + ((tt % 4) * 128 + pp) * 4 + j).astype(np.int32)
        tsl = slice(j * 1024, (j + 1) * 1024)
        m["rcos"] = np.ascontiguousarray(
            L["_cos_full"][tsl].reshape(8, 128, 2, 512).transpose(2, 0, 1, 3))
        m["rsin"] = np.ascontiguousarray(
            L["_sin_full"][tsl].reshape(8, 128, 2, 512).transpose(2, 0, 1, 3))
        in_maps.append(m)
    nc, stack = build()
    res = run_bass_kernel_spmd(nc, in_maps, core_ids=list(range(8)))
    stack.close()
    out = np.empty((2, 4096, 2048), dtype=np.float32)
    for c in range(8):
        b, j = c // 4, c % 4
        out[b, j * 1024:(j + 1) * 1024, :] = res.results[c]["out_x"]
    return out


# version 16 (final)
# speedup vs baseline: 1.1138x; 1.0044x over previous
import contextlib
import numpy as np
import ml_dtypes
import concourse.bass as bass
import concourse.mybir as mybir
from concourse.bass_utils import run_bass_kernel_spmd

F32 = mybir.dt.float32
BF16 = mybir.dt.bfloat16
I32 = mybir.dt.int32
AF = mybir.ActivationFunctionType
ALU = mybir.AluOpType
AX = mybir.AxisListType
BF = ml_dtypes.bfloat16
RG = [[0, 1, 2, 3], [4, 5, 6, 7]]
ENGS = ["sync", "scalar", "vector", "tensor", "gpsimd"]
RSQRT_SCALE = float(1.0 / np.sqrt(128.0))


class P:

    def __init__(self, nc, stack):
        self.nc = nc
        self.sems = {e: stack.enter_context(nc.semaphore(f"S_{e}")) for e in ENGS}
        self.cnt = {e: 0 for e in ENGS}
        self.lastw = {}
        self.eng = {e: getattr(nc, e) for e in ENGS}

    def emit(self, ename, instr, dma=False, cc=False):
        n = 16 if dma else 1
        instr.then_inc(self.sems[ename], n)
        self.cnt[ename] += n
        return (ename, self.cnt[ename], dma or cc)

    def wait(self, consumer, *tokens):
        for t in tokens:
            if t is None:
                continue
            pe, v, _ = t
            key = (consumer, pe)
            if self.lastw.get(key, -1) >= v:
                continue
            self.eng[consumer].wait_ge(self.sems[pe], v)
            self.lastw[key] = v

    def tok(self, ename):
        return (ename, self.cnt[ename], False)


def build():
    nc = bass.Bass("TRN2", target_bir_lowering=False)

    def din(name, shape, dt=F32):
        return nc.dram_tensor(name, shape, dt, kind="ExternalInput").ap()

    x_own = din("x_own", [1024, 2048])
    wq_o = din("wq_o", [128, 8192])
    wk_o = din("wk_o", [128, 2048])
    wv_o = din("wv_o", [128, 2048])
    bq_o = din("bq_o", [128, 4])
    bk_o = din("bk_o", [128, 1])
    bv_b = din("bv_b", [128, 128])
    oidx = din("oidx", [128, 32], I32)
    g0 = din("g0", [44, 128, 2048])
    u0 = din("u0", [44, 128, 2048])
    d0 = din("d0", [2, 44, 128, 1024])
    gs0 = din("gs0", [128, 44])
    us0 = din("us0", [128, 44])
    ds_b = din("ds_b", [128, 2048])
    w1q = din("w1q", [16, 128, 2048])
    b1q = din("b1q", [128, 16])
    w1kv = din("w1kv", [2, 4, 128, 8192])
    b1kv_b = din("b1kv_b", [2, 4, 128, 512])
    w1o = din("w1o", [4, 128, 8192])
    b1o_b = din("b1o_b", [4, 128, 512])
    g1a_i = din("g1a", [128, 16])
    be1_i = din("be1", [128, 16])
    clf = din("clf", [128, 192])
    eb_b = din("eb_b", [128, 12])
    g1 = din("g1", [8, 128, 8, 2048])
    u1 = din("u1", [8, 128, 8, 2048])
    d1 = din("d1", [8, 128, 8, 2048])
    rcos = din("rcos", [2, 8, 128, 512])
    rsin = din("rsin", [2, 8, 128, 512])

    out_x = nc.dram_tensor("out_x", [1024, 2048], F32, kind="ExternalOutput").ap()

    ag_in = nc.dram_tensor("ag_in", [2048, 1024], BF16)
    ag_out = nc.dram_tensor("ag_out", [8192, 1024], BF16)
    rs_in = [nc.dram_tensor(f"rs_in{m}", [8192, 512], BF16) for m in range(2)]
    rs_out = [nc.dram_tensor(f"rs_out{m}", [2048, 512], BF16) for m in range(2)]
    arA_in = nc.dram_tensor("arA_in", [128, 16], F32)
    arA_out = nc.dram_tensor("arA_out", [128, 16], F32)
    ppd = nc.dram_tensor("ppd", [8, 128, 2048], BF16)

    stack = contextlib.ExitStack()
    p = P(nc, stack)
    E, W = p.emit, p.wait
    SY, SC, VE, TE, GP = nc.sync, nc.scalar, nc.vector, nc.tensor, nc.gpsimd

    def sb(name, shape, dt=F32, st=stack):
        return st.enter_context(nc.sbuf_tensor(name, shape, dt))

    def ps(name, shape, dt=F32, st=stack):
        return st.enter_context(nc.psum_tensor(name, shape, dt))

    def fence(q, engs=("tensor", "vector", "scalar")):
        W(q, *[p.tok(e) for e in engs])

    x_sb = sb("x_sb", [128, 8 * 2048])
    fmA = sb("fmA", [128, 16384], BF16)
    masks = sb("masks", [128, 4 * 512], BF16)
    ident = sb("ident", [128, 128], BF16)
    bqs = sb("bqs", [128, 4]); bks = sb("bks", [128, 1])
    bvbc = sb("bvbc", [128, 128]); oix = sb("oix", [128, 32], I32)
    gss = sb("gss", [128, 44]); uss = sb("uss", [128, 44])
    b1qs = sb("b1qs", [128, 16])
    g1as = sb("g1as", [128, 16]); be1s = sb("be1s", [128, 16])
    clfs = sb("clfs", [128, 192], BF16); ebbc = sb("ebbc", [128, 12])
    stats_m = sb("stats_m", [128, 8]); stats_r = sb("stats_r", [128, 8])
    stats_v = sb("stats_v", [128, 8])
    s2c = sb("s2c", [128, 8]); o2c = sb("o2c", [128, 8])
    w_all = sb("w_all", [128, 96])
    A1 = sb("A1", [128, 16]); A_raw = sb("A_raw", [128, 16])
    ssK = sb("ssK", [128, 8]); ssV = sb("ssV", [128, 8])
    scol = sb("scol", [128, 2], BF16)
    tmp1 = sb("tmp1", [128, 1]); tmp2 = sb("tmp2", [128, 1]); tmp3 = sb("tmp3", [128, 1])
    wks = sb("wks", [128, 2048], BF16)
    wvs = sb("wvs", [128, 2048], BF16)
    st_i = contextlib.ExitStack()
    iota = sb("iota", [128, 512], I32, st=st_i)

    t_x = [E("sync", SY.dma_start(out=x_sb[:, ti * 2048:(ti + 1) * 2048],
                                  in_=x_own[ti * 128:(ti + 1) * 128, :]), dma=True)
           for ti in range(8)]
    smalls = [(bqs, bq_o), (bks, bk_o), (bvbc, bv_b), (gss, gs0), (uss, us0),
              (b1qs, b1q), (g1as, g1a_i), (be1s, be1_i), (ebbc, eb_b)]
    t_small = None
    for buf, src in smalls:
        t_small = E("sync", SY.dma_start(out=buf[:], in_=src), dma=True)
    t_oix = E("sync", SY.dma_start(out=oix[:], in_=oidx), dma=True)
    t_clf = E("gpsimd", GP.dma_start(out=clfs[:], in_=clf), dma=True)
    t_io = E("gpsimd", GP.iota(out=iota[:], pattern=[[-1, 512]], base=0,
                               channel_multiplier=1))
    W("vector", t_io)
    t_id = E("vector", VE.tensor_scalar(out=ident[:], in0=iota[:, 0:128], scalar1=0.0,
                                        scalar2=None, op0=ALU.is_equal))
    t_mk = []
    for jm in range(4):
        t_mk.append(E("vector", VE.tensor_scalar(
            out=masks[:, jm * 512:(jm + 1) * 512], in0=iota[:], scalar1=float(-128 * jm),
            scalar2=-1e30, op0=ALU.is_gt, op1=ALU.mult)))
    st_i.close()

    def ln_stats(ti):
        xt = x_sb[:, ti * 2048:(ti + 1) * 2048]
        t_sum = E("vector", VE.reduce_sum(out=tmp1[:], in_=xt, axis=AX.X))
        t_ssq = E("scalar", SC.activation(out=sq_scratch[0][:], in_=xt, func=AF.Square,
                                          accum_out=tmp2[:]))
        W("vector", t_sum)
        t_m = E("vector", VE.tensor_scalar(out=stats_m[:, ti:ti + 1], in0=tmp1[:],
                                           scalar1=1.0 / 2048.0, scalar2=None, op0=ALU.mult))
        W("vector", t_m, t_ssq)
        t_a = E("vector", VE.tensor_tensor(out=tmp1[:], in0=stats_m[:, ti:ti + 1],
                                           in1=stats_m[:, ti:ti + 1], op=ALU.mult))
        t_b = E("vector", VE.tensor_scalar(out=tmp3[:], in0=tmp2[:],
                                           scalar1=1.0 / 2048.0, scalar2=None, op0=ALU.mult))
        W("vector", t_a, t_b)
        t_v = E("vector", VE.tensor_tensor(out=stats_v[:, ti:ti + 1], in0=tmp3[:],
                                           in1=tmp1[:], op=ALU.subtract))
        W("vector", t_v)
        t_e = E("vector", VE.tensor_scalar(out=tmp1[:], in0=stats_v[:, ti:ti + 1],
                                           scalar1=1e-5, scalar2=None, op0=ALU.add))
        W("scalar", t_e)
        t_s = E("scalar", SC.activation(out=tmp2[:], in_=tmp1[:], func=AF.Sqrt))
        W("vector", t_s)
        t_r = E("vector", VE.reciprocal(stats_r[:, ti:ti + 1], tmp2[:]))
        return t_r, t_ssq

    def build_fmT(dest_fn, stage_buf, t_stage, tr_pss):
        toks = []
        t_tr = None
        for ec in range(16):
            b = ec % len(tr_pss)
            tp, t_ev_prev = tr_pss[b]
            W("tensor", t_stage, t_ev_prev, t_id)
            t_tr = E("tensor", TE.transpose(tp[:], stage_buf[:, ec * 128:(ec + 1) * 128],
                                            ident[:]))
            t_ev = dest_fn(tp, ec, t_tr)
            tr_pss[b] = (tp, t_ev)
            toks.append(t_ev)
        return toks, t_tr

    stA = contextlib.ExitStack()
    xn_stage = [sb(f"xn_st{i}", [128, 2048], BF16, st=stA) for i in range(2)]
    sq_scratch = [sb("sq_scr", [128, 2048], BF16, st=stA)]
    zbuf = sb("zbuf", [128, 2048], BF16, st=stA)
    tr_ps_l = [(ps(f"trp{i}", [128, 128], BF16, st=stA), None) for i in range(4)]
    t_zb = E("gpsimd", GP.memset(zbuf[:], 0.0))
    t_zf = None
    for m in range(2):
        for k in range(16):
            t_zf = E("gpsimd", GP.dma_start(
                out=rs_in[m].ap()[k * 512:(k + 1) * 512, :].rearrange(
                    "(a p) m -> p a m", p=128),
                in_=zbuf[:].rearrange("p (a m) -> p a m", a=4)), dma=True)
    t_fmA_done = []
    stage_rd = [None, None]
    for ti in range(8):
        W("vector", t_x[ti])
        W("scalar", t_x[ti])
        t_r, _ = ln_stats(ti)
        sbuf_st = xn_stage[ti % 2]
        W("vector", t_r, stage_rd[ti % 2])
        t_st = E("vector", VE.tensor_scalar(out=sbuf_st[:],
                                            in0=x_sb[:, ti * 2048:(ti + 1) * 2048],
                                            scalar1=stats_m[:, ti:ti + 1],
                                            scalar2=stats_r[:, ti:ti + 1],
                                            op0=ALU.subtract, op1=ALU.mult))

        def evacA(tp, ec, t_tr, ti=ti):
            W("scalar", t_tr)
            return E("scalar", SC.copy(
                out=fmA[:, ec * 1024 + ti * 128: ec * 1024 + ti * 128 + 128], in_=tp[:]))
        toks, t_trl = build_fmT(evacA, sbuf_st, t_st, tr_ps_l)
        t_fmA_done += toks
        stage_rd[ti % 2] = t_trl
    W("sync", *t_fmA_done)
    t_agst = []
    for ec in range(16):
        t_agst.append(E("sync", SY.dma_start(
            out=ag_in.ap()[ec * 128:(ec + 1) * 128, :],
            in_=fmA[:, ec * 1024:(ec + 1) * 1024]), dma=True))
    stA.close()
    agc = [None]

    stF = contextlib.ExitStack()
    H = sb("H", [128, 44 * 512], BF16, st=stF)
    gbuf = [sb(f"gb{i}", [128, 2048], BF16, st=stF) for i in range(2)]
    ubuf = [sb(f"ub{i}", [128, 2048], BF16, st=stF) for i in range(2)]
    sgb = [sb(f"sg{i}", [128, 512], F32, st=stF) for i in range(2)]
    dbuf = [sb(f"db{i}", [128, 4096], BF16, st=stF) for i in range(2)]
    dtmp = [sb("dt0", [128, 1024], F32, st=stF)] * 2
    ds_bc = sb("ds_bc", [128, 2048], st=stF)
    fence("gpsimd")
    W("gpsimd", t_zf)
    fence("sync")
    t_ds = E("sync", SY.dma_start(out=ds_bc[:], in_=ds_b), dma=True)
    W("tensor", p.tok("scalar"))

    gtok = [None, None]
    utok = [None, None]
    last_sw = [None]

    def ffn_half(th, kick_ag=False):
        stG = contextlib.ExitStack()
        psG = [ps(f"psG{th}{i}", [128, 512], st=stG) for i in range(2)]
        psU = [ps(f"psU{th}{i}", [128, 512], st=stG) for i in range(2)]
        hvtok = [None, None]
        sgtok = [None, None]
        gmt = [None, None]
        umt = [None, None]
        for ic in range(44):
            b = ic % 2
            W("gpsimd", agc[0], gtok[b])
            t_g = E("gpsimd", GP.dma_start(out=gbuf[b][:], in_=g0[ic]), dma=True)
            W("gpsimd", utok[b])
            t_u = E("gpsimd", GP.dma_start(out=ubuf[b][:], in_=u0[ic]), dma=True)
            last_sw[0] = t_u
            W("tensor", t_g, sgtok[b])
            tG = None
            for ec in range(16):
                tG = E("tensor", TE.matmul(psG[b][:], gbuf[b][:, ec * 128:(ec + 1) * 128],
                                           fmA[:, ec * 1024 + th * 512: ec * 1024 + th * 512 + 512],
                                           start=(ec == 0), stop=(ec == 15)))
            gtok[b] = tG
            gmt[b] = tG
            W("scalar", tG, t_small, hvtok[b])
            t_sg = E("scalar", SC.activation(out=sgb[b][:], in_=psG[b][:], func=AF.Silu,
                                             scale=gss[:, ic:ic + 1]))
            sgtok[b] = t_sg
            W("tensor", t_u, hvtok[b])
            tU = None
            for ec in range(16):
                tU = E("tensor", TE.matmul(psU[b][:], ubuf[b][:, ec * 128:(ec + 1) * 128],
                                           fmA[:, ec * 1024 + th * 512: ec * 1024 + th * 512 + 512],
                                           start=(ec == 0), stop=(ec == 15)))
            utok[b] = tU
            W("vector", tU, t_sg)
            hvtok[b] = E("vector", VE.scalar_tensor_tensor(
                out=H[:, ic * 512:(ic + 1) * 512], in0=psU[b][:], scalar=uss[:, ic:ic + 1],
                in1=sgb[b][:], op0=ALU.mult, op1=ALU.mult))
        t_hv = p.tok("vector")
        if kick_ag:
            W("gpsimd", *t_agst)
            W("gpsimd", t_clf, t_zf, last_sw[0])
            agc[0] = E("gpsimd", GP.collective_compute(
                "AllGather", ALU.bypass, replica_groups=RG,
                ins=[ag_in.ap()], outs=[ag_out.ap()]), cc=True)
        W("tensor", p.tok("scalar"), t_hv)
        stG.close()
        stD = contextlib.ExitStack()
        psD = [ps(f"psD{th}{i}", [128, 1024], st=stD) for i in range(4)]
        dtok = [None, None]
        evtok = [None] * 4
        for half in range(2):
            for icb in range(11):
                b = icb % 2
                W("gpsimd", agc[0], dtok[b])
                t_d = E("gpsimd", GP.dma_start(
                    out=dbuf[b][:].rearrange("p (a m) -> p a m", a=4),
                    in_=d0[half, icb * 4: icb * 4 + 4].rearrange("a p m -> p a m")),
                    dma=True)
                W("tensor", t_d)
                tmm = None
                for ici in range(4):
                    ic = icb * 4 + ici
                    for t4 in range(4):
                        if ic == 0:
                            W("tensor", evtok[t4])
                        for h2 in range(2):
                            tmm = E("tensor", TE.matmul(
                                psD[t4][:, h2 * 512:(h2 + 1) * 512],
                                H[:, ic * 512 + t4 * 128: ic * 512 + t4 * 128 + 128],
                                dbuf[b][:, ici * 1024 + h2 * 512: ici * 1024 + h2 * 512 + 512],
                                start=(ic == 0), stop=(ic == 43)))
                dtok[b] = tmm
            t_last_mm = p.tok("tensor")
            for t4 in range(4):
                db = t4 % 2
                xsl = x_sb[:, (th * 4 + t4) * 2048 + half * 1024:
                           (th * 4 + t4) * 2048 + half * 1024 + 1024]
                W("vector", t_last_mm, t_ds, evtok[t4])
                t_e1 = E("vector", VE.tensor_tensor(out=dtmp[db][:], in0=psD[t4][:],
                                                    in1=ds_bc[:, half * 1024:half * 1024 + 1024],
                                                    op=ALU.mult))
                W("vector", t_e1)
                evtok[t4] = E("vector", VE.tensor_tensor(out=xsl, in0=xsl, in1=dtmp[db][:],
                                                         op=ALU.add))
        W("tensor", p.tok("vector"))
        stD.close()

    W("sync", t_zf)
    W("scalar", p.tok("tensor"), p.tok("vector"), t_zf)
    ffn_half(0, kick_ag=True)
    ffn_half(1)
    t_wk = E("gpsimd", GP.dma_start(out=wks[:], in_=wk_o), dma=True)
    t_wv = E("gpsimd", GP.dma_start(out=wvs[:], in_=wv_o), dma=True)
    stF.close()

    stB = contextlib.ExitStack()
    fmB = sb("fmB", [128, 16384], BF16, st=stB)
    stAT0 = contextlib.ExitStack()
    KT = sb("KT", [128, 4096], BF16, st=stAT0)
    V_aug = sb("V_aug", [128, 32 * 129], BF16, st=stAT0)
    fence("gpsimd")
    t_vone = E("gpsimd", GP.memset(V_aug[:], 1.0))

    stAT1 = contextlib.ExitStack()
    wqs = sb("wqs", [128, 8192], BF16, st=stAT1)
    xt_buf = [sb(f"xtb{i}", [128, 4096], BF16, st=stAT1) for i in range(2)]
    t_wq = E("gpsimd", GP.dma_start(out=wqs[:], in_=wq_o), dma=True)
    fence("sync")
    stQ = contextlib.ExitStack()
    psK = ps("psK", [128, 256], st=stQ)
    psQ = [ps(f"psQ{i}", [128, 256], st=stQ) for i in range(2)]
    psV = [ps(f"psV{i}", [128, 128], st=stQ) for i in range(2)]
    ag3 = ag_out.ap().rearrange("(a p) m -> a p m", p=128)
    xttok = [None, None]
    ktok = None
    qtok = [None, None]
    vtok = [None, None]
    for tb2 in range(16):
        r, s = tb2 // 4, tb2 % 4
        b = tb2 % 2
        W("sync", agc[0], xttok[b])
        t_xt = E("sync", SY.dma_start(
            out=xt_buf[b][:].rearrange("p (a m) -> p a m", a=16),
            in_=ag3[r * 16:(r + 1) * 16, :, s * 256:(s + 1) * 256].rearrange("a p m -> p a m")),
            dma=True)
        W("tensor", t_xt, t_wk, ktok)
        tK = None
        for ec in range(16):
            tK = E("tensor", TE.matmul(psK[:], wks[:, ec * 128:(ec + 1) * 128],
                                       xt_buf[b][:, ec * 256:(ec + 1) * 256],
                                       start=(ec == 0), stop=(ec == 15)))
        W("scalar", tK, t_small)
        ktok = E("scalar", SC.activation(out=KT[:, tb2 * 256:(tb2 + 1) * 256], in_=psK[:],
                                         func=AF.Identity, bias=bks[:],
                                         scale=np.float32(1.0)))
        for hl in range(4):
            qb = hl % 2
            W("tensor", t_wq, qtok[qb])
            tQ = None
            for ec in range(16):
                tQ = E("tensor", TE.matmul(psQ[qb][:],
                                           wqs[:, hl * 2048 + ec * 128: hl * 2048 + ec * 128 + 128],
                                           xt_buf[b][:, ec * 256:(ec + 1) * 256],
                                           start=(ec == 0), stop=(ec == 15)))
            W("scalar", tQ)
            qtok[qb] = E("scalar", SC.activation(
                out=fmB[:, hl * 4096 + tb2 * 256: hl * 4096 + tb2 * 256 + 256], in_=psQ[qb][:],
                func=AF.Identity, bias=bqs[:, hl:hl + 1], scale=np.float32(1.0)))
        for t4 in range(2):
            vb = t4 % 2
            W("tensor", t_wv, vtok[vb])
            tV = None
            for ec in range(16):
                tV = E("tensor", TE.matmul(psV[vb][:],
                                           xt_buf[b][:, ec * 256 + t4 * 128: ec * 256 + t4 * 128 + 128],
                                           wvs[:, ec * 128:(ec + 1) * 128],
                                           start=(ec == 0), stop=(ec == 15)))
            kt = tb2 * 2 + t4
            W("vector", tV, t_small, t_vone)
            vtok[vb] = E("vector", VE.tensor_tensor(
                out=V_aug[:, kt * 129: kt * 129 + 128], in0=psV[vb][:], in1=bvbc[:],
                op=ALU.add))
        xttok[b] = p.tok("tensor")
    W("tensor", p.tok("scalar"), p.tok("vector"))
    stQ.close()
    stAT1.close()

    stS = contextlib.ExitStack()
    eTb = [sb(f"eT{i}", [128, 512], BF16, st=stS) for i in range(2)]
    O_q = [sb(f"Oq{i}", [128, 2048], BF16, st=stS) for i in range(2)]
    rzc = [sb(f"rz{i}", [128, 1], F32, st=stS) for i in range(4)]
    ao_wide = sb("ao_wide", [128, 8192], BF16, st=stS)
    xn_st2 = [sb(f"xn2st{i}", [128, 2048], BF16, st=stS) for i in range(2)]
    sq_scratch = [sb("sq_scr2", [128, 2048], BF16, st=stS)]
    tr_ps2 = [(ps(f"trq{i}", [128, 128], BF16, st=stS), None) for i in range(2)]
    stage_rd2 = [None, None]

    def phase5_tile(ti, t_xadd):
        W("vector", t_xadd)
        W("scalar", t_xadd)
        t_r, t_ssq = ln_stats(ti)
        W("vector", t_r)
        t_a = E("vector", VE.tensor_tensor(out=tmp1[:], in0=stats_r[:, ti:ti + 1],
                                           in1=stats_r[:, ti:ti + 1], op=ALU.mult))
        W("vector", t_a)
        t_b = E("vector", VE.tensor_tensor(out=tmp2[:], in0=stats_v[:, ti:ti + 1],
                                           in1=tmp1[:], op=ALU.mult))
        W("vector", t_b)
        t_c = E("vector", VE.tensor_scalar(out=tmp1[:], in0=tmp2[:], scalar1=1e-5,
                                           scalar2=None, op0=ALU.add))
        W("scalar", t_c)
        t_s = E("scalar", SC.activation(out=tmp2[:], in_=tmp1[:], func=AF.Sqrt))
        W("vector", t_s)
        t_r2 = E("vector", VE.reciprocal(tmp3[:], tmp2[:]))
        W("vector", t_r2)
        t_s2 = E("vector", VE.tensor_tensor(out=s2c[:, ti:ti + 1], in0=stats_r[:, ti:ti + 1],
                                            in1=tmp3[:], op=ALU.mult))
        W("vector", t_s2)
        t_o2a = E("vector", VE.tensor_tensor(out=tmp1[:], in0=stats_m[:, ti:ti + 1],
                                             in1=s2c[:, ti:ti + 1], op=ALU.mult))
        W("vector", t_o2a)
        t_o2 = E("vector", VE.tensor_scalar(out=o2c[:, ti:ti + 1], in0=tmp1[:],
                                            scalar1=-1.0, scalar2=None, op0=ALU.mult))
        W("vector", stage_rd2[0])
        t_st = E("vector", VE.tensor_scalar(out=xn_st2[0][:],
                                            in0=x_sb[:, ti * 2048:(ti + 1) * 2048],
                                            scalar1=stats_m[:, ti:ti + 1],
                                            scalar2=stats_r[:, ti:ti + 1],
                                            op0=ALU.subtract, op1=ALU.mult))

        def evacA1(tp, ec, t_tr, ti=ti):
            W("scalar", t_tr)
            return E("scalar", SC.copy(
                out=fmA[:, ec * 1024 + ti * 128: ec * 1024 + ti * 128 + 128], in_=tp[:]))
        _, t_trl = build_fmT(evacA1, xn_st2[0], t_st, tr_ps2)
        stage_rd2[0] = t_trl
        W("vector", t_o2, stage_rd2[1])
        t_st2 = E("vector", VE.tensor_scalar(out=xn_st2[1][:],
                                             in0=x_sb[:, ti * 2048:(ti + 1) * 2048],
                                             scalar1=s2c[:, ti:ti + 1],
                                             scalar2=o2c[:, ti:ti + 1],
                                             op0=ALU.mult, op1=ALU.add))

        def evacB(tp, ec, t_tr, ti=ti):
            W("vector", t_tr, t_small)
            return E("vector", VE.tensor_scalar(
                out=fmB[:, ec * 1024 + ti * 128: ec * 1024 + ti * 128 + 128], in0=tp[:],
                scalar1=g1as[:, ec:ec + 1], scalar2=be1s[:, ec:ec + 1],
                op0=ALU.mult, op1=ALU.add))
        _, t_trl2 = build_fmT(evacB, xn_st2[1], t_st2, tr_ps2)
        stage_rd2[1] = t_trl2
    psS = [ps(f"psS{i}", [128, 512], st=stS) for i in range(2)]
    psAV = [ps(f"psAV{i}", [128, 129], st=stS) for i in range(4)]
    setok = [None, None]
    avtok = [None] * 4
    ettok = [None, None]
    sc_tok = [None, None]
    t_scat = []
    rs6 = [rs_out[m].ap().rearrange("(t f) c -> f t c", f=4) for m in range(2)]
    t_aol = [None] * 16
    t_rs0 = None
    for qc in range(8):
        ob = qc % 2
        ev_toks = []
        for hl in range(4):
            nkc = 4 * qc + 4
            for kc in range(nkc):
                b = kc % 2
                W("tensor", setok[b])
                t_s = E("tensor", TE.matmul(psS[b][:], KT[:, kc * 128:(kc + 1) * 128],
                                            fmB[:, hl * 4096 + qc * 512: hl * 4096 + qc * 512 + 512],
                                            start=True, stop=True))
                if kc >= 4 * qc:
                    jm = kc - 4 * qc
                    W("vector", t_s, t_mk[jm])
                    t_s = E("vector", VE.tensor_tensor(
                        out=psS[b][:], in0=psS[b][:],
                        in1=masks[:, jm * 512:(jm + 1) * 512], op=ALU.add))
                W("scalar", t_s, ettok[b])
                t_e = E("scalar", SC.activation(out=eTb[b][:], in_=psS[b][:], func=AF.Exp,
                                                scale=np.float32(RSQRT_SCALE)))
                setok[b] = t_e
                tAV = None
                W("tensor", t_e)
                for qt in range(4):
                    if kc == 0:
                        W("tensor", avtok[qt])
                    tAV = E("tensor", TE.matmul(psAV[qt][:], eTb[b][:, qt * 128:(qt + 1) * 128],
                                                V_aug[:, kc * 129: kc * 129 + 129],
                                                start=(kc == 0), stop=(kc == nkc - 1)))
                ettok[b] = tAV
            t_fin = p.tok("tensor")
            for qt in range(4):
                W("vector", t_fin, sc_tok[ob] if hl == 0 else None)
                t_rz = E("vector", VE.reciprocal(rzc[qt][:], psAV[qt][:, 128:129]))
                W("vector", t_rz)
                avtok[qt] = E("vector", VE.tensor_scalar(
                    out=O_q[ob][:, qt * 512 + hl * 128: qt * 512 + hl * 128 + 128],
                    in0=psAV[qt][:, 0:128], scalar1=rzc[qt][:], scalar2=None, op0=ALU.mult))
                ev_toks.append(avtok[qt])
        m2 = qc % 2
        W("gpsimd", t_oix, t_zf, *ev_toks)
        if qc == 7:
            W("gpsimd", t_rs0)
        last_sc = None
        for qt in range(4):
            tt_idx = qc * 4 + qt
            last_sc = E("gpsimd", GP.indirect_dma_start(
                out=rs_in[m2].ap(),
                out_offset=bass.IndirectOffsetOnAxis(ap=oix[:, tt_idx:tt_idx + 1], axis=0),
                in_=O_q[ob][:, qt * 512:(qt + 1) * 512], in_offset=None), dma=True)
            t_scat.append(last_sc)
        sc_tok[ob] = last_sc
        if qc == 6:
            W("gpsimd", *t_scat)
            t_rs0 = E("gpsimd", GP.collective_compute(
                "ReduceScatter", ALU.add, replica_groups=RG,
                ins=[rs_in[0].ap()], outs=[rs_out[0].ap()]), cc=True)
            W("sync", t_rs0)
            for sl in range(16):
                ti_, j_ = sl // 4, sl % 4
                t_aol[sl] = E("sync", SY.dma_start(
                    out=ao_wide[:, sl * 512:(sl + 1) * 512],
                    in_=rs6[0][j_, (ti_ % 4) * 128:(ti_ % 4) * 128 + 128, :]), dma=True)
    W("gpsimd", *t_scat)
    W("gpsimd", t_aol[15])
    t_rs1 = E("gpsimd", GP.collective_compute(
        "ReduceScatter", ALU.add, replica_groups=RG,
        ins=[rs_in[1].ap()], outs=[rs_out[1].ap()]), cc=True)
    ao_add = [None] * 16
    for sl in range(16):
        ti_, j_ = sl // 4, sl % 4
        xsl = x_sb[:, ti_ * 2048 + j_ * 512: ti_ * 2048 + j_ * 512 + 512]
        W("vector", t_aol[sl])
        ao_add[sl] = E("vector", VE.tensor_tensor(
            out=xsl, in0=xsl, in1=ao_wide[:, sl * 512:(sl + 1) * 512], op=ALU.add))
    W("sync", t_rs1)
    for sl in range(16):
        ti_, j_ = 4 + sl // 4, sl % 4
        W("sync", ao_add[sl])
        t_aol[sl] = E("sync", SY.dma_start(
            out=ao_wide[:, sl * 512:(sl + 1) * 512],
            in_=rs6[1][j_, (ti_ % 4) * 128:(ti_ % 4) * 128 + 128, :]), dma=True)
    for ti in range(4):
        phase5_tile(ti, ao_add[ti * 4 + 3])
    for ti in range(4, 8):
        for j_ in range(4):
            sl = (ti - 4) * 4 + j_
            xsl = x_sb[:, ti * 2048 + j_ * 512: ti * 2048 + j_ * 512 + 512]
            W("vector", t_aol[sl])
            ao_add[sl] = E("vector", VE.tensor_tensor(
                out=xsl, in0=xsl, in1=ao_wide[:, sl * 512:(sl + 1) * 512], op=ALU.add))
        phase5_tile(ti, ao_add[(ti - 4) * 4 + 3])
    W("tensor", p.tok("scalar"), p.tok("vector"))
    stS.close()
    stAT0.close()


    st7 = contextlib.ExitStack()
    kvw = [[sb(f"kvw{s}{i}", [128, 4096], BF16, st=st7) for i in range(2)]
           for s in range(2)]
    cosb = [sb(f"cosb{i}", [128, 512], F32, st=st7) for i in range(2)]
    sinb = [sb(f"sinb{i}", [128, 512], F32, st=st7) for i in range(2)]
    ktmp = [sb(f"ktmp{i}", [128, 512], F32, st=st7) for i in range(2)]
    c1b = sb("c1b", [128, 512], F32, st=st7)
    c2b = sb("c2b", [128, 512], F32, st=st7)
    bias_kv = sb("bias_kv", [128, 512], F32, st=st7)
    Vst = sb("Vst", [128, 4096], BF16, st=st7)
    Kb = sb("Kb", [128, 512], BF16, st=st7)
    ppst = [sb(f"ppst{i}", [128, 512], BF16, st=st7) for i in range(2)]
    ssp = [sb(f"ssp{i}", [128, 1], F32, st=st7) for i in range(2)]
    psKV = [ps(f"psKV{i}", [128, 512], st=st7) for i in range(2)]
    kvwtok = [[None, None], [None, None]]
    kvwrd = [None, None]
    kvpass = 0
    fence("gpsimd")
    W("gpsimd", t_rs1)
    for pidx, kvp in enumerate([1, 0]):
        for half in range(2):
            kvwtok[pidx][half] = E("gpsimd", GP.dma_start(
                out=kvw[pidx][half][:],
                in_=w1kv[kvp, 0][:, half * 4096: half * 4096 + 4096]), dma=True)

    st6 = contextlib.ExitStack()
    e_r = sb("e_r", [128, 12], st=st7)
    probs = sb("probs", [128, 12], st=st7)
    biased = sb("biased", [128, 12], st=st7)
    mk1 = sb("mk1", [128, 12], st=st7)
    b2v = sb("b2v", [128, 12], st=st7)
    psL = ps("psL", [128, 12], st=st6)
    t_ltok = None
    for ti in range(8):
        W("tensor", t_clf, t_ltok)
        tL = None
        for ec in range(16):
            tL = E("tensor", TE.matmul(psL[:], fmA[:, ec * 1024 + ti * 128: ec * 1024 + ti * 128 + 128],
                                       clfs[:, ec * 12:(ec + 1) * 12],
                                       start=(ec == 0), stop=(ec == 15)))
        W("vector", tL)
        t1 = E("vector", VE.reduce_max(out=tmp1[:], in_=psL[:], axis=AX.X, negate=True))
        W("scalar", t1)
        t2 = E("scalar", SC.activation(out=e_r[:], in_=psL[:], func=AF.Exp, bias=tmp1[:],
                                       scale=np.float32(1.0), accum_out=tmp2[:]))
        t_ltok = t2
        W("vector", t2)
        t3 = E("vector", VE.reciprocal(tmp3[:], tmp2[:]))
        W("vector", t3)
        t4 = E("vector", VE.tensor_scalar(out=probs[:], in0=e_r[:], scalar1=tmp3[:],
                                          scalar2=None, op0=ALU.mult))
        W("vector", t4, t_small)
        t5 = E("vector", VE.tensor_tensor(out=biased[:], in0=probs[:], in1=ebbc[:], op=ALU.add))
        W("vector", t5)
        t6 = E("vector", VE.reduce_max(out=tmp1[:], in_=biased[:], axis=AX.X))
        W("vector", t6)
        t7 = E("vector", VE.tensor_scalar(out=mk1[:], in0=biased[:], scalar1=tmp1[:],
                                          scalar2=None, op0=ALU.is_ge))
        W("vector", t7)
        t8 = E("vector", VE.scalar_tensor_tensor(out=b2v[:], in0=mk1[:], scalar=-1e30,
                                                 in1=biased[:], op0=ALU.mult, op1=ALU.add))
        W("vector", t8)
        t9 = E("vector", VE.reduce_max(out=tmp1[:], in_=b2v[:], axis=AX.X))
        W("vector", t9)
        t10 = E("vector", VE.tensor_scalar(out=b2v[:], in0=b2v[:], scalar1=tmp1[:],
                                           scalar2=None, op0=ALU.is_ge))
        W("vector", t10)
        t11 = E("vector", VE.tensor_tensor(out=mk1[:], in0=mk1[:], in1=b2v[:], op=ALU.add))
        W("vector", t11)
        t12 = E("vector", VE.scalar_tensor_tensor(out=w_all[:, ti * 12:(ti + 1) * 12],
                                                  in0=probs[:], scalar=1.5, in1=mk1[:],
                                                  op0=ALU.mult, op1=ALU.mult))
        W("vector", t12)
        t13 = E("vector", VE.reduce_sum(out=tmp1[:], in_=w_all[:, ti * 12 + 8: ti * 12 + 12],
                                        axis=AX.X))
        W("vector", t13)
        t14 = E("vector", VE.tensor_scalar(out=tmp2[:], in0=tmp1[:], scalar1=1.0,
                                           scalar2=None, op0=ALU.add))
        W("vector", t14)
        t15 = E("vector", VE.tensor_tensor(out=tmp1[:], in0=tmp2[:], in1=stats_r[:, ti:ti + 1],
                                           op=ALU.mult))
        W("vector", t15)
        t16 = E("vector", VE.tensor_scalar(out=tmp2[:], in0=tmp1[:], scalar1=1.0,
                                           scalar2=None, op0=ALU.add))
        t17 = E("vector", VE.tensor_tensor(out=tmp3[:], in0=stats_m[:, ti:ti + 1],
                                           in1=tmp1[:], op=ALU.mult))
        W("vector", t16, t17)
        xt = x_sb[:, ti * 2048:(ti + 1) * 2048]
        E("vector", VE.tensor_scalar(out=xt, in0=xt, scalar1=tmp2[:], scalar2=tmp3[:],
                                     op0=ALU.mult, op1=ALU.subtract))
    st6.close()

    ktmptok = [None, None]
    ktmprd = [[], []]
    ssptok = [None, None]
    tabtok = [None, None]
    tabrd = [None, None]
    c1tok = None
    c2tok = None
    kbtok = None
    vsttok = [None] * 8
    ppdma = [None, None]
    t_biaskv = None
    fence("gpsimd")
    W("gpsimd", t_rs1)
    fence("sync")
    W("tensor", p.tok("scalar"), p.tok("vector"))
    for ds in range(4):
        for kv in (1, 0):
            W("sync", t_biaskv)
            t_biaskv_l = E("sync", SY.dma_start(out=bias_kv[:], in_=b1kv_b[kv, ds]), dma=True)
            sset = kvpass % 2
            if kvpass >= 2:
                for half in range(2):
                    W("gpsimd", kvwtok[sset][half], kvwrd[sset])
                    kvwtok[sset][half] = E("gpsimd", GP.dma_start(
                        out=kvw[sset][half][:],
                        in_=w1kv[kv, ds][:, half * 4096: half * 4096 + 4096]),
                        dma=True)
            kvpass += 1
            for ti in range(8):
                b = ti % 2
                if ds < 2:
                    W("sync", tabtok[b], tabrd[b])
                    tabtok[b] = E("sync", SY.dma_start(out=cosb[b][:], in_=rcos[ds, ti]), dma=True)
                    tabtok[b] = E("sync", SY.dma_start(out=sinb[b][:], in_=rsin[ds, ti]), dma=True)
                W("tensor", kvwtok[sset][0], kvwtok[sset][1], ktmptok[b])
                tkv = None
                for ec in range(16):
                    buf = kvw[sset][ec // 8]
                    tkv = E("tensor", TE.matmul(
                        psKV[b][:], fmB[:, ec * 1024 + ti * 128: ec * 1024 + ti * 128 + 128],
                        buf[:, (ec % 8) * 512: (ec % 8) * 512 + 512],
                        start=(ec == 0), stop=(ec == 15)))
                kvwrd[sset] = tkv
                W("vector", tkv, t_biaskv_l, *ktmprd[b])
                ktmprd[b] = []
                t_kt = E("vector", VE.tensor_tensor(out=ktmp[b][:], in0=psKV[b][:],
                                                    in1=bias_kv[:], op=ALU.add))
                ktmptok[b] = t_kt
                t_biaskv = t_kt
                sb_ = b
                W("scalar", t_kt, ssptok[sb_], c1tok)
                t_sq = E("scalar", SC.activation(out=c1b[:], in_=ktmp[b][:], func=AF.Square,
                                                 accum_out=ssp[sb_][:]))
                ktmprd[b].append(t_sq)
                ss_dst = ssK if kv == 0 else ssV
                W("vector", t_sq)
                if ds == 0:
                    tss = E("vector", VE.tensor_copy(ss_dst[:, ti:ti + 1], ssp[sb_][:]))
                else:
                    tss = E("vector", VE.tensor_tensor(out=ss_dst[:, ti:ti + 1],
                                                       in0=ss_dst[:, ti:ti + 1],
                                                       in1=ssp[sb_][:], op=ALU.add))
                ssptok[sb_] = tss
                if ds < 2:
                    W("vector", t_kt, tabtok[b], c2tok, t_sq)
                    ts1 = E("vector", VE.tensor_copy(c2b[:, 0:512:2], ktmp[b][:, 1:512:2]))
                    ts2 = E("vector", VE.tensor_copy(c2b[:, 1:512:2], ktmp[b][:, 0:512:2]))
                    tc1 = E("vector", VE.tensor_tensor(out=c1b[:], in0=ktmp[b][:],
                                                       in1=cosb[b][:], op=ALU.mult))
                    c1tok = tc1
                    ktmprd[b].append(tc1)
                    tabrd[b] = tc1
                    W("vector", ts1, ts2)
                    tc2 = E("vector", VE.tensor_tensor(out=c2b[:], in0=c2b[:],
                                                       in1=sinb[b][:], op=ALU.mult))
                    c2tok = tc2
                    tabrd[b] = tc2
                    W("vector", tc1, tc2)
                    if kv == 1:
                        W("vector", vsttok[ti])
                        t_rot = E("vector", VE.tensor_tensor(
                            out=Vst[:, ti * 512:(ti + 1) * 512], in0=c1b[:], in1=c2b[:],
                            op=ALU.add))
                        vsttok[ti] = t_rot
                        c1tok = t_rot
                        c2tok = t_rot
                    else:
                        W("vector", kbtok)
                        t_kb = E("vector", VE.tensor_tensor(out=Kb[:], in0=c1b[:],
                                                            in1=c2b[:], op=ALU.add))
                        c1tok = t_kb
                        c2tok = t_kb
                        W("vector", t_kb, vsttok[ti], ppdma[b])
                        t_pp = E("vector", VE.tensor_tensor(
                            out=ppst[b][:], in0=Vst[:, ti * 512:(ti + 1) * 512],
                            in1=Kb[:], op=ALU.mult))
                        kbtok = t_pp
                        vsttok[ti] = t_pp
                        W("sync", t_pp)
                        ppdma[b] = E("sync", SY.dma_start(
                            out=ppd.ap()[ti, :, ds * 512:(ds + 1) * 512], in_=ppst[b][:]),
                            dma=True)
                else:
                    if kv == 1:
                        W("scalar", t_kt, vsttok[ti], t_sq)
                        t_cp = E("scalar", SC.copy(out=Vst[:, ti * 512:(ti + 1) * 512],
                                                   in_=ktmp[b][:]))
                        vsttok[ti] = t_cp
                        ktmprd[b].append(t_cp)
                    else:
                        W("scalar", t_kt, kbtok, t_sq)
                        t_kb = E("scalar", SC.copy(out=Kb[:], in_=ktmp[b][:]))
                        ktmprd[b].append(t_kb)
                        W("vector", t_kb, vsttok[ti], ppdma[b])
                        t_pp = E("vector", VE.tensor_tensor(
                            out=ppst[b][:], in0=Vst[:, ti * 512:(ti + 1) * 512],
                            in1=Kb[:], op=ALU.mult))
                        kbtok = t_pp
                        vsttok[ti] = t_pp
                        W("sync", t_pp)
                        ppdma[b] = E("sync", SY.dma_start(
                            out=ppd.ap()[ti, :, ds * 512:(ds + 1) * 512], in_=ppst[b][:]),
                            dma=True)
    t_pp_all = p.tok("sync")
    st7.close()

    stA2 = contextlib.ExitStack()
    ppb = [sb(f"ppb{i}", [128, 2048], BF16, st=stA2) for i in range(2)]
    psA = ps("psA", [128, 16], st=stA2)
    fence("sync")
    ppbtok = [None, None]
    scoltok = [None, None]
    tA = None
    for ti in range(8):
        b = ti % 2
        W("sync", t_pp_all, ppbtok[b])
        t_ld = E("sync", SY.dma_start(out=ppb[b][:], in_=ppd.ap()[ti]), dma=True)
        W("vector", p.tok("vector") if ti == 0 else None)
        tm = E("vector", VE.tensor_tensor(out=tmp1[:], in0=ssK[:, ti:ti + 1],
                                          in1=ssV[:, ti:ti + 1], op=ALU.mult))
        W("scalar", tm)
        tsq = E("scalar", SC.activation(out=tmp2[:], in_=tmp1[:], func=AF.Sqrt))
        W("vector", tsq)
        tc = E("vector", VE.reciprocal(tmp3[:], tmp2[:]))
        W("vector", tc, scoltok[b])
        tsc = E("vector", VE.tensor_copy(scol[:, b:b + 1], tmp3[:]))
        W("tensor", tsc, t_ld)
        for ch in range(16):
            tA = E("tensor", TE.matmul(psA[:, ch:ch + 1], ppb[b][:, ch * 128:(ch + 1) * 128],
                                       scol[:, b:b + 1], start=(ti == 0), stop=(ti == 7)))
        ppbtok[b] = tA
        scoltok[b] = tA
    W("vector", tA)
    t_Ar = E("vector", VE.tensor_copy(A_raw[:], psA[:]))
    W("sync", t_Ar)
    t_As = E("sync", SY.dma_start(out=arA_in.ap(), in_=A_raw[:]), dma=True)
    W("tensor", p.tok("vector"))
    stA2.close()

    stB2 = contextlib.ExitStack()
    Q2T = sb("Q2T", [128, 16384], BF16, st=stB2)
    st8b = contextlib.ExitStack()
    owb = [sb(f"owb{i}", [128, 2048], BF16, st=st8b) for i in range(2)]
    ytmp = [sb(f"ytmp{i}", [128, 512], F32, st=st8b) for i in range(2)]
    b1ob = [sb(f"b1ob{i}", [128, 512], F32, st=st8b) for i in range(2)]
    st8a = contextlib.ExitStack()
    qwb = [sb(f"qwb{i}", [128, 2048], BF16, st=st8a) for i in range(2)]
    psQ2 = [ps(f"psQ2{i}", [128, 512], st=st8a) for i in range(2)]
    fence("gpsimd")
    pre_q = [E("gpsimd", GP.dma_start(out=qwb[i][:], in_=w1q[i]), dma=True)
             for i in range(2)]
    W("gpsimd", pre_q[1], t_As)
    t_arc = E("gpsimd", GP.collective_compute(
        "AllReduce", ALU.add, replica_groups=RG,
        ins=[arA_in.ap()], outs=[arA_out.ap()]), cc=True)
    pre_ow = [None, None]
    qwtok = [None, None]
    q2tok = [None, None]
    q2ev = []
    for qch in range(16):
        b = qch % 2
        if qch < 2:
            t_w = pre_q[qch]
        else:
            W("gpsimd", qwtok[b], t_arc)
            t_w = E("gpsimd", GP.dma_start(out=qwb[b][:], in_=w1q[qch]), dma=True)
        for h2 in range(2):
            W("tensor", t_w, q2tok[h2])
            tq = None
            for ec in range(16):
                tq = E("tensor", TE.matmul(psQ2[h2][:], qwb[b][:, ec * 128:(ec + 1) * 128],
                                           fmB[:, ec * 1024 + h2 * 512: ec * 1024 + h2 * 512 + 512],
                                           start=(ec == 0), stop=(ec == 15)))
            qwtok[b] = tq
            W("vector", tq, t_small)
            q2tok[h2] = E("vector", VE.tensor_scalar(
                out=Q2T[:, qch * 1024 + h2 * 512: qch * 1024 + h2 * 512 + 512],
                in0=psQ2[h2][:], scalar1=b1qs[:, qch:qch + 1], scalar2=None,
                op0=ALU.add))
            q2ev.append(q2tok[h2])
    for qg in range(2):
        pre_ow[qg] = E("gpsimd", GP.dma_start(
            out=owb[qg][:], in_=w1o[0][:, qg * 2048:(qg + 1) * 2048]), dma=True)
    W("sync", t_arc)
    t_Al = E("sync", SY.dma_start(out=A_raw[:], in_=arA_out.ap()), dma=True)
    W("vector", t_Al)
    t_A1 = E("vector", VE.tensor_scalar(out=A1[:], in0=A_raw[:], scalar1=0.5, scalar2=0.5,
                                        op0=ALU.mult, op1=ALU.add))
    sc_toks = []
    for qch in range(16):
        W("vector", t_A1, q2ev[2 * qch], q2ev[2 * qch + 1])
        sc_toks.append(E("vector", VE.tensor_scalar(
            out=Q2T[:, qch * 1024:(qch + 1) * 1024], in0=Q2T[:, qch * 1024:(qch + 1) * 1024],
            scalar1=A1[:, qch:qch + 1], scalar2=None, op0=ALU.mult)))
    W("tensor", p.tok("vector"))
    st8a.close()
    psY = [ps(f"psY{i}", [128, 512], st=st8b) for i in range(8)]
    fence("sync")
    owtok = [None, None]
    btok = [None, None]
    ytok = [None] * 8
    for cs in range(4):
        cb = cs % 2
        W("sync", btok[cb])
        t_b = E("sync", SY.dma_start(out=b1ob[cb][:], in_=b1o_b[cs]), dma=True)
        for qg in range(4):
            b = qg % 2
            if cs == 0 and qg < 2:
                t_w = pre_ow[qg]
            else:
                W("gpsimd", owtok[b])
                t_w = E("gpsimd", GP.dma_start(
                    out=owb[b][:], in_=w1o[cs][:, qg * 2048:(qg + 1) * 2048]), dma=True)
            W("tensor", t_w)
            tY = None
            for ti in range(8):
                if qg == 0:
                    W("tensor", ytok[ti])
                for k in range(4):
                    qch = qg * 4 + k
                    tY = E("tensor", TE.matmul(
                        psY[ti][:],
                        Q2T[:, qch * 1024 + ti * 128: qch * 1024 + ti * 128 + 128],
                        owb[b][:, k * 512:(k + 1) * 512],
                        start=(qg == 0 and k == 0), stop=(qg == 3 and k == 3)))
            owtok[b] = tY
        t_fin = p.tok("tensor")
        for ti in range(8):
            yb = ti % 2
            W("vector", t_fin, t_b, ytok[ti])
            t_y1 = E("vector", VE.tensor_tensor(out=ytmp[yb][:], in0=psY[ti][:],
                                                in1=b1ob[cb][:], op=ALU.add))
            xsl = x_sb[:, ti * 2048 + cs * 512: ti * 2048 + cs * 512 + 512]
            W("vector", t_y1)
            ytok[ti] = E("vector", VE.tensor_tensor(out=xsl, in0=xsl, in1=ytmp[yb][:],
                                                    op=ALU.add))
        btok[cb] = ytok[7]
    W("tensor", p.tok("vector"))
    st8b.close()
    stB2.close()
    stB.close()

    st9 = contextlib.ExitStack()
    H1 = sb("H1", [128, 8192], BF16, st=st9)
    g1b = [sb(f"g1b{i}", [128, 2048], BF16, st=st9) for i in range(2)]
    u1b = [sb(f"u1b{i}", [128, 2048], BF16, st=st9) for i in range(2)]
    sgb2 = [sb(f"sgb2{i}", [128, 512], F32, st=st9) for i in range(2)]
    d1b = sb("d1b", [128, 16384], BF16, st=st9)
    psG1 = [ps(f"psGm{i}", [128, 512], st=st9) for i in range(2)]
    psU1 = [ps(f"psUm{i}", [128, 512], st=st9) for i in range(2)]
    psM = [ps(f"psM{i}", [128, 512], st=st9) for i in range(2)]
    fence("gpsimd")
    g1tok = [None, None]
    u1tok = [None, None]
    sg1tok = [None, None]
    h1tok = [None, None]
    d1tok = None
    mevtok = [None, None]
    for e in range(8):
        t_h1_all = []
        for ic in range(8):
            b = ic % 2
            W("gpsimd", g1tok[b])
            t_g = E("gpsimd", GP.dma_start(out=g1b[b][:], in_=g1[e, :, ic, :]), dma=True)
            W("gpsimd", u1tok[b])
            t_u = E("gpsimd", GP.dma_start(out=u1b[b][:], in_=u1[e, :, ic, :]), dma=True)
            for th in range(2):
                pb = (ic * 2 + th) % 2
                W("tensor", t_g, sg1tok[pb])
                tG = None
                for hc in range(16):
                    tG = E("tensor", TE.matmul(psG1[pb][:], g1b[b][:, hc * 128:(hc + 1) * 128],
                                               fmA[:, hc * 1024 + th * 512: hc * 1024 + th * 512 + 512],
                                               start=(hc == 0), stop=(hc == 15)))
                g1tok[b] = tG
                W("scalar", tG, h1tok[pb])
                t_sg = E("scalar", SC.activation(out=sgb2[pb][:], in_=psG1[pb][:], func=AF.Silu))
                sg1tok[pb] = t_sg
                W("tensor", t_u, h1tok[pb])
                tU = None
                for hc in range(16):
                    tU = E("tensor", TE.matmul(psU1[pb][:], u1b[b][:, hc * 128:(hc + 1) * 128],
                                               fmA[:, hc * 1024 + th * 512: hc * 1024 + th * 512 + 512],
                                               start=(hc == 0), stop=(hc == 15)))
                u1tok[b] = tU
                W("vector", tU, t_sg)
                th1 = E("vector", VE.tensor_tensor(
                    out=H1[:, ic * 1024 + th * 512: ic * 1024 + th * 512 + 512],
                    in0=sgb2[pb][:], in1=psU1[pb][:], op=ALU.mult))
                h1tok[pb] = th1
                t_h1_all.append(th1)
        W("gpsimd", d1tok)
        t_d = E("gpsimd", GP.dma_start(
            out=d1b[:].rearrange("p (a m) -> p a m", a=8), in_=d1[e]), dma=True)
        for ti in range(8):
            for cs in range(4):
                mb = (ti * 4 + cs) % 2
                W("tensor", t_d, mevtok[mb], *t_h1_all)
                t_h1_all = []
                tM = None
                for icl in range(8):
                    tM = E("tensor", TE.matmul(
                        psM[mb][:],
                        H1[:, icl * 1024 + ti * 128: icl * 1024 + ti * 128 + 128],
                        d1b[:, icl * 2048 + cs * 512: icl * 2048 + cs * 512 + 512],
                        start=(icl == 0), stop=(icl == 7)))
                d1tok = tM
                xsl = x_sb[:, ti * 2048 + cs * 512: ti * 2048 + cs * 512 + 512]
                W("vector", tM)
                mevtok[mb] = E("vector", VE.scalar_tensor_tensor(
                    out=xsl, in0=psM[mb][:], scalar=w_all[:, ti * 12 + e: ti * 12 + e + 1],
                    in1=xsl, op0=ALU.mult, op1=ALU.add))
    W("tensor", p.tok("vector"))
    st9.close()

    W("sync", p.tok("vector"))
    st_toks = []
    for ti in range(8):
        st_toks.append(E("sync", SY.dma_start(out=out_x[ti * 128:(ti + 1) * 128, :],
                                              in_=x_sb[:, ti * 2048:(ti + 1) * 2048]), dma=True))
    W("sync", *st_toks)
    return nc, stack


def _host_layouts(inp):
    L = {}
    L["g0"] = np.ascontiguousarray(
        inp["L0_ffn_gate_w"].reshape(44, 128, 16, 128).transpose(0, 3, 2, 1)).reshape(44, 128, 2048)
    L["u0"] = np.ascontiguousarray(
        inp["L0_ffn_up_w"].reshape(44, 128, 16, 128).transpose(0, 3, 2, 1)).reshape(44, 128, 2048)
    L["d0"] = np.ascontiguousarray(
        inp["L0_ffn_down_w"].T.reshape(44, 128, 2, 1024).transpose(2, 0, 1, 3))
    L["gs0"] = np.ascontiguousarray(inp["L0_ffn_gate_s"].reshape(44, 128).T)
    L["us0"] = np.ascontiguousarray(inp["L0_ffn_up_s"].reshape(44, 128).T)
    L["ds_b"] = np.ascontiguousarray(
        np.broadcast_to(inp["L0_ffn_down_s"].reshape(1, 2048), (128, 2048)))
    W_in = inp["L1_attn_W_in"]
    L["w1q"] = np.ascontiguousarray(
        W_in[:, :2048].reshape(16, 128, 16, 128).transpose(2, 1, 0, 3)).reshape(16, 128, 2048)
    L["w1kv"] = np.ascontiguousarray(
        W_in[:, 2048:].reshape(16, 128, 2, 4, 512).transpose(2, 3, 1, 0, 4)).reshape(2, 4, 128, 8192)
    b_in = inp["L1_attn_b_in"]
    L["b1q"] = np.ascontiguousarray(b_in[:2048].reshape(16, 128).T)
    L["b1kv_b"] = np.ascontiguousarray(
        np.broadcast_to(b_in[2048:].reshape(2, 4, 1, 512), (2, 4, 128, 512)))
    L["w1o"] = np.ascontiguousarray(
        inp["L1_attn_W_out"].reshape(16, 128, 4, 512).transpose(2, 1, 0, 3)).reshape(4, 128, 8192)
    L["b1o_b"] = np.ascontiguousarray(
        np.broadcast_to(inp["L1_attn_b_out"].reshape(4, 1, 512), (4, 128, 512)))
    L["g1a"] = np.ascontiguousarray(inp["L1_attn_gamma"].reshape(16, 128).T)
    L["be1"] = np.ascontiguousarray(inp["L1_attn_beta"].reshape(16, 128).T)
    L["clf"] = np.ascontiguousarray(
        inp["L1_ffn_classifier_w"].T.reshape(16, 128, 12).transpose(1, 0, 2)).reshape(128, 192)
    L["eb_b"] = np.ascontiguousarray(
        np.broadcast_to(inp["L1_ffn_e_score_correction_bias"].reshape(1, 12), (128, 12)))
    L["g1"] = np.ascontiguousarray(
        inp["L1_ffn_gate_w"].reshape(8, 8, 128, 16, 128).transpose(0, 4, 1, 3, 2)).reshape(8, 128, 8, 2048)
    L["u1"] = np.ascontiguousarray(
        inp["L1_ffn_up_w"].reshape(8, 8, 128, 16, 128).transpose(0, 4, 1, 3, 2)).reshape(8, 128, 8, 2048)
    L["d1"] = np.ascontiguousarray(
        inp["L1_ffn_down_w"].reshape(8, 2048, 8, 128).transpose(0, 3, 2, 1)).reshape(8, 128, 8, 2048)
    inv = (10000.0 ** (-(np.arange(0, 1024, 2, dtype=np.float64) / 1024.0)))
    pos = np.arange(4096, dtype=np.float64)
    fr = pos[:, None] * inv[None, :]
    cosr = np.repeat(np.cos(fr), 2, axis=1)
    sinr = np.repeat(np.sin(fr), 2, axis=1)
    sinr[:, 0::2] *= -1.0
    L["_cos_full"] = cosr.astype(np.float32)
    L["_sin_full"] = sinr.astype(np.float32)
    return L


def kernel(**inp):
    inp = {k: np.asarray(v) for k, v in inp.items()}
    L = _host_layouts(inp)
    x = inp["L0_attn_x"]
    Wq, bq = inp["L0_attn_Wq"], inp["L0_attn_bq"]
    Wk, bk = inp["L0_attn_Wk"], inp["L0_attn_bk"]
    Wv, bv = inp["L0_attn_Wv"], inp["L0_attn_bv"]
    shared_keys = ["g0", "u0", "d0", "gs0", "us0", "ds_b", "w1q", "w1kv", "b1q",
                   "b1kv_b", "w1o", "b1o_b", "g1a", "be1", "clf", "eb_b",
                   "g1", "u1", "d1"]
    in_maps = []
    for c in range(8):
        b, j = c // 4, c % 4
        m = {k: L[k] for k in shared_keys}
        m["x_own"] = np.ascontiguousarray(x[b, j * 1024:(j + 1) * 1024, :])
        m["wq_o"] = np.ascontiguousarray(
            Wq[4 * j:4 * j + 4].reshape(4, 16, 128, 128).transpose(2, 0, 1, 3)).reshape(128, 8192)
        m["bq_o"] = np.ascontiguousarray(bq[4 * j:4 * j + 4].T)
        m["wk_o"] = np.ascontiguousarray(
            Wk[j].reshape(16, 128, 128).transpose(1, 0, 2)).reshape(128, 2048)
        m["wv_o"] = np.ascontiguousarray(
            Wv[j].reshape(16, 128, 128).transpose(1, 0, 2)).reshape(128, 2048)
        m["bk_o"] = bk[j].reshape(128, 1).copy()
        m["bv_b"] = np.ascontiguousarray(np.broadcast_to(bv[j].reshape(1, 128), (128, 128)))
        tt = np.arange(32)[None, :]
        pp = np.arange(128)[:, None]
        m["oidx"] = ((tt // 8) * 2048 + ((tt % 4) * 128 + pp) * 4 + j).astype(np.int32)
        tsl = slice(j * 1024, (j + 1) * 1024)
        m["rcos"] = np.ascontiguousarray(
            L["_cos_full"][tsl].reshape(8, 128, 2, 512).transpose(2, 0, 1, 3))
        m["rsin"] = np.ascontiguousarray(
            L["_sin_full"][tsl].reshape(8, 128, 2, 512).transpose(2, 0, 1, 3))
        in_maps.append(m)
    nc, stack = build()
    res = run_bass_kernel_spmd(nc, in_maps, core_ids=list(range(8)))
    stack.close()
    out = np.empty((2, 4096, 2048), dtype=np.float32)
    for c in range(8):
        b, j = c // 4, c % 4
        out[b, j * 1024:(j + 1) * 1024, :] = res.results[c]["out_x"]
    return out
